# version 18
# baseline (speedup 1.0000x reference)
import math
import numpy as np
import ml_dtypes
from contextlib import ExitStack

import concourse.bass as bass
import concourse.mybir as mybir
from concourse.bass_utils import run_bass_kernel_spmd

F32 = mybir.dt.float32
BF16 = mybir.dt.bfloat16
I32 = mybir.dt.int32
U32 = mybir.dt.uint32
AF = mybir.ActivationFunctionType
ALU = mybir.AluOpType
AX = mybir.AxisListType
BF16NP = ml_dtypes.bfloat16

NCORES = 8
D = 1024
NB = 2
S = 8192
T_ALL = NB * S
TC = T_ALL // NCORES
DCH = D // 128
TWO_PI = 6.283185307179586

ENGINES = ("sync", "vector", "scalar", "tensor", "gpsimd")


class _SemPH:
    __slots__ = ("name", "phase_idx", "sw", "real", "base")

    def __init__(self, name, phase_idx):
        self.name, self.phase_idx = name, phase_idx
        self.sw = False
        self.real = None
        self.base = 0


class _QInst:
    __slots__ = ("b", "eng", "method", "args", "kwargs", "incs", "label")

    def __init__(self, b, eng, method, args, kwargs):
        self.b, self.eng, self.method = b, eng, method
        self.args, self.kwargs = args, kwargs
        self.incs = []
        self.label = b.prefix

    def then_inc(self, sem, val):
        self.b.semval[id(sem)] += val
        self.incs.append((sem, val, self.b.semval[id(sem)]))
        if (isinstance(sem, _SemPH) and self.eng == "gpsimd"
                and self.method in ("dma_start", "indirect_dma_start")):
            sem.sw = True
        return self

    def inc(self, sem, val):
        self.then_inc(sem, val)
        return self.b.semval[id(sem)]


class _QEng:
    def __init__(self, b, eng):
        self._b, self._eng = b, eng

    def __getattr__(self, name):
        b, eng = self._b, self._eng

        def call(*a, **k):
            inst = _QInst(b, eng, name, a, k)
            b.q[eng].append(inst)
            return inst

        return call


class Prog:

    def __init__(self, name):
        self.nc = bass.Bass("TRN2", target_bir_lowering=False)
        self.name = name
        self.ctx = ExitStack()
        self.q = {e: [] for e in ENGINES}
        self.semval = {}
        self._finalized = False
        for e in ENGINES:
            setattr(self, e, _QEng(self, e))
        self.ext = None
        self.prefix = ""
        self._scope = None
        self._barsem = None
        self._barn = 0
        self.inputs_seen = []
        self._phase_idx = -1
        self._all_phs = []

    def din(self, name, shape, dtype=F32):
        if self.ext is not None:
            return self.ext[name]
        self.inputs_seen.append(name)
        return self.nc.dram_tensor(name, list(shape), dtype, kind="ExternalInput").ap()

    def dout(self, name, shape, dtype=F32):
        if self.ext is not None:
            return self.ext[name]
        return self.nc.dram_tensor(name, list(shape), dtype, kind="ExternalOutput").ap()

    def dtmp(self, name, shape, dtype=F32, addr_space=None):
        kw = {"addr_space": addr_space} if addr_space else {}
        ap = self.nc.dram_tensor(self.prefix + name, list(shape), dtype,
                                 kind="Internal", **kw).ap()
        if not hasattr(self, "_dtmps"):
            self._dtmps = {}
        self._dtmps[self.prefix + name] = (ap, list(shape), dtype)
        return ap

    def sb(self, name, shape, dtype=F32):
        stack = self._scope[0] if self._scope is not None else self.ctx
        return stack.enter_context(self.nc.sbuf_tensor(self.prefix + name, list(shape), dtype))

    def ps(self, name, shape, dtype=F32):
        stack = self._scope[0] if self._scope is not None else self.ctx
        return stack.enter_context(self.nc.psum_tensor(self.prefix + name, list(shape), dtype))

    def sem(self, name):
        h = _SemPH(self.prefix + name, self._phase_idx if self._scope is not None else -1)
        self._all_phs.append(h)
        if self._scope is not None:
            self._scope[1].append(h)
        self.semval[id(h)] = 0
        return h

    def val(self, sem):
        return self.semval[id(sem)]

    def barrier(self):
        if self._barsem is None:
            self._barsem = self.ctx.enter_context(self.nc.semaphore("barsem"))
            self.semval[id(self._barsem)] = 0
        self._barn += 1
        target = 5 * self._barn
        for e in ENGINES:
            getattr(self, e).sem_inc(self._barsem, 1)
        self.semval[id(self._barsem)] += 5
        for e in ENGINES:
            getattr(self, e).wait_ge(self._barsem, target)

    from contextlib import contextmanager as _cm

    @_cm
    def phase(self, name):
        assert self._scope is None
        self.prefix = name + "_"
        self._phase_idx += 1
        self._scope = (ExitStack(), [])
        try:
            yield self
        finally:
            self.barrier()
            sems = self._scope[1]
            for h in sems:
                self.gpsimd.sem_clear(h)
            if sems:
                self.barrier()
            for e in ENGINES:
                self.q[e].append(_QInst(self, e, "__phase_end__", (), {}))
            self._scope[0].close()
            self._scope = None
            self.prefix = ""

    _AUTOSYNC = ("vector", "scalar", "gpsimd")
    _NOSYNC_METHODS = {"wait_ge", "dma_start", "indirect_dma_start", "nop", "sem_inc", "sem_clear",
                       "dma_start_transpose", "collective_compute", "trigger_dma"}

    def _alloc_sems(self):
        for ph in self._all_phs:
            if ph.phase_idx == -1:
                ph.real = self.ctx.enter_context(self.nc.semaphore(ph.name))
        nph = self._phase_idx + 1
        by_phase = [[] for _ in range(nph)]
        for ph in self._all_phs:
            if ph.phase_idx >= 0:
                by_phase[ph.phase_idx].append(ph)
        hwpool, swpool = [], []
        for phs in by_phase:
            hw_avail, sw_avail = list(hwpool), list(swpool)
            for ph in phs:
                if ph.sw:
                    if sw_avail:
                        rec = sw_avail.pop()
                    else:
                        rec = [self.ctx.enter_context(
                            self.nc.semaphore(f"swp{len(swpool)}")), 0]
                        swpool.append(rec)
                    ph.real, ph.base = rec[0], rec[1]
                    rec[1] += self.semval[id(ph)]
                else:
                    if hw_avail:
                        ph.real = hw_avail.pop()
                    else:
                        h = self.ctx.enter_context(
                            self.nc.semaphore(f"hwp{len(hwpool)}"))
                        hwpool.append(h)
                        ph.real = h

    @staticmethod
    def _rs(s):
        return s.real if isinstance(s, _SemPH) else s

    @staticmethod
    def _rw(s, v):
        return v + s.base if isinstance(s, _SemPH) else v

    def finalize(self):
        assert not self._finalized
        self._finalized = True
        selfsem = {e: self.sem(f"auto_{e}") for e in self._AUTOSYNC}
        self._alloc_sems()
        with self.nc.Block() as block:
            for eng in ENGINES:
                items = self.q[eng]
                if not items:
                    continue
                auto = selfsem.get(eng)

                def emit(engine, items=items, auto=auto):
                    auto_n = 0
                    last_edge = None
                    cur_scope = None
                    for inst in items:
                        m = inst.method
                        if m == "__phase_end__":
                            last_edge = None
                            continue
                        lbl = inst.label or "main"
                        if cur_scope is None or cur_scope[0] != lbl:
                            if cur_scope is not None:
                                self.nc.leave_named_scope(cur_scope[0], cur_scope[1], False)
                            sid, _ = self.nc.enter_named_scope(lbl, False)
                            cur_scope = (lbl, sid)
                        if m == "wait_ge":
                            s, v = inst.args
                            engine.wait_ge(self._rs(s), self._rw(s, v))
                            continue
                        if m == "sem_clear":
                            (s,) = inst.args
                            if isinstance(s, _SemPH) and s.sw:
                                continue
                            engine.sem_clear(self._rs(s))
                            continue
                        if m == "sem_inc":
                            s, v = inst.args
                            engine.sem_inc(self._rs(s), v)
                            continue
                        is_compute = auto is not None and m not in self._NOSYNC_METHODS
                        if is_compute and last_edge is not None:
                            es, ev = last_edge
                            engine.wait_ge(self._rs(es), self._rw(es, ev))
                        real = getattr(engine, m)(*inst.args, **inst.kwargs)
                        incs = list(inst.incs)
                        if is_compute and not incs:
                            auto_n += 1
                            real.then_inc(auto.real, 1)
                            last_edge = (auto, auto_n)
                        for (sem, val, tot) in incs:
                            if m == "collective_compute":
                                assert val == 1
                                real.then_inc(self._rs(sem))
                            else:
                                real.then_inc(self._rs(sem), val)
                            if is_compute:
                                last_edge = (sem, tot)
                    if cur_scope is not None:
                        self.nc.leave_named_scope(cur_scope[0], cur_scope[1], False)

                getattr(block, eng)(emit)
        self.ctx.close()
        return self.nc



def emit_consts(p):
    if getattr(p, "_consts", None) is not None:
        return p._consts
    c = {}
    c["ones"] = p.sb("c_ones", [128, 128], F32)
    c["ident"] = p.sb("c_ident", [128, 128], BF16)
    c["iota_p"] = p.sb("c_iota_p", [128, 1], F32)
    gsem = p.sem("c_g")
    vsem = p.sem("c_v")
    p.gpsimd.memset(c["ones"][:], 1.0).then_inc(gsem, 1)
    p.gpsimd.iota(c["iota_p"][:], pattern=[[0, 1]], base=0, channel_multiplier=1,
                  allow_small_or_imprecise_dtypes=True).then_inc(gsem, 1)
    tmp_stack = ExitStack()
    tmpi = tmp_stack.enter_context(p.nc.sbuf_tensor("c_tmpi", [128, 128], F32))
    p.gpsimd.iota(tmpi[:], pattern=[[-1, 128]], base=0, channel_multiplier=1,
                  allow_small_or_imprecise_dtypes=True).then_inc(gsem, 1)
    p.vector.wait_ge(gsem, 3)
    p.vector.tensor_scalar(c["ident"][:], tmpi[:], 0.0, None, op0=ALU.is_equal).then_inc(vsem, 1)
    p.tensor.wait_ge(gsem, 3)
    p.tensor.wait_ge(vsem, 1)
    p.scalar.wait_ge(gsem, 3)
    p.scalar.wait_ge(vsem, 1)
    p.gpsimd.wait_ge(vsem, 1)
    p.sync.wait_ge(vsem, 1)
    tmp_stack.close()
    c["sem"], c["val"] = vsem, 1
    p._consts = c
    return c


class LN:

    def __init__(self, p, eps=1e-5):
        self.p = p
        self.ps_sx = p.ps("lnA_sx", [1, 512], F32)
        self.ps_sq = p.ps("lnA_sq", [1, 512], F32)
        self.ps_bm = p.ps("lnA_bm", [128, 512], F32)
        self.ps_br = p.ps("lnA_br", [128, 512], F32)
        self.sq = [p.sb("lnA_sq0", [128, 512], F32), p.sb("lnA_sq1", [128, 512], F32)]
        self.mrow = p.sb("lnA_m", [1, 512], F32)
        self.vrow = p.sb("lnA_v", [1, 512], F32)
        self.rrow = p.sb("lnA_r", [1, 512], F32)
        self.eps_c = p.sb("lnA_eps", [1, 1], F32)
        self.vsem = p.sem("lnA_v_s")
        self.ssem = p.sem("lnA_s_s")
        self.tsem = p.sem("lnA_t_s")
        self.sq_free = [0, 0]
        self.reads_done = 0
        p.vector.memset(self.eps_c[:], eps).then_inc(self.vsem, 1)

    def stats(self, x_chunks, g0, w, consts):
        p = self.p
        p.tensor.wait_ge(self.vsem, self.reads_done)
        for ch in range(DCH):
            sl = x_chunks[ch][:, g0:g0 + w]
            p.tensor.matmul(self.ps_sx[0:1, 0:w], consts["ones"][:, 0:1], sl,
                            start=(ch == 0), stop=(ch == DCH - 1))
        for ch in range(DCH):
            sl = x_chunks[ch][:, g0:g0 + w]
            buf = self.sq[ch % 2]
            p.scalar.wait_ge(self.tsem, self.sq_free[ch % 2])
            sv = p.scalar.activation(out=buf[:, 0:w], in_=sl, func=AF.Square).inc(self.ssem, 1)
            p.tensor.wait_ge(self.ssem, sv)
            tv = p.tensor.matmul(self.ps_sq[0:1, 0:w], consts["ones"][:, 0:1], buf[:, 0:w],
                                 start=(ch == 0), stop=(ch == DCH - 1)).inc(self.tsem, 1)
            self.sq_free[ch % 2] = tv
        p.vector.wait_ge(self.tsem, p.val(self.tsem))
        p.vector.tensor_scalar(self.mrow[0:1, 0:w], self.ps_sx[0:1, 0:w], -1.0 / D, None, op0=ALU.mult)
        self.reads_done = p.vector.tensor_scalar(self.vrow[0:1, 0:w], self.ps_sq[0:1, 0:w], 1.0 / D,
                                                 None, op0=ALU.mult).inc(self.vsem, 1)
        p.vector.tensor_tensor(self.rrow[0:1, 0:w], self.mrow[0:1, 0:w], self.mrow[0:1, 0:w], op=ALU.mult)
        vv = p.vector.tensor_tensor(self.vrow[0:1, 0:w], self.vrow[0:1, 0:w], self.rrow[0:1, 0:w],
                                    op=ALU.subtract).inc(self.vsem, 1)
        p.scalar.wait_ge(self.vsem, vv)
        p.scalar.activation(out=self.rrow[0:1, 0:w], in_=self.vrow[0:1, 0:w], func=AF.Ln,
                            bias=self.eps_c[0:1, 0:1])
        sv = p.scalar.activation(out=self.rrow[0:1, 0:w], in_=self.rrow[0:1, 0:w], func=AF.Exp,
                                 scale=-0.5).inc(self.ssem, 1)
        p.tensor.wait_ge(self.ssem, sv)
        p.tensor.matmul(self.ps_bm[:, 0:w], consts["ones"][0:1, :], self.mrow[0:1, 0:w],
                        start=True, stop=True)
        tv = p.tensor.matmul(self.ps_br[:, 0:w], consts["ones"][0:1, :], self.rrow[0:1, 0:w],
                             start=True, stop=True).inc(self.tsem, 1)
        p.vector.wait_ge(self.tsem, tv)
        return self.ps_bm, self.ps_br


def emit_fold_lora(p, name, wT, ab, scale, psl, n_in=D, n_out=D,
                   free_scratch=False):
    r = ab.shape[0]
    nch = n_in // 128
    nhalf = (n_out + 511) // 512
    wtiles = [p.sb(f"{name}_w{ch}", [128, n_out], BF16) for ch in range(nch)]
    _scr = ExitStack()
    if free_scratch:
        _sv2 = p._scope
        p._scope = (_scr, _sv2[1] if _sv2 else [])
    ab_s = p.sb(f"{name}_ab", [r, n_in + n_out], BF16)
    wf = [p.sb(f"{name}_wf{i}", [128, n_out], F32) for i in range(2)]
    if free_scratch:
        p._scope = _sv2
    tsem = p.sem(f"{name}_ts")
    vsem = p.sem(f"{name}_vs")
    absem = p.sem(f"{name}_abs")
    wfsem = [p.sem(f"{name}_wfs0"), p.sem(f"{name}_wfs1")]
    p.gpsimd.dma_start(out=ab_s[:], in_=ab[:, :]).then_inc(absem, 16)
    wf_free = [0, 0]
    for ch in range(nch):
        i2 = ch % 2
        p.gpsimd.wait_ge(vsem, wf_free[i2])
        lv = p.gpsimd.dma_start(out=wf[i2][:], in_=wT[ch * 128:(ch + 1) * 128, :]).inc(wfsem[i2], 16)
        p.tensor.wait_ge(absem, 16)
        for h in range(nhalf):
            c0, c1 = h * 512, min((h + 1) * 512, n_out)
            tv = p.tensor.matmul(psl[:, c0:c1], ab_s[0:r, ch * 128:(ch + 1) * 128],
                                 ab_s[0:r, n_in + c0:n_in + c1], start=True, stop=True).inc(tsem, 1)
        p.vector.wait_ge(tsem, tv)
        p.vector.wait_ge(wfsem[i2], lv)
        vv = p.vector.scalar_tensor_tensor(wtiles[ch][:, :], psl[:, :], scale, wf[i2][:],
                                           op0=ALU.mult, op1=ALU.add).inc(vsem, 1)
        wf_free[i2] = vv
        p.tensor.wait_ge(vsem, vv)
    _scr.close()
    return wtiles, tsem, vsem


def emit_cast_load(p, dst, src, lsem):
    return p.gpsimd.dma_start(out=dst, in_=src).inc(lsem, 16)



def build_attn_a_pre(p=None):
    own = p is None
    if own:
        p = Prog("attn_a_pre")
    xs = p.din("xs", [D, TC], F32)
    wqT = p.din("wqT", [D, D], F32)
    wkT = p.din("wkT", [D, D], F32)
    wvT = p.din("wvT", [D, D], F32)
    abq = p.din("abq", [8, 2 * D], F32)
    abk = p.din("abk", [8, 2 * D], F32)
    abv = p.din("abv", [8, 2 * D], F32)
    biasp = p.din("biasp", [128, 3 * DCH], F32)
    q_out = p.dout("q_out", [D, TC], BF16)
    k_out = p.dout("k_out", [D, TC], BF16)
    v_out = p.dout("v_out", [D, TC], BF16)
    lnx_out = p.dout("lnx_out", [D, TC], F32)

    consts = emit_consts(p)
    lsem = p.sem("lsem")
    gld = p.sem("gld")
    stsem = p.sem("stsem")
    vsem = p.sem("vsem")
    tsem = p.sem("tsem")
    ssem = p.sem("ssem")

    xbuf = p.sb("xbuf", [128, DCH * TC], F32)
    x_ch = [xbuf[:, ch * TC:(ch + 1) * TC] for ch in range(DCH)]
    p.sync.dma_start(out=xbuf[:].rearrange("p (c t) -> p c t", c=DCH),
                     in_=xs.rearrange("(c p) t -> p c t", p=128)).then_inc(lsem, 16)
    x_loaded = p.val(lsem)

    bsem = p.sem("bsem")
    bias_s = p.sb("bias_s", [128, 3 * DCH], F32)
    p.sync.dma_start(out=bias_s[:], in_=biasp[:, :]).then_inc(bsem, 16)

    psl = p.ps("fold_ps", [128, D], F32)
    wq_t, _, wq_vs = emit_fold_lora(p, "wq", wqT, abq, 1.0 / 8.0, psl)
    wk_t, _, wk_vs = emit_fold_lora(p, "wk", wkT, abk, 1.0 / 8.0, psl)
    wv_t, _, wv_vs = emit_fold_lora(p, "wv", wvT, abv, 1.0 / 8.0, psl)
    w_ready = (wq_vs, p.val(wq_vs), wk_vs, p.val(wk_vs), wv_vs, p.val(wv_vs))

    ln = LN(p)
    xn_bf = [p.sb(f"xn{ch}", [128, TC], BF16) for ch in range(DCH)]
    atmp = p.sb("atmp", [128, 512], F32)
    lnx_f = [p.sb(f"lnx_f{i}", [128, 512], F32) for i in range(2)]
    lnx_sem = [p.sem("lnxs0"), p.sem("lnxs1")]
    lnx_free = [0, 0]
    p.tensor.wait_ge(lsem, x_loaded)
    p.scalar.wait_ge(lsem, x_loaded)

    NG = TC // 512
    psA = p.ps("projA", [128, 512], F32)
    psB = p.ps("projB", [128, 512], F32)
    proj_ps = [psA, psB]
    ps_free = [0, 0]

    for g in range(NG):
        g0 = g * 512
        bm, br = ln.stats(x_ch, g0, 512, consts)
        for ch in range(DCH):
            lf = lnx_f[ch % 2]
            p.vector.tensor_tensor(atmp[:, 0:512], x_ch[ch][:, g0:g0 + 512], bm[:, 0:512], op=ALU.add)
            p.vector.tensor_tensor(xn_bf[ch][:, g0:g0 + 512], atmp[:, 0:512], br[:, 0:512], op=ALU.mult)
            p.vector.wait_ge(lnx_sem[ch % 2], lnx_free[ch % 2])
            vv = p.vector.tensor_tensor(lf[:, 0:512], atmp[:, 0:512], br[:, 0:512],
                                        op=ALU.mult).inc(vsem, 1)
            p.sync.wait_ge(vsem, vv)
            sv = p.sync.dma_start(out=lnx_out[ch * 128:(ch + 1) * 128, g0:g0 + 512],
                                  in_=lf[:, 0:512]).inc(lnx_sem[ch % 2], 16)
            lnx_free[ch % 2] = sv

    xn_ready = p.val(vsem)

    stage = [p.sb(f"stg{i}", [128, 512], BF16) for i in range(2)]
    stage_sem = [p.sem("stgs0"), p.sem("stgs1")]
    stage_free = [0, 0]
    p.tensor.wait_ge(vsem, xn_ready)
    p.tensor.wait_ge(w_ready[0], w_ready[1])
    p.tensor.wait_ge(w_ready[2], w_ready[3])
    p.tensor.wait_ge(w_ready[4], w_ready[5])
    p.tensor.wait_ge(bsem, 16)
    si = 0
    for pi, (wt, outd) in enumerate(((wq_t, q_out), (wk_t, k_out), (wv_t, v_out))):
        for g in range(NG):
            g0 = g * 512
            for oc in range(DCH):
                pp = proj_ps[si % 2]
                p.tensor.wait_ge(vsem, ps_free[si % 2])
                for ich in range(DCH):
                    mm = p.tensor.matmul(pp[:, 0:512], wt[ich][:, oc * 128:(oc + 1) * 128],
                                         xn_bf[ich][:, g0:g0 + 512],
                                         start=(ich == 0), stop=(ich == DCH - 1))
                tv = mm.inc(tsem, 1)
                p.vector.wait_ge(tsem, tv)
                p.vector.wait_ge(stage_sem[si % 2], stage_free[si % 2])
                vv = p.vector.tensor_scalar(stage[si % 2][:, 0:512], pp[:, 0:512],
                                            bias_s[:, pi * DCH + oc:pi * DCH + oc + 1], None,
                                            op0=ALU.add).inc(vsem, 1)
                ps_free[si % 2] = vv
                p.sync.wait_ge(vsem, vv)
                sv = p.sync.dma_start(out=outd[oc * 128:(oc + 1) * 128, g0:g0 + 512],
                                      in_=stage[si % 2][:, 0:512]).inc(stage_sem[si % 2], 16)
                stage_free[si % 2] = sv
                si += 1

    for sm in (lnx_sem[0], lnx_sem[1], stage_sem[0], stage_sem[1]):
        p.sync.wait_ge(sm, p.val(sm))
    return p.finalize() if own else None



def _fm(x):
    return np.ascontiguousarray(x.T)


def _packpost(postw):
    pt = np.ascontiguousarray(postw.T)
    slabs = []
    for oc in range(8):
        for h in range(2):
            blk = pt[h * 2048:(h + 1) * 2048, oc * 128:(oc + 1) * 128]
            slabs.append(np.ascontiguousarray(
                blk.reshape(16, 128, 128).transpose(1, 0, 2).reshape(128, 2048)))
    return np.ascontiguousarray(np.concatenate(slabs, axis=0))


def _packab(A, B):
    return np.ascontiguousarray(np.concatenate([A, B.T], axis=1))


def _packbias(*biases):
    cols = []
    for b in biases:
        for ch in range(DCH):
            cols.append(b[ch * 128:(ch + 1) * 128])
    return np.ascontiguousarray(np.stack(cols, axis=1))


def prepare_state(I):
    st = dict(I)
    x = I["L0_attn_a_x"].reshape(T_ALL, D)
    st["x_fm"] = _fm(x)
    return st


def run_attn_a_pre(st):
    nc = build_attn_a_pre()
    in_maps = []
    for c in range(NCORES):
        sl = slice(c * TC, (c + 1) * TC)
        in_maps.append({
            "xs": np.ascontiguousarray(st["x_fm"][:, sl]),
            "wqT": _fm(st["L0_attn_a_Wq"]), "wkT": _fm(st["L0_attn_a_Wk"]),
            "wvT": _fm(st["L0_attn_a_Wv"]),
            "abq": _packab(st["L0_attn_a_Aq"], st["L0_attn_a_Bq"]),
            "abk": _packab(st["L0_attn_a_Ak"], st["L0_attn_a_Bk"]),
            "abv": _packab(st["L0_attn_a_Av"], st["L0_attn_a_Bv"]),
            "biasp": _packbias(st["L0_attn_a_bq"], st["L0_attn_a_bk"], st["L0_attn_a_bv"]),
        })
    res = run_bass_kernel_spmd(nc, in_maps, core_ids=list(range(NCORES)))
    for key in ("q_out", "k_out", "v_out", "lnx_out"):
        st[key[0:-4] + "_a_fm" if key != "lnx_out" else "lnx_fm"] = np.concatenate(
            [r[key] for r in res.results], axis=1)
    return st


def kernel(**inputs):
    st = prepare_state(inputs)
    st = run_attn_a_pre(st)
    raise NotImplementedError



def build_attn_a(p=None):
    own = p is None
    if own:
        p = Prog("attn_a")
    NHEAD, HD = 16, 64
    qf = p.din("qf", [D, TC], BF16)
    kf = p.din("kf", [D, S], BF16)
    vf = p.din("vf", [D, S], BF16)
    lnxs = p.din("lnxs", [D, TC], F32)
    xs = p.din("xs", [D, TC], F32)
    woT = p.din("woT", [D, D], F32)
    abo = p.din("abo", [8, 2 * D], F32)
    bop = p.din("bop", [128, DCH], F32)
    rz = p.din("rz", [1, 1], F32)
    x2_out = p.dout("x2_out", [D, TC], F32)

    consts = emit_consts(p)
    tsem = p.sem("tsem")
    vsem = p.sem("vsem")
    ssem = p.sem("ssem")
    qsem = p.sem("qsem")
    bsem = p.sem("bsem")
    rsem = p.sem("rsem")

    _fs = ExitStack()
    _sv = p._scope
    p._scope = (_fs, _sv[1] if _sv else [])
    psl = p.ps("fold_ps", [128, D], F32)
    p._scope = _sv
    wo_t, _, wo_vs = emit_fold_lora(p, "wo", woT, abo, 1.0 / 8.0, psl,
                                    free_scratch=True)
    _fs.close()
    wo_ready = p.val(wo_vs)
    p.tensor.wait_ge(wo_vs, wo_ready)
    p.sync.wait_ge(wo_vs, wo_ready)

    bo_s = p.sb("bo_s", [128, DCH], F32)
    p.sync.dma_start(out=bo_s[:], in_=bop[:, :]).then_inc(bsem, 16)
    rz_s = p.sb("rz_s", [128, 1], F32)
    p.sync.dma_start(out=rz_s[:], in_=rz[0:1, 0:1].broadcast_to([128, 1])).then_inc(rsem, 16)

    kh = [p.sb(f"kh{i}", [64, S], BF16) for i in range(2)]
    vh = [p.sb(f"vh{i}", [64, S], BF16) for i in range(2)]
    vt = [p.sb(f"vt{i}", [128, 65 * 64], BF16) for i in range(2)]
    attn = p.sb("attnb", [128, DCH * TC], BF16)
    biash = p.sb("biash", [128, 64], F32)
    biasi = p.sb("biasi", [128, 64], F32)
    pT = [p.sb(f"pT{i}", [128, 512], BF16) for i in range(4)]
    rden = p.sb("rden", [1, 512], F32)
    bcs = p.sb("bcs", [64, 512], F32)
    ps_S = [p.ps(f"psS{i}", [128, 512], F32) for i in range(4)]
    ps_pv = [p.ps(f"pspv{i}", [65, 512], F32) for i in range(2)]
    ps_vt = p.ps("psvt", [128, 64], BF16)
    ps_bc = p.ps("psbc", [64, 512], F32)
    khsem = p.sem("khsem")
    vhsem = p.sem("vhsem")

    p.gpsimd.iota(biasi[:], pattern=[[128, 64]], base=-(S - 1), channel_multiplier=1,
                  allow_small_or_imprecise_dtypes=True).then_inc(vsem, 1)
    p.vector.wait_ge(vsem, p.val(vsem))

    NQC = TC // 512
    slopes = [0.5 ** h for h in range(NHEAD)]

    kh_used = [0, 0]
    kvk = [0, 0]
    kvv = [0, 0]
    vt_rdy = [0, 0]
    ps_vt_free = 0
    pv_evac = [0, 0]
    _qs = ExitStack()
    _sv2 = p._scope
    p._scope = (_qs, _sv2[1] if _sv2 else [])
    qbuf = p.sb("qbuf", [64, 16 * TC], BF16)
    p._scope = _sv2
    p.sync.dma_start(out=qbuf[:].rearrange("p (h t) -> p h t", h=16),
                     in_=qf.rearrange("(h p) t -> p h t", p=64)).then_inc(qsem, 16)
    p.scalar.wait_ge(qsem, 16)
    p.tensor.wait_ge(qsem, 16)
    p.tensor.wait_ge(consts["sem"], consts["val"])

    def load_kv(hh):
        i2 = hh % 2
        p.sync.wait_ge(tsem, kh_used[i2])
        kvk[i2] = p.sync.dma_start(out=kh[i2][:],
                                   in_=kf[hh * 64:(hh + 1) * 64, :]).inc(khsem, 16)
        kvv[i2] = p.sync.dma_start(out=vh[i2][:],
                                   in_=vf[hh * 64:(hh + 1) * 64, :]).inc(vhsem, 16)

    def vt_step(hh, kt):
        nonlocal ps_vt_free
        i2 = hh % 2
        p.tensor.wait_ge(vsem, ps_vt_free)
        tv = p.tensor.transpose(ps_vt[:, :], vh[i2][0:64, kt * 128:(kt + 1) * 128],
                                consts["ident"][0:64, 0:64]).inc(tsem, 1)
        p.vector.wait_ge(tsem, tv)
        ev = p.vector.tensor_copy(vt[i2][:, kt * 65:kt * 65 + 64],
                                  ps_vt[:, :]).inc(vsem, 1)
        ps_vt_free = ev
        vt_rdy[i2] = ev

    load_kv(0)
    p.vector.memset(vt[0][:], 1.0).then_inc(vsem, 1)
    p.tensor.wait_ge(vhsem, kvv[0])
    for kt in range(64):
        vt_step(0, kt)

    for h in range(NHEAD):
        ch, rb = h // 2, (h % 2) * 64
        i = h % 2
        if h + 1 < NHEAD:
            load_kv(h + 1)
        p.vector.wait_ge(ssem, p.val(ssem))
        p.vector.tensor_scalar(biash[:], biasi[:], slopes[h], None, op0=ALU.mult).then_inc(vsem, 1)
        bias_ready = p.val(vsem)
        if h + 1 < NHEAD:
            p.vector.wait_ge(tsem, kh_used[(h + 1) % 2])
            p.vector.memset(vt[(h + 1) % 2][:], 1.0).then_inc(vsem, 1)
        p.tensor.wait_ge(khsem, kvk[i])
        p.tensor.wait_ge(vsem, vt_rdy[i])
        p.scalar.wait_ge(vsem, bias_ready)

        for qp in range(NQC // 2):
            exp_val = [0] * 66
            for kt in range(65):
                if kt < 64:
                    p.tensor.wait_ge(ssem, exp_val[kt - 2] if kt >= 2 else 0)
                    if kt == 0:
                        for qi in range(2):
                            p.tensor.wait_ge(vsem, pv_evac[qi])
                    for qi in range(2):
                        g0 = (qp * 2 + qi) * 512
                        mm = p.tensor.matmul(
                            ps_S[qi * 2 + kt % 2][:, :],
                            kh[i][0:64, kt * 128:(kt + 1) * 128],
                            qbuf[0:64, h * TC + g0:h * TC + g0 + 512],
                            start=True, stop=True)
                    qkv = mm.inc(tsem, 1)
                    if qp == 0 and h + 1 < NHEAD:
                        if kt == 0:
                            p.tensor.wait_ge(vhsem, kvv[(h + 1) % 2])
                        vt_step(h + 1, kt)
                    p.scalar.wait_ge(tsem, qkv)
                    for qi in range(2):
                        sv = p.scalar.activation(
                            out=pT[qi * 2 + kt % 2][:, :],
                            in_=ps_S[qi * 2 + kt % 2][:, :], func=AF.Exp,
                            bias=biash[:, kt:kt + 1], scale=0.125)
                    exp_val[kt] = sv.inc(ssem, 1)
                if kt >= 1:
                    k2 = kt - 1
                    p.tensor.wait_ge(ssem, exp_val[k2])
                    for qi in range(2):
                        pv = p.tensor.matmul(ps_pv[qi][:, :],
                                             vt[i][:, k2 * 65:(k2 + 1) * 65],
                                             pT[qi * 2 + k2 % 2][:, :],
                                             start=(k2 == 0), stop=(k2 == 63))
                    pv.inc(tsem, 1)
            for qi in range(2):
                qc = qp * 2 + qi
                g0 = qc * 512
                p.vector.wait_ge(tsem, p.val(tsem))
                p.vector.tensor_copy(rden[0:1, :], ps_pv[qi][64:65, :])
                rv = p.vector.reciprocal(rden[0:1, :], rden[0:1, :]).inc(vsem, 1)
                p.tensor.wait_ge(vsem, rv)
                bb = p.tensor.matmul(ps_bc[:, :], consts["ones"][0:1, 0:64], rden[0:1, :],
                                     start=True, stop=True).inc(tsem, 1)
                p.vector.wait_ge(tsem, bb)
                p.vector.tensor_copy(bcs[:, :], ps_bc[:, :]).then_inc(vsem, 1)
                ev = p.vector.tensor_tensor(
                    attn[rb:rb + 64, ch * TC + g0:ch * TC + g0 + 512],
                    ps_pv[qi][0:64, :], bcs[0:64, :], op=ALU.mult).inc(vsem, 1)
                pv_evac[qi] = ev
                p.tensor.wait_ge(vsem, ev)
        kh_used[i] = p.val(tsem)

    attn_ready = p.val(vsem)
    p.sync.wait_ge(tsem, p.val(tsem))
    _qs.close()

    lnx_t = [p.sb(f"lnx_t{i}", [128, 512], F32) for i in range(2)]
    xs_t = [p.sb(f"xs_t{i}", [128, 512], F32) for i in range(2)]
    lnsem = [p.sem("lns0"), p.sem("lns1")]
    xssem = [p.sem("xss0"), p.sem("xss1")]
    osem = [p.sem("oss0"), p.sem("oss1")]
    ostg = [p.sb(f"ostg{i}", [128, 512], F32) for i in range(2)]
    t1 = p.sb("t1", [128, 512], F32)
    lnx_free = [0, 0]
    xs_free = [0, 0]
    o_free = [0, 0]
    ps_pr = [ps_S[0], ps_S[1]]
    ps_free = [0, 0]
    p.tensor.wait_ge(vsem, attn_ready)
    p.tensor.wait_ge(ssem, p.val(ssem))
    p.tensor.wait_ge(wo_vs, wo_ready)
    p.tensor.wait_ge(bsem, 16)
    si = 0
    for g in range(NQC):
        g0 = g * 512
        for oc in range(DCH):
            i2 = si % 2
            p.sync.wait_ge(vsem, lnx_free[i2])
            lv = p.sync.dma_start(out=lnx_t[i2][:],
                                  in_=lnxs[oc * 128:(oc + 1) * 128, g0:g0 + 512]).inc(lnsem[i2], 16)
            xv = p.sync.dma_start(out=xs_t[i2][:],
                                  in_=xs[oc * 128:(oc + 1) * 128, g0:g0 + 512]).inc(xssem[i2], 16)
            p.tensor.wait_ge(vsem, ps_free[i2])
            for ich in range(DCH):
                mm = p.tensor.matmul(ps_pr[i2][:, :], wo_t[ich][:, oc * 128:(oc + 1) * 128],
                                     attn[:, ich * TC + g0:ich * TC + g0 + 512],
                                     start=(ich == 0), stop=(ich == DCH - 1))
            tv = mm.inc(tsem, 1)
            p.vector.wait_ge(tsem, tv)
            p.vector.wait_ge(lnsem[i2], lv)
            p.vector.wait_ge(xssem[i2], xv)
            p.vector.wait_ge(rsem, 16)
            sfree = p.vector.tensor_scalar(t1[:, :], ps_pr[i2][:, :], bo_s[:, oc:oc + 1], None,
                                           op0=ALU.add).inc(vsem, 1)
            ps_free[i2] = sfree
            p.vector.scalar_tensor_tensor(t1[:, :], t1[:, :], rz_s[:, 0:1],
                                          lnx_t[i2][:], op0=ALU.mult, op1=ALU.add)
            p.vector.wait_ge(osem[i2], o_free[i2])
            ov = p.vector.tensor_tensor(ostg[i2][:], t1[:], xs_t[i2][:], op=ALU.add).inc(vsem, 1)
            lnx_free[i2] = ov
            p.sync.wait_ge(vsem, ov)
            so = p.sync.dma_start(out=x2_out[oc * 128:(oc + 1) * 128, g0:g0 + 512],
                                  in_=ostg[i2][:]).inc(osem[i2], 16)
            o_free[i2] = so
            si += 1
    for sm in osem:
        p.sync.wait_ge(sm, p.val(sm))
    return p.finalize() if own else None


def run_attn_a(st):
    nc = build_attn_a()
    in_maps = []
    for c in range(NCORES):
        sl = slice(c * TC, (c + 1) * TC)
        b = c // (NCORES // NB)
        bsl = slice(b * S, (b + 1) * S)
        in_maps.append({
            "qf": np.ascontiguousarray(st["q_a_fm"][:, sl]),
            "kf": np.ascontiguousarray(st["k_a_fm"][:, bsl]),
            "vf": np.ascontiguousarray(st["v_a_fm"][:, bsl]),
            "lnxs": np.ascontiguousarray(st["lnx_fm"][:, sl]),
            "xs": np.ascontiguousarray(st["x_fm"][:, sl]),
            "woT": _fm(st["L0_attn_a_Wo"]),
            "abo": _packab(st["L0_attn_a_Ao"], st["L0_attn_a_Bo"]),
            "bop": _packbias(st["L0_attn_a_bo"]),
            "rz": st["L0_attn_a_rezero"].reshape(1, 1),
        })
    res = run_bass_kernel_spmd(nc, in_maps, core_ids=list(range(NCORES)))
    st["x_fm"] = np.concatenate([r["x2_out"] for r in res.results], axis=1)
    return st



def build_ffn0(p=None):
    own = p is None
    if own:
        p = Prog("ffn0")
    xs = p.din("xs", [D, TC], F32)
    wk = p.din("wk", [D, D], F32)
    abf = p.din("abf", [16, 2 * D], F32)
    bp = p.din("bp", [128, DCH], F32)
    x_out = p.dout("x_out", [D, TC], F32)

    consts = emit_consts(p)
    lsem = p.sem("lsem")
    bsem = p.sem("bsem")
    gld = p.sem("gld")
    tsem = p.sem("tsem")
    vsem = p.sem("vsem")
    gsem = p.sem("gsem")

    xbuf = p.sb("xbuf", [128, DCH * TC], F32)
    x_ch = [xbuf[:, ch * TC:(ch + 1) * TC] for ch in range(DCH)]
    p.sync.dma_start(out=xbuf[:].rearrange("p (c t) -> p c t", c=DCH),
                     in_=xs.rearrange("(c p) t -> p c t", p=128)).then_inc(lsem, 16)
    bias_s = p.sb("bias_s", [128, DCH], F32)
    p.sync.dma_start(out=bias_s[:], in_=bp[:, :]).then_inc(bsem, 16)

    wbuf = p.sb("wbuf", [128, DCH * D], F32)
    w_ch = [wbuf[:, ch * D:(ch + 1) * D] for ch in range(DCH)]
    p.gpsimd.dma_start(out=wbuf[:].rearrange("p (c t) -> p c t", c=DCH),
                       in_=wk.rearrange("(c p) t -> p c t", p=128)).then_inc(gld, 16)
    ab_s = p.sb("ab_s", [16, 2 * D], BF16)
    p.gpsimd.dma_start(out=ab_s[:], in_=abf[:, :]).then_inc(gld, 32 - 16)
    p.gpsimd.wait_ge(gld, 32)
    colmax = p.sb("colmax", [128, DCH], F32)
    onecol = p.sb("onecol", [128, 1], F32)
    gmaxr = p.sb("gmaxr", [1, 1], F32)
    gmax = p.sb("gmax", [128, 1], F32)
    psl = p.ps("fold_ps", [128, D], F32)
    identf = p.sb("identf", [128, 128], F32)
    p.vector.wait_ge(consts["sem"], consts["val"])
    p.vector.tensor_copy(identf[:], consts["ident"][:])
    p.vector.wait_ge(gld, 32)
    for ch in range(DCH):
        p.vector.tensor_reduce(colmax[:, ch:ch + 1], w_ch[ch], axis=AX.X, op=ALU.max,
                               apply_absolute_value=True)
    p.vector.tensor_reduce(onecol[:, 0:1], colmax[:, 0:DCH], axis=AX.X, op=ALU.max).then_inc(vsem, 1)
    p.tensor.wait_ge(vsem, p.val(vsem))
    tg = p.tensor.transpose(psl[0:1, 0:128], onecol[:, 0:1], identf[:, :]).inc(tsem, 1)
    p.vector.wait_ge(tsem, tg)
    p.vector.tensor_reduce(gmaxr[0:1, 0:1], psl[0:1, 0:128], axis=AX.X, op=ALU.max).then_inc(vsem, 1)
    p.tensor.wait_ge(vsem, p.val(vsem))
    tb = p.tensor.matmul(psl[0:128, 0:1], consts["ones"][0:1, :], gmaxr[0:1, 0:1],
                         start=True, stop=True).inc(tsem, 1)
    scl = p.sb("scl", [128, 1], F32)
    inv = p.sb("inv", [128, 1], F32)
    p.vector.wait_ge(tsem, tb)
    p.vector.tensor_scalar(scl[:, 0:1], psl[0:128, 0:1], 1.0 / 7.0, None, op0=ALU.mult)
    p.vector.reciprocal(inv[:, 0:1], scl[:, 0:1]).then_inc(vsem, 1)
    wt = [p.sb(f"wt{ch}", [128, D], BF16) for ch in range(DCH)]
    wi = p.sb("wi", [128, D], I32)
    wf2 = p.sb("wf2", [128, D], F32)
    wr0 = p.sb("wr0", [128, D], F32)
    wd = p.sb("wd", [128, D], F32)
    for ch in range(DCH):
        p.vector.tensor_scalar(wf2[:], w_ch[ch], inv[:, 0:1], None, op0=ALU.mult)
        p.vector.tensor_copy(wi[:], wf2[:])
        p.vector.tensor_copy(wr0[:], wi[:])
        p.vector.tensor_tensor(wd[:], wf2[:], wr0[:], op=ALU.subtract)
        p.vector.tensor_scalar(wd[:], wd[:], 0.5, None, op0=ALU.is_ge)
        p.vector.tensor_tensor(wr0[:], wr0[:], wd[:], op=ALU.add)
        p.vector.tensor_tensor(wd[:], wf2[:], wr0[:], op=ALU.subtract)
        p.vector.tensor_scalar(wd[:], wd[:], -0.5, None, op0=ALU.is_le)
        p.vector.tensor_tensor(wf2[:], wr0[:], wd[:], op=ALU.subtract)
        p.vector.tensor_scalar(wf2[:], wf2[:], -8.0, 7.0, op0=ALU.max, op1=ALU.min)
        p.vector.tensor_scalar(wf2[:], wf2[:], scl[:, 0:1], None, op0=ALU.mult).then_inc(vsem, 1)
        p.tensor.wait_ge(gld, 32)
        p.tensor.wait_ge(vsem, p.val(vsem))
        for hh in range(2):
            mm = p.tensor.matmul(psl[:, hh * 512:(hh + 1) * 512],
                                 ab_s[0:16, ch * 128:(ch + 1) * 128],
                                 ab_s[0:16, D + hh * 512:D + (hh + 1) * 512],
                                 start=True, stop=True)
        tv = mm.inc(tsem, 1)
        p.vector.wait_ge(tsem, tv)
        vv = p.vector.tensor_tensor(wt[ch][:], wf2[:], psl[:], op=ALU.add).inc(vsem, 1)
        p.tensor.wait_ge(vsem, vv)
    w_ready = p.val(vsem)

    ln = LN(p)
    xn_bf = [p.sb(f"xn{ch}", [128, TC], BF16) for ch in range(DCH)]
    atmp = p.sb("atmp", [128, 512], F32)
    p.tensor.wait_ge(lsem, 16)
    p.scalar.wait_ge(lsem, 16)
    p.vector.wait_ge(lsem, 16)
    NG = TC // 512
    for g in range(NG):
        g0 = g * 512
        bm, br = ln.stats(x_ch, g0, 512, consts)
        for ch in range(DCH):
            p.vector.tensor_tensor(atmp[:, 0:512], x_ch[ch][:, g0:g0 + 512], bm[:, 0:512], op=ALU.add)
            p.vector.tensor_tensor(xn_bf[ch][:, g0:g0 + 512], atmp[:, 0:512], br[:, 0:512],
                                   op=ALU.mult).then_inc(vsem, 1)
    xn_ready = p.val(vsem)

    ps_pr = [p.ps("prA", [128, 512], F32), p.ps("prB", [128, 512], F32)]
    ps_free = [0, 0]
    ostg = [p.sb(f"ostg{i}", [128, 512], F32) for i in range(2)]
    osem = [p.sem("oss0"), p.sem("oss1")]
    o_free = [0, 0]
    p.tensor.wait_ge(vsem, xn_ready)
    p.tensor.wait_ge(bsem, 16)
    si = 0
    for g in range(NG):
        g0 = g * 512
        for oc in range(DCH):
            i2 = si % 2
            p.tensor.wait_ge(vsem, ps_free[i2])
            for ich in range(DCH):
                mm = p.tensor.matmul(ps_pr[i2][:, :], wt[ich][:, oc * 128:(oc + 1) * 128],
                                     xn_bf[ich][:, g0:g0 + 512],
                                     start=(ich == 0), stop=(ich == DCH - 1))
            tv = mm.inc(tsem, 1)
            p.vector.wait_ge(tsem, tv)
            p.vector.wait_ge(osem[i2], o_free[i2])
            p.vector.tensor_scalar(ostg[i2][:], ps_pr[i2][:, :], bias_s[:, oc:oc + 1], None,
                                   op0=ALU.add)
            pf = p.vector.tensor_tensor(ostg[i2][:], ostg[i2][:],
                                        x_ch[oc][:, g0:g0 + 512], op=ALU.add).inc(vsem, 1)
            ps_free[i2] = pf
            p.sync.wait_ge(vsem, pf)
            so = p.sync.dma_start(out=x_out[oc * 128:(oc + 1) * 128, g0:g0 + 512],
                                  in_=ostg[i2][:]).inc(osem[i2], 16)
            o_free[i2] = so
            si += 1
    for sm in osem:
        p.sync.wait_ge(sm, p.val(sm))
    return p.finalize() if own else None


def run_ffn0(st):
    nc = build_ffn0()
    in_maps = []
    for c in range(NCORES):
        sl = slice(c * TC, (c + 1) * TC)
        in_maps.append({
            "xs": np.ascontiguousarray(st["x_fm"][:, sl]),
            "wk": st["L0_ffn_kernel"],
            "abf": np.ascontiguousarray(
                np.concatenate([st["L0_ffn_a"].T, st["L0_ffn_b"]], axis=1)),
            "bp": _packbias(st["L0_ffn_bias"]),
        })
    res = run_bass_kernel_spmd(nc, in_maps, core_ids=list(range(NCORES)))
    st["x_fm"] = np.concatenate([r["x_out"] for r in res.results], axis=1)
    return st



def build_gdn_pre(p=None):
    own = p is None
    if own:
        p = Prog("gdn_pre")
    xs = p.din("xs", [D, TC], F32)
    wq = p.din("wq", [D, D], F32)
    wk = p.din("wk", [D, D], F32)
    wv = p.din("wv", [D, D], F32)
    wba = p.din("wba", [D, 16], F32)
    lnp = p.din("lnp", [128, 2 * DCH], F32)
    y0_out = p.dout("y0_out", [D, TC], F32)
    q_out = p.dout("q_out", [D, TC], BF16)
    k_out = p.dout("k_out", [D, TC], BF16)
    v_out = p.dout("v_out", [D, TC], BF16)
    lnab_out = p.dout("lnab_out", [16, TC], F32)

    consts = emit_consts(p)
    lsem = p.sem("lsem")
    psem = p.sem("psem")
    gld = p.sem("gld")
    tsem = p.sem("tsem")
    vsem = p.sem("vsem")
    ssem = p.sem("ssem")
    stv = [p.sem(f"stv{i}") for i in range(2)]

    xbuf = p.sb("xbuf", [128, DCH * TC], F32)
    x_ch = [xbuf[:, ch * TC:(ch + 1) * TC] for ch in range(DCH)]
    p.sync.dma_start(out=xbuf[:].rearrange("p (c t) -> p c t", c=DCH),
                     in_=xs.rearrange("(c p) t -> p c t", p=128)).then_inc(lsem, 16)
    lnw = p.sb("lnw", [128, 2 * DCH], F32)
    p.sync.dma_start(out=lnw[:], in_=lnp[:, :]).then_inc(psem, 16)
    wq_s = p.sb("wq_s", [128, DCH * D], BF16)
    wk_s = p.sb("wk_s", [128, DCH * D], BF16)
    wv_s = p.sb("wv_s", [128, DCH * D], BF16)
    wba_s = p.sb("wba_s", [128, DCH * 16], BF16)
    for wsrc, wdst in ((wq, wq_s), (wk, wk_s), (wv, wv_s)):
        p.gpsimd.dma_start(out=wdst[:].rearrange("p (c t) -> p c t", c=DCH),
                           in_=wsrc.rearrange("(c p) t -> p c t", p=128)).then_inc(gld, 16)
    p.gpsimd.dma_start(out=wba_s[:].rearrange("p (c t) -> p c t", c=DCH),
                       in_=wba.rearrange("(c p) t -> p c t", p=128)).then_inc(gld, 16)
    wload = p.val(gld)

    ln = LN(p)
    y0_ch = x_ch
    n_bf = [p.sb(f"nbf{ch}", [128, TC], BF16) for ch in range(DCH)]
    atmp = p.sb("atmp", [128, 512], F32)
    y0st = [p.sb(f"y0st{i}", [128, 512], F32) for i in range(2)]
    y0_free = [0, 0]
    p.tensor.wait_ge(lsem, 16)
    p.scalar.wait_ge(lsem, 16)
    p.vector.wait_ge(lsem, 16)
    p.vector.wait_ge(psem, 16)

    NG = TC // 512
    for g in range(NG):
        g0 = g * 512
        bm, br = ln.stats(x_ch, g0, 512, consts)
        for ch in range(DCH):
            i2 = ch % 2
            p.vector.tensor_tensor(atmp[:, 0:512], x_ch[ch][:, g0:g0 + 512], bm[:, 0:512], op=ALU.add)
            p.vector.wait_ge(stv[i2], y0_free[i2])
            vv = p.vector.tensor_tensor(y0st[i2][:], atmp[:, 0:512], br[:, 0:512],
                                        op=ALU.mult).inc(vsem, 1)
            p.vector.tensor_copy(x_ch[ch][:, g0:g0 + 512], y0st[i2][:])
            p.sync.wait_ge(vsem, vv)
            sv = p.sync.dma_start(out=y0_out[ch * 128:(ch + 1) * 128, g0:g0 + 512],
                                  in_=y0st[i2][:]).inc(stv[i2], 16)
            y0_free[i2] = sv
    for g in range(NG):
        g0 = g * 512
        bm, br = ln.stats(y0_ch, g0, 512, consts)
        for ch in range(DCH):
            p.vector.tensor_tensor(atmp[:, 0:512], y0_ch[ch][:, g0:g0 + 512], bm[:, 0:512], op=ALU.add)
            p.vector.tensor_tensor(atmp[:, 0:512], atmp[:, 0:512], br[:, 0:512], op=ALU.mult)
            p.vector.tensor_scalar(n_bf[ch][:, g0:g0 + 512], atmp[:, 0:512],
                                   lnw[:, ch:ch + 1], lnw[:, DCH + ch:DCH + ch + 1],
                                   op0=ALU.mult, op1=ALU.add).then_inc(vsem, 1)
    n_ready = p.val(vsem)

    ps_pr = [p.ps("prA", [128, 512], F32), p.ps("prB", [128, 512], F32)]
    ps_ss = p.ps("ss_ps", [1, 512], F32)
    ps_bc = p.ps("bc_ps", [128, 512], F32)
    sil = p.sb("sil", [128, 512], F32)
    sq2 = p.sb("sq2", [128, 512], F32)
    ssrow = p.sb("ssrow", [1, 512], F32)
    e6 = p.sb("e6", [1, 1], F32)
    p.vector.memset(e6[:], 1e-6).then_inc(vsem, 1)
    stq = [p.sem("stq0"), p.sem("stq1")]
    qstg = [p.sb(f"qstg{i}", [128, 512], BF16) for i in range(2)]
    q_free = [0, 0]
    ps_free = [0, 0]
    sil_free = 0
    p.tensor.wait_ge(vsem, n_ready)
    p.tensor.wait_ge(gld, wload)
    si = 0
    for pi, (wt_s, outd, do_l2) in enumerate(((wq_s, q_out, True), (wk_s, k_out, True),
                                              (wv_s, v_out, False))):
        for g in range(NG):
            g0 = g * 512
            for oc in range(DCH):
                i2 = si % 2
                p.tensor.wait_ge(ssem, ps_free[i2])
                for ich in range(DCH):
                    mm = p.tensor.matmul(
                        ps_pr[i2][:, :], wt_s[:, ich * D + oc * 128:ich * D + (oc + 1) * 128],
                        n_bf[ich][:, g0:g0 + 512], start=(ich == 0), stop=(ich == DCH - 1))
                tv = mm.inc(tsem, 1)
                p.scalar.wait_ge(tsem, tv)
                p.scalar.wait_ge(vsem, sil_free)
                sv = p.scalar.activation(out=sil[:, :], in_=ps_pr[i2][:, :],
                                         func=AF.Silu).inc(ssem, 1)
                ps_free[i2] = sv
                if do_l2:
                    s2 = p.scalar.activation(out=sq2[:, :], in_=sil[:, :],
                                             func=AF.Square).inc(ssem, 1)
                    p.tensor.wait_ge(ssem, s2)
                    t2 = p.tensor.matmul(ps_ss[0:1, :], consts["ones"][:, 0:1], sq2[:, :],
                                         start=True, stop=True).inc(tsem, 1)
                    p.vector.wait_ge(tsem, t2)
                    rv = p.vector.tensor_copy(ssrow[0:1, :], ps_ss[0:1, :]).inc(vsem, 1)
                    p.scalar.wait_ge(vsem, rv)
                    p.scalar.activation(out=ssrow[0:1, :], in_=ssrow[0:1, :], func=AF.Ln,
                                        bias=e6[0:1, 0:1])
                    s3 = p.scalar.activation(out=ssrow[0:1, :], in_=ssrow[0:1, :], func=AF.Exp,
                                             scale=-0.5).inc(ssem, 1)
                    p.tensor.wait_ge(ssem, s3)
                    t3 = p.tensor.matmul(ps_bc[:, :], consts["ones"][0:1, :], ssrow[0:1, :],
                                         start=True, stop=True).inc(tsem, 1)
                    p.vector.wait_ge(tsem, t3)
                    p.vector.wait_ge(stq[i2], q_free[i2])
                    ov = p.vector.tensor_tensor(qstg[i2][:], sil[:, :], ps_bc[:, :],
                                                op=ALU.mult).inc(vsem, 1)
                else:
                    p.vector.wait_ge(stq[i2], q_free[i2])
                    p.vector.wait_ge(ssem, sv)
                    ov = p.vector.tensor_copy(qstg[i2][:], sil[:, :]).inc(vsem, 1)
                sil_free = ov
                p.sync.wait_ge(vsem, ov)
                so = p.sync.dma_start(out=outd[oc * 128:(oc + 1) * 128, g0:g0 + 512],
                                      in_=qstg[i2][:]).inc(stq[i2], 16)
                q_free[i2] = so
                si += 1

    lg = p.sb("lg", [16, 512], F32)
    ab1 = p.sb("ab1", [16, 512], F32)
    ab2 = p.sb("ab2", [16, 512], F32)
    onep = p.sb("onep", [16, 1], F32)
    abst = [p.sb(f"abst{i}", [16, 512], F32) for i in range(2)]
    stab = [p.sem("stab0"), p.sem("stab1")]
    ab_free = [0, 0]
    p.vector.memset(onep[:], 1.0).then_inc(vsem, 1)
    p.tensor.wait_ge(ssem, p.val(ssem))
    for g in range(NG):
        g0 = g * 512
        i2 = g % 2
        p.tensor.wait_ge(vsem, p.val(vsem))
        for ich in range(DCH):
            mm = p.tensor.matmul(ps_pr[0][0:16, :], wba_s[:, ich * 16:(ich + 1) * 16],
                                 n_bf[ich][:, g0:g0 + 512], start=(ich == 0),
                                 stop=(ich == DCH - 1))
        tv = mm.inc(tsem, 1)
        p.vector.wait_ge(tsem, tv)
        p.vector.tensor_copy(lg[:], ps_pr[0][0:16, :]).then_inc(vsem, 1)
        a1v = p.vector.tensor_copy(ab1[:], lg[:]).inc(vsem, 1)
        p.vector.tensor_scalar(ab2[:], lg[:], 0.0, None, op0=ALU.min).then_inc(vsem, 1)
        p.scalar.wait_ge(vsem, a1v)
        p.scalar.activation(out=ab1[:], in_=ab1[:], func=AF.Abs)
        s1 = p.scalar.activation(out=ab1[:], in_=ab1[:], func=AF.Exp, scale=-1.0)
        s2 = p.scalar.activation(out=ab1[:], in_=ab1[:], func=AF.Ln,
                                 bias=onep[0:16, 0:1]).inc(ssem, 1)
        p.vector.wait_ge(ssem, s2)
        p.vector.wait_ge(stab[i2], ab_free[i2])
        ov = p.vector.tensor_tensor(abst[i2][:], ab2[:], ab1[:], op=ALU.subtract).inc(vsem, 1)
        p.sync.wait_ge(vsem, ov)
        so = p.sync.dma_start(out=lnab_out[0:16, g0:g0 + 512], in_=abst[i2][:]).inc(stab[i2], 16)
        ab_free[i2] = so

    for sm in (stv[0], stv[1], stq[0], stq[1], stab[0], stab[1]):
        p.sync.wait_ge(sm, p.val(sm))
    return p.finalize() if own else None


def run_gdn_pre(st):
    nc = build_gdn_pre()
    in_maps = []
    for c in range(NCORES):
        sl = slice(c * TC, (c + 1) * TC)
        lnp = np.ascontiguousarray(np.concatenate(
            [_packbias(st["L0_attn_b_ln1_w"]), _packbias(st["L0_attn_b_ln1_b"])], axis=1))
        in_maps.append({
            "xs": np.ascontiguousarray(st["x_fm"][:, sl]),
            "wq": st["L0_attn_b_Wq"], "wk": st["L0_attn_b_Wk"], "wv": st["L0_attn_b_Wv"],
            "wba": np.ascontiguousarray(np.concatenate(
                [st["L0_attn_b_Wb"], st["L0_attn_b_Wa"]], axis=1)),
            "lnp": lnp,
        })
    res = run_bass_kernel_spmd(nc, in_maps, core_ids=list(range(NCORES)))
    st["gdn_y0"] = np.concatenate([r["y0_out"] for r in res.results], axis=1)
    st["gdn_q"] = np.concatenate([r["q_out"] for r in res.results], axis=1)
    st["gdn_k"] = np.concatenate([r["k_out"] for r in res.results], axis=1)
    st["gdn_v"] = np.concatenate([r["v_out"] for r in res.results], axis=1)
    st["gdn_lnab"] = np.concatenate([r["lnab_out"] for r in res.results], axis=1)
    return st



class Seq:

    def __init__(self, p):
        self.p = p
        self.t = p.sem("sq_t")
        self.v = p.sem("sq_v")
        self.s = p.sem("sq_s")

    def mm(self, *a, **k):
        p = self.p
        p.tensor.wait_ge(self.v, p.val(self.v))
        p.tensor.wait_ge(self.s, p.val(self.s))
        return p.tensor.matmul(*a, **k).inc(self.t, 1)

    def tr(self, *a, **k):
        p = self.p
        p.tensor.wait_ge(self.v, p.val(self.v))
        p.tensor.wait_ge(self.s, p.val(self.s))
        return p.tensor.transpose(*a, **k).inc(self.t, 1)

    def _pre(self, eng):
        p = self.p
        e = getattr(p, eng)
        e.wait_ge(self.t, p.val(self.t))
        if eng != "vector":
            e.wait_ge(self.v, p.val(self.v))
        if eng != "scalar":
            e.wait_ge(self.s, p.val(self.s))
        return e

    def v_(self, method, *a, **k):
        e = self._pre("vector")
        return getattr(_QEng(self.p, "vector"), method)(*a, **k).inc(self.v, 1)

    def s_(self, method, *a, **k):
        e = self._pre("scalar")
        return getattr(_QEng(self.p, "scalar"), method)(*a, **k).inc(self.s, 1)


def build_gdn_run(p=None):
    own = p is None
    if own:
        p = Prog("gdn_run")
    NCHUNK = S // 128
    ins = {}
    for i in range(2):
        ins[f"q{i}"] = p.din(f"q{i}", [128, S], BF16)
        ins[f"k{i}"] = p.din(f"k{i}", [128, S], BF16)
        ins[f"v{i}"] = p.din(f"v{i}", [128, S], BF16)
        ins[f"lnr{i}"] = p.din(f"lnr{i}", [2, S], F32)
    o_out = [p.dout(f"o{i}", [128, S], BF16) for i in range(2)]

    consts = emit_consts(p)
    sq = Seq(p)
    ld = p.sem("ld")
    sto = p.sem("sto")

    maskM = p.sb("maskM", [128, 128], F32)
    maskN = p.sb("maskN", [128, 128], F32)
    uplus = p.sb("uplus", [128, 128], F32)
    identf = p.sb("identf", [128, 128], F32)
    ii = p.sb("ii", [128, 128], F32)
    gsemc = p.sem("gsemc")
    p.gpsimd.iota(ii[:], pattern=[[1, 128]], base=0, channel_multiplier=-1,
                  allow_small_or_imprecise_dtypes=True).then_inc(gsemc, 1)
    p.vector.wait_ge(gsemc, 1)
    sq.v_("tensor_scalar", maskM[:], ii[:], 0.0, None, op0=ALU.is_ge)
    sq.v_("tensor_scalar", maskM[:], maskM[:], -1e30, None, op0=ALU.mult)
    sq.v_("tensor_scalar", maskN[:], ii[:], 0.0, None, op0=ALU.is_lt)
    sq.v_("tensor_scalar", maskN[:], maskN[:], -1e30, None, op0=ALU.mult)
    sq.v_("tensor_scalar", uplus[:], ii[:], 0.0, None, op0=ALU.is_ge)
    sq.v_("tensor_copy", identf[:], consts["ident"][:])

    qb = p.sb("qb", [128, S], BF16)
    kb = p.sb("kb", [128, S], BF16)
    vb = p.sb("vb", [128, S], BF16)
    ob = p.sb("ob", [128, S], BF16)
    lnrow = p.sb("lnrow", [2, S], F32)
    lncol = p.sb("lncol", [128, 2], F32)
    S0 = p.sb("S0", [128, 128], F32)
    S0b = p.sb("S0b", [128, 128], BF16)
    Lrow = p.sb("Lrow", [1, S], F32)
    Lcol = p.sb("Lcol", [128, 1], F32)
    r1row = p.sb("r1row", [1, 128], F32)
    r1col = p.sb("r1col", [128, 1], F32)
    eLrow = p.sb("eLrow", [1, 128], F32)
    ghrow = p.sb("ghrow", [1, 128], F32)
    ghcol = p.sb("ghcol", [128, 1], F32)
    gccol = p.sb("gccol", [128, 1], F32)
    DM = p.sb("DM", [128, 128], F32)
    D2 = p.sb("D2", [128, 128], F32)
    EM = p.sb("EM", [128, 128], F32)
    E2 = p.sb("E2", [128, 128], F32)
    Msb = p.sb("Msb", [128, 128], BF16)
    Nsb = p.sb("Nsb", [128, 128], BF16)
    P0 = p.sb("P0", [128, 128], BF16)
    Uj = [p.sb(f"Uj{j}", [128, 128], BF16) for j in range(6)]
    Pj = [p.sb(f"Pj{j}", [128, 128], BF16) for j in range(6)]
    Bsb = p.sb("Bsb", [128, 128], BF16)
    Tt = p.sb("Tt", [128, 128], BF16)
    Ktok = p.sb("Ktok", [128, 128], BF16)
    Khat = p.sb("Khat", [128, 128], BF16)
    Vtok = p.sb("Vtok", [128, 128], BF16)
    Ktl = p.sb("Ktl", [128, 128], BF16)
    Qtl = p.sb("Qtl", [128, 128], BF16)
    Rsb = p.sb("Rsb", [128, 128], BF16)
    Usb = p.sb("Usb", [128, 128], BF16)
    Osb = p.sb("Osb", [128, 128], BF16)
    psA = p.ps("psA", [128, 512], F32)[:, 0:128]
    psB = p.ps("psB", [128, 512], F32)[:, 0:128]
    psR = p.ps("psR", [128, 512], F32)[:, 0:128]
    psO = p.ps("psO", [128, 512], F32)[:, 0:128]
    psT = p.ps("psT", [128, 1024], BF16)[:, 0:128]
    psL = p.ps("psL", [128, 512], F32)[:, 0:128]

    for chain in range(2):
        p.sync.dma_start(out=qb[:], in_=ins[f"q{chain}"][:, :]).then_inc(ld, 16)
        p.sync.dma_start(out=kb[:], in_=ins[f"k{chain}"][:, :]).then_inc(ld, 16)
        p.sync.dma_start(out=vb[:], in_=ins[f"v{chain}"][:, :]).then_inc(ld, 16)
        p.sync.dma_start(out=lnrow[:], in_=ins[f"lnr{chain}"][:, :]).then_inc(ld, 16)
        ldv = p.val(ld)
        p.tensor.wait_ge(ld, ldv)
        p.vector.wait_ge(ld, ldv)
        p.scalar.wait_ge(ld, ldv)
        sq.v_("memset", S0[:], 0.0)
        sq.v_("memset", S0b[:], 0.0)

        for c in range(NCHUNK):
            sl = slice(c * 128, (c + 1) * 128)
            lac = lncol[:, 0:1]
            lbc = lncol[:, 1:2]
            sq.tr(psL[0:128, 0:2], lnrow[0:2, sl], identf[0:2, 0:2])
            sq.v_("tensor_copy", lncol[:, 1:2], psL[0:128, 0:1])
            sq.v_("tensor_copy", lncol[:, 0:1], psL[0:128, 1:2])
            sq.mm(psL[0:128, 0:1], uplus[:, :], lac, start=True, stop=True)
            sq.v_("tensor_copy", Lcol[:, 0:1], psL[0:128, 0:1])
            sq.mm(psL[0:1, 0:128], lac, uplus[:, :], start=True, stop=True)
            sq.v_("tensor_copy", Lrow[0:1, sl], psL[0:1, 0:128])
            sq.v_("tensor_tensor", r1row[0:1, :], lnrow[0:1, sl], Lrow[0:1, sl], op=ALU.subtract)
            sq.v_("tensor_tensor", r1col[:, 0:1], lbc, Lcol[:, 0:1], op=ALU.subtract)
            sq.s_("activation", out=eLrow[0:1, :], in_=Lrow[0:1, sl], func=AF.Exp)
            sq.s_("activation", out=ghrow[0:1, :], in_=r1row[0:1, :], func=AF.Exp,
                  bias=Lrow[0:1, c * 128 + 127:c * 128 + 128])
            sq.mm(psR[:, :], consts["ones"][0:1, :], eLrow[0:1, :], start=True, stop=True)
            sq.v_("tensor_tensor", Ktl[:, :], kb[:, sl], psR[:, :], op=ALU.mult)
            sq.v_("tensor_tensor", Qtl[:, :], qb[:, sl], psR[:, :], op=ALU.mult)
            sq.mm(psL[0:128, 0:1], consts["ones"][0:1, :],
                  eLrow[0:1, 127:128], start=True, stop=True)
            sq.v_("tensor_copy", gccol[:, 0:1], psL[0:128, 0:1])
            sq.tr(psL[0:128, 0:1], ghrow[0:1, :], identf[0:1, 0:1])
            sq.v_("tensor_copy", ghcol[:, 0:1], psL[0:128, 0:1])
            sq.v_("tensor_scalar", DM[:, :], maskM[:, :], Lcol[:, 0:1], None, op0=ALU.add)
            sq.mm(psR[:, :], consts["ones"][0:1, :], r1row[0:1, :], start=True, stop=True)
            sq.v_("tensor_tensor", DM[:, :], DM[:, :], psR[:, :], op=ALU.add)
            sq.s_("activation", out=EM[:, :], in_=DM[:, :], func=AF.Exp)
            sq.v_("tensor_scalar", D2[:, :], maskN[:, :], r1col[:, 0:1], None, op0=ALU.add)
            sq.mm(psR[:, :], consts["ones"][0:1, :], Lrow[0:1, sl], start=True, stop=True)
            sq.v_("tensor_tensor", D2[:, :], D2[:, :], psR[:, :], op=ALU.add)
            sq.s_("activation", out=E2[:, :], in_=D2[:, :], func=AF.Exp)
            sq.mm(psA[:, :], kb[:, sl], kb[:, sl], start=True, stop=True)
            sq.v_("tensor_tensor", Msb[:, :], EM[:, :], psA[:, :], op=ALU.mult)
            sq.mm(psB[:, :], kb[:, sl], qb[:, sl], start=True, stop=True)
            sq.v_("tensor_tensor", Nsb[:, :], E2[:, :], psB[:, :], op=ALU.mult)
            sq.tr(psT[:, :], Msb[:, :], consts["ident"][:, :])
            sq.v_("tensor_copy", P0[:, :], psT[:, :])
            prevU, prevP = Msb, P0
            for j in range(6):
                sq.mm(psA[:, :], prevP[:, :], prevU[:, :], start=True, stop=True)
                sq.v_("tensor_copy", Uj[j][:, :], psA[:, :])
                sq.tr(psT[:, :], Uj[j][:, :], consts["ident"][:, :])
                sq.v_("tensor_copy", Pj[j][:, :], psT[:, :])
                prevU, prevP = Uj[j], Pj[j]
            sq.v_("tensor_tensor", Bsb[:, :], consts["ident"][:, :], Uj[5][:, :], op=ALU.add)
            for j in (4, 3, 2, 1, 0):
                sq.mm(psA[:, :], Pj[j][:, :], Bsb[:, :], start=True, stop=True)
                sq.v_("tensor_tensor", Bsb[:, :], Bsb[:, :], psA[:, :], op=ALU.add)
            sq.mm(psA[:, :], P0[:, :], Bsb[:, :], start=True, stop=True)
            sq.v_("tensor_tensor", Bsb[:, :], Bsb[:, :], psA[:, :], op=ALU.subtract)
            sq.tr(psT[:, :], Bsb[:, :], consts["ident"][:, :])
            sq.v_("tensor_copy", Tt[:, :], psT[:, :])
            sq.tr(psT[:, :], kb[:, sl], consts["ident"][:, :])
            sq.v_("tensor_copy", Ktok[:, :], psT[:, :])
            sq.v_("tensor_scalar", Khat[:, :], Ktok[:, :], ghcol[:, 0:1], None, op0=ALU.mult)
            sq.tr(psT[:, :], vb[:, sl], consts["ident"][:, :])
            sq.v_("tensor_copy", Vtok[:, :], psT[:, :])
            sq.mm(psO[:, :], Ktl[:, :], S0b[:, :], start=True, stop=True)
            sq.v_("tensor_tensor", Rsb[:, :], Vtok[:, :], psO[:, :], op=ALU.subtract)
            sq.mm(psO[:, :], Tt[:, :], Rsb[:, :], start=True, stop=True)
            sq.v_("tensor_copy", Usb[:, :], psO[:, :])
            sq.mm(psO[:, :], Qtl[:, :], S0b[:, :], start=True, stop=False)
            sq.mm(psO[:, :], Nsb[:, :], Usb[:, :], start=False, stop=True)
            sq.v_("tensor_copy", Osb[:, :], psO[:, :])
            sq.tr(psT[:, :], Osb[:, :], consts["ident"][:, :])
            sq.v_("tensor_copy", ob[:, sl], psT[:, :])
            sq.mm(psA[:, :], Khat[:, :], Usb[:, :], start=True, stop=True)
            sq.v_("scalar_tensor_tensor", S0[:, :], S0[:, :], gccol[:, 0:1], psA[:, :],
                  op0=ALU.mult, op1=ALU.add)
            sq.v_("tensor_copy", S0b[:, :], S0[:, :])

        p.sync.wait_ge(sq.v, p.val(sq.v))
        p.sync.dma_start(out=o_out[chain][:, :], in_=ob[:]).then_inc(sto, 16)
        if chain == 0:
            p.sync.wait_ge(sq.t, p.val(sq.t))
            p.vector.wait_ge(sto, p.val(sto))
    p.sync.wait_ge(sto, p.val(sto))
    return p.finalize() if own else None


def run_gdn(st):
    nc = build_gdn_run()
    in_maps = []
    for c in range(NCORES):
        im = {}
        for i in range(2):
            chain = 2 * c + i
            b, h = chain // 8, chain % 8
            bsl = slice(b * S, (b + 1) * S)
            im[f"q{i}"] = np.ascontiguousarray(st["gdn_q"][h * 128:(h + 1) * 128, bsl])
            im[f"k{i}"] = np.ascontiguousarray(st["gdn_k"][h * 128:(h + 1) * 128, bsl])
            im[f"v{i}"] = np.ascontiguousarray(st["gdn_v"][h * 128:(h + 1) * 128, bsl])
            lnr = np.ascontiguousarray(st["gdn_lnab"][[h, 8 + h], :][:, bsl])
            im[f"lnr{i}"] = lnr
        in_maps.append(im)
    res = run_bass_kernel_spmd(nc, in_maps, core_ids=list(range(NCORES)))
    O = np.zeros((D, T_ALL), BF16NP)
    for c in range(NCORES):
        for i in range(2):
            chain = 2 * c + i
            b, h = chain // 8, chain % 8
            O[h * 128:(h + 1) * 128, b * S:(b + 1) * S] = res.results[c][f"o{i}"]
    st["gdn_o"] = O
    return st



def build_gdn_post(p=None):
    own = p is None
    if own:
        p = Prog("gdn_post")
    xs = p.din("xs", [D, TC], F32)
    y0s = p.din("y0s", [D, TC], F32)
    os_ = p.din("os", [D, TC], BF16)
    wo = p.din("wo", [D, D], F32)
    w1 = p.din("w1", [D, 4 * D], F32)
    w2 = p.din("w2", [4 * D, D], F32)
    lnp = p.din("lnp", [128, 2 * DCH], F32)
    b1p = p.din("b1p", [128, 32], F32)
    b2p = p.din("b2p", [128, DCH], F32)
    x_out = p.dout("x_out", [D, TC], F32)
    xs4_d = p.dtmp("xs4_d", [D, TC], F32)
    t_d = p.dtmp("t_d", [D, TC], F32)

    consts = emit_consts(p)
    gld = p.sem("gld")
    psem = p.sem("psem")
    tsem = p.sem("tsem")
    vsem = p.sem("vsem")
    ssem = p.sem("ssem")

    wo_s = p.sb("wo_s", [128, DCH * D], BF16)
    w1_s = p.sb("w1_s", [128, DCH * 4 * D], BF16)
    p.gpsimd.dma_start(out=wo_s[:].rearrange("p (c t) -> p c t", c=DCH),
                       in_=wo.rearrange("(c p) t -> p c t", p=128)).then_inc(gld, 16)
    p.gpsimd.dma_start(out=w1_s[:].rearrange("p (c t) -> p c t", c=DCH),
                       in_=w1.rearrange("(c p) t -> p c t", p=128)).then_inc(gld, 16)
    wld = p.val(gld)
    lnw = p.sb("lnw", [128, 2 * DCH], F32)
    b1s = p.sb("b1s", [128, 32], F32)
    b2s = p.sb("b2s", [128, DCH], F32)
    p.sync.dma_start(out=lnw[:], in_=lnp[:, :]).then_inc(psem, 16)
    p.sync.dma_start(out=b1s[:], in_=b1p[:, :]).then_inc(psem, 16)
    p.sync.dma_start(out=b2s[:], in_=b2p[:, :]).then_inc(psem, 16)
    pld = p.val(psem)

    NG = TC // 512
    lsem = p.sem("lsem")
    og = p.sb("og", [128, DCH * 512], BF16)
    yg2 = [p.sb(f"yg{i}", [128, 512], F32) for i in range(2)]
    xg32 = [p.sb(f"xg3{i}", [128, 512], F32) for i in range(2)]
    ygsem = [p.sem("ygs0"), p.sem("ygs1")]
    xgsem = [p.sem("xgs0"), p.sem("xgs1")]
    yg_free = [0, 0]
    xg_free = [0, 0]
    ps_pr = [p.ps("prA", [128, 512], F32), p.ps("prB", [128, 512], F32)]
    st4 = [p.sb(f"st4_{i}", [128, 512], F32) for i in range(2)]
    stt = [p.sb(f"stt_{i}", [128, 512], F32) for i in range(2)]
    sem4 = [p.sem("sem4_0"), p.sem("sem4_1")]
    semt = [p.sem("semt_0"), p.sem("semt_1")]
    free4 = [0, 0]
    freet = [0, 0]
    ps_free = [0, 0]
    p.tensor.wait_ge(gld, wld)
    si = 0
    for g in range(NG):
        g0 = g * 512
        p.sync.wait_ge(tsem, p.val(tsem))
        p.sync.dma_start(out=og[:].rearrange("p (c t) -> p c t", c=DCH),
                         in_=os_[:, g0:g0 + 512].rearrange("(c p) t -> p c t", p=128)
                         ).then_inc(lsem, 16)
        lv = p.val(lsem)
        p.tensor.wait_ge(lsem, lv)
        for oc in range(DCH):
            i2 = si % 2
            p.sync.wait_ge(vsem, yg_free[i2])
            yl = p.sync.dma_start(out=yg2[i2][:],
                                  in_=y0s[oc * 128:(oc + 1) * 128, g0:g0 + 512]).inc(ygsem[i2], 16)
            xl = p.sync.dma_start(out=xg32[i2][:],
                                  in_=xs[oc * 128:(oc + 1) * 128, g0:g0 + 512]).inc(xgsem[i2], 16)
            p.tensor.wait_ge(vsem, ps_free[i2])
            for ich in range(DCH):
                mm = p.tensor.matmul(ps_pr[i2][:, :],
                                     wo_s[:, ich * D + oc * 128:ich * D + (oc + 1) * 128],
                                     og[:, ich * 512:(ich + 1) * 512],
                                     start=(ich == 0), stop=(ich == DCH - 1))
            tv = mm.inc(tsem, 1)
            p.vector.wait_ge(tsem, tv)
            p.vector.wait_ge(ygsem[i2], yl)
            p.vector.wait_ge(xgsem[i2], xl)
            p.vector.wait_ge(sem4[i2], free4[i2])
            pf = p.vector.tensor_tensor(st4[i2][:], ps_pr[i2][:, :],
                                        yg2[i2][:], op=ALU.add).inc(vsem, 1)
            ps_free[i2] = pf
            p.vector.wait_ge(semt[i2], freet[i2])
            tv2 = p.vector.tensor_tensor(stt[i2][:], st4[i2][:],
                                         xg32[i2][:], op=ALU.add).inc(vsem, 1)
            yg_free[i2] = tv2
            p.sync.wait_ge(vsem, pf)
            s4 = p.sync.dma_start(out=xs4_d[oc * 128:(oc + 1) * 128, g0:g0 + 512],
                                  in_=st4[i2][:]).inc(sem4[i2], 16)
            free4[i2] = s4
            p.sync.wait_ge(vsem, tv2)
            s5 = p.sync.dma_start(out=t_d[oc * 128:(oc + 1) * 128, g0:g0 + 512],
                                  in_=stt[i2][:]).inc(semt[i2], 16)
            freet[i2] = s5
            si += 1
    for sm in (sem4[0], sem4[1], semt[0], semt[1]):
        p.sync.wait_ge(sm, p.val(sm))

    ln = LN(p)
    xg = p.sb("xg", [128, DCH * 512], F32)
    xg_ch = [xg[:, ch * 512:(ch + 1) * 512] for ch in range(DCH)]
    h2g = [p.sb(f"h2g{ch}", [128, 512], BF16) for ch in range(DCH)]
    ff1g = p.sb("ff1g", [128, 32 * 512], BF16)
    w2s = [p.sb(f"w2s{i}", [128, 32 * 128], BF16) for i in range(2)]
    w2sem = [p.sem("w2s0"), p.sem("w2s1")]
    tg = [p.sb(f"tg{i}", [128, 512], F32) for i in range(2)]
    tgsem = [p.sem("tg0"), p.sem("tg1")]
    ostg = [p.sb(f"ostg{i}", [128, 512], F32) for i in range(2)]
    osem = [p.sem("oss0"), p.sem("oss1")]
    atmp = p.sb("atmp", [128, 512], F32)
    xld = p.sem("xld")
    w2_free = [0, 0]
    tg_free = [0, 0]
    o_free = [0, 0]
    ps_free = [0, 0]
    p.vector.wait_ge(psem, pld)
    for g in range(NG):
        g0 = g * 512
        p.sync.wait_ge(tsem, p.val(tsem))
        p.sync.wait_ge(ssem, p.val(ssem))
        p.sync.wait_ge(vsem, p.val(vsem))
        xl = p.sync.dma_start(out=xg[:].rearrange("p (c t) -> p c t", c=DCH),
                              in_=xs4_d[:, g0:g0 + 512].rearrange("(c p) t -> p c t", p=128)
                              ).inc(xld, 16)
        p.tensor.wait_ge(xld, xl)
        p.scalar.wait_ge(xld, xl)
        p.vector.wait_ge(xld, xl)
        bm, br = ln.stats(xg_ch, 0, 512, consts)
        for ch in range(DCH):
            p.vector.tensor_tensor(atmp[:, 0:512], xg_ch[ch], bm[:, 0:512], op=ALU.add)
            p.vector.tensor_tensor(atmp[:, 0:512], atmp[:, 0:512], br[:, 0:512], op=ALU.mult)
            p.vector.tensor_scalar(h2g[ch][:, 0:512], atmp[:, 0:512],
                                   lnw[:, ch:ch + 1], lnw[:, DCH + ch:DCH + ch + 1],
                                   op0=ALU.mult, op1=ALU.add).then_inc(vsem, 1)
        hv = p.val(vsem)
        p.tensor.wait_ge(vsem, hv)
        for fc in range(32):
            i2 = fc % 2
            p.tensor.wait_ge(ssem, ps_free[i2])
            for ich in range(DCH):
                mm = p.tensor.matmul(ps_pr[i2][:, :],
                                     w1_s[:, ich * 4 * D + fc * 128:ich * 4 * D + (fc + 1) * 128],
                                     h2g[ich][:, 0:512], start=(ich == 0), stop=(ich == DCH - 1))
            tv = mm.inc(tsem, 1)
            p.scalar.wait_ge(tsem, tv)
            sv = p.scalar.activation(out=ff1g[:, fc * 512:(fc + 1) * 512], in_=ps_pr[i2][:, :],
                                     func=AF.Gelu, bias=b1s[:, fc:fc + 1]).inc(ssem, 1)
            ps_free[i2] = sv
        ffv = p.val(ssem)
        p.tensor.wait_ge(ssem, ffv)
        for oc in range(DCH):
            i2 = oc % 2
            p.gpsimd.wait_ge(tsem, w2_free[i2])
            wv = p.gpsimd.dma_start(
                out=w2s[i2][:].rearrange("p (c t) -> p c t", c=32),
                in_=w2[:, oc * 128:(oc + 1) * 128].rearrange("(c p) t -> p c t", p=128)
                ).inc(w2sem[i2], 16)
            p.sync.wait_ge(vsem, tg_free[i2])
            tl = p.sync.dma_start(out=tg[i2][:],
                                  in_=t_d[oc * 128:(oc + 1) * 128, g0:g0 + 512]).inc(tgsem[i2], 16)
            p.tensor.wait_ge(w2sem[i2], wv)
            p.tensor.wait_ge(vsem, ps_free[i2] if False else 0)
            pp = ps_pr[i2]
            p.tensor.wait_ge(vsem, p.val(vsem))
            for kc in range(32):
                mm = p.tensor.matmul(pp[:, :], w2s[i2][:, kc * 128:(kc + 1) * 128],
                                     ff1g[:, kc * 512:(kc + 1) * 512],
                                     start=(kc == 0), stop=(kc == 31))
            tv = mm.inc(tsem, 1)
            w2_free[i2] = tv
            p.vector.wait_ge(tsem, tv)
            p.vector.wait_ge(tgsem[i2], tl)
            p.vector.wait_ge(osem[i2], o_free[i2])
            p.vector.tensor_scalar(ostg[i2][:], pp[:, :], b2s[:, oc:oc + 1], None, op0=ALU.add)
            ov = p.vector.tensor_tensor(ostg[i2][:], ostg[i2][:], tg[i2][:], op=ALU.add).inc(vsem, 1)
            tg_free[i2] = ov
            p.sync.wait_ge(vsem, ov)
            so = p.sync.dma_start(out=x_out[oc * 128:(oc + 1) * 128, g0:g0 + 512],
                                  in_=ostg[i2][:]).inc(osem[i2], 16)
            o_free[i2] = so
    for sm in osem:
        p.sync.wait_ge(sm, p.val(sm))
    return p.finalize() if own else None


def run_gdn_post(st):
    nc = build_gdn_post()
    in_maps = []
    lnp = np.ascontiguousarray(np.concatenate(
        [_packbias(st["L0_attn_b_ln2_w"]), _packbias(st["L0_attn_b_ln2_b"])], axis=1))
    b1p = np.ascontiguousarray(np.stack(
        [st["L0_attn_b_b1"][i * 128:(i + 1) * 128] for i in range(32)], axis=1))
    for c in range(NCORES):
        sl = slice(c * TC, (c + 1) * TC)
        in_maps.append({
            "xs": np.ascontiguousarray(st["x_fm"][:, sl]),
            "y0s": np.ascontiguousarray(st["gdn_y0"][:, sl]),
            "os": np.ascontiguousarray(st["gdn_o"][:, sl]),
            "wo": st["L0_attn_b_Wo"], "w1": st["L0_attn_b_W1"], "w2": st["L0_attn_b_W2"],
            "lnp": lnp, "b1p": b1p, "b2p": _packbias(st["L0_attn_b_b2"]),
        })
    res = run_bass_kernel_spmd(nc, in_maps, core_ids=list(range(NCORES)))
    st["x_fm"] = np.concatenate([r["x_out"] for r in res.results], axis=1)
    return st


def run_gdn_all(st):
    st = run_gdn_pre(st)
    st = run_gdn(st)
    st = run_gdn_post(st)
    return st



def build_lin_pre(p=None):
    own = p is None
    if own:
        p = Prog("lin_pre")
    xs = p.din("xs", [D, TC], F32)
    wqkvT = p.din("wqkvT", [D, 3 * D], F32)
    bqp = p.din("bqp", [128, 24], F32)
    scl = p.din("scl", [1, 1], F32)
    q_out = p.dout("q_out", [D, TC], F32)
    k_out = p.dout("k_out", [D, TC], F32)
    kv_out = p.dout("kv_out", [D, TC], F32)

    consts = emit_consts(p)
    lsem = p.sem("lsem")
    bsem = p.sem("bsem")
    gld = p.sem("gld")
    tsem = p.sem("tsem")
    vsem = p.sem("vsem")
    ssem = p.sem("ssem")

    xbuf = p.sb("xbuf", [128, DCH * TC], F32)
    x_ch = [xbuf[:, ch * TC:(ch + 1) * TC] for ch in range(DCH)]
    p.sync.dma_start(out=xbuf[:].rearrange("p (c t) -> p c t", c=DCH),
                     in_=xs.rearrange("(c p) t -> p c t", p=128)).then_inc(lsem, 16)
    bias_s = p.sb("bias_s", [128, 24], F32)
    p.sync.dma_start(out=bias_s[:], in_=bqp[:, :]).then_inc(bsem, 16)
    scl_s = p.sb("scl_s", [128, 1], F32)
    p.sync.dma_start(out=scl_s[:], in_=scl[0:1, 0:1].broadcast_to([128, 1])).then_inc(bsem, 16)
    wbuf = p.sb("wbuf", [128, DCH * 3 * D], BF16)
    p.gpsimd.dma_start(out=wbuf[:].rearrange("p (c t) -> p c t", c=DCH),
                       in_=wqkvT.rearrange("(c p) t -> p c t", p=128)).then_inc(gld, 16)

    ln = LN(p)
    xn_bf = [p.sb(f"xn{ch}", [128, TC], BF16) for ch in range(DCH)]
    atmp = p.sb("atmp", [128, 512], F32)
    p.tensor.wait_ge(lsem, 16)
    p.scalar.wait_ge(lsem, 16)
    p.vector.wait_ge(lsem, 16)
    NG = TC // 512
    for g in range(NG):
        g0 = g * 512
        bm, br = ln.stats(x_ch, g0, 512, consts)
        for ch in range(DCH):
            p.vector.tensor_tensor(atmp[:, 0:512], x_ch[ch][:, g0:g0 + 512], bm[:, 0:512], op=ALU.add)
            p.vector.tensor_tensor(xn_bf[ch][:, g0:g0 + 512], atmp[:, 0:512], br[:, 0:512],
                                   op=ALU.mult).then_inc(vsem, 1)
    xn_ready = p.val(vsem)

    ps_pr = [p.ps("prA", [128, 512], F32), p.ps("prB", [128, 512], F32)]
    ps_free = [0, 0]
    tq = p.sb("tq", [128, 512], F32)
    te = p.sb("te", [128, 512], F32)
    kkeep = p.sb("kkeep", [128, 512], F32)
    ostg = [p.sb(f"lostg{i}", [128, 512], F32) for i in range(3)]
    osem = [p.sem(f"loss{i}") for i in range(3)]
    o_free = [0, 0, 0]
    p.tensor.wait_ge(vsem, xn_ready)
    p.tensor.wait_ge(gld, 16)
    p.tensor.wait_ge(bsem, 32)
    si = 0

    def proj(oc, pp):
        for ich in range(DCH):
            mm = p.tensor.matmul(pp[:, :], wbuf[:, ich * 3 * D + oc * 128:ich * 3 * D + (oc + 1) * 128],
                                 xn_bf[ich][:, g0:g0 + 512], start=(ich == 0), stop=(ich == DCH - 1))
        return mm.inc(tsem, 1)

    def elu1p(pp, oc, dst):
        p.vector.tensor_scalar(tq[:, :], pp[:, :], bias_s[:, oc:oc + 1], scl_s[:, 0:1],
                               op0=ALU.add, op1=ALU.mult)
        fv = p.vector.tensor_scalar(te[:, :], tq[:, :], 0.0, None, op0=ALU.min).inc(vsem, 1)
        p.scalar.wait_ge(vsem, fv)
        sv = p.scalar.activation(out=te[:, :], in_=te[:, :], func=AF.Exp).inc(ssem, 1)
        p.vector.tensor_scalar(tq[:, :], tq[:, :], 0.0, None, op0=ALU.max)
        p.vector.wait_ge(ssem, sv)
        return p.vector.tensor_tensor(dst, tq[:, :], te[:, :], op=ALU.add).inc(vsem, 1)

    for g in range(NG):
        g0 = g * 512
        for j in range(DCH):
            i2 = si % 2
            p.tensor.wait_ge(vsem, ps_free[i2])
            tv = proj(j, ps_pr[i2])
            p.vector.wait_ge(tsem, tv)
            p.vector.wait_ge(osem[0], o_free[0])
            ev = elu1p(ps_pr[i2], j, ostg[0][:, :])
            ps_free[i2] = ev
            p.sync.wait_ge(vsem, ev)
            o_free[0] = p.sync.dma_start(out=q_out[j * 128:(j + 1) * 128, g0:g0 + 512],
                                         in_=ostg[0][:]).inc(osem[0], 16)
            si += 1
        for j in range(DCH):
            i2 = si % 2
            p.tensor.wait_ge(vsem, ps_free[i2])
            tv = proj(8 + j, ps_pr[i2])
            p.vector.wait_ge(tsem, tv)
            ev = elu1p(ps_pr[i2], 8 + j, kkeep[:, :])
            ps_free[i2] = ev
            p.vector.wait_ge(osem[1], o_free[1])
            kv1 = p.vector.tensor_copy(ostg[1][:], kkeep[:]).inc(vsem, 1)
            p.sync.wait_ge(vsem, kv1)
            o_free[1] = p.sync.dma_start(out=k_out[j * 128:(j + 1) * 128, g0:g0 + 512],
                                         in_=ostg[1][:]).inc(osem[1], 16)
            si += 1
            i2 = si % 2
            p.tensor.wait_ge(vsem, ps_free[i2])
            tv = proj(16 + j, ps_pr[i2])
            p.vector.wait_ge(tsem, tv)
            p.vector.tensor_scalar(tq[:, :], ps_pr[i2][:, :], bias_s[:, 16 + j:17 + j], None,
                                   op0=ALU.add)
            p.vector.wait_ge(osem[2], o_free[2])
            ev2 = p.vector.tensor_tensor(ostg[2][:], kkeep[:], tq[:], op=ALU.mult).inc(vsem, 1)
            ps_free[i2] = ev2
            p.sync.wait_ge(vsem, ev2)
            o_free[2] = p.sync.dma_start(out=kv_out[j * 128:(j + 1) * 128, g0:g0 + 512],
                                         in_=ostg[2][:]).inc(osem[2], 16)
            si += 1
    for sm in osem:
        p.sync.wait_ge(sm, p.val(sm))
    return p.finalize() if own else None


def run_lin_pre(st):
    nc = build_lin_pre()
    in_maps = []
    for c in range(NCORES):
        sl = slice(c * TC, (c + 1) * TC)
        in_maps.append({
            "xs": np.ascontiguousarray(st["x_fm"][:, sl]),
            "wqkvT": _fm(st["L1_attn_a_Wqkv"]),
            "bqp": np.ascontiguousarray(np.stack(
                [st["L1_attn_a_bqkv"][i * 128:(i + 1) * 128] for i in range(24)], axis=1)),
            "scl": st["L1_attn_a_scale"].reshape(1, 1),
        })
    res = run_bass_kernel_spmd(nc, in_maps, core_ids=list(range(NCORES)))
    st["lin_q"] = np.concatenate([r["q_out"] for r in res.results], axis=1)
    st["lin_k"] = np.concatenate([r["k_out"] for r in res.results], axis=1)
    st["lin_kv"] = np.concatenate([r["kv_out"] for r in res.results], axis=1)
    return st



def build_lin_run(p=None):
    own = p is None
    if own:
        p = Prog("lin_run")
    qi = [p.din(f"qi{i}", [128, S], F32) for i in range(2)]
    ki = [p.din(f"ki{i}", [128, S], F32) for i in range(2)]
    kvi = [p.din(f"kvi{i}", [128, S], F32) for i in range(2)]
    y_out = [p.dout(f"y{i}", [128, S], BF16) for i in range(2)]

    consts = emit_consts(p)
    ldk = p.sem("ldk")
    ldkv = p.sem("ldkv")
    ldq = p.sem("ldq")
    sto = p.sem("sto")
    tsem = p.sem("tsem")
    vsem = p.sem("vsem")

    bufA = p.sb("bufA", [128, S], F32)
    bufB = p.sb("bufB", [128, S], F32)
    bufC = p.sb("bufC", [128, S], F32)
    bufD = p.sb("bufD", [128, S], F32)
    qb = p.sb("qb2", [128, S], F32)
    yb = p.sb("yb", [128, S], BF16)
    prod = p.sb("prod", [128, 512], F32)
    dr = p.sb("dr", [1, 512], F32)
    bc64 = p.sb("bc64", [128, 512], F32)
    ps_d = p.ps("ps_d", [1, 512], F32)
    ps_b = p.ps("ps_b", [128, 512], F32)

    gsem = p.sem("gcum")

    def cumsum(eng, sem, src, other):
        cur, nxt = src, other
        for s in [1 << i for i in range(13)]:
            eng.tensor_tensor(nxt[:, s:S], cur[:, s:S], cur[:, 0:S - s], op=ALU.add).then_inc(sem, 1)
            eng.tensor_copy(nxt[:, 0:s], cur[:, 0:s]).then_inc(sem, 1)
            cur, nxt = nxt, cur
        return cur

    for i in range(2):
        p.sync.wait_ge(tsem, p.val(tsem))
        p.sync.wait_ge(vsem, p.val(vsem))
        p.sync.wait_ge(gsem, p.val(gsem))
        p.sync.dma_start(out=bufA[:], in_=ki[i][:, :]).then_inc(ldk, 16)
        p.sync.dma_start(out=bufC[:], in_=kvi[i][:, :]).then_inc(ldkv, 16)
        p.vector.wait_ge(ldk, p.val(ldk))
        kc = cumsum(p.vector, vsem, bufA, bufB)
        p.gpsimd.wait_ge(ldkv, p.val(ldkv))
        kvc = cumsum(p.gpsimd, gsem, bufC, bufD)
        p.sync.dma_start(out=qb[:], in_=qi[i][:, :]).then_inc(ldq, 16)
        p.vector.wait_ge(ldq, p.val(ldq))
        p.vector.wait_ge(gsem, p.val(gsem))
        for g in range(S // 512):
            g0 = g * 512
            for hh in range(2):
                rb = hh * 64
                dv = p.vector.tensor_tensor(prod[rb:rb + 64, :], qb[rb:rb + 64, g0:g0 + 512],
                                            kc[rb:rb + 64, g0:g0 + 512], op=ALU.mult).inc(vsem, 1)
                p.tensor.wait_ge(vsem, dv)
                tv = p.tensor.matmul(ps_d[0:1, :], consts["ones"][rb:rb + 64, 0:1],
                                     prod[rb:rb + 64, :], start=True, stop=True).inc(tsem, 1)
                p.vector.wait_ge(tsem, tv)
                p.vector.tensor_scalar(dr[0:1, :], ps_d[0:1, :], 1e-3, None, op0=ALU.max)
                rv = p.vector.reciprocal(dr[0:1, :], dr[0:1, :]).inc(vsem, 1)
                p.tensor.wait_ge(vsem, rv)
                tb = p.tensor.matmul(ps_b[:, :], consts["ones"][0:1, 0:128], dr[0:1, :],
                                     start=True, stop=True).inc(tsem, 1)
                p.vector.wait_ge(tsem, tb)
                p.vector.tensor_tensor(prod[rb:rb + 64, :], qb[rb:rb + 64, g0:g0 + 512],
                                       kvc[rb:rb + 64, g0:g0 + 512], op=ALU.mult)
                yv = p.vector.tensor_tensor(yb[rb:rb + 64, g0:g0 + 512], prod[rb:rb + 64, :],
                                            ps_b[rb:rb + 64, :], op=ALU.mult).inc(vsem, 1)
                p.tensor.wait_ge(vsem, yv)
        yv = p.val(vsem)
        p.sync.wait_ge(vsem, yv)
        p.sync.dma_start(out=y_out[i][:, :], in_=yb[:]).then_inc(sto, 16)
        p.vector.wait_ge(sto, p.val(sto))
    p.sync.wait_ge(sto, p.val(sto))
    return p.finalize() if own else None


def run_lin_run(st):
    nc = build_lin_run()
    in_maps = []
    for c in range(NCORES):
        im = {}
        for i in range(2):
            grp = 4 * c + 2 * i
            b = grp // 16
            h0 = grp % 16
            bsl = slice(b * S, (b + 1) * S)
            rows = slice(h0 * 64, (h0 + 2) * 64)
            im[f"qi{i}"] = np.ascontiguousarray(st["lin_q"][rows, bsl])
            im[f"ki{i}"] = np.ascontiguousarray(st["lin_k"][rows, bsl])
            im[f"kvi{i}"] = np.ascontiguousarray(st["lin_kv"][rows, bsl])
        in_maps.append(im)
    res = run_bass_kernel_spmd(nc, in_maps, core_ids=list(range(NCORES)))
    Y = np.zeros((D, T_ALL), BF16NP)
    for c in range(NCORES):
        for i in range(2):
            grp = 4 * c + 2 * i
            b = grp // 16
            h0 = grp % 16
            Y[h0 * 64:(h0 + 2) * 64, b * S:(b + 1) * S] = res.results[c][f"y{i}"]
    st["lin_y"] = Y
    return st



def build_lin_post(p=None):
    own = p is None
    if own:
        p = Prog("lin_post")
    xs = p.din("xs", [D, TC], F32)
    ys = p.din("ys", [D, TC], BF16)
    wpT = p.din("wpT", [D, D], F32)
    bpp = p.din("bpp", [128, DCH], F32)
    rtT = p.din("rtT", [D, 8], F32)
    x_out = p.dout("x_out", [D, TC], F32)
    htm_out = p.dout("htm_out", [TC, D], BF16)
    lg_out = p.dout("lg_out", [TC, 8], F32)

    consts = emit_consts(p)
    lsem = p.sem("lsem")
    bsem = p.sem("bsem")
    gld = p.sem("gld")
    tsem = p.sem("tsem")
    vsem = p.sem("vsem")
    ssem = p.sem("ssem")

    xbuf = p.sb("xbuf", [128, DCH * TC], F32)
    x_ch = [xbuf[:, ch * TC:(ch + 1) * TC] for ch in range(DCH)]
    ybuf = p.sb("ybuf", [128, DCH * TC], BF16)
    y_ch = [ybuf[:, ch * TC:(ch + 1) * TC] for ch in range(DCH)]
    p.sync.dma_start(out=xbuf[:].rearrange("p (c t) -> p c t", c=DCH),
                     in_=xs.rearrange("(c p) t -> p c t", p=128)).then_inc(lsem, 16)
    p.sync.dma_start(out=ybuf[:].rearrange("p (c t) -> p c t", c=DCH),
                     in_=ys.rearrange("(c p) t -> p c t", p=128)).then_inc(gld, 16)
    bias_s = p.sb("bias_s", [128, DCH], F32)
    p.sync.dma_start(out=bias_s[:], in_=bpp[:, :]).then_inc(bsem, 16)
    rt_s = p.sb("rt_s", [128, DCH * 8], F32)
    p.sync.dma_start(out=rt_s[:].rearrange("p (c t) -> p c t", c=DCH),
                     in_=rtT.rearrange("(c p) t -> p c t", p=128)).then_inc(bsem, 16)
    wpbuf = p.sb("wpbuf", [128, DCH * D], BF16)
    wp_s = [wpbuf[:, i * D:(i + 1) * D] for i in range(DCH)]
    p.gpsimd.dma_start(out=wpbuf[:].rearrange("p (c t) -> p c t", c=DCH),
                       in_=wpT.rearrange("(c p) t -> p c t", p=128)).then_inc(gld, 16)
    wld = p.val(gld)

    ps_pr = [p.ps("prA", [128, 512], F32), p.ps("prB", [128, 512], F32)]
    ps_free = [0, 0]
    ostg = [p.sb(f"ostg{i}", [128, 512], F32) for i in range(2)]
    osem = [p.sem("oss0"), p.sem("oss1")]
    o_free = [0, 0]
    NG = TC // 512
    p.tensor.wait_ge(lsem, 16)
    p.tensor.wait_ge(gld, wld)
    p.tensor.wait_ge(bsem, 32)
    si = 0
    for g in range(NG):
        g0 = g * 512
        for oc in range(DCH):
            i2 = si % 2
            p.tensor.wait_ge(vsem, ps_free[i2])
            for ich in range(DCH):
                mm = p.tensor.matmul(ps_pr[i2][:, :], wp_s[ich][:, oc * 128:(oc + 1) * 128],
                                     y_ch[ich][:, g0:g0 + 512], start=(ich == 0),
                                     stop=(ich == DCH - 1))
            tv = mm.inc(tsem, 1)
            p.vector.wait_ge(tsem, tv)
            p.vector.tensor_scalar(ostg[0][:], ps_pr[i2][:, :], bias_s[:, oc:oc + 1], None,
                                   op0=ALU.add)
            pf = p.vector.tensor_tensor(x_ch[oc][:, g0:g0 + 512], ostg[0][:],
                                        x_ch[oc][:, g0:g0 + 512], op=ALU.add).inc(vsem, 1)
            ps_free[i2] = pf
            p.sync.wait_ge(vsem, pf)
            p.sync.dma_start(out=x_out[oc * 128:(oc + 1) * 128, g0:g0 + 512],
                             in_=x_ch[oc][:, g0:g0 + 512]).then_inc(osem[0], 16)
            si += 1

    ln = LN(p)
    hf = [p.sb(f"hf{ch}", [128, 512], F32) for ch in range(DCH)]
    hb = [p.sb(f"hb{ch}", [128, 512], BF16) for ch in range(DCH)]
    atmp = p.sb("atmp", [128, 512], F32)
    stg_tm = [p.sb(f"stgtm{i}", [128, D], BF16) for i in range(2)]
    stg_lg = [p.sb(f"stglg{i}", [128, 8], F32) for i in range(2)]
    tmsem = [p.sem("tms0"), p.sem("tms1")]
    lgsem = [p.sem("lgs0"), p.sem("lgs1")]
    ps_t = p.ps("ps_t", [128, 512], BF16)
    ps_l = p.ps("ps_l", [128, 512], F32)
    tm_free = [0, 0]
    lg_free = [0, 0]
    for g in range(NG):
        g0 = g * 512
        p.tensor.wait_ge(vsem, p.val(vsem))
        p.scalar.wait_ge(vsem, p.val(vsem))
        bm, br = ln.stats(x_ch, g0, 512, consts)
        for ch in range(DCH):
            p.vector.tensor_tensor(atmp[:, 0:512], x_ch[ch][:, g0:g0 + 512], bm[:, 0:512], op=ALU.add)
            p.vector.tensor_tensor(hf[ch][:, 0:512], atmp[:, 0:512], br[:, 0:512], op=ALU.mult)
            p.vector.tensor_copy(hb[ch][:, 0:512], hf[ch][:, 0:512]).then_inc(vsem, 1)
        hv = p.val(vsem)
        p.tensor.wait_ge(vsem, hv)
        for j in range(4):
            i2 = j % 2
            for ich in range(DCH):
                mm = p.tensor.matmul(ps_l[0:128, 0:8], hf[ich][:, j * 128:(j + 1) * 128],
                                     rt_s[:, ich * 8:(ich + 1) * 8],
                                     start=(ich == 0), stop=(ich == DCH - 1))
            tv = mm.inc(tsem, 1)
            p.vector.wait_ge(tsem, tv)
            p.vector.wait_ge(lgsem[i2], lg_free[i2])
            lgv = p.vector.tensor_copy(stg_lg[i2][:], ps_l[0:128, 0:8]).inc(vsem, 1)
            p.sync.wait_ge(vsem, lgv)
            lg_free[i2] = p.sync.dma_start(out=lg_out[g0 + j * 128:g0 + (j + 1) * 128, :],
                                           in_=stg_lg[i2][:]).inc(lgsem[i2], 16)
            p.vector.wait_ge(tmsem[i2], tm_free[i2])
            for ich in range(DCH):
                tt = p.tensor.transpose(ps_t[:, 0:128], hb[ich][:, j * 128:(j + 1) * 128],
                                        consts["ident"][:, :]).inc(tsem, 1)
                p.vector.wait_ge(tsem, tt)
                ev = p.vector.tensor_copy(stg_tm[i2][:, ich * 128:(ich + 1) * 128],
                                          ps_t[:, 0:128]).inc(vsem, 1)
                p.tensor.wait_ge(vsem, ev)
            p.sync.wait_ge(vsem, p.val(vsem))
            tm_free[i2] = p.sync.dma_start(out=htm_out[g0 + j * 128:g0 + (j + 1) * 128, :],
                                           in_=stg_tm[i2][:]).inc(tmsem[i2], 16)
    p.sync.wait_ge(osem[0], p.val(osem[0]))
    for sm in (tmsem[0], tmsem[1], lgsem[0], lgsem[1]):
        p.sync.wait_ge(sm, p.val(sm))
    return p.finalize() if own else None


def run_lin_post(st):
    nc = build_lin_post()
    in_maps = []
    for c in range(NCORES):
        sl = slice(c * TC, (c + 1) * TC)
        in_maps.append({
            "xs": np.ascontiguousarray(st["x_fm"][:, sl]),
            "ys": np.ascontiguousarray(st["lin_y"][:, sl]),
            "wpT": _fm(st["L1_attn_a_Wproj"]),
            "bpp": _packbias(st["L1_attn_a_bproj"]),
            "rtT": _fm(st["L1_ffn_router_weight"]),
        })
    res = run_bass_kernel_spmd(nc, in_maps, core_ids=list(range(NCORES)))
    st["x_fm"] = np.concatenate([r["x_out"] for r in res.results], axis=1)
    st["moe_h_tm"] = np.concatenate([r["htm_out"] for r in res.results], axis=0)
    st["moe_lg"] = np.concatenate([r["lg_out"] for r in res.results], axis=0)
    return st


def run_lin(st):
    st = run_lin_pre(st)
    st = run_lin_run(st)
    st = run_lin_post(st)
    return st



MOE_CAP = 5120
MOE_EPS = 1.1920929e-07
NEXP = 8
NTILE = T_ALL // 128


def emit_routing(p, lgbuf, consts, sq):
    r = {}
    NT = NTILE
    lg3 = lgbuf[:].rearrange("p (t e) -> p t e", e=8)
    m1 = p.sb("rt_m1", [128, NT], F32)
    m2 = p.sb("rt_m2", [128, NT], F32)
    m1e = p.sb("rt_m1e", [128, NT * 8], F32)
    msk = p.sb("rt_msk", [128, NT * 8], F32)
    lgm = p.sb("rt_lgm", [128, NT * 8], F32)
    w1 = p.sb("rt_w1", [128, NT], F32)
    w2 = p.sb("rt_w2", [128, NT], F32)
    eid1 = p.sb("rt_eid1", [128, NT], F32)
    eid2 = p.sb("rt_eid2", [128, NT], F32)
    tmp1 = p.sb("rt_tmp1", [128, NT], F32)
    sq.v_("tensor_reduce", m1[:], lg3, axis=AX.X, op=ALU.max)
    for j in range(8):
        sq.v_("tensor_copy", m1e[:, j::8], m1[:])
    sq.v_("tensor_tensor", msk[:], lgbuf[:], m1e[:], op=ALU.is_ge)
    sq.v_("scalar_tensor_tensor", lgm[:], msk[:], -1e30, lgbuf[:], op0=ALU.mult, op1=ALU.add)
    sq.v_("tensor_reduce", m2[:], lgm[:].rearrange("p (t e) -> p t e", e=8), axis=AX.X, op=ALU.max)
    sq.v_("tensor_tensor", tmp1[:], m1[:], m2[:], op=ALU.subtract)
    sq.s_("activation", out=w1[:], in_=tmp1[:], func=AF.Sigmoid)
    sq.s_("activation", out=w2[:], in_=tmp1[:], func=AF.Sigmoid, scale=-1.0)
    sq.v_("memset", eid1[:], 0.0)
    for e in range(1, 8):
        sq.v_("scalar_tensor_tensor", eid1[:], msk[:, e::8], float(e), eid1[:],
              op0=ALU.mult, op1=ALU.add)
    for j in range(8):
        sq.v_("tensor_copy", m1e[:, j::8], m2[:])
    sq.v_("tensor_tensor", msk[:], lgm[:], m1e[:], op=ALU.is_ge)
    sq.v_("memset", eid2[:], 0.0)
    for e in range(1, 8):
        sq.v_("scalar_tensor_tensor", eid2[:], msk[:, e::8], float(e), eid2[:],
              op0=ALU.mult, op1=ALU.add)
    r.update(m1=m1, m2=m2, w1=w1, w2=w2, eid1=eid1, eid2=eid2)

    wc = [p.sb(f"rt_wc{e}", [128, NT], F32) for e in range(NEXP)]
    a_ = p.sb("rt_a", [128, NT], F32)
    b_ = p.sb("rt_b", [128, NT], F32)
    w1p = p.sb("rt_w1p", [128, NT], F32)
    w2p = p.sb("rt_w2p", [128, NT], F32)
    sq.v_("tensor_scalar", w1p[:], w1[:], 1.0, None, op0=ALU.add)
    sq.v_("tensor_scalar", w2p[:], w2[:], 1.0, None, op0=ALU.add)
    for e in range(NEXP):
        sq.v_("tensor_scalar", a_[:], eid1[:], float(e), None, op0=ALU.is_equal)
        sq.v_("tensor_scalar", b_[:], eid2[:], float(e), None, op0=ALU.is_equal)
        sq.v_("tensor_tensor", a_[:], a_[:], w1p[:], op=ALU.mult)
        sq.v_("tensor_tensor", b_[:], b_[:], w2p[:], op=ALU.mult)
        sq.v_("tensor_tensor", a_[:], a_[:], b_[:], op=ALU.add)
        sq.v_("tensor_scalar", wc[e][:], a_[:], -1.0, None, op0=ALU.add)
    r["wc"] = wc

    lo = p.sb("rt_lo", [128, NEXP], F32)
    hi = p.sb("rt_hi", [128, NEXP], F32)
    mid = p.sb("rt_mid", [128, NEXP], F32)
    d1 = p.sb("rt_d1", [128, NEXP], F32)
    d2 = p.sb("rt_d2", [128, NEXP], F32)
    nov = p.sb("rt_nov", [128, NEXP], F32)
    cnt_row = p.sb("rt_cntr", [1, NEXP], F32)
    over_row = p.sb("rt_over", [1, NEXP], F32)
    mtile = p.sb("rt_mtile", [128, NT], F32)
    csum = p.sb("rt_csum", [128, 1], F32)
    ps_cnt = p.ps("rt_pscnt", [1, 512], F32)[:, 0:NEXP]
    ps_ov = p.ps("rt_psov", [128, 512], F32)[:, 0:NEXP]
    sq.v_("memset", lo[:], -1.0)
    sq.v_("memset", hi[:], 1.0)
    for it in range(26):
        sq.v_("tensor_tensor", mid[:], lo[:], hi[:], op=ALU.add)
        sq.v_("tensor_scalar", mid[:], mid[:], 0.5, None, op0=ALU.mult)
        for e in range(NEXP):
            sq.v_("tensor_scalar", mtile[:], wc[e][:], mid[:, e:e + 1], None, op0=ALU.is_gt)
            sq.v_("tensor_reduce", csum[:, 0:1], mtile[:], axis=AX.X, op=ALU.add)
            sq.mm(ps_cnt[0:1, e:e + 1], csum[:, 0:1], consts["ones"][:, 0:1],
                  start=True, stop=True)
        sq.v_("tensor_scalar", over_row[0:1, :], ps_cnt[0:1, :], float(MOE_CAP), None,
              op0=ALU.is_gt)
        sq.mm(ps_ov[:, :], consts["ones"][0:1, :], over_row[0:1, :], start=True, stop=True)
        sq.v_("tensor_tensor", d1[:], mid[:], lo[:], op=ALU.subtract)
        sq.v_("tensor_tensor", d1[:], d1[:], ps_ov[:, :], op=ALU.mult)
        sq.v_("tensor_tensor", lo[:], lo[:], d1[:], op=ALU.add)
        sq.v_("tensor_scalar", nov[:], ps_ov[:, :], -1.0, 1.0, op0=ALU.mult, op1=ALU.add)
        sq.v_("tensor_tensor", d2[:], mid[:], hi[:], op=ALU.subtract)
        sq.v_("tensor_tensor", d2[:], d2[:], nov[:], op=ALU.mult)
        sq.v_("tensor_tensor", hi[:], hi[:], d2[:], op=ALU.add)
    r["thr"] = hi

    keep = [p.sb(f"rt_kp{e}", [128, NT], F32) for e in range(NEXP)]
    slot = [p.sb(f"rt_sl{e}", [128, NT], F32) for e in range(NEXP)]
    sa = p.sb("rt_sa", [128, NT], F32)
    sb2 = p.sb("rt_sb", [128, NT], F32)
    ustrict = p.sb("rt_ustrict", [128, 128], F32)
    iot2 = p.sb("rt_iot2", [128, 128], F32)
    rtot = p.sb("rt_rtot", [128, 1], F32)
    ps_off = p.ps("rt_psoff", [128, 512], F32)[:, 0:1]
    giota2 = p.sem("rt_giota2")
    p.gpsimd.iota(iot2[:], pattern=[[1, 128]], base=0, channel_multiplier=-1,
                  allow_small_or_imprecise_dtypes=True).then_inc(giota2, 1)
    p.vector.wait_ge(giota2, 1)
    sq.v_("tensor_scalar", ustrict[:], iot2[:], 0.5, None, op0=ALU.is_gt)
    for e in range(NEXP):
        sq.v_("tensor_scalar", keep[e][:], wc[e][:], hi[:, e:e + 1], None, op0=ALU.is_gt)
        cur, nxt = keep[e], sa
        first = True
        for s in [1 << i for i in range(7)]:
            sq.v_("tensor_tensor", nxt[:, s:NT], cur[:, s:NT], cur[:, 0:NT - s], op=ALU.add)
            sq.v_("tensor_copy", nxt[:, 0:s], cur[:, 0:s])
            if first:
                cur, nxt = nxt, sb2
                first = False
            else:
                cur, nxt = nxt, cur
        inc = cur
        sq.v_("tensor_copy", rtot[:, 0:1], inc[:, NT - 1:NT])
        sq.mm(ps_off[:, 0:1], ustrict[:, :], rtot[:, 0:1], start=True, stop=True)
        sq.v_("tensor_tensor", slot[e][:], inc[:], keep[e][:], op=ALU.subtract)
        sq.v_("tensor_copy", sb2[:, 0:1], ps_off[:, 0:1])
        sq.v_("tensor_scalar", slot[e][:], slot[e][:], sb2[:, 0:1], None, op0=ALU.add)
    r["keep"] = keep
    r["slot"] = slot
    return r


MOEB_PREFETCH = True
MOEB_HALFPOST = True
MOEB_SKIP_POST = False
MOEB_SKIP_FC = False
MOEB_NOFREE = False
MOEB_SKIP_B = False
MOEB_SKIP_GATH = False
MOEB_GW = 512
MOEB_NSLOT = 8
MOEB_DELAY0 = False


def build_moeb(p=None):
    own = p is None
    if own:
        p = Prog("moeb")
    lg = p.din("lg", [T_ALL, 8], F32)
    htm = p.din("htm", [T_ALL, D], BF16)
    preT = p.din("preT", [D, 4 * D], F32)
    gateT = p.din("gateT", [D, 4 * D], F32)
    postT = p.din("postT", [16 * 128, 16 * 128], F32)
    ohp = p.din("ohp", [128, 8], F32)
    eo_out = p.dout("eo_out", [MOE_CAP, D], BF16)
    rid1_out = p.dout("rid1_out", [128, NTILE], F32)
    rid2_out = p.dout("rid2_out", [128, NTILE], F32)
    c1_out = p.dout("c1_out", [128, NTILE], F32)
    c2_out = p.dout("c2_out", [128, NTILE], F32)
    tokl = p.dtmp("tokl", [MOE_CAP + 256, 1], I32)

    consts = emit_consts(p)
    sq = Seq(p)
    lsem = p.sem("lsem")
    gld = p.sem("gld")
    gsc = p.sem("gsc")
    sto = p.sem("sto")

    meta_sem = p.sem("meta_sem")
    stg_r = [p.sb(f"stg_r{i}", [128, NTILE], F32) for i in range(2)]
    stg_c = [p.sb(f"stg_c{i}", [128, NTILE], F32) for i in range(2)]
    _outer_scope = p._scope
    _rt_stack = ExitStack()
    if not MOEB_NOFREE:
        p._scope = (_rt_stack, _outer_scope[1] if _outer_scope else [])

    lgbuf = p.sb("lgbuf", [128, NTILE * 8], F32)
    p.sync.dma_start(out=lgbuf[:].rearrange("p (t e) -> p t e", e=8),
                     in_=lg.rearrange("(t p) e -> p t e", p=128)).then_inc(lsem, 16)
    oh_s = p.sb("oh_s", [128, 8], F32)
    p.sync.dma_start(out=oh_s[:], in_=ohp[:, :]).then_inc(lsem, 16)
    lsv = p.val(lsem)
    for eng in ("tensor", "vector", "scalar", "gpsimd"):
        getattr(p, eng).wait_ge(lsem, lsv)
    rt = emit_routing(p, lgbuf, consts, sq)

    kmine = p.sb("kmine", [128, NTILE], F32)
    smine = p.sb("smine", [128, NTILE], F32)
    sq.v_("memset", kmine[:], 0.0)
    sq.v_("memset", smine[:], 0.0)
    for e in range(NEXP):
        sq.v_("scalar_tensor_tensor", kmine[:], rt["keep"][e][:], oh_s[:, e:e + 1], kmine[:],
              op0=ALU.mult, op1=ALU.add)
        sq.v_("scalar_tensor_tensor", smine[:], rt["slot"][e][:], oh_s[:, e:e + 1], smine[:],
              op0=ALU.mult, op1=ALU.add)
    nk = p.sb("nk", [128, NTILE], F32)
    sq.v_("tensor_scalar", nk[:], kmine[:], -1.0, 1.0, op0=ALU.mult, op1=ALU.add)
    sq.v_("scalar_tensor_tensor", smine[:], nk[:], 1e7, smine[:], op0=ALU.mult, op1=ALU.add)
    sq.v_("tensor_scalar", smine[:], smine[:], float(MOE_CAP + 8), None, op0=ALU.min)
    ti_row = p.sb("ti_row", [128, NTILE], F32)
    g_tir = p.sem("g_tir")
    p.gpsimd.iota(ti_row[:], pattern=[[1, NTILE]], base=0, channel_multiplier=0,
                  allow_small_or_imprecise_dtypes=True).then_inc(g_tir, 1)
    p.vector.wait_ge(g_tir, 1)
    sq.v_("tensor_scalar", nk[:], smine[:], float(MOE_CAP + 8), None, op0=ALU.is_ge)
    sq.v_("tensor_tensor", nk[:], nk[:], ti_row[:], op=ALU.mult)
    sq.v_("tensor_tensor", smine[:], smine[:], nk[:], op=ALU.add)
    smi = p.sb("smi", [128, NTILE], I32)
    sq.v_("tensor_copy", smi[:], smine[:])
    tokid = p.sb("tokid", [128, NTILE], I32)
    g_tok = p.sem("g_tok")
    g_pref = p.sem("g_pref")
    g_pfd = p.sem("g_pfd")
    p.gpsimd.iota(tokid[:], pattern=[[128, NTILE]], base=0, channel_multiplier=1,
                  allow_small_or_imprecise_dtypes=True).then_inc(g_tok, 1)
    pref = p.sb("pref", [128, (MOE_CAP + 256) // 128], I32)
    p.gpsimd.wait_ge(sq.v, p.val(sq.v))
    p.gpsimd.memset(pref[:], 0).then_inc(g_pref, 1)
    p.gpsimd.wait_ge(g_pref, 1)
    p.gpsimd.dma_start(out=tokl[0:MOE_CAP + 256, 0:1].rearrange("(b a) one -> b (a one)", b=128),
                       in_=pref[:, 0:(MOE_CAP + 256) // 128]).then_inc(g_pfd, 16)
    p.gpsimd.wait_ge(g_pfd, 16)
    p.gpsimd.wait_ge(g_tok, 1)
    for ti in range(NTILE):
        p.gpsimd.indirect_dma_start(
            out=tokl[0:128, 0:1], out_offset=bass.IndirectOffsetOnAxis(smi[:, ti:ti + 1], 0),
            in_=tokid[:, ti:ti + 1], in_offset=None).then_inc(gsc, 16)
    scat_done = p.val(gsc)
    p.gpsimd.wait_ge(gsc, scat_done)

    sel1s = p.sb("sel1s", [128, NTILE], F32)
    sel1k = p.sb("sel1k", [128, NTILE], F32)
    sel2s = p.sb("sel2s", [128, NTILE], F32)
    sel2k = p.sb("sel2k", [128, NTILE], F32)
    mt1 = p.sb("mt1", [128, NTILE], F32)
    mt2 = p.sb("mt2", [128, NTILE], F32)
    for dst in (sel1s, sel1k, sel2s, sel2k):
        sq.v_("memset", dst[:], 0.0)
    for e in range(NEXP):
        for (eidt, ss, kk) in ((rt["eid1"], sel1s, sel1k), (rt["eid2"], sel2s, sel2k)):
            sq.v_("tensor_scalar", mt1[:], eidt[:], float(e), None, op0=ALU.is_equal)
            sq.v_("tensor_tensor", mt2[:], mt1[:], rt["slot"][e][:], op=ALU.mult)
            sq.v_("tensor_tensor", ss[:], ss[:], mt2[:], op=ALU.add)
            sq.v_("tensor_tensor", mt2[:], mt1[:], rt["keep"][e][:], op=ALU.mult)
            sq.v_("tensor_tensor", kk[:], kk[:], mt2[:], op=ALU.add)
    for i, (eidt, ss, kk, rout) in enumerate(((rt["eid1"], sel1s, sel1k, rid1_out),
                                              (rt["eid2"], sel2s, sel2k, rid2_out))):
        sq.v_("tensor_scalar", mt1[:], eidt[:], float(MOE_CAP), None, op0=ALU.mult)
        sq.v_("tensor_tensor", mt1[:], mt1[:], ss[:], op=ALU.add)
        sq.v_("tensor_tensor", mt1[:], mt1[:], kk[:], op=ALU.mult)
        sq.v_("tensor_scalar", mt2[:], kk[:], -float(NEXP * MOE_CAP), float(NEXP * MOE_CAP),
              op0=ALU.mult, op1=ALU.add)
        sq.v_("tensor_tensor", stg_r[i][:], mt1[:], mt2[:], op=ALU.add)
        p.sync.wait_ge(sq.v, p.val(sq.v))
        p.sync.dma_start(out=rout[:, :], in_=stg_r[i][:]).then_inc(meta_sem, 16)
    sq.v_("tensor_tensor", mt1[:], rt["w1"][:], sel1k[:], op=ALU.mult)
    sq.v_("tensor_tensor", mt2[:], rt["w2"][:], sel2k[:], op=ALU.mult)
    twt = p.sb("tw", [128, NTILE], F32)
    sq.v_("tensor_tensor", twt[:], mt1[:], mt2[:], op=ALU.add)
    sq.v_("tensor_scalar", twt[:], twt[:], MOE_EPS, None, op0=ALU.max)
    sq.v_("reciprocal", twt[:], twt[:])
    sq.v_("tensor_tensor", stg_c[0][:], mt1[:], twt[:], op=ALU.mult)
    p.sync.wait_ge(sq.v, p.val(sq.v))
    p.sync.dma_start(out=c1_out[:, :], in_=stg_c[0][:]).then_inc(meta_sem, 16)
    sq.v_("tensor_tensor", stg_c[1][:], mt2[:], twt[:], op=ALU.mult)
    p.sync.wait_ge(sq.v, p.val(sq.v))
    p.sync.dma_start(out=c2_out[:, :], in_=stg_c[1][:]).then_inc(meta_sem, 16)

    p.gpsimd.wait_ge(sq.s, p.val(sq.s))
    p.gpsimd.wait_ge(sq.v, p.val(sq.v))
    p.gpsimd.wait_ge(sq.t, p.val(sq.t))
    if not MOEB_NOFREE:
        p._scope = _outer_scope
        _rt_stack.close()

    pre_s = p.sb("pre_s", [128, DCH * 4 * D], BF16)
    gate_s = p.sb("gate_s", [128, DCH * 4 * D], BF16)
    p.gpsimd.dma_start(out=pre_s[:].rearrange("p (c t) -> p c t", c=DCH),
                       in_=preT.rearrange("(c p) t -> p c t", p=128)).then_inc(gld, 16)
    p.gpsimd.dma_start(out=gate_s[:].rearrange("p (c t) -> p c t", c=DCH),
                       in_=gateT.rearrange("(c p) t -> p c t", p=128)).then_inc(gld, 16)
    wld = p.val(gld)
    p.tensor.wait_ge(gld, wld)

    GW = MOEB_GW
    NGRP = MOE_CAP // GW
    NSUB = GW // 128
    NFC = 32
    idxt = [p.sb(f"idxt{i}", [128, NSUB], I32) for i in range(2)]
    xtm = [[p.sb(f"xtm{i}_{j}", [128, D], BF16) for j in range(NSUB)] for i in range(2)]
    xfm = p.sb("xfm", [128, DCH * GW], BF16)
    act = p.sb("act", [128, NFC * GW], BF16) if not MOEB_SKIP_FC else None
    eo_st = [p.sb(f"eo_st{j}", [128, D], BF16) for j in range(NSUB)]
    post_s = [p.sb(f"post_s{i}", [128, 16 * 128], BF16) for i in range(2)] if not MOEB_SKIP_POST else None
    silb = p.sb("silb", [128, GW], F32) if not MOEB_SKIP_FC else None
    obf = [p.sb(f"obf{i}", [128, GW], BF16) for i in range(2)] if not MOEB_SKIP_POST else None
    psem2 = [p.sem("ps2_0"), p.sem("ps2_1")]
    lsem_ix = [p.sem("lsx0"), p.sem("lsx1")]
    gsem_ix = [p.sem("gsx0"), p.sem("gsx1")]
    t_fc = p.sem("t_fc")
    s_fc = p.sem("s_fc")
    v_fc = p.sem("v_fc")
    t_tr = p.sem("t_tr")
    v_tr = p.sem("v_tr")
    t_pc = p.sem("t_pc")
    v_ob = p.sem("v_ob")
    psP = [p.ps(f"psP{i}", [128, 512], F32) for i in range(2)] if not MOEB_SKIP_FC else None
    psG = [p.ps(f"psG{i}", [128, 512], F32) for i in range(2)] if not MOEB_SKIP_FC else None
    psO = [p.ps(f"psO{i}", [128, 512], F32) for i in range(2)] if not MOEB_SKIP_POST else None
    ps_trs = [p.ps(f"ps_tr{i}", [128, 1024], BF16) for i in range(2)]
    NSLOT = 2
    tr_slot_owner = [0] * NSLOT
    gath_done = [0, 0]
    xtm_free = [0, 0]
    post_dma = [0, 0]
    psO_free = [0, 0]
    stores_done = 0

    def prefetch(g):
        if MOEB_SKIP_GATH:
            return
        i = g % 2
        if g == 0 and MOEB_DELAY0:
            p.gpsimd.wait_ge(gld, wld)
        p.sync.wait_ge(gsc, scat_done)
        p.sync.wait_ge(gsem_ix[i], gath_done[i])
        ixl = p.sync.dma_start(out=idxt[i][:, 0:NSUB],
                               in_=tokl[g * GW:(g + 1) * GW, 0:1].rearrange(
                                   "(f pp) one -> pp (f one)", pp=128),
                               allow_slow_non_contiguous=True).inc(lsem_ix[i], 16)
        p.gpsimd.wait_ge(lsem_ix[i], ixl)
        p.gpsimd.wait_ge(t_tr, xtm_free[i])
        for j in range(NSUB):
            p.gpsimd.indirect_dma_start(
                out=xtm[i][j][:, :], out_offset=None,
                in_=htm[:, :], in_offset=bass.IndirectOffsetOnAxis(idxt[i][:, j:j + 1], 0)
            ).then_inc(gsem_ix[i], 16)
        gath_done[i] = p.val(gsem_ix[i])

    if MOEB_PREFETCH:
        prefetch(0)
    for g in range(NGRP):
        i = g % 2
        if not MOEB_PREFETCH:
            prefetch(g)
        p.tensor.wait_ge(gsem_ix[i], gath_done[i])
        p.vector.wait_ge(t_fc, g * NFC)
        for j in range(NSUB if not MOEB_SKIP_B else 0):
            for ch in range(DCH):
                k = j * DCH + ch
                slot = k % NSLOT
                p.tensor.wait_ge(v_tr, tr_slot_owner[slot])
                tv = p.tensor.transpose(ps_trs[slot][:, 0:128],
                                        xtm[i][j][:, ch * 128:(ch + 1) * 128],
                                        consts["ident"][:, :]).inc(t_tr, 1)
                p.vector.wait_ge(t_tr, tv)
                vv = p.vector.tensor_copy(
                    xfm[:, ch * GW + j * 128:ch * GW + (j + 1) * 128],
                    ps_trs[slot][:, 0:128]).inc(v_tr, 1)
                tr_slot_owner[slot] = vv
        xtm_free[i] = p.val(t_tr)
        p.tensor.wait_ge(v_tr, p.val(v_tr))
        if MOEB_PREFETCH and g + 1 < NGRP:
            prefetch(g + 1)
        tbase, sbase, vbase = p.val(t_fc), p.val(s_fc), p.val(v_fc)
        if MOEB_SKIP_FC:
            continue
        p.vector.wait_ge(t_pc, g * 16)
        for fc in range(NFC):
            b = fc % 2
            p.tensor.wait_ge(v_fc, max(0, vbase + fc - 1))
            for ich in range(DCH):
                p.tensor.matmul(psP[b][:, 0:GW],
                                pre_s[:, ich * 4 * D + fc * 128:ich * 4 * D + (fc + 1) * 128],
                                xfm[:, ich * GW:(ich + 1) * GW],
                                start=(ich == 0), stop=(ich == DCH - 1))
            for ich in range(DCH):
                m2_ = p.tensor.matmul(psG[b][:, 0:GW],
                                      gate_s[:, ich * 4 * D + fc * 128:ich * 4 * D + (fc + 1) * 128],
                                      xfm[:, ich * GW:(ich + 1) * GW],
                                      start=(ich == 0), stop=(ich == DCH - 1))
            tv = m2_.inc(t_fc, 1)
            p.scalar.wait_ge(t_fc, tv)
            p.scalar.wait_ge(v_fc, max(0, vbase + fc))
            sv = p.scalar.activation(out=silb[:, :], in_=psG[b][:, 0:GW],
                                     func=AF.Silu).inc(s_fc, 1)
            p.vector.wait_ge(s_fc, sv)
            p.vector.tensor_tensor(act[:, fc * GW:(fc + 1) * GW], psP[b][:, 0:GW],
                                   silb[:, :], op=ALU.mult).then_inc(v_fc, 1)
        pcbase = p.val(t_pc)
        if MOEB_SKIP_POST:
            continue

        def post_dma_issue(s, gate):
            bb = s % 2
            oc2, h2 = divmod(s, 2)
            p.gpsimd.wait_ge(t_pc, max(0, gate))
            post_dma[bb] = p.gpsimd.dma_start(
                out=post_s[bb][:],
                in_=postT[s * 128:(s + 1) * 128, :]).inc(psem2[bb], 16)

        for s in range(2):
            post_dma_issue(s, pcbase - 1 + s)
        for oc in range(DCH):
            ob = oc % 2
            tv = None
            for h in range(2):
                s = oc * 2 + h
                bb = s % 2
                p.tensor.wait_ge(psem2[bb], post_dma[bb])
                if s == 0:
                    p.tensor.wait_ge(v_fc, vbase + NFC)
                if h == 0:
                    p.tensor.wait_ge(v_ob, psO_free[ob])
                for k in range(16):
                    kc = h * 16 + k
                    mm = p.tensor.matmul(psO[ob][:, 0:GW],
                                         post_s[bb][:, k * 128:(k + 1) * 128],
                                         act[:, kc * GW:(kc + 1) * GW],
                                         start=(kc == 0), stop=(kc == 31))
                tv = mm.inc(t_pc, 1)
                if s + 2 < 16:
                    post_dma_issue(s + 2, tv)
            p.vector.wait_ge(t_pc, tv)
            if g and oc == 0:
                p.vector.wait_ge(sto, stores_done)
            cv = p.vector.tensor_copy(obf[ob][:, :], psO[ob][:, 0:GW]).inc(v_ob, 1)
            psO_free[ob] = cv
            for j in range(NSUB):
                slot = j % NSLOT
                p.tensor.wait_ge(v_tr, tr_slot_owner[slot])
                p.tensor.wait_ge(v_ob, cv)
                tv2 = p.tensor.transpose(ps_trs[slot][:, 0:128],
                                         obf[ob][:, j * 128:(j + 1) * 128],
                                         consts["ident"][:, :]).inc(t_tr, 1)
                p.vector.wait_ge(t_tr, tv2)
                vv2 = p.vector.tensor_copy(eo_st[j][:, oc * 128:(oc + 1) * 128],
                                           ps_trs[slot][:, 0:128]).inc(v_tr, 1)
                tr_slot_owner[slot] = vv2
        ev = p.val(v_tr)
        p.sync.wait_ge(v_tr, ev)
        for j in range(NSUB):
            p.sync.dma_start(out=eo_out[g * GW + j * 128:g * GW + (j + 1) * 128, :],
                             in_=eo_st[j][:, :]).then_inc(sto, 16)
        stores_done = p.val(sto)
    p.sync.wait_ge(sto, p.val(sto))
    p.sync.wait_ge(meta_sem, p.val(meta_sem))
    return p.finalize() if own else None


def build_moec(p=None):
    own = p is None
    if own:
        p = Prog("moec")
    xs = p.din("xs", [D, TC], F32)
    eo = p.din("eo", [NEXP * MOE_CAP + 1, D], BF16)
    rid1 = p.din("rid1", [128, 16], F32)
    rid2 = p.din("rid2", [128, 16], F32)
    c1i = p.din("c1i", [128, 16], F32)
    c2i = p.din("c2i", [128, 16], F32)
    x_out = p.dout("x_out", [D, TC], F32)

    consts = emit_consts(p)
    sq = Seq(p)
    lsem = p.sem("lsem")
    gsem2 = p.sem("gsem2")
    sto = p.sem("sto")

    xbuf = p.sb("xbuf", [128, DCH * TC], F32)
    x_ch = [xbuf[:, ch * TC:(ch + 1) * TC] for ch in range(DCH)]
    p.sync.dma_start(out=xbuf[:].rearrange("p (c t) -> p c t", c=DCH),
                     in_=xs.rearrange("(c p) t -> p c t", p=128)).then_inc(lsem, 16)
    meta = p.sb("meta", [128, 4 * 16], F32)
    p.sync.dma_start(out=meta[:, 0:16], in_=rid1[:, :]).then_inc(lsem, 16)
    p.sync.dma_start(out=meta[:, 16:32], in_=rid2[:, :]).then_inc(lsem, 16)
    p.sync.dma_start(out=meta[:, 32:48], in_=c1i[:, :]).then_inc(lsem, 16)
    p.sync.dma_start(out=meta[:, 48:64], in_=c2i[:, :]).then_inc(lsem, 16)
    lv = p.val(lsem)
    for eng in ("tensor", "vector", "scalar", "gpsimd"):
        getattr(p, eng).wait_ge(lsem, lv)
    ridi = p.sb("ridi", [128, 32], I32)
    sq.v_("tensor_copy", ridi[:, 0:16], meta[:, 0:16])
    sq.v_("tensor_copy", ridi[:, 16:32], meta[:, 16:32])
    civ = p.val(sq.v)
    p.gpsimd.wait_ge(sq.v, civ)

    r1 = p.sb("r1", [128, D], BF16)
    r2 = p.sb("r2", [128, D], BF16)
    otm = p.sb("otm", [128, D], BF16)
    ps_t2 = p.ps("ps_t2", [128, 1024], BF16)[:, 0:128]
    for j in range(16):
        g1 = p.gpsimd.indirect_dma_start(
            out=r1[:, :], out_offset=None, in_=eo[:, :],
            in_offset=bass.IndirectOffsetOnAxis(ridi[:, j:j + 1], 0)).inc(gsem2, 16)
        g2 = p.gpsimd.indirect_dma_start(
            out=r2[:, :], out_offset=None, in_=eo[:, :],
            in_offset=bass.IndirectOffsetOnAxis(ridi[:, 16 + j:17 + j], 0)).inc(gsem2, 16)
        p.vector.wait_ge(gsem2, g2)
        sq.v_("tensor_scalar", otm[:, :], r1[:, :], meta[:, 32 + j:33 + j], None, op0=ALU.mult)
        ov = sq.v_("scalar_tensor_tensor", otm[:, :], r2[:, :], meta[:, 48 + j:49 + j],
                   otm[:, :], op0=ALU.mult, op1=ALU.add)
        for ch in range(DCH):
            sq.tr(ps_t2[:, 0:128], otm[:, ch * 128:(ch + 1) * 128], consts["ident"][:, :])
            sq.v_("tensor_tensor", x_ch[ch][:, j * 128:(j + 1) * 128],
                  x_ch[ch][:, j * 128:(j + 1) * 128], ps_t2[:, 0:128], op=ALU.add)
        p.gpsimd.wait_ge(sq.v, p.val(sq.v))
    fv = p.val(sq.v)
    p.sync.wait_ge(sq.v, fv)
    p.sync.dma_start(out=x_out.rearrange("(c p) t -> p c t", p=128),
                     in_=xbuf[:].rearrange("p (c t) -> p c t", c=DCH)).then_inc(sto, 16)
    p.sync.wait_ge(sto, 16)
    return p.finalize() if own else None


def run_moe(st, tag):
    nc = build_moeb()
    in_maps = []
    for c in range(NCORES):
        in_maps.append({
            "lg": st["moe_lg"],
            "htm": st["moe_h_tm"],
            "preT": _fm(st["L1_ffn_ff_pre_act_weight"][c]),
            "gateT": _fm(st["L1_ffn_gate_weight"][c]),
            "postT": _packpost(st["L1_ffn_ff_post_act_weight"][c]),
            "ohp": np.ascontiguousarray(
                np.broadcast_to(np.eye(NEXP, dtype=np.float32)[c], (128, NEXP))),
        })
    res = run_bass_kernel_spmd(nc, in_maps, core_ids=list(range(NCORES)))
    eo = np.concatenate([r["eo_out"] for r in res.results] +
                        [np.zeros((1, D), BF16NP)], axis=0)
    meta = {k: res.results[0][k] for k in ("rid1_out", "rid2_out", "c1_out", "c2_out")}
    nc2 = build_moec()
    in_maps2 = []
    for c in range(NCORES):
        sl = slice(c * TC, (c + 1) * TC)
        tsl = slice(c * 16, (c + 1) * 16)
        in_maps2.append({
            "xs": np.ascontiguousarray(st["x_fm"][:, sl]),
            "eo": eo,
            "rid1": np.ascontiguousarray(meta["rid1_out"][:, tsl]),
            "rid2": np.ascontiguousarray(meta["rid2_out"][:, tsl]),
            "c1i": np.ascontiguousarray(meta["c1_out"][:, tsl]),
            "c2i": np.ascontiguousarray(meta["c2_out"][:, tsl]),
        })
    res2 = run_bass_kernel_spmd(nc2, in_maps2, core_ids=list(range(NCORES)))
    st["x_fm"] = np.concatenate([r["x_out"] for r in res2.results], axis=1)
    return st




def build_txl_pre(p=None):
    own = p is None
    if own:
        p = Prog("txl_pre")
    xs = p.din("xs", [D, TC], F32)
    lnwb = p.din("lnwb", [128, 2 * DCH], F32)
    wq = p.din("wq", [128, 128], F32)
    wk = p.din("wk", [128, 128], F32)
    wv = p.din("wv", [128, 128], F32)
    wpT = p.din("wpT", [128, 128], F32)
    bqs = p.din("bqs", [128, 3], F32)
    ubvb = p.din("ubvb", [128, 16], F32)
    k_out = p.dout("k_out", [D, TC], BF16)
    qu_out = p.dout("qu_out", [D, TC], BF16)
    v_out = p.dout("v_out", [D, TC], BF16)
    zq_out = p.dout("zq_out", [DCH * TC, 128], F32)

    consts = emit_consts(p)
    lsem = p.sem("lsem")
    bsem = p.sem("bsem")
    gld = p.sem("gld")
    tsem = p.sem("tsem")
    vsem = p.sem("vsem")
    ssem = p.sem("ssem")

    xbuf = p.sb("xbuf", [128, DCH * TC], F32)
    x_ch = [xbuf[:, ch * TC:(ch + 1) * TC] for ch in range(DCH)]
    p.sync.dma_start(out=xbuf[:].rearrange("p (c t) -> p c t", c=DCH),
                     in_=xs.rearrange("(c p) t -> p c t", p=128)).then_inc(lsem, 16)
    lnw = p.sb("lnw", [128, 2 * DCH], F32)
    p.sync.dma_start(out=lnw[:], in_=lnwb[:, :]).then_inc(bsem, 16)
    bq_s = p.sb("bq_s", [128, 3], F32)
    p.sync.dma_start(out=bq_s[:], in_=bqs[:, :]).then_inc(bsem, 16)
    ub_s = p.sb("ub_s", [128, 16], F32)
    p.sync.dma_start(out=ub_s[:], in_=ubvb[:, :]).then_inc(bsem, 16)
    wbig = p.sb("wbig", [128, 4 * 128], BF16)
    p.gpsimd.dma_start(out=wbig[:, 0:128], in_=wq[:, :]).then_inc(gld, 16)
    p.gpsimd.dma_start(out=wbig[:, 128:256], in_=wk[:, :]).then_inc(gld, 16)
    p.gpsimd.dma_start(out=wbig[:, 256:384], in_=wv[:, :]).then_inc(gld, 16)
    p.gpsimd.dma_start(out=wbig[:, 384:512], in_=wpT[:, :]).then_inc(gld, 16)
    wld = p.val(gld)

    ln = LN(p)
    xn_bf = [p.sb(f"xn{ch}", [128, TC], BF16) for ch in range(DCH)]
    atmp = p.sb("atmp", [128, 512], F32)
    p.tensor.wait_ge(lsem, 16)
    p.scalar.wait_ge(lsem, 16)
    p.vector.wait_ge(lsem, 16)
    p.vector.wait_ge(bsem, 48)
    NG = TC // 512
    for g in range(NG):
        g0 = g * 512
        bm, br = ln.stats(x_ch, g0, 512, consts)
        for ch in range(DCH):
            p.vector.tensor_tensor(atmp[:, 0:512], x_ch[ch][:, g0:g0 + 512], bm[:, 0:512], op=ALU.add)
            p.vector.tensor_tensor(atmp[:, 0:512], atmp[:, 0:512], br[:, 0:512], op=ALU.mult)
            p.vector.tensor_scalar(xn_bf[ch][:, g0:g0 + 512], atmp[:, 0:512],
                                   lnw[:, ch:ch + 1], lnw[:, DCH + ch:DCH + ch + 1],
                                   op0=ALU.mult, op1=ALU.add).then_inc(vsem, 1)
    xn_ready = p.val(vsem)

    ps_pr = [p.ps("prA", [128, 512], F32), p.ps("prB", [128, 512], F32)]
    ps_z = p.ps("ps_z", [128, 512], F32)
    zpre = p.sb("zpre", [128, 512], BF16)
    stg = [p.sb(f"stg{i}", [128, 512], BF16) for i in range(3)]
    zstg = [p.sb(f"zstg{i}", [128, 128], F32) for i in range(2)]
    osem = [p.sem(f"os{i}") for i in range(3)]
    zsem = [p.sem("zs0"), p.sem("zs1")]
    o_free = [0, 0, 0]
    z_free = [0, 0]
    ps_free = [0, 0]
    zpsfree = [0]
    p.tensor.wait_ge(vsem, xn_ready)
    p.tensor.wait_ge(gld, wld)
    p.tensor.wait_ge(bsem, 48)
    si = 0
    for g in range(NG):
        g0 = g * 512
        for h in range(DCH):
            i2 = si % 2
            p.tensor.wait_ge(vsem, ps_free[i2])
            tq = p.tensor.matmul(ps_pr[i2][:, :], wbig[:, 0:128], xn_bf[h][:, g0:g0 + 512],
                                 start=True, stop=True).inc(tsem, 1)
            p.vector.wait_ge(tsem, tq)
            p.vector.wait_ge(osem[0], o_free[0])
            p.vector.tensor_scalar(stg[0][:], ps_pr[i2][:, :], bq_s[:, 0:1], ub_s[:, h:h + 1],
                                   op0=ALU.add, op1=ALU.add)
            zv = p.vector.tensor_scalar(zpre[:], ps_pr[i2][:, :], bq_s[:, 0:1],
                                        ub_s[:, 8 + h:9 + h], op0=ALU.add,
                                        op1=ALU.add).inc(vsem, 1)
            ps_free[i2] = zv
            p.sync.wait_ge(vsem, zv)
            o_free[0] = p.sync.dma_start(out=qu_out[h * 128:(h + 1) * 128, g0:g0 + 512],
                                         in_=stg[0][:]).inc(osem[0], 16)
            for j in range(4):
                jz = j % 2
                p.tensor.wait_ge(vsem, max(zv, zpsfree[0]))
                tz = p.tensor.matmul(ps_z[:, 0:128], zpre[:, j * 128:(j + 1) * 128],
                                     wbig[:, 384:512], start=True, stop=True).inc(tsem, 1)
                p.vector.wait_ge(tsem, tz)
                p.vector.wait_ge(zsem[jz], z_free[jz])
                ze = p.vector.tensor_copy(zstg[jz][:], ps_z[:, 0:128]).inc(vsem, 1)
                zpsfree[0] = ze
                p.sync.wait_ge(vsem, ze)
                z_free[jz] = p.sync.dma_start(
                    out=zq_out[h * TC + g0 + j * 128:h * TC + g0 + (j + 1) * 128, 0:128],
                    in_=zstg[jz][:]).inc(zsem[jz], 16)
            si += 1
            i2 = si % 2
            p.tensor.wait_ge(vsem, ps_free[i2])
            tk = p.tensor.matmul(ps_pr[i2][:, :], wbig[:, 128:256], xn_bf[h][:, g0:g0 + 512],
                                 start=True, stop=True).inc(tsem, 1)
            p.vector.wait_ge(tsem, tk)
            p.vector.wait_ge(osem[1], o_free[1])
            kv2 = p.vector.tensor_scalar(stg[1][:], ps_pr[i2][:, :], bq_s[:, 1:2], None,
                                         op0=ALU.add).inc(vsem, 1)
            ps_free[i2] = kv2
            p.sync.wait_ge(vsem, kv2)
            o_free[1] = p.sync.dma_start(out=k_out[h * 128:(h + 1) * 128, g0:g0 + 512],
                                         in_=stg[1][:]).inc(osem[1], 16)
            si += 1
            i2 = si % 2
            p.tensor.wait_ge(vsem, ps_free[i2])
            tv2 = p.tensor.matmul(ps_pr[i2][:, :], wbig[:, 256:384], xn_bf[h][:, g0:g0 + 512],
                                  start=True, stop=True).inc(tsem, 1)
            p.vector.wait_ge(tsem, tv2)
            p.vector.wait_ge(osem[2], o_free[2])
            vv2 = p.vector.tensor_scalar(stg[2][:], ps_pr[i2][:, :], bq_s[:, 2:3], None,
                                         op0=ALU.add).inc(vsem, 1)
            ps_free[i2] = vv2
            p.sync.wait_ge(vsem, vv2)
            o_free[2] = p.sync.dma_start(out=v_out[h * 128:(h + 1) * 128, g0:g0 + 512],
                                         in_=stg[2][:]).inc(osem[2], 16)
            si += 1
    for sm in osem + zsem:
        p.sync.wait_ge(sm, p.val(sm))
    return p.finalize() if own else None


def run_txl_pre(st):
    nc = build_txl_pre()
    lnwb = np.ascontiguousarray(np.concatenate(
        [_packbias(st["L1_attn_b_ln_gamma"]), _packbias(st["L1_attn_b_ln_beta"])], axis=1))
    bqs = np.ascontiguousarray(np.stack(
        [st["L1_attn_b_bq"], st["L1_attn_b_bk"], st["L1_attn_b_bv"]], axis=1))
    ubvb = np.ascontiguousarray(np.concatenate(
        [st["L1_attn_b_u_bias"].T, st["L1_attn_b_v_bias"].T], axis=1))
    in_maps = []
    for c in range(NCORES):
        sl = slice(c * TC, (c + 1) * TC)
        in_maps.append({
            "xs": np.ascontiguousarray(st["x_fm"][:, sl]),
            "lnwb": lnwb, "bqs": bqs, "ubvb": ubvb,
            "wq": st["L1_attn_b_Wq"], "wk": st["L1_attn_b_Wk"], "wv": st["L1_attn_b_Wv"],
            "wpT": _fm(st["L1_attn_b_Wp"]),
        })
    res = run_bass_kernel_spmd(nc, in_maps, core_ids=list(range(NCORES)))
    st["txl_k"] = np.concatenate([r["k_out"] for r in res.results], axis=1)
    st["txl_qu"] = np.concatenate([r["qu_out"] for r in res.results], axis=1)
    st["txl_v"] = np.concatenate([r["v_out"] for r in res.results], axis=1)
    zq = np.zeros((T_ALL, D), np.float32)
    for c in range(NCORES):
        zc = res.results[c]["zq_out"].reshape(DCH, TC, 128)
        zq[c * TC:(c + 1) * TC] = np.transpose(zc, (1, 0, 2)).reshape(TC, D)
    st["txl_zq"] = zq
    return st



def build_txl_attn(p=None):
    own = p is None
    if own:
        p = Prog("txl_attn")
    NCH = 2
    JOFF = -2050
    JW = 18560
    kf = [p.din(f"kf{i}", [128, S], BF16) for i in range(NCH)]
    vf = [p.din(f"vf{i}", [128, S], BF16) for i in range(NCH)]
    quf = [p.din(f"quf{i}", [128, S], BF16) for i in range(NCH)]
    zqf = [p.din(f"zqf{i}", [S + 1, 128], F32) for i in range(NCH)]
    hbase = [p.din(f"hbase{i}", [1, 1], F32) for i in range(NCH)]
    a_out = [p.dout(f"a_out{i}", [128, S], BF16) for i in range(NCH)]

    consts = emit_consts(p)
    sqq = Seq(p)
    tsem = p.sem("tsem")
    vsem = p.sem("vsem")
    ssem = p.sem("ssem")
    gld = p.sem("gld")
    bsem = p.sem("bsem")
    khsem = p.sem("khsem")
    vhsem = p.sem("vhsem")
    qsem = p.sem("qsem")
    zsem = p.sem("zsem")
    stsem = p.sem("stsem")
    giota = p.sem("giota")

    qbuf = p.sb("qbuf", [128, S], BF16)

    par = p.sb("par", [128, 1], F32)
    pari = p.sb("pari", [128, 1], I32)
    iotai = p.sb("iotai", [128, 1], I32)
    ihalf = p.sb("ihalf", [128, 1], F32)
    iotf = p.sb("iotf", [128, 1], F32)
    p.gpsimd.iota(iotai[:], pattern=[[0, 1]], base=0, channel_multiplier=1).then_inc(giota, 1)
    p.vector.wait_ge(giota, 1)
    sqq.v_("tensor_scalar", pari[:], iotai[:], 1, None, op0=ALU.bitwise_and)
    sqq.v_("tensor_copy", par[:], pari[:])
    sqq.v_("tensor_copy", iotf[:], iotai[:])
    sqq.v_("tensor_tensor", ihalf[:], iotf[:], par[:], op=ALU.subtract)
    sqq.v_("tensor_scalar", ihalf[:], ihalf[:], 0.5, None, op0=ALU.mult)

    mio = p.sb("mio", [128, 128], F32)
    giota2 = p.sem("giota2")
    p.gpsimd.iota(mio[:], pattern=[[-1, 128]], base=0, channel_multiplier=1,
                  allow_small_or_imprecise_dtypes=True).then_inc(giota2, 1)
    p.vector.wait_ge(giota2, 1)
    flow = [p.sb(f"flow{i}", [128, 128], F32) for i in range(2)]
    fupp = [p.sb(f"fupp{i}", [128, 128], F32) for i in range(2)]
    fdia = p.sb("fdia", [128, 128], F32)
    for i, dd in enumerate((0, 128)):
        sqq.v_("tensor_scalar", flow[i][:], mio[:], float(-dd), None, op0=ALU.is_le)
        sqq.v_("tensor_scalar", fdia[:], mio[:], float(1 - dd), None, op0=ALU.is_equal)
        sqq.v_("tensor_tensor", fupp[i][:], flow[i][:], fdia[:], op=ALU.add)
        sqq.v_("tensor_scalar", fupp[i][:], fupp[i][:], -1.0, 1.0, op0=ALU.mult, op1=ALU.add)
    identf = p.sb("identf", [128, 128], F32)
    sqq.v_("tensor_copy", identf[:], consts["ident"][:])

    kh = p.sb("kh", [128, S], BF16)
    vh = p.sb("vh", [128, S], BF16)
    vt = p.sb("vt", [128, S], BF16)
    pext = p.sb("pext", [128, JW], BF16)
    qh1 = p.sb("qh1", [128, S], BF16)
    qh2 = p.sb("qh2", [128, S], BF16)
    attn = p.sb("attnb", [128, S], BF16)
    hb_c = p.sb("hb_c", [128, 1], F32)
    onecolb = p.sb("onecolb", [128, 1], BF16)
    p.vector.memset(onecolb[:], 1.0).then_inc(vsem, 1)

    omg = p.sb("omg", [128, 1], F32)
    sA = p.sb("sA", [128, 64], F32)
    cA = p.sb("cA", [128, 64], F32)
    sa_t = p.sb("sa_t", [128, 145], F32)
    ca_t = p.sb("ca_t", [128, 145], F32)
    Pt = p.sb("Pt", [128, 145], F32)
    Qt = p.sb("Qt", [128, 145], F32)
    sb_t = p.sb("sb_t", [128, 128], F32)
    cb_t = p.sb("cb_t", [128, 128], F32)
    tg1 = p.sb("tg1", [128, 145], F32)
    tg2 = p.sb("tg2", [128, 145], F32)
    tgi = p.sb("tgi", [128, 145], I32)
    wrow = p.sb("wrow", [1, 145], F32)
    zq_t = p.sb("zq_t", [128, 128], F32)
    zq_s = p.sb("zq_s", [128, 128], F32)
    rot = p.sb("rot", [128, 128], F32)
    rotb = p.sb("rotb", [128, 128], BF16)
    stA = p.sb("stA", [128, 128], F32)
    sS = p.sb("sS", [128, 512], F32)
    dcol = p.sb("dcol", [128, 1], F32)
    drow = p.sb("drow", [1, 128], F32)
    bcsb = p.sb("bcsb", [128, 128], F32)
    pT = [p.sb(f"pT{i}", [128, 512], BF16) for i in range(2)]

    ps_bc2 = p.ps("ps_bc2", [128, 512], F32)
    ps_tr = p.ps("ps_tr", [128, 1024], BF16)[:, 0:128]
    ps_S = [p.ps(f"psS{i}", [128, 512], F32) for i in range(2)]
    ps_p1 = p.ps("ps_p1", [128, 512], F32)
    ps_p2 = p.ps("ps_p2", [128, 512], F32)
    ps_pv = p.ps("ps_pv", [128, 512], F32)
    ps_dn = p.ps("ps_dn", [128, 512], F32)[:, 0:64]

    def fold_round(buf, w):
        sqq.v_("tensor_copy", tgi[:, 0:w], buf[:, 0:w])
        sqq.v_("tensor_copy", tg2[:, 0:w], tgi[:, 0:w])
        sqq.v_("tensor_tensor", buf[:, 0:w], buf[:, 0:w], tg2[:, 0:w], op=ALU.subtract)
        sqq.v_("tensor_scalar", tg2[:, 0:w], buf[:, 0:w], 0.5, None, op0=ALU.is_ge)
        sqq.v_("tensor_tensor", buf[:, 0:w], buf[:, 0:w], tg2[:, 0:w], op=ALU.subtract)
        sqq.v_("tensor_scalar", tg2[:, 0:w], buf[:, 0:w], -0.5, None, op0=ALU.is_le)
        sqq.v_("tensor_tensor", buf[:, 0:w], buf[:, 0:w], tg2[:, 0:w], op=ALU.add)

    NQT = S // 128
    NKT = S // 128
    LN1E4 = math.log(10000.0)

    for h in range(NCH):
        p.vector.wait_ge(tsem, p.val(tsem))
        p.vector.wait_ge(ssem, p.val(ssem))
        p.sync.wait_ge(tsem, p.val(tsem))
        hbl = p.sync.dma_start(out=hb_c[:], in_=hbase[h][0:1, 0:1].broadcast_to([128, 1])
                               ).inc(bsem, 16)
        p.vector.wait_ge(bsem, hbl)
        ql = p.sync.dma_start(out=qbuf[:], in_=quf[h][:, :]).inc(qsem, 16)
        p.tensor.wait_ge(qsem, ql)
        sqq.v_("tensor_scalar", tg1[:, 0:1], ihalf[:, 0:1], hb_c[:, 0:1], None, op0=ALU.add)
        sqq.s_("activation", out=omg[:, 0:1], in_=tg1[:, 0:1], func=AF.Exp, scale=-LN1E4 / 512.0)
        sqq.v_("tensor_scalar", omg[:, 0:1], omg[:, 0:1], 1.0 / TWO_PI, None, op0=ALU.mult)
        p.gpsimd.wait_ge(sqq.s, p.val(sqq.s))
        p.gpsimd.wait_ge(sqq.t, p.val(sqq.t))
        p.gpsimd.iota(wrow[0:1, 0:64], pattern=[[1, 64]], base=0, channel_multiplier=0,
                      allow_small_or_imprecise_dtypes=True).then_inc(giota, 1)
        p.vector.wait_ge(giota, p.val(giota))
        p.scalar.wait_ge(giota, p.val(giota))
        sqq.v_("tensor_scalar", wrow[0:1, 0:64], wrow[0:1, 0:64], hb_c[0:1, 0:1], None,
               op0=ALU.add)
        sqq.s_("activation", out=wrow[0:1, 0:64], in_=wrow[0:1, 0:64], func=AF.Exp,
               scale=-LN1E4 / 512.0)
        sqq.v_("tensor_scalar", wrow[0:1, 0:64], wrow[0:1, 0:64], 1.0 / TWO_PI, None,
               op0=ALU.mult)
        bb = sqq.mm(ps_bc2[:, 0:64], consts["ones"][0:1, :], wrow[0:1, 0:64],
                    start=True, stop=True)
        sqq.v_("tensor_scalar", sA[:, 0:64], ps_bc2[:, 0:64], iotf[:, 0:1], None, op0=ALU.mult)
        fold_round(sA, 64)
        sqq.v_("tensor_scalar", cA[:, 0:64], sA[:, 0:64], 0.25, None, op0=ALU.add)
        fold_round(cA, 64)
        sqq.s_("activation", out=sA[:, 0:64], in_=sA[:, 0:64], func=AF.Sin, scale=TWO_PI)
        sqq.s_("activation", out=cA[:, 0:64], in_=cA[:, 0:64], func=AF.Sin, scale=TWO_PI)
        p.gpsimd.wait_ge(sqq.s, p.val(sqq.s))
        p.gpsimd.wait_ge(sqq.t, p.val(sqq.t))
        p.gpsimd.iota(wrow[0:1, 0:128], pattern=[[1, 128]], base=0, channel_multiplier=0,
                      allow_small_or_imprecise_dtypes=True).then_inc(giota, 1)
        p.vector.wait_ge(giota, p.val(giota))
        p.tensor.wait_ge(giota, p.val(giota))
        bb = sqq.mm(ps_bc2[:, 0:128], consts["ones"][0:1, :], wrow[0:1, 0:128],
                    start=True, stop=True)
        sqq.v_("tensor_scalar", sb_t[:, :], ps_bc2[:, 0:128], omg[:, 0:1], None, op0=ALU.mult)
        fold_round(sb_t, 128)
        sqq.v_("tensor_scalar", cb_t[:, :], sb_t[:, :], 0.25, None, op0=ALU.add)
        fold_round(cb_t, 128)
        sqq.s_("activation", out=sb_t[:, :], in_=sb_t[:, :], func=AF.Sin, scale=TWO_PI)
        sqq.s_("activation", out=cb_t[:, :], in_=cb_t[:, :], func=AF.Sin, scale=TWO_PI)
        p.gpsimd.wait_ge(sqq.s, p.val(sqq.s))
        p.gpsimd.wait_ge(sqq.t, p.val(sqq.t))
        p.gpsimd.iota(wrow[0:1, 0:145], pattern=[[128, 145]], base=JOFF, channel_multiplier=0,
                      allow_small_or_imprecise_dtypes=True).then_inc(giota, 1)
        p.vector.wait_ge(giota, p.val(giota))
        p.tensor.wait_ge(giota, p.val(giota))
        bb = sqq.mm(ps_bc2[:, 0:145], consts["ones"][0:1, :], wrow[0:1, 0:145],
                    start=True, stop=True)
        sqq.v_("tensor_scalar", sa_t[:, :], ps_bc2[:, 0:145], omg[:, 0:1], None, op0=ALU.mult)
        fold_round(sa_t, 145)
        sqq.v_("tensor_scalar", ca_t[:, :], sa_t[:, :], 0.25, None, op0=ALU.add)
        fold_round(ca_t, 145)
        sqq.s_("activation", out=sa_t[:, :], in_=sa_t[:, :], func=AF.Sin, scale=TWO_PI)
        sqq.s_("activation", out=ca_t[:, :], in_=ca_t[:, :], func=AF.Sin, scale=TWO_PI)
        sqq.v_("tensor_tensor", tg1[:, 0:145], sa_t[:, :], ca_t[:, :], op=ALU.subtract)
        sqq.v_("tensor_scalar", tg1[:, 0:145], tg1[:, 0:145], par[:, 0:1], None, op0=ALU.mult)
        sqq.v_("tensor_tensor", Pt[:, :], sa_t[:, :], tg1[:, 0:145], op=ALU.subtract)
        sqq.v_("tensor_tensor", tg1[:, 0:145], ca_t[:, :], sa_t[:, :], op=ALU.add)
        sqq.v_("tensor_scalar", tg1[:, 0:145], tg1[:, 0:145], par[:, 0:1], None, op0=ALU.mult)
        sqq.v_("tensor_tensor", Qt[:, :], ca_t[:, :], tg1[:, 0:145], op=ALU.subtract)
        for a in range(145):
            w = min(128, JW - a * 128)
            sqq.v_("tensor_scalar", sS[:, 0:w], cb_t[:, 0:w], Pt[:, a:a + 1], None, op0=ALU.mult)
            sqq.v_("scalar_tensor_tensor", pext[:, a * 128:a * 128 + w], sb_t[:, 0:w],
                   Qt[:, a:a + 1], sS[:, 0:w], op0=ALU.mult, op1=ALU.add)
        p.sync.wait_ge(tsem, p.val(tsem))
        p.sync.wait_ge(sqq.t, p.val(sqq.t))
        kv_ = p.sync.dma_start(out=kh[:], in_=kf[h][:, :]).inc(khsem, 16)
        vv_ = p.sync.dma_start(out=vh[:], in_=vf[h][:, :]).inc(vhsem, 16)
        p.tensor.wait_ge(vhsem, vv_)
        for kt in range(NKT):
            sqq.tr(ps_tr[:, 0:128], vh[:, kt * 128:(kt + 1) * 128], consts["ident"][:, :])
            sqq.v_("tensor_copy", vt[:, kt * 128:(kt + 1) * 128], ps_tr[:, 0:128])
        p.tensor.wait_ge(khsem, kv_)
        for j in range(NQT):
            p.sync.wait_ge(sqq.v, p.val(sqq.v))
            zl = p.sync.dma_start(out=zq_t[:, :],
                                  in_=zqf[h][j * 128:(j + 1) * 128, 0:128]).inc(zsem, 16)
            zl2 = p.sync.dma_start(out=zq_s[:, :],
                                   in_=zqf[h][j * 128 + 1:(j + 1) * 128 + 1, 0:128]
                                   ).inc(zsem, 16)
            p.vector.wait_ge(zsem, zl2)
            for (zsrc, dstq) in ((zq_t, qh1), (zq_s, qh2)):
                sqq.v_("tensor_tensor", rot[:, 0:128:2], zsrc[:, 0:128:2], cA[:, 0:64], op=ALU.mult)
                sqq.v_("tensor_tensor", tg1[:, 0:64], zsrc[:, 1:128:2], sA[:, 0:64], op=ALU.mult)
                sqq.v_("tensor_tensor", rot[:, 0:128:2], rot[:, 0:128:2], tg1[:, 0:64], op=ALU.add)
                sqq.v_("tensor_tensor", rot[:, 1:128:2], zsrc[:, 1:128:2], cA[:, 0:64], op=ALU.mult)
                sqq.v_("tensor_tensor", tg1[:, 0:64], zsrc[:, 0:128:2], sA[:, 0:64], op=ALU.mult)
                sqq.v_("tensor_tensor", rot[:, 1:128:2], rot[:, 1:128:2], tg1[:, 0:64],
                       op=ALU.subtract)
                sqq.v_("tensor_copy", rotb[:, :], rot[:, :])
                sqq.tr(ps_tr[:, 0:128], rotb[:, :], consts["ident"][:, :])
                sqq.v_("tensor_copy", dstq[:, j * 128:(j + 1) * 128], ps_tr[:, 0:128])
        qrot_done = p.val(sqq.v)
        p.tensor.wait_ge(sqq.v, qrot_done)

        S_done = [0, 0]
        pT_used = [0, 0]
        pv_evac = 0
        for qc in range(NQT // 4):
            q0l = qc * 512
            for kt in range(NKT):
                k0 = kt * 128
                sb_ = kt % 2
                p.tensor.wait_ge(ssem, S_done[sb_])
                straddles = []
                strad_plan = []
                for qi in range(4):
                    q0t = q0l + qi * 128
                    dd = k0 - q0t
                    jj1 = 10241 + k0 - q0t
                    jj2 = 2048 + k0 - q0t
                    qsl = slice(q0t, q0t + 128)
                    osl = slice(qi * 128, (qi + 1) * 128)
                    mmq = qbuf[:, q0t:q0t + 128]
                    if dd <= -128:
                        p.tensor.matmul(ps_S[sb_][:, osl], kh[:, k0:k0 + 128],
                                        mmq, start=True, stop=False)
                        mm = p.tensor.matmul(ps_S[sb_][:, osl], pext[:, jj1:jj1 + 128],
                                             qh1[:, qsl], start=False, stop=True)
                    elif dd >= 256:
                        p.tensor.matmul(ps_S[sb_][:, osl], kh[:, k0:k0 + 128],
                                        mmq, start=True, stop=False)
                        mm = p.tensor.matmul(ps_S[sb_][:, osl], pext[:, jj2:jj2 + 128],
                                             qh2[:, qsl], start=False, stop=True)
                    else:
                        ci = 0 if dd == 0 else 1
                        mm = p.tensor.matmul(ps_S[sb_][:, osl], kh[:, k0:k0 + 128],
                                             mmq, start=True, stop=True)
                        strad_plan.append((qi, ci, jj1, jj2, qsl, osl))
                        straddles.append((qi, ci))
                for (qi, ci, jj1, jj2, qsl, osl) in strad_plan:
                    p.tensor.wait_ge(vsem, p.val(vsem))
                    p.tensor.matmul(ps_p1[:, osl], pext[:, jj1:jj1 + 128],
                                    qh1[:, qsl], start=True, stop=True)
                    mm = p.tensor.matmul(ps_p2[:, osl], pext[:, jj2:jj2 + 128],
                                         qh2[:, qsl], start=True, stop=True)
                tv = mm.inc(tsem, 1)
                p.scalar.wait_ge(tsem, tv)
                p.scalar.wait_ge(tsem, pT_used[sb_])
                if not straddles:
                    sv = p.scalar.activation(out=pT[sb_][:, :], in_=ps_S[sb_][:, :],
                                             func=AF.Exp, scale=1.0 / 32.0).inc(ssem, 1)
                else:
                    p.vector.wait_ge(tsem, tv)
                    done = set()
                    for (qi, ci) in straddles:
                        osl = slice(qi * 128, (qi + 1) * 128)
                        p.vector.tensor_tensor(stA[:, :], ps_p1[:, osl], flow[ci][:, :],
                                               op=ALU.mult)
                        p.vector.tensor_tensor(bcsb[:, :], ps_p2[:, osl], fupp[ci][:, :],
                                               op=ALU.mult)
                        p.vector.tensor_tensor(stA[:, :], stA[:, :], bcsb[:, :], op=ALU.add)
                        p.vector.tensor_tensor(sS[:, osl], stA[:, :], ps_S[sb_][:, osl],
                                               op=ALU.add).then_inc(vsem, 1)
                        done.add(qi)
                    p.scalar.wait_ge(vsem, p.val(vsem))
                    sv = None
                    for qi in range(4):
                        osl = slice(qi * 128, (qi + 1) * 128)
                        src = sS[:, osl] if qi in done else ps_S[sb_][:, osl]
                        sv = p.scalar.activation(out=pT[sb_][:, osl], in_=src,
                                                 func=AF.Exp, scale=1.0 / 32.0).inc(ssem, 1)
                S_done[sb_] = sv
                p.tensor.wait_ge(ssem, sv)
                if kt == 0:
                    p.tensor.wait_ge(vsem, pv_evac)
                p.tensor.matmul(ps_pv[:, :], vt[:, k0:k0 + 128], pT[sb_][:, :],
                                start=(kt == 0), stop=(kt == NKT - 1))
                for qi in range(4):
                    osl = slice(qi * 128, (qi + 1) * 128)
                    mm = p.tensor.matmul(ps_dn[:, qc * 4 + qi:qc * 4 + qi + 1],
                                         pT[sb_][:, osl], onecolb[:, 0:1],
                                         start=(kt == 0 and qi == 0),
                                         stop=(kt == NKT - 1 and qi == 3),
                                         skip_group_check=True)
                pT_used[sb_] = mm.inc(tsem, 1)
            tvf = p.val(tsem)
            p.vector.wait_ge(tsem, tvf)
            for qi in range(4):
                q0t = q0l + qi * 128
                osl = slice(qi * 128, (qi + 1) * 128)
                p.vector.tensor_copy(dcol[:, 0:1], ps_dn[:, qc * 4 + qi:qc * 4 + qi + 1])
                rv = p.vector.reciprocal(dcol[:, 0:1], dcol[:, 0:1]).inc(vsem, 1)
                p.tensor.wait_ge(vsem, rv)
                t1 = p.tensor.transpose(ps_bc2[0:1, 0:128], dcol[:, 0:1],
                                        identf[:, :]).inc(tsem, 1)
                p.vector.wait_ge(tsem, t1)
                cv = p.vector.tensor_copy(drow[0:1, :], ps_bc2[0:1, 0:128]).inc(vsem, 1)
                p.tensor.wait_ge(vsem, cv)
                t2 = p.tensor.matmul(ps_bc2[:, 0:128], consts["ones"][0:1, :], drow[0:1, :],
                                     start=True, stop=True).inc(tsem, 1)
                p.vector.wait_ge(tsem, t2)
                p.vector.tensor_copy(bcsb[:, :], ps_bc2[:, 0:128])
                ev = p.vector.tensor_tensor(attn[:, q0t:q0t + 128],
                                            ps_pv[:, osl], bcsb[:, :], op=ALU.mult).inc(vsem, 1)
            pv_evac = p.val(vsem)
            p.tensor.wait_ge(vsem, pv_evac)
        av = p.val(vsem)
        p.sync.wait_ge(vsem, av)
        p.sync.dma_start(out=a_out[h][:, :], in_=attn[:]).then_inc(stsem, 16)
        p.vector.wait_ge(stsem, p.val(stsem))

    p.sync.wait_ge(stsem, p.val(stsem))
    return p.finalize() if own else None


def run_txl_attn(st):
    nc = build_txl_attn()
    in_maps = []
    for c in range(NCORES):
        im = {}
        for i in range(2):
            pair = 2 * c + i
            b, h = pair // 8, pair % 8
            bsl = slice(b * S, (b + 1) * S)
            rows = slice(h * 128, (h + 1) * 128)
            im[f"kf{i}"] = np.ascontiguousarray(st["txl_k"][rows, bsl])
            im[f"vf{i}"] = np.ascontiguousarray(st["txl_v"][rows, bsl])
            im[f"quf{i}"] = np.ascontiguousarray(st["txl_qu"][rows, bsl])
            zq = st["txl_zq"][b * S:(b + 1) * S, h * 128:(h + 1) * 128]
            zq = np.concatenate([zq, zq[-1:]], axis=0)
            im[f"zqf{i}"] = np.ascontiguousarray(zq)
            im[f"hbase{i}"] = np.full((1, 1), 64.0 * h, np.float32)
        in_maps.append(im)
    res = run_bass_kernel_spmd(nc, in_maps, core_ids=list(range(NCORES)))
    A = np.zeros((D, T_ALL), BF16NP)
    for c in range(NCORES):
        for i in range(2):
            pair = 2 * c + i
            b, h = pair // 8, pair % 8
            A[h * 128:(h + 1) * 128, b * S:(b + 1) * S] = res.results[c][f"a_out{i}"]
    st["txl_attn_fm"] = A
    return st



def build_txl_post(p=None):
    own = p is None
    if own:
        p = Prog("txl_post")
    xs = p.din("xs", [D, TC], F32)
    af = p.din("af", [D, TC], BF16)
    wo = p.din("wo", [D, D], F32)
    bop = p.din("bop", [128, DCH], F32)
    x_out = p.dout("x_out", [D, TC], F32)

    consts = emit_consts(p)
    tsem = p.sem("tsem")
    vsem = p.sem("vsem")
    gld = p.sem("gld")
    bsem = p.sem("bsem")
    asem = p.sem("asem")
    lsem = p.sem("lsem")

    abuf = p.sb("abuf", [128, DCH * TC], BF16)
    p.sync.dma_start(out=abuf[:].rearrange("p (c t) -> p c t", c=DCH),
                     in_=af.rearrange("(c p) t -> p c t", p=128)).then_inc(asem, 16)
    xbuf = p.sb("xbuf", [128, DCH * TC], F32)
    x_ch = [xbuf[:, ch * TC:(ch + 1) * TC] for ch in range(DCH)]
    p.sync.dma_start(out=xbuf[:].rearrange("p (c t) -> p c t", c=DCH),
                     in_=xs.rearrange("(c p) t -> p c t", p=128)).then_inc(lsem, 16)
    wo_s = p.sb("wo_s", [128, DCH * D], BF16)
    p.gpsimd.dma_start(out=wo_s[:].rearrange("p (c t) -> p c t", c=DCH),
                       in_=wo.rearrange("(c p) t -> p c t", p=128)).then_inc(gld, 16)
    bo_s = p.sb("bo_s", [128, DCH], F32)
    p.sync.dma_start(out=bo_s[:], in_=bop[:, :]).then_inc(bsem, 16)

    ps_pr = [p.ps("prA", [128, 512], F32), p.ps("prB", [128, 512], F32)]
    ostg = [p.sb(f"ostg{i}", [128, 512], F32) for i in range(2)]
    osem = [p.sem("oss0"), p.sem("oss1")]
    ps_free = [0, 0]
    o_free = [0, 0]
    p.tensor.wait_ge(asem, 16)
    p.tensor.wait_ge(lsem, 16)
    p.tensor.wait_ge(gld, 16)
    p.tensor.wait_ge(bsem, 16)
    p.vector.wait_ge(lsem, 16)
    si = 0
    for g in range(TC // 512):
        g0 = g * 512
        for oc in range(DCH):
            i2 = si % 2
            p.tensor.wait_ge(vsem, ps_free[i2])
            for ich in range(DCH):
                mm = p.tensor.matmul(ps_pr[i2][:, :],
                                     wo_s[:, ich * D + oc * 128:ich * D + (oc + 1) * 128],
                                     abuf[:, ich * TC + g0:ich * TC + g0 + 512],
                                     start=(ich == 0), stop=(ich == DCH - 1))
            tv = mm.inc(tsem, 1)
            p.vector.wait_ge(tsem, tv)
            p.vector.wait_ge(osem[i2], o_free[i2])
            pf = p.vector.tensor_scalar(ostg[i2][:], ps_pr[i2][:, :], bo_s[:, oc:oc + 1],
                                        None, op0=ALU.add).inc(vsem, 1)
            ps_free[i2] = pf
            ov = p.vector.tensor_tensor(ostg[i2][:], ostg[i2][:],
                                        x_ch[oc][:, g0:g0 + 512], op=ALU.add).inc(vsem, 1)
            p.sync.wait_ge(vsem, ov)
            so = p.sync.dma_start(out=x_out[oc * 128:(oc + 1) * 128, g0:g0 + 512],
                                  in_=ostg[i2][:]).inc(osem[i2], 16)
            o_free[i2] = so
            si += 1
    for sm in osem:
        p.sync.wait_ge(sm, p.val(sm))
    return p.finalize() if own else None


def run_txl_post(st):
    nc = build_txl_post()
    in_maps = []
    for c in range(NCORES):
        sl = slice(c * TC, (c + 1) * TC)
        in_maps.append({
            "xs": np.ascontiguousarray(st["x_fm"][:, sl]),
            "af": np.ascontiguousarray(st["txl_attn_fm"][:, sl]),
            "wo": st["L1_attn_b_Wo"],
            "bop": _packbias(st["L1_attn_b_bo"]),
        })
    res = run_bass_kernel_spmd(nc, in_maps, core_ids=list(range(NCORES)))
    st["x_fm"] = np.concatenate([r["x_out"] for r in res.results], axis=1)
    return st


def run_txl(st):
    st = run_txl_pre(st)
    st = run_txl_attn(st)
    st = run_txl_post(st)
    return st



def build_moe_prep(p=None):
    own = p is None
    if own:
        p = Prog("moe_prep")
    xs = p.din("xs", [D, TC], F32)
    rtT = p.din("rtT", [D, 8], F32)
    htm_out = p.dout("htm_out", [TC, D], BF16)
    lg_out = p.dout("lg_out", [TC, 8], F32)

    consts = emit_consts(p)
    lsem = p.sem("lsem")
    bsem = p.sem("bsem")
    tsem = p.sem("tsem")
    vsem = p.sem("vsem")

    xbuf = p.sb("xbuf", [128, DCH * TC], F32)
    x_ch = [xbuf[:, ch * TC:(ch + 1) * TC] for ch in range(DCH)]
    p.sync.dma_start(out=xbuf[:].rearrange("p (c t) -> p c t", c=DCH),
                     in_=xs.rearrange("(c p) t -> p c t", p=128)).then_inc(lsem, 16)
    rt_s = p.sb("rt_s", [128, DCH * 8], F32)
    p.sync.dma_start(out=rt_s[:].rearrange("p (c t) -> p c t", c=DCH),
                     in_=rtT.rearrange("(c p) t -> p c t", p=128)).then_inc(bsem, 16)

    ln = LN(p)
    hf = [p.sb(f"hf{ch}", [128, 512], F32) for ch in range(DCH)]
    hb = [p.sb(f"hb{ch}", [128, 512], BF16) for ch in range(DCH)]
    atmp = p.sb("atmp", [128, 512], F32)
    stg_tm = [p.sb(f"stgtm{i}", [128, D], BF16) for i in range(2)]
    stg_lg = [p.sb(f"stglg{i}", [128, 8], F32) for i in range(2)]
    tmsem = [p.sem("tms0"), p.sem("tms1")]
    lgsem = [p.sem("lgs0"), p.sem("lgs1")]
    ps_t = p.ps("ps_t", [128, 1024], BF16)[:, 0:512]
    ps_l = p.ps("ps_l", [128, 512], F32)
    tm_free = [0, 0]
    lg_free = [0, 0]
    p.tensor.wait_ge(lsem, 16)
    p.scalar.wait_ge(lsem, 16)
    p.vector.wait_ge(lsem, 16)
    p.tensor.wait_ge(bsem, 16)
    for g in range(TC // 512):
        g0 = g * 512
        p.tensor.wait_ge(vsem, p.val(vsem))
        p.scalar.wait_ge(vsem, p.val(vsem))
        bm, br = ln.stats(x_ch, g0, 512, consts)
        for ch in range(DCH):
            p.vector.tensor_tensor(atmp[:, 0:512], x_ch[ch][:, g0:g0 + 512], bm[:, 0:512], op=ALU.add)
            p.vector.tensor_tensor(hf[ch][:, 0:512], atmp[:, 0:512], br[:, 0:512], op=ALU.mult)
            p.vector.tensor_copy(hb[ch][:, 0:512], hf[ch][:, 0:512]).then_inc(vsem, 1)
        hv = p.val(vsem)
        p.tensor.wait_ge(vsem, hv)
        for j in range(4):
            i2 = j % 2
            for ich in range(DCH):
                mm = p.tensor.matmul(ps_l[0:128, 0:8], hf[ich][:, j * 128:(j + 1) * 128],
                                     rt_s[:, ich * 8:(ich + 1) * 8],
                                     start=(ich == 0), stop=(ich == DCH - 1))
            tv = mm.inc(tsem, 1)
            p.vector.wait_ge(tsem, tv)
            p.vector.wait_ge(lgsem[i2], lg_free[i2])
            lgv = p.vector.tensor_copy(stg_lg[i2][:], ps_l[0:128, 0:8]).inc(vsem, 1)
            p.sync.wait_ge(vsem, lgv)
            lg_free[i2] = p.sync.dma_start(out=lg_out[g0 + j * 128:g0 + (j + 1) * 128, :],
                                           in_=stg_lg[i2][:]).inc(lgsem[i2], 16)
            p.vector.wait_ge(tmsem[i2], tm_free[i2])
            for ich in range(DCH):
                tt = p.tensor.transpose(ps_t[:, 0:128], hb[ich][:, j * 128:(j + 1) * 128],
                                        consts["ident"][:, :]).inc(tsem, 1)
                p.vector.wait_ge(tsem, tt)
                ev = p.vector.tensor_copy(stg_tm[i2][:, ich * 128:(ich + 1) * 128],
                                          ps_t[:, 0:128]).inc(vsem, 1)
                p.tensor.wait_ge(vsem, ev)
            p.sync.wait_ge(vsem, p.val(vsem))
            tm_free[i2] = p.sync.dma_start(out=htm_out[g0 + j * 128:g0 + (j + 1) * 128, :],
                                           in_=stg_tm[i2][:]).inc(tmsem[i2], 16)
    for sm in (tmsem[0], tmsem[1], lgsem[0], lgsem[1]):
        p.sync.wait_ge(sm, p.val(sm))
    return p.finalize() if own else None


def run_moe_prep(st):
    nc = build_moe_prep()
    in_maps = []
    for c in range(NCORES):
        sl = slice(c * TC, (c + 1) * TC)
        in_maps.append({
            "xs": np.ascontiguousarray(st["x_fm"][:, sl]),
            "rtT": _fm(st["L1_ffn_router_weight"]),
        })
    res = run_bass_kernel_spmd(nc, in_maps, core_ids=list(range(NCORES)))
    st["moe_h_tm"] = np.concatenate([r["htm_out"] for r in res.results], axis=0)
    st["moe_lg"] = np.concatenate([r["lg_out"] for r in res.results], axis=0)
    return st


def kernel(**inputs):
    st = prepare_state(inputs)
    st = run_attn_a_pre(st)
    st = run_attn_a(st)
    st = run_ffn0(st)
    st = run_gdn_pre(st)
    st = run_gdn(st)
    st = run_gdn_post(st)
    st = run_ffn0(st)
    st = run_lin_pre(st)
    st = run_lin_run(st)
    st = run_lin_post(st)
    st = run_moe(st, "1")
    st = run_txl_pre(st)
    st = run_txl_attn(st)
    st = run_txl_post(st)
    st = run_moe_prep(st)
    st = run_moe(st, "2")
    x = np.ascontiguousarray(st["x_fm"].T).reshape(NB, S, D)
    return x.astype(np.float32, copy=False)



GRP_ALL = [[0, 1, 2, 3, 4, 5, 6, 7]]
GRP_BATCH = [[0, 1, 2, 3], [4, 5, 6, 7]]
IXW = 176
AG_ADDR = None


def _ag(p, tag, src, rows, cols, dtype, groups, csem):
    g = len(groups[0])
    out = p.dtmp(f"ag_{tag}", [g * rows, cols], dtype,
                 addr_space=AG_ADDR if g == 8 else None)
    p.gpsimd.collective_compute("AllGather", ALU.bypass, groups,
                                [src[0:rows, 0:cols]], [out[:, :]]).then_inc(csem, 1)
    return out


def _agfence(p, csem):
    for eng in ("sync", "gpsimd", "tensor", "vector", "scalar"):
        getattr(p, eng).wait_ge(csem, p.val(csem))


def _gather(p, tag, items, src, nrows, width, dtype):
    sg = p.sem(f"{tag}_g")
    ss = p.sem(f"{tag}_s")
    bufs = [p.sb(f"{tag}_b{i}", [nrows, width], dtype) for i in range(2)]
    bv = [0, 0]
    for n, (dst, idx) in enumerate(items):
        i2 = n % 2
        p.gpsimd.wait_ge(ss, bv[i2])
        gv = p.gpsimd.indirect_dma_start(
            out=bufs[i2][0:nrows, 0:width], out_offset=None, in_=src,
            in_offset=bass.IndirectOffsetOnAxis(idx, 0)).inc(sg, 16)
        p.sync.wait_ge(sg, gv)
        bv[i2] = p.sync.dma_start(out=dst, in_=bufs[i2][0:nrows, 0:width]).inc(ss, 16)
    p.sync.wait_ge(ss, p.val(ss))


def build_fused(upto=None, dumps=()):
    p = Prog("fused")
    W = {}

    def din(name, shape, dtype=F32):
        W[name] = p.nc.dram_tensor(name, list(shape), dtype, kind="ExternalInput").ap()
        return W[name]

    din("xs", [D, TC])
    din("ixp", [128, IXW], I32)
    for nm in ("aa_wqT", "aa_wkT", "aa_wvT", "aa_woT"):
        din(nm, [D, D])
    for nm in ("aa_abq", "aa_abk", "aa_abv", "aa_abo"):
        din(nm, [8, 2 * D])
    din("aa_biasp", [128, 3 * DCH])
    din("aa_bop", [128, DCH])
    din("aa_rz", [1, 1])
    din("f0_wk", [D, D])
    din("f0_abf", [16, 2 * D])
    din("f0_bp", [128, DCH])
    for nm in ("gd_wq", "gd_wk", "gd_wv"):
        din(nm, [D, D])
    din("gd_wba", [D, 16])
    din("gd_lnp", [128, 2 * DCH])
    din("gd_wo", [D, D])
    din("gd_w1", [D, 4 * D])
    din("gd_w2", [4 * D, D])
    din("gd_lnp2", [128, 2 * DCH])
    din("gd_b1p", [128, 32])
    din("gd_b2p", [128, DCH])
    din("li_wqkvT", [D, 3 * D])
    din("li_bqp", [128, 24])
    din("li_scl", [1, 1])
    din("li_wpT", [D, D])
    din("li_bpp", [128, DCH])
    din("li_rtT", [D, 8])
    din("mo_preT", [D, 4 * D])
    din("mo_gateT", [D, 4 * D])
    din("mo_postT", [16 * 128, 16 * 128])
    din("mo_ohp", [128, 8])
    din("tx_lnwb", [128, 2 * DCH])
    for nm in ("tx_wq", "tx_wk", "tx_wv", "tx_wpT"):
        din(nm, [128, 128])
    din("tx_bqs", [128, 3])
    din("tx_ubvb", [128, 16])
    din("tx_wo", [D, D])
    din("tx_bop", [128, DCH])
    din("tx_hbase0", [1, 1])
    din("tx_hbase1", [1, 1])
    x_out = p.nc.dram_tensor("x_out", [D, TC], F32, kind="ExternalOutput").ap()

    consts = emit_consts(p)
    csem = p.sem("csem")
    ld0 = p.sem("ld0")
    ixs = p.sb("ixs_tab", [128, IXW], I32)
    p.sync.dma_start(out=ixs[:], in_=W["ixp"][:, :]).then_inc(ld0, 16)
    p.gpsimd.wait_ge(ld0, 16)

    xA = p.dtmp("xA", [D, TC], F32)
    xB = p.dtmp("xB", [D, TC], F32)

    steps = []

    def step(label, xafter):
        def deco(fn):
            steps.append((label, fn, xafter))
        return deco

    qd = p.dtmp("qd", [D, TC], BF16)
    kd = p.dtmp("kd", [D, TC], BF16)
    vd = p.dtmp("vd", [D, TC], BF16)
    lnxd = p.dtmp("lnxd", [D, TC], F32)
    kfd = p.dtmp("kfd", [D, S], BF16)
    vfd = p.dtmp("vfd", [D, S], BF16)

    @step("p0", None)
    def _():
        with p.phase("p0"):
            p.ext = {"xs": W["xs"], "wqT": W["aa_wqT"], "wkT": W["aa_wkT"],
                     "wvT": W["aa_wvT"], "abq": W["aa_abq"], "abk": W["aa_abk"],
                     "abv": W["aa_abv"], "biasp": W["aa_biasp"], "q_out": qd,
                     "k_out": kd, "v_out": vd, "lnx_out": lnxd}
            build_attn_a_pre(p)
            p.ext = None

    @step("s0", None)
    def _():
        with p.phase("s0"):
            kag = _ag(p, "k", kd, D, TC, BF16, GRP_BATCH, csem)
            vag = _ag(p, "v", vd, D, TC, BF16, GRP_BATCH, csem)
            _agfence(p, csem)
            ps0 = p.sem("ps0")
            p.sync.dma_start(out=kfd.rearrange("d (r t) -> d r t", r=4),
                             in_=kag.rearrange("(r d) t -> d r t", d=D)).then_inc(ps0, 16)
            p.sync.dma_start(out=vfd.rearrange("d (r t) -> d r t", r=4),
                             in_=vag.rearrange("(r d) t -> d r t", d=D)).then_inc(ps0, 16)
            p.sync.wait_ge(ps0, 32)

    @step("p1", xA)
    def _():
        with p.phase("p1"):
            p.ext = {"qf": qd, "kf": kfd, "vf": vfd, "lnxs": lnxd, "xs": W["xs"],
                     "woT": W["aa_woT"], "abo": W["aa_abo"], "bop": W["aa_bop"],
                     "rz": W["aa_rz"], "x2_out": xA}
            build_attn_a(p)
            p.ext = None

    @step("p2", xB)
    def _():
        with p.phase("p2"):
            p.ext = {"xs": xA, "wk": W["f0_wk"], "abf": W["f0_abf"],
                     "bp": W["f0_bp"], "x_out": xB}
            build_ffn0(p)
            p.ext = None

    y0d = p.dtmp("y0d", [D, TC], F32)
    gq = p.dtmp("gq", [D, TC], BF16)
    gk = p.dtmp("gk", [D, TC], BF16)
    gv_ = p.dtmp("gv", [D, TC], BF16)
    glnab = p.dtmp("glnab", [16, TC], F32)
    chq = [p.dtmp(f"chq{i}", [128, S], BF16) for i in range(2)]
    chk = [p.dtmp(f"chk{i}", [128, S], BF16) for i in range(2)]
    chv = [p.dtmp(f"chv{i}", [128, S], BF16) for i in range(2)]
    chln = [p.dtmp(f"chln{i}", [2, S], F32) for i in range(2)]
    god = [p.dtmp(f"god{i}", [128, S], BF16) for i in range(2)]
    osd = p.dtmp("osd", [D, TC], BF16)

    @step("p3", None)
    def _():
        with p.phase("p3"):
            p.ext = {"xs": xB, "wq": W["gd_wq"], "wk": W["gd_wk"], "wv": W["gd_wv"],
                     "wba": W["gd_wba"], "lnp": W["gd_lnp"], "y0_out": y0d,
                     "q_out": gq, "k_out": gk, "v_out": gv_, "lnab_out": glnab}
            build_gdn_pre(p)
            p.ext = None

    @step("s3", None)
    def _():
        with p.phase("s3"):
            qag = _ag(p, "gq", gq, D, TC, BF16, GRP_BATCH, csem)
            kag = _ag(p, "gk", gk, D, TC, BF16, GRP_BATCH, csem)
            vag = _ag(p, "gv", gv_, D, TC, BF16, GRP_BATCH, csem)
            lnag = _ag(p, "gln", glnab, 16, TC, F32, GRP_BATCH, csem)
            _agfence(p, csem)
            for nm, dsts, src in (("q", chq, qag), ("k", chk, kag), ("v", chv, vag)):
                items = [(dsts[i][:, r * TC:(r + 1) * TC], ixs[:, i * 4 + r:i * 4 + r + 1])
                         for i in range(2) for r in range(4)]
                _gather(p, f"g{nm}", items, src[:, :], 128, TC, BF16)
            items = [(chln[i][0:2, r * TC:(r + 1) * TC],
                      ixs[0:2, 160 + i * 4 + r:161 + i * 4 + r])
                     for i in range(2) for r in range(4)]
            _gather(p, "gln", items, lnag[:, :], 2, TC, F32)

    @step("p4", None)
    def _():
        with p.phase("p4"):
            ext = {}
            for i in range(2):
                ext[f"q{i}"] = chq[i]
                ext[f"k{i}"] = chk[i]
                ext[f"v{i}"] = chv[i]
                ext[f"lnr{i}"] = chln[i]
                ext[f"o{i}"] = god[i]
            p.ext = ext
            build_gdn_run(p)
            p.ext = None

    @step("s4", None)
    def _():
        with p.phase("s4"):
            oqb = p.dtmp("oqb", [2 * 512, TC], BF16)
            ps4 = p.sem("ps4")
            for i in range(2):
                p.sync.dma_start(
                    out=oqb[i * 512:(i + 1) * 512, :].rearrange("(qr pp) t -> pp qr t", pp=128),
                    in_=god[i][:, :].rearrange("pp (qr t) -> pp qr t", qr=4)).then_inc(ps4, 16)
            p.sync.wait_ge(ps4, 32)
            p.gpsimd.wait_ge(ps4, 32)
            oag = _ag(p, "o", oqb, 2 * 512, TC, BF16, GRP_BATCH, csem)
            _agfence(p, csem)
            items = [(osd[h * 128:(h + 1) * 128, :], ixs[:, 8 + h:9 + h])
                     for h in range(DCH)]
            _gather(p, "go", items, oag[:, :], 128, TC, BF16)

    @step("p5", xA)
    def _():
        with p.phase("p5"):
            p.ext = {"xs": xB, "y0s": y0d, "os": osd, "wo": W["gd_wo"],
                     "w1": W["gd_w1"], "w2": W["gd_w2"], "lnp": W["gd_lnp2"],
                     "b1p": W["gd_b1p"], "b2p": W["gd_b2p"], "x_out": xA}
            build_gdn_post(p)
            p.ext = None

    @step("p6", xB)
    def _():
        with p.phase("p6"):
            p.ext = {"xs": xA, "wk": W["f0_wk"], "abf": W["f0_abf"],
                     "bp": W["f0_bp"], "x_out": xB}
            build_ffn0(p)
            p.ext = None

    lq = p.dtmp("lq", [D, TC], F32)
    lk = p.dtmp("lk", [D, TC], F32)
    lkv = p.dtmp("lkv", [D, TC], F32)
    lcq = [p.dtmp(f"lcq{i}", [128, S], F32) for i in range(2)]
    lck = [p.dtmp(f"lck{i}", [128, S], F32) for i in range(2)]
    lckv = [p.dtmp(f"lckv{i}", [128, S], F32) for i in range(2)]
    lyd = [p.dtmp(f"lyd{i}", [128, S], BF16) for i in range(2)]
    lys = p.dtmp("lys", [D, TC], BF16)
    htm1 = p.dtmp("htm1", [TC, D], BF16)
    lg1 = p.dtmp("lg1", [TC, 8], F32)

    @step("p7", None)
    def _():
        with p.phase("p7"):
            p.ext = {"xs": xB, "wqkvT": W["li_wqkvT"], "bqp": W["li_bqp"],
                     "scl": W["li_scl"], "q_out": lq, "k_out": lk, "kv_out": lkv}
            build_lin_pre(p)
            p.ext = None

    @step("s7", None)
    def _():
        with p.phase("s7"):
            qag = _ag(p, "lq", lq, D, TC, F32, GRP_BATCH, csem)
            kag = _ag(p, "lk", lk, D, TC, F32, GRP_BATCH, csem)
            kvag = _ag(p, "lkv", lkv, D, TC, F32, GRP_BATCH, csem)
            _agfence(p, csem)
            for nm, dsts, src in (("q", lcq, qag), ("k", lck, kag), ("kv", lckv, kvag)):
                items = [(dsts[i][:, r * TC:(r + 1) * TC], ixs[:, 16 + i * 4 + r:17 + i * 4 + r])
                         for i in range(2) for r in range(4)]
                _gather(p, f"l{nm}", items, src[:, :], 128, TC, F32)

    @step("p8", None)
    def _():
        with p.phase("p8"):
            ext = {}
            for i in range(2):
                ext[f"qi{i}"] = lcq[i]
                ext[f"ki{i}"] = lck[i]
                ext[f"kvi{i}"] = lckv[i]
                ext[f"y{i}"] = lyd[i]
            p.ext = ext
            build_lin_run(p)
            p.ext = None

    @step("s8", None)
    def _():
        with p.phase("s8"):
            yqb = p.dtmp("yqb", [2 * 512, TC], BF16)
            ps8 = p.sem("ps8")
            for i in range(2):
                p.sync.dma_start(
                    out=yqb[i * 512:(i + 1) * 512, :].rearrange("(qr pp) t -> pp qr t", pp=128),
                    in_=lyd[i][:, :].rearrange("pp (qr t) -> pp qr t", qr=4)).then_inc(ps8, 16)
            p.sync.wait_ge(ps8, 32)
            p.gpsimd.wait_ge(ps8, 32)
            yag = _ag(p, "y", yqb, 2 * 512, TC, BF16, GRP_BATCH, csem)
            _agfence(p, csem)
            items = [(lys[j * 128:(j + 1) * 128, :], ixs[:, 24 + j:25 + j])
                     for j in range(DCH)]
            _gather(p, "gy", items, yag[:, :], 128, TC, BF16)

    @step("p9", xA)
    def _():
        with p.phase("p9"):
            p.ext = {"xs": xB, "ys": lys, "wpT": W["li_wpT"], "bpp": W["li_bpp"],
                     "rtT": W["li_rtT"], "x_out": xA, "htm_out": htm1, "lg_out": lg1}
            build_lin_post(p)
            p.ext = None

    def moe_steps(tag, htm_d, lg_d, x_in, x_to):
        htmag = p.dtmp(f"htmag{tag}", [T_ALL, D], BF16, addr_space=AG_ADDR)
        lgag = p.dtmp(f"lgag{tag}", [T_ALL, 8], F32, addr_space=AG_ADDR)
        eod = p.dtmp(f"eod{tag}", [MOE_CAP, D], BF16)
        mraw = [p.dtmp(f"mraw{j}{tag}", [128, NTILE], F32) for j in range(4)]
        eoag = p.dtmp(f"eoag{tag}", [NEXP * MOE_CAP + 128, D], BF16, addr_space=AG_ADDR)
        mslc = [p.dtmp(f"mslc{j}{tag}", [128, 16], F32) for j in range(4)]

        @step(f"sg{tag}", None)
        def _():
            with p.phase(f"sg{tag}"):
                p.gpsimd.collective_compute("AllGather", ALU.bypass, GRP_ALL,
                                            [htm_d[:, :]], [htmag[:, :]]).then_inc(csem, 1)
                p.gpsimd.collective_compute("AllGather", ALU.bypass, GRP_ALL,
                                            [lg_d[:, :]], [lgag[:, :]]).then_inc(csem, 1)
                _agfence(p, csem)

        @step(f"mb{tag}", None)
        def _():
            with p.phase(f"mb{tag}"):
                p.ext = {"lg": lgag, "htm": htmag, "preT": W["mo_preT"],
                         "gateT": W["mo_gateT"], "postT": W["mo_postT"],
                         "ohp": W["mo_ohp"], "eo_out": eod, "rid1_out": mraw[0],
                         "rid2_out": mraw[1], "c1_out": mraw[2], "c2_out": mraw[3]}
                build_moeb(p)
                p.ext = None

        @step(f"sm{tag}", None)
        def _():
            with p.phase(f"sm{tag}"):
                p.gpsimd.collective_compute(
                    "AllGather", ALU.bypass, GRP_ALL, [eod[:, :]],
                    [eoag[0:NEXP * MOE_CAP, :]]).then_inc(csem, 1)
                zs = p.sem("zs")
                stz = p.sem("stz")
                zrow = p.sb("zrow", [1, D], BF16)
                zv = p.vector.memset(zrow[:], 0.0).inc(zs, 1)
                p.sync.wait_ge(zs, zv)
                p.sync.dma_start(out=eoag[NEXP * MOE_CAP:NEXP * MOE_CAP + 1, :],
                                 in_=zrow[:, :]).then_inc(stz, 16)
                lsm = p.sem("lsm")
                tsm = p.sem("tsm")
                vsm = p.sem("vsm")
                gsm = p.sem("gsm")
                ssm = p.sem("ssm")
                identf3 = p.sb("identf3", [128, 128], F32)
                p.vector.wait_ge(consts["sem"], consts["val"])
                idv = p.vector.tensor_copy(identf3[:], consts["ident"][:]).inc(vsm, 1)
                mbuf = p.sb("mbuf", [128, 128], F32)
                mtt = p.sb("mtt", [16, 128], F32)
                mo_ = p.sb("mo", [128, 16], F32)
                pst = p.ps("pst", [128, 512], F32)
                mdt = p.dtmp("mdt", [128, 128], F32)
                for j in range(4):
                    if j:
                        for eng in ("sync", "gpsimd", "tensor", "vector"):
                            getattr(p, eng).wait_ge(ssm, p.val(ssm))
                    lv = p.sync.dma_start(out=mbuf[:], in_=mraw[j][:, :]).inc(lsm, 16)
                    p.tensor.wait_ge(lsm, lv)
                    p.tensor.wait_ge(vsm, idv)
                    tv = p.tensor.transpose(pst[:, 0:128], mbuf[:], identf3[:]).inc(tsm, 1)
                    p.vector.wait_ge(tsm, tv)
                    vv = p.vector.tensor_copy(mbuf[:], pst[:, 0:128]).inc(vsm, 1)
                    p.sync.wait_ge(vsm, vv)
                    sv = p.sync.dma_start(out=mdt[:, :], in_=mbuf[:]).inc(lsm, 16)
                    p.gpsimd.wait_ge(lsm, sv)
                    gv2 = p.gpsimd.indirect_dma_start(
                        out=mtt[:, :], out_offset=None, in_=mdt[:, :],
                        in_offset=bass.IndirectOffsetOnAxis(ixs[0:16, 168:169], 0)
                        ).inc(gsm, 16)
                    p.tensor.wait_ge(gsm, gv2)
                    tv2 = p.tensor.transpose(pst[:, 128:144], mtt[0:16, :],
                                             identf3[0:16, 0:16]).inc(tsm, 1)
                    p.vector.wait_ge(tsm, tv2)
                    vv2 = p.vector.tensor_copy(mo_[:], pst[:, 128:144]).inc(vsm, 1)
                    p.sync.wait_ge(vsm, vv2)
                    p.sync.dma_start(out=mslc[j][:, :], in_=mo_[:]).then_inc(ssm, 16)
                p.sync.wait_ge(ssm, p.val(ssm))
                p.sync.wait_ge(stz, 16)
                _agfence(p, csem)

        @step(f"mc{tag}", x_to)
        def _():
            with p.phase(f"mc{tag}"):
                p.ext = {"xs": x_in, "eo": eoag, "rid1": mslc[0], "rid2": mslc[1],
                         "c1i": mslc[2], "c2i": mslc[3], "x_out": x_to}
                build_moec(p)
                p.ext = None

    moe_steps("1", htm1, lg1, xA, xB)

    tk = p.dtmp("tk", [D, TC], BF16)
    tqu = p.dtmp("tqu", [D, TC], BF16)
    tv_ = p.dtmp("tvv", [D, TC], BF16)
    tzq = p.dtmp("tzq", [DCH * TC, 128], F32)
    tck = [p.dtmp(f"tck{i}", [128, S], BF16) for i in range(2)]
    tcv = [p.dtmp(f"tcv{i}", [128, S], BF16) for i in range(2)]
    tcq = [p.dtmp(f"tcq{i}", [128, S], BF16) for i in range(2)]
    tczq = [p.dtmp(f"tczq{i}", [S + 1, 128], F32) for i in range(2)]
    tad = [p.dtmp(f"tad{i}", [128, S], BF16) for i in range(2)]
    atd = p.dtmp("atd", [D, TC], BF16)
    htm2 = p.dtmp("htm2", [TC, D], BF16)
    lg2 = p.dtmp("lg2", [TC, 8], F32)

    @step("p12", None)
    def _():
        with p.phase("p12"):
            p.ext = {"xs": xB, "lnwb": W["tx_lnwb"], "wq": W["tx_wq"],
                     "wk": W["tx_wk"], "wv": W["tx_wv"], "wpT": W["tx_wpT"],
                     "bqs": W["tx_bqs"], "ubvb": W["tx_ubvb"], "k_out": tk,
                     "qu_out": tqu, "v_out": tv_, "zq_out": tzq}
            build_txl_pre(p)
            p.ext = None

    @step("s12", None)
    def _():
        with p.phase("s12"):
            kag = _ag(p, "tk", tk, D, TC, BF16, GRP_BATCH, csem)
            vag = _ag(p, "tv", tv_, D, TC, BF16, GRP_BATCH, csem)
            qag = _ag(p, "tq", tqu, D, TC, BF16, GRP_BATCH, csem)
            zag = _ag(p, "tz", tzq, DCH * TC, 128, F32, GRP_BATCH, csem)
            _agfence(p, csem)
            for nm, dsts, src in (("k", tck, kag), ("v", tcv, vag), ("q", tcq, qag)):
                items = [(dsts[i][:, r * TC:(r + 1) * TC], ixs[:, i * 4 + r:i * 4 + r + 1])
                         for i in range(2) for r in range(4)]
                _gather(p, f"t{nm}", items, src[:, :], 128, TC, BF16)
            sgz = p.sem("zq_g")
            ssz = p.sem("zq_s")
            zbufs = [p.sb(f"zq_b{i}", [128, 128], F32) for i in range(2)]
            zv2 = [0, 0]
            for i in range(2):
                for jb in range(64):
                    i2 = jb % 2
                    p.gpsimd.wait_ge(ssz, zv2[i2])
                    c0 = 32 + i * 64 + jb
                    gvz = p.gpsimd.indirect_dma_start(
                        out=zbufs[i2][:, :], out_offset=None, in_=zag[:, :],
                        in_offset=bass.IndirectOffsetOnAxis(ixs[:, c0:c0 + 1], 0)
                        ).inc(sgz, 16)
                    p.sync.wait_ge(sgz, gvz)
                    sv = p.sync.dma_start(out=tczq[i][jb * 128:(jb + 1) * 128, :],
                                          in_=zbufs[i2][:, :]).inc(ssz, 16)
                    if jb == 63:
                        sv = p.sync.dma_start(out=tczq[i][S:S + 1, :],
                                              in_=zbufs[i2][127:128, :]).inc(ssz, 16)
                    zv2[i2] = sv
            p.sync.wait_ge(ssz, p.val(ssz))

    @step("p13", None)
    def _():
        with p.phase("p13"):
            ext = {}
            for i in range(2):
                ext[f"kf{i}"] = tck[i]
                ext[f"vf{i}"] = tcv[i]
                ext[f"quf{i}"] = tcq[i]
                ext[f"zqf{i}"] = tczq[i]
                ext[f"hbase{i}"] = W[f"tx_hbase{i}"]
                ext[f"a_out{i}"] = tad[i]
            p.ext = ext
            build_txl_attn(p)
            p.ext = None

    @step("s13", None)
    def _():
        with p.phase("s13"):
            aqb = p.dtmp("aqb", [2 * 512, TC], BF16)
            ps13 = p.sem("ps13")
            for i in range(2):
                p.sync.dma_start(
                    out=aqb[i * 512:(i + 1) * 512, :].rearrange("(qr pp) t -> pp qr t", pp=128),
                    in_=tad[i][:, :].rearrange("pp (qr t) -> pp qr t", qr=4)).then_inc(ps13, 16)
            p.sync.wait_ge(ps13, 32)
            p.gpsimd.wait_ge(ps13, 32)
            aag = _ag(p, "a", aqb, 2 * 512, TC, BF16, GRP_BATCH, csem)
            _agfence(p, csem)
            items = [(atd[h * 128:(h + 1) * 128, :], ixs[:, 8 + h:9 + h])
                     for h in range(DCH)]
            _gather(p, "ga", items, aag[:, :], 128, TC, BF16)

    @step("p14", xA)
    def _():
        with p.phase("p14"):
            p.ext = {"xs": xB, "af": atd, "wo": W["tx_wo"], "bop": W["tx_bop"],
                     "x_out": xA}
            build_txl_post(p)
            p.ext = None

    @step("p15", None)
    def _():
        with p.phase("p15"):
            p.ext = {"xs": xA, "rtT": W["li_rtT"], "htm_out": htm2, "lg_out": lg2}
            build_moe_prep(p)
            p.ext = None

    moe_steps("2", htm2, lg2, xA, x_out)

    cur_x = W["xs"]
    for label, fn, xafter in steps:
        fn()
        if xafter is not None:
            cur_x = xafter
        if upto == label:
            with p.phase("dump"):
                dsem = p.sem("dsem")
                p.sync.dma_start(out=x_out[:, :], in_=cur_x[:, :]).then_inc(dsem, 16)
                for nm in dumps:
                    ap, shp, dt = p._dtmps[nm]
                    od = p.nc.dram_tensor(f"dmp_{nm}", shp, dt, kind="ExternalOutput").ap()
                    p.sync.dma_start(out=od[:, :], in_=ap[:, :]).then_inc(dsem, 16)
                p.sync.wait_ge(dsem, p.val(dsem))
            break
    return p.finalize()


def _fused_ixp(c):
    ix = np.zeros((128, IXW), np.int32)
    ar = np.arange(128, dtype=np.int32)
    b_c, qr_c = c // 4, c % 4
    for i in range(2):
        chain = 2 * c + i
        h = chain % 8
        for r in range(4):
            ix[:, i * 4 + r] = r * D + h * 128 + ar
            ix[0, 160 + i * 4 + r] = r * 16 + h
            ix[1, 160 + i * 4 + r] = r * 16 + 8 + h
        for jb in range(64):
            r = jb // 16
            ix[:, 32 + i * 64 + jb] = r * DCH * TC + h * TC + (jb % 16) * 128 + ar
    for h in range(DCH):
        pair = b_c * 8 + h
        ix[:, 8 + h] = ((pair // 2) % 4) * 1024 + (pair % 2) * 512 + qr_c * 128 + ar
    for i in range(2):
        grp = 4 * c + 2 * i
        h0 = grp % 16
        for r in range(4):
            ix[:, 16 + i * 4 + r] = r * D + (h0 // 2) * 128 + ar
    for j in range(DCH):
        grp = b_c * 16 + 2 * j
        ix[:, 24 + j] = ((grp // 4) % 4) * 1024 + ((grp % 4) // 2) * 512 + qr_c * 128 + ar
    ix[0:16, 168] = c * 16 + ar[0:16]
    return np.ascontiguousarray(ix)


def fused_in_maps(st):
    lnp1 = np.ascontiguousarray(np.concatenate(
        [_packbias(st["L0_attn_b_ln1_w"]), _packbias(st["L0_attn_b_ln1_b"])], axis=1))
    lnp2 = np.ascontiguousarray(np.concatenate(
        [_packbias(st["L0_attn_b_ln2_w"]), _packbias(st["L0_attn_b_ln2_b"])], axis=1))
    b1p = np.ascontiguousarray(np.stack(
        [st["L0_attn_b_b1"][i * 128:(i + 1) * 128] for i in range(32)], axis=1))
    lnwb = np.ascontiguousarray(np.concatenate(
        [_packbias(st["L1_attn_b_ln_gamma"]), _packbias(st["L1_attn_b_ln_beta"])], axis=1))
    bqs = np.ascontiguousarray(np.stack(
        [st["L1_attn_b_bq"], st["L1_attn_b_bk"], st["L1_attn_b_bv"]], axis=1))
    ubvb = np.ascontiguousarray(np.concatenate(
        [st["L1_attn_b_u_bias"].T, st["L1_attn_b_v_bias"].T], axis=1))
    common = {
        "aa_wqT": _fm(st["L0_attn_a_Wq"]), "aa_wkT": _fm(st["L0_attn_a_Wk"]),
        "aa_wvT": _fm(st["L0_attn_a_Wv"]), "aa_woT": _fm(st["L0_attn_a_Wo"]),
        "aa_abq": _packab(st["L0_attn_a_Aq"], st["L0_attn_a_Bq"]),
        "aa_abk": _packab(st["L0_attn_a_Ak"], st["L0_attn_a_Bk"]),
        "aa_abv": _packab(st["L0_attn_a_Av"], st["L0_attn_a_Bv"]),
        "aa_abo": _packab(st["L0_attn_a_Ao"], st["L0_attn_a_Bo"]),
        "aa_biasp": _packbias(st["L0_attn_a_bq"], st["L0_attn_a_bk"], st["L0_attn_a_bv"]),
        "aa_bop": _packbias(st["L0_attn_a_bo"]),
        "aa_rz": st["L0_attn_a_rezero"].reshape(1, 1),
        "f0_wk": st["L0_ffn_kernel"],
        "f0_abf": np.ascontiguousarray(
            np.concatenate([st["L0_ffn_a"].T, st["L0_ffn_b"]], axis=1)),
        "f0_bp": _packbias(st["L0_ffn_bias"]),
        "gd_wq": st["L0_attn_b_Wq"], "gd_wk": st["L0_attn_b_Wk"],
        "gd_wv": st["L0_attn_b_Wv"],
        "gd_wba": np.ascontiguousarray(np.concatenate(
            [st["L0_attn_b_Wb"], st["L0_attn_b_Wa"]], axis=1)),
        "gd_lnp": lnp1, "gd_wo": st["L0_attn_b_Wo"], "gd_w1": st["L0_attn_b_W1"],
        "gd_w2": st["L0_attn_b_W2"], "gd_lnp2": lnp2, "gd_b1p": b1p,
        "gd_b2p": _packbias(st["L0_attn_b_b2"]),
        "li_wqkvT": _fm(st["L1_attn_a_Wqkv"]),
        "li_bqp": np.ascontiguousarray(np.stack(
            [st["L1_attn_a_bqkv"][i * 128:(i + 1) * 128] for i in range(24)], axis=1)),
        "li_scl": st["L1_attn_a_scale"].reshape(1, 1),
        "li_wpT": _fm(st["L1_attn_a_Wproj"]),
        "li_bpp": _packbias(st["L1_attn_a_bproj"]),
        "li_rtT": _fm(st["L1_ffn_router_weight"]),
        "mo_postT": None,
        "tx_lnwb": lnwb, "tx_bqs": bqs, "tx_ubvb": ubvb,
        "tx_wq": st["L1_attn_b_Wq"], "tx_wk": st["L1_attn_b_Wk"],
        "tx_wv": st["L1_attn_b_Wv"], "tx_wpT": _fm(st["L1_attn_b_Wp"]),
        "tx_wo": st["L1_attn_b_Wo"], "tx_bop": _packbias(st["L1_attn_b_bo"]),
    }
    del common["mo_postT"]
    in_maps = []
    for c in range(NCORES):
        sl = slice(c * TC, (c + 1) * TC)
        im = dict(common)
        im["xs"] = np.ascontiguousarray(st["x_fm"][:, sl])
        im["ixp"] = _fused_ixp(c)
        im["mo_preT"] = _fm(st["L1_ffn_ff_pre_act_weight"][c])
        im["mo_gateT"] = _fm(st["L1_ffn_gate_weight"][c])
        im["mo_postT"] = _packpost(st["L1_ffn_ff_post_act_weight"][c])
        im["mo_ohp"] = np.ascontiguousarray(
            np.broadcast_to(np.eye(NEXP, dtype=np.float32)[c], (128, NEXP)))
        for i in range(2):
            im[f"tx_hbase{i}"] = np.full((1, 1), 64.0 * ((2 * c + i) % 8), np.float32)
        in_maps.append(im)
    return in_maps


def kernel_fused(**inputs):
    st = prepare_state(inputs)
    nc = build_fused()
    in_maps = fused_in_maps(st)
    res = run_bass_kernel_spmd(nc, in_maps, core_ids=list(range(NCORES)))
    x_fm = np.concatenate([r["x_out"] for r in res.results], axis=1)
    x = np.ascontiguousarray(x_fm.T).reshape(NB, S, D)
    return x.astype(np.float32, copy=False)


kernel = kernel_fused


# version 20 (final)
# speedup vs baseline: 1.1283x; 1.1283x over previous
import math
import numpy as np
import ml_dtypes
from contextlib import ExitStack

import concourse.bass as bass
import concourse.mybir as mybir
from concourse.bass_utils import run_bass_kernel_spmd

F32 = mybir.dt.float32
BF16 = mybir.dt.bfloat16
I32 = mybir.dt.int32
U32 = mybir.dt.uint32
AF = mybir.ActivationFunctionType
ALU = mybir.AluOpType
AX = mybir.AxisListType
BF16NP = ml_dtypes.bfloat16

NCORES = 8
D = 1024
NB = 2
S = 8192
T_ALL = NB * S
TC = T_ALL // NCORES
DCH = D // 128
TWO_PI = 6.283185307179586

ENGINES = ("sync", "vector", "scalar", "tensor", "gpsimd")


class _SemPH:
    __slots__ = ("name", "phase_idx", "sw", "real", "base")

    def __init__(self, name, phase_idx):
        self.name, self.phase_idx = name, phase_idx
        self.sw = False
        self.real = None
        self.base = 0


class _QInst:
    __slots__ = ("b", "eng", "method", "args", "kwargs", "incs", "label")

    def __init__(self, b, eng, method, args, kwargs):
        self.b, self.eng, self.method = b, eng, method
        self.args, self.kwargs = args, kwargs
        self.incs = []
        self.label = b.prefix

    def then_inc(self, sem, val):
        self.b.semval[id(sem)] += val
        self.incs.append((sem, val, self.b.semval[id(sem)]))
        if (isinstance(sem, _SemPH) and self.eng == "gpsimd"
                and self.method in ("dma_start", "indirect_dma_start")):
            sem.sw = True
        return self

    def inc(self, sem, val):
        self.then_inc(sem, val)
        return self.b.semval[id(sem)]


class _QEng:
    def __init__(self, b, eng):
        self._b, self._eng = b, eng

    def __getattr__(self, name):
        b, eng = self._b, self._eng

        def call(*a, **k):
            inst = _QInst(b, eng, name, a, k)
            b.q[eng].append(inst)
            return inst

        return call


class Prog:

    def __init__(self, name):
        self.nc = bass.Bass("TRN2", target_bir_lowering=False)
        self.name = name
        self.ctx = ExitStack()
        self.q = {e: [] for e in ENGINES}
        self.semval = {}
        self._finalized = False
        for e in ENGINES:
            setattr(self, e, _QEng(self, e))
        self.ext = None
        self.prefix = ""
        self._scope = None
        self._barsem = None
        self._barn = 0
        self.inputs_seen = []
        self._phase_idx = -1
        self._all_phs = []

    def din(self, name, shape, dtype=F32):
        if self.ext is not None:
            return self.ext[name]
        self.inputs_seen.append(name)
        return self.nc.dram_tensor(name, list(shape), dtype, kind="ExternalInput").ap()

    def dout(self, name, shape, dtype=F32):
        if self.ext is not None:
            return self.ext[name]
        return self.nc.dram_tensor(name, list(shape), dtype, kind="ExternalOutput").ap()

    def dtmp(self, name, shape, dtype=F32, addr_space=None):
        kw = {"addr_space": addr_space} if addr_space else {}
        ap = self.nc.dram_tensor(self.prefix + name, list(shape), dtype,
                                 kind="Internal", **kw).ap()
        if not hasattr(self, "_dtmps"):
            self._dtmps = {}
        self._dtmps[self.prefix + name] = (ap, list(shape), dtype)
        return ap

    def sb(self, name, shape, dtype=F32):
        stack = self._scope[0] if self._scope is not None else self.ctx
        return stack.enter_context(self.nc.sbuf_tensor(self.prefix + name, list(shape), dtype))

    def ps(self, name, shape, dtype=F32):
        stack = self._scope[0] if self._scope is not None else self.ctx
        return stack.enter_context(self.nc.psum_tensor(self.prefix + name, list(shape), dtype))

    def sem(self, name):
        h = _SemPH(self.prefix + name, self._phase_idx if self._scope is not None else -1)
        self._all_phs.append(h)
        if self._scope is not None:
            self._scope[1].append(h)
        self.semval[id(h)] = 0
        return h

    def val(self, sem):
        return self.semval[id(sem)]

    def barrier(self):
        if self._barsem is None:
            self._barsem = self.ctx.enter_context(self.nc.semaphore("barsem"))
            self.semval[id(self._barsem)] = 0
        self._barn += 1
        target = 5 * self._barn
        for e in ENGINES:
            getattr(self, e).sem_inc(self._barsem, 1)
        self.semval[id(self._barsem)] += 5
        for e in ENGINES:
            getattr(self, e).wait_ge(self._barsem, target)

    from contextlib import contextmanager as _cm

    @_cm
    def phase(self, name):
        assert self._scope is None
        self.prefix = name + "_"
        self._phase_idx += 1
        self._scope = (ExitStack(), [])
        try:
            yield self
        finally:
            self.barrier()
            sems = self._scope[1]
            for h in sems:
                self.gpsimd.sem_clear(h)
            if sems:
                self.barrier()
            for e in ENGINES:
                self.q[e].append(_QInst(self, e, "__phase_end__", (), {}))
            self._scope[0].close()
            self._scope = None
            self.prefix = ""

    _AUTOSYNC = ("vector", "scalar", "gpsimd")
    _NOSYNC_METHODS = {"wait_ge", "dma_start", "indirect_dma_start", "nop", "sem_inc", "sem_clear",
                       "dma_start_transpose", "collective_compute", "trigger_dma"}

    def _alloc_sems(self):
        for ph in self._all_phs:
            if ph.phase_idx == -1:
                ph.real = self.ctx.enter_context(self.nc.semaphore(ph.name))
        nph = self._phase_idx + 1
        by_phase = [[] for _ in range(nph)]
        for ph in self._all_phs:
            if ph.phase_idx >= 0:
                by_phase[ph.phase_idx].append(ph)
        hwpool, swpool = [], []
        for phs in by_phase:
            hw_avail, sw_avail = list(hwpool), list(swpool)
            for ph in phs:
                if ph.sw:
                    if sw_avail:
                        rec = sw_avail.pop()
                    else:
                        rec = [self.ctx.enter_context(
                            self.nc.semaphore(f"swp{len(swpool)}")), 0]
                        swpool.append(rec)
                    ph.real, ph.base = rec[0], rec[1]
                    rec[1] += self.semval[id(ph)]
                else:
                    if hw_avail:
                        ph.real = hw_avail.pop()
                    else:
                        h = self.ctx.enter_context(
                            self.nc.semaphore(f"hwp{len(hwpool)}"))
                        hwpool.append(h)
                        ph.real = h

    @staticmethod
    def _rs(s):
        return s.real if isinstance(s, _SemPH) else s

    @staticmethod
    def _rw(s, v):
        return v + s.base if isinstance(s, _SemPH) else v

    def finalize(self):
        assert not self._finalized
        self._finalized = True
        selfsem = {e: self.sem(f"auto_{e}") for e in self._AUTOSYNC}
        self._alloc_sems()
        with self.nc.Block() as block:
            for eng in ENGINES:
                items = self.q[eng]
                if not items:
                    continue
                auto = selfsem.get(eng)

                def emit(engine, items=items, auto=auto):
                    auto_n = 0
                    last_edge = None
                    cur_scope = None
                    for inst in items:
                        m = inst.method
                        if m == "__phase_end__":
                            last_edge = None
                            continue
                        lbl = inst.label or "main"
                        if cur_scope is None or cur_scope[0] != lbl:
                            if cur_scope is not None:
                                self.nc.leave_named_scope(cur_scope[0], cur_scope[1], False)
                            sid, _ = self.nc.enter_named_scope(lbl, False)
                            cur_scope = (lbl, sid)
                        if m == "wait_ge":
                            s, v = inst.args
                            engine.wait_ge(self._rs(s), self._rw(s, v))
                            continue
                        if m == "sem_clear":
                            (s,) = inst.args
                            if isinstance(s, _SemPH) and s.sw:
                                continue
                            engine.sem_clear(self._rs(s))
                            continue
                        if m == "sem_inc":
                            s, v = inst.args
                            engine.sem_inc(self._rs(s), v)
                            continue
                        is_compute = auto is not None and m not in self._NOSYNC_METHODS
                        if is_compute and last_edge is not None:
                            es, ev = last_edge
                            engine.wait_ge(self._rs(es), self._rw(es, ev))
                        real = getattr(engine, m)(*inst.args, **inst.kwargs)
                        incs = list(inst.incs)
                        if is_compute and not incs:
                            auto_n += 1
                            real.then_inc(auto.real, 1)
                            last_edge = (auto, auto_n)
                        for (sem, val, tot) in incs:
                            if m == "collective_compute":
                                assert val == 1
                                real.then_inc(self._rs(sem))
                            else:
                                real.then_inc(self._rs(sem), val)
                            if is_compute:
                                last_edge = (sem, tot)
                    if cur_scope is not None:
                        self.nc.leave_named_scope(cur_scope[0], cur_scope[1], False)

                getattr(block, eng)(emit)
        self.ctx.close()
        return self.nc



def emit_consts(p):
    if getattr(p, "_consts", None) is not None:
        return p._consts
    c = {}
    c["ones"] = p.sb("c_ones", [128, 128], F32)
    c["ident"] = p.sb("c_ident", [128, 128], BF16)
    c["iota_p"] = p.sb("c_iota_p", [128, 1], F32)
    gsem = p.sem("c_g")
    vsem = p.sem("c_v")
    p.gpsimd.memset(c["ones"][:], 1.0).then_inc(gsem, 1)
    p.gpsimd.iota(c["iota_p"][:], pattern=[[0, 1]], base=0, channel_multiplier=1,
                  allow_small_or_imprecise_dtypes=True).then_inc(gsem, 1)
    tmp_stack = ExitStack()
    tmpi = tmp_stack.enter_context(p.nc.sbuf_tensor("c_tmpi", [128, 128], F32))
    p.gpsimd.iota(tmpi[:], pattern=[[-1, 128]], base=0, channel_multiplier=1,
                  allow_small_or_imprecise_dtypes=True).then_inc(gsem, 1)
    p.vector.wait_ge(gsem, 3)
    p.vector.tensor_scalar(c["ident"][:], tmpi[:], 0.0, None, op0=ALU.is_equal).then_inc(vsem, 1)
    p.tensor.wait_ge(gsem, 3)
    p.tensor.wait_ge(vsem, 1)
    p.scalar.wait_ge(gsem, 3)
    p.scalar.wait_ge(vsem, 1)
    p.gpsimd.wait_ge(vsem, 1)
    p.sync.wait_ge(vsem, 1)
    tmp_stack.close()
    c["sem"], c["val"] = vsem, 1
    p._consts = c
    return c


class LN:

    def __init__(self, p, eps=1e-5):
        self.p = p
        self.ps_sx = p.ps("lnA_sx", [1, 512], F32)
        self.ps_sq = p.ps("lnA_sq", [1, 512], F32)
        self.ps_bm = p.ps("lnA_bm", [128, 512], F32)
        self.ps_br = p.ps("lnA_br", [128, 512], F32)
        self.sq = [p.sb("lnA_sq0", [128, 512], F32), p.sb("lnA_sq1", [128, 512], F32)]
        self.mrow = p.sb("lnA_m", [1, 512], F32)
        self.vrow = p.sb("lnA_v", [1, 512], F32)
        self.rrow = p.sb("lnA_r", [1, 512], F32)
        self.eps_c = p.sb("lnA_eps", [1, 1], F32)
        self.vsem = p.sem("lnA_v_s")
        self.ssem = p.sem("lnA_s_s")
        self.tsem = p.sem("lnA_t_s")
        self.sq_free = [0, 0]
        self.reads_done = 0
        p.vector.memset(self.eps_c[:], eps).then_inc(self.vsem, 1)

    def stats(self, x_chunks, g0, w, consts):
        p = self.p
        p.tensor.wait_ge(self.vsem, self.reads_done)
        for ch in range(DCH):
            sl = x_chunks[ch][:, g0:g0 + w]
            p.tensor.matmul(self.ps_sx[0:1, 0:w], consts["ones"][:, 0:1], sl,
                            start=(ch == 0), stop=(ch == DCH - 1))
        for ch in range(DCH):
            sl = x_chunks[ch][:, g0:g0 + w]
            buf = self.sq[ch % 2]
            p.scalar.wait_ge(self.tsem, self.sq_free[ch % 2])
            sv = p.scalar.activation(out=buf[:, 0:w], in_=sl, func=AF.Square).inc(self.ssem, 1)
            p.tensor.wait_ge(self.ssem, sv)
            tv = p.tensor.matmul(self.ps_sq[0:1, 0:w], consts["ones"][:, 0:1], buf[:, 0:w],
                                 start=(ch == 0), stop=(ch == DCH - 1)).inc(self.tsem, 1)
            self.sq_free[ch % 2] = tv
        p.vector.wait_ge(self.tsem, p.val(self.tsem))
        p.vector.tensor_scalar(self.mrow[0:1, 0:w], self.ps_sx[0:1, 0:w], -1.0 / D, None, op0=ALU.mult)
        self.reads_done = p.vector.tensor_scalar(self.vrow[0:1, 0:w], self.ps_sq[0:1, 0:w], 1.0 / D,
                                                 None, op0=ALU.mult).inc(self.vsem, 1)
        p.vector.tensor_tensor(self.rrow[0:1, 0:w], self.mrow[0:1, 0:w], self.mrow[0:1, 0:w], op=ALU.mult)
        vv = p.vector.tensor_tensor(self.vrow[0:1, 0:w], self.vrow[0:1, 0:w], self.rrow[0:1, 0:w],
                                    op=ALU.subtract).inc(self.vsem, 1)
        p.scalar.wait_ge(self.vsem, vv)
        p.scalar.activation(out=self.rrow[0:1, 0:w], in_=self.vrow[0:1, 0:w], func=AF.Ln,
                            bias=self.eps_c[0:1, 0:1])
        sv = p.scalar.activation(out=self.rrow[0:1, 0:w], in_=self.rrow[0:1, 0:w], func=AF.Exp,
                                 scale=-0.5).inc(self.ssem, 1)
        p.tensor.wait_ge(self.ssem, sv)
        p.tensor.matmul(self.ps_bm[:, 0:w], consts["ones"][0:1, :], self.mrow[0:1, 0:w],
                        start=True, stop=True)
        tv = p.tensor.matmul(self.ps_br[:, 0:w], consts["ones"][0:1, :], self.rrow[0:1, 0:w],
                             start=True, stop=True).inc(self.tsem, 1)
        p.vector.wait_ge(self.tsem, tv)
        return self.ps_bm, self.ps_br


def emit_fold_lora(p, name, wT, ab, scale, psl, n_in=D, n_out=D,
                   free_scratch=False):
    r = ab.shape[0]
    nch = n_in // 128
    nhalf = (n_out + 511) // 512
    wtiles = [p.sb(f"{name}_w{ch}", [128, n_out], BF16) for ch in range(nch)]
    _scr = ExitStack()
    if free_scratch:
        _sv2 = p._scope
        p._scope = (_scr, _sv2[1] if _sv2 else [])
    ab_s = p.sb(f"{name}_ab", [r, n_in + n_out], BF16)
    wf = [p.sb(f"{name}_wf{i}", [128, n_out], F32) for i in range(2)]
    if free_scratch:
        p._scope = _sv2
    tsem = p.sem(f"{name}_ts")
    vsem = p.sem(f"{name}_vs")
    absem = p.sem(f"{name}_abs")
    wfsem = [p.sem(f"{name}_wfs0"), p.sem(f"{name}_wfs1")]
    p.gpsimd.dma_start(out=ab_s[:], in_=ab[:, :]).then_inc(absem, 16)
    wf_free = [0, 0]
    for ch in range(nch):
        i2 = ch % 2
        p.gpsimd.wait_ge(vsem, wf_free[i2])
        lv = p.gpsimd.dma_start(out=wf[i2][:], in_=wT[ch * 128:(ch + 1) * 128, :]).inc(wfsem[i2], 16)
        p.tensor.wait_ge(absem, 16)
        for h in range(nhalf):
            c0, c1 = h * 512, min((h + 1) * 512, n_out)
            tv = p.tensor.matmul(psl[:, c0:c1], ab_s[0:r, ch * 128:(ch + 1) * 128],
                                 ab_s[0:r, n_in + c0:n_in + c1], start=True, stop=True).inc(tsem, 1)
        p.vector.wait_ge(tsem, tv)
        p.vector.wait_ge(wfsem[i2], lv)
        vv = p.vector.scalar_tensor_tensor(wtiles[ch][:, :], psl[:, :], scale, wf[i2][:],
                                           op0=ALU.mult, op1=ALU.add).inc(vsem, 1)
        wf_free[i2] = vv
        p.tensor.wait_ge(vsem, vv)
    _scr.close()
    return wtiles, tsem, vsem


def emit_cast_load(p, dst, src, lsem):
    return p.gpsimd.dma_start(out=dst, in_=src).inc(lsem, 16)



def build_attn_a_pre(p=None):
    own = p is None
    if own:
        p = Prog("attn_a_pre")
    xs = p.din("xs", [D, TC], F32)
    wqT = p.din("wqT", [D, D], F32)
    wkT = p.din("wkT", [D, D], F32)
    wvT = p.din("wvT", [D, D], F32)
    abq = p.din("abq", [8, 2 * D], F32)
    abk = p.din("abk", [8, 2 * D], F32)
    abv = p.din("abv", [8, 2 * D], F32)
    biasp = p.din("biasp", [128, 3 * DCH], F32)
    q_out = p.dout("q_out", [D, TC], BF16)
    k_out = p.dout("k_out", [D, TC], BF16)
    v_out = p.dout("v_out", [D, TC], BF16)
    lnx_out = p.dout("lnx_out", [D, TC], F32)

    consts = emit_consts(p)
    lsem = p.sem("lsem")
    gld = p.sem("gld")
    stsem = p.sem("stsem")
    vsem = p.sem("vsem")
    tsem = p.sem("tsem")
    ssem = p.sem("ssem")

    xbuf = p.sb("xbuf", [128, DCH * TC], F32)
    x_ch = [xbuf[:, ch * TC:(ch + 1) * TC] for ch in range(DCH)]
    p.sync.dma_start(out=xbuf[:].rearrange("p (c t) -> p c t", c=DCH),
                     in_=xs.rearrange("(c p) t -> p c t", p=128)).then_inc(lsem, 16)
    x_loaded = p.val(lsem)

    bsem = p.sem("bsem")
    bias_s = p.sb("bias_s", [128, 3 * DCH], F32)
    p.sync.dma_start(out=bias_s[:], in_=biasp[:, :]).then_inc(bsem, 16)

    psl = p.ps("fold_ps", [128, D], F32)
    wq_t, _, wq_vs = emit_fold_lora(p, "wq", wqT, abq, 1.0 / 8.0, psl)
    wk_t, _, wk_vs = emit_fold_lora(p, "wk", wkT, abk, 1.0 / 8.0, psl)
    wv_t, _, wv_vs = emit_fold_lora(p, "wv", wvT, abv, 1.0 / 8.0, psl)
    w_ready = (wq_vs, p.val(wq_vs), wk_vs, p.val(wk_vs), wv_vs, p.val(wv_vs))

    ln = LN(p)
    xn_bf = [p.sb(f"xn{ch}", [128, TC], BF16) for ch in range(DCH)]
    atmp = p.sb("atmp", [128, 512], F32)
    lnx_f = [p.sb(f"lnx_f{i}", [128, 512], F32) for i in range(2)]
    lnx_sem = [p.sem("lnxs0"), p.sem("lnxs1")]
    lnx_free = [0, 0]
    p.tensor.wait_ge(lsem, x_loaded)
    p.scalar.wait_ge(lsem, x_loaded)

    NG = TC // 512
    psA = p.ps("projA", [128, 512], F32)
    psB = p.ps("projB", [128, 512], F32)
    proj_ps = [psA, psB]
    ps_free = [0, 0]

    for g in range(NG):
        g0 = g * 512
        bm, br = ln.stats(x_ch, g0, 512, consts)
        for ch in range(DCH):
            lf = lnx_f[ch % 2]
            p.vector.tensor_tensor(atmp[:, 0:512], x_ch[ch][:, g0:g0 + 512], bm[:, 0:512], op=ALU.add)
            p.vector.tensor_tensor(xn_bf[ch][:, g0:g0 + 512], atmp[:, 0:512], br[:, 0:512], op=ALU.mult)
            p.vector.wait_ge(lnx_sem[ch % 2], lnx_free[ch % 2])
            vv = p.vector.tensor_tensor(lf[:, 0:512], atmp[:, 0:512], br[:, 0:512],
                                        op=ALU.mult).inc(vsem, 1)
            p.sync.wait_ge(vsem, vv)
            sv = p.sync.dma_start(out=lnx_out[ch * 128:(ch + 1) * 128, g0:g0 + 512],
                                  in_=lf[:, 0:512]).inc(lnx_sem[ch % 2], 16)
            lnx_free[ch % 2] = sv

    xn_ready = p.val(vsem)

    stage = [p.sb(f"stg{i}", [128, 512], BF16) for i in range(2)]
    stage_sem = [p.sem("stgs0"), p.sem("stgs1")]
    stage_free = [0, 0]
    p.tensor.wait_ge(vsem, xn_ready)
    p.tensor.wait_ge(w_ready[0], w_ready[1])
    p.tensor.wait_ge(w_ready[2], w_ready[3])
    p.tensor.wait_ge(w_ready[4], w_ready[5])
    p.tensor.wait_ge(bsem, 16)
    si = 0
    for pi, (wt, outd) in enumerate(((wq_t, q_out), (wk_t, k_out), (wv_t, v_out))):
        for g in range(NG):
            g0 = g * 512
            for oc in range(DCH):
                pp = proj_ps[si % 2]
                p.tensor.wait_ge(vsem, ps_free[si % 2])
                for ich in range(DCH):
                    mm = p.tensor.matmul(pp[:, 0:512], wt[ich][:, oc * 128:(oc + 1) * 128],
                                         xn_bf[ich][:, g0:g0 + 512],
                                         start=(ich == 0), stop=(ich == DCH - 1))
                tv = mm.inc(tsem, 1)
                p.vector.wait_ge(tsem, tv)
                p.vector.wait_ge(stage_sem[si % 2], stage_free[si % 2])
                vv = p.vector.tensor_scalar(stage[si % 2][:, 0:512], pp[:, 0:512],
                                            bias_s[:, pi * DCH + oc:pi * DCH + oc + 1], None,
                                            op0=ALU.add).inc(vsem, 1)
                ps_free[si % 2] = vv
                p.sync.wait_ge(vsem, vv)
                sv = p.sync.dma_start(out=outd[oc * 128:(oc + 1) * 128, g0:g0 + 512],
                                      in_=stage[si % 2][:, 0:512]).inc(stage_sem[si % 2], 16)
                stage_free[si % 2] = sv
                si += 1

    for sm in (lnx_sem[0], lnx_sem[1], stage_sem[0], stage_sem[1]):
        p.sync.wait_ge(sm, p.val(sm))
    return p.finalize() if own else None



def _fm(x):
    return np.ascontiguousarray(x.T)


def _packpost(postw):
    pt = np.ascontiguousarray(postw.T)
    slabs = []
    for oc in range(8):
        for h in range(2):
            blk = pt[h * 2048:(h + 1) * 2048, oc * 128:(oc + 1) * 128]
            slabs.append(np.ascontiguousarray(
                blk.reshape(16, 128, 128).transpose(1, 0, 2).reshape(128, 2048)))
    return np.ascontiguousarray(np.concatenate(slabs, axis=0))


def _packab(A, B):
    return np.ascontiguousarray(np.concatenate([A, B.T], axis=1))


def _packbias(*biases):
    cols = []
    for b in biases:
        for ch in range(DCH):
            cols.append(b[ch * 128:(ch + 1) * 128])
    return np.ascontiguousarray(np.stack(cols, axis=1))


def prepare_state(I):
    st = dict(I)
    x = I["L0_attn_a_x"].reshape(T_ALL, D)
    st["x_fm"] = _fm(x)
    return st


def run_attn_a_pre(st):
    nc = build_attn_a_pre()
    in_maps = []
    for c in range(NCORES):
        sl = slice(c * TC, (c + 1) * TC)
        in_maps.append({
            "xs": np.ascontiguousarray(st["x_fm"][:, sl]),
            "wqT": _fm(st["L0_attn_a_Wq"]), "wkT": _fm(st["L0_attn_a_Wk"]),
            "wvT": _fm(st["L0_attn_a_Wv"]),
            "abq": _packab(st["L0_attn_a_Aq"], st["L0_attn_a_Bq"]),
            "abk": _packab(st["L0_attn_a_Ak"], st["L0_attn_a_Bk"]),
            "abv": _packab(st["L0_attn_a_Av"], st["L0_attn_a_Bv"]),
            "biasp": _packbias(st["L0_attn_a_bq"], st["L0_attn_a_bk"], st["L0_attn_a_bv"]),
        })
    res = run_bass_kernel_spmd(nc, in_maps, core_ids=list(range(NCORES)))
    for key in ("q_out", "k_out", "v_out", "lnx_out"):
        st[key[0:-4] + "_a_fm" if key != "lnx_out" else "lnx_fm"] = np.concatenate(
            [r[key] for r in res.results], axis=1)
    return st


def kernel(**inputs):
    st = prepare_state(inputs)
    st = run_attn_a_pre(st)
    raise NotImplementedError



def build_attn_a(p=None):
    own = p is None
    if own:
        p = Prog("attn_a")
    NHEAD, HD = 16, 64
    qf = p.din("qf", [D, TC], BF16)
    kf = p.din("kf", [D, S], BF16)
    vf = p.din("vf", [D, S], BF16)
    lnxs = p.din("lnxs", [D, TC], F32)
    xs = p.din("xs", [D, TC], F32)
    woT = p.din("woT", [D, D], F32)
    abo = p.din("abo", [8, 2 * D], F32)
    bop = p.din("bop", [128, DCH], F32)
    rz = p.din("rz", [1, 1], F32)
    x2_out = p.dout("x2_out", [D, TC], F32)

    consts = emit_consts(p)
    tsem = p.sem("tsem")
    vsem = p.sem("vsem")
    ssem = p.sem("ssem")
    qsem = p.sem("qsem")
    bsem = p.sem("bsem")
    rsem = p.sem("rsem")

    _fs = ExitStack()
    _sv = p._scope
    p._scope = (_fs, _sv[1] if _sv else [])
    psl = p.ps("fold_ps", [128, D], F32)
    p._scope = _sv
    wo_t, _, wo_vs = emit_fold_lora(p, "wo", woT, abo, 1.0 / 8.0, psl,
                                    free_scratch=True)
    _fs.close()
    wo_ready = p.val(wo_vs)
    p.tensor.wait_ge(wo_vs, wo_ready)
    p.sync.wait_ge(wo_vs, wo_ready)

    bo_s = p.sb("bo_s", [128, DCH], F32)
    p.sync.dma_start(out=bo_s[:], in_=bop[:, :]).then_inc(bsem, 16)
    rz_s = p.sb("rz_s", [128, 1], F32)
    p.sync.dma_start(out=rz_s[:], in_=rz[0:1, 0:1].broadcast_to([128, 1])).then_inc(rsem, 16)

    kh = [p.sb(f"kh{i}", [64, S], BF16) for i in range(2)]
    vh = [p.sb(f"vh{i}", [64, S], BF16) for i in range(2)]
    vt = [p.sb(f"vt{i}", [128, 65 * 64], BF16) for i in range(2)]
    attn = p.sb("attnb", [128, DCH * TC], BF16)
    biash = p.sb("biash", [128, 64], F32)
    biasi = p.sb("biasi", [128, 64], F32)
    pT = [p.sb(f"pT{i}", [128, 512], BF16) for i in range(4)]
    rden = p.sb("rden", [1, 512], F32)
    bcs = p.sb("bcs", [64, 512], F32)
    ps_S = [p.ps(f"psS{i}", [128, 512], F32) for i in range(4)]
    ps_pv = [p.ps(f"pspv{i}", [65, 512], F32) for i in range(2)]
    ps_vt = p.ps("psvt", [128, 64], BF16)
    ps_bc = p.ps("psbc", [64, 512], F32)
    khsem = p.sem("khsem")
    vhsem = p.sem("vhsem")

    p.gpsimd.iota(biasi[:], pattern=[[128, 64]], base=-(S - 1), channel_multiplier=1,
                  allow_small_or_imprecise_dtypes=True).then_inc(vsem, 1)
    p.vector.wait_ge(vsem, p.val(vsem))

    NQC = TC // 512
    slopes = [0.5 ** h for h in range(NHEAD)]

    kh_used = [0, 0]
    kvk = [0, 0]
    kvv = [0, 0]
    vt_rdy = [0, 0]
    ps_vt_free = 0
    pv_evac = [0, 0]
    _qs = ExitStack()
    _sv2 = p._scope
    p._scope = (_qs, _sv2[1] if _sv2 else [])
    qbuf = p.sb("qbuf", [64, 16 * TC], BF16)
    p._scope = _sv2
    p.sync.dma_start(out=qbuf[:].rearrange("p (h t) -> p h t", h=16),
                     in_=qf.rearrange("(h p) t -> p h t", p=64)).then_inc(qsem, 16)
    p.scalar.wait_ge(qsem, 16)
    p.tensor.wait_ge(qsem, 16)
    p.tensor.wait_ge(consts["sem"], consts["val"])

    def load_kv(hh):
        i2 = hh % 2
        p.sync.wait_ge(tsem, kh_used[i2])
        kvk[i2] = p.sync.dma_start(out=kh[i2][:],
                                   in_=kf[hh * 64:(hh + 1) * 64, :]).inc(khsem, 16)
        kvv[i2] = p.sync.dma_start(out=vh[i2][:],
                                   in_=vf[hh * 64:(hh + 1) * 64, :]).inc(vhsem, 16)

    def vt_step(hh, kt):
        nonlocal ps_vt_free
        i2 = hh % 2
        p.tensor.wait_ge(vsem, ps_vt_free)
        tv = p.tensor.transpose(ps_vt[:, :], vh[i2][0:64, kt * 128:(kt + 1) * 128],
                                consts["ident"][0:64, 0:64]).inc(tsem, 1)
        p.vector.wait_ge(tsem, tv)
        ev = p.vector.tensor_copy(vt[i2][:, kt * 65:kt * 65 + 64],
                                  ps_vt[:, :]).inc(vsem, 1)
        ps_vt_free = ev
        vt_rdy[i2] = ev

    load_kv(0)
    p.vector.memset(vt[0][:], 1.0).then_inc(vsem, 1)
    p.tensor.wait_ge(vhsem, kvv[0])
    for kt in range(64):
        vt_step(0, kt)

    for h in range(NHEAD):
        ch, rb = h // 2, (h % 2) * 64
        i = h % 2
        if h + 1 < NHEAD:
            load_kv(h + 1)
        p.vector.wait_ge(ssem, p.val(ssem))
        p.vector.tensor_scalar(biash[:], biasi[:], slopes[h], None, op0=ALU.mult).then_inc(vsem, 1)
        bias_ready = p.val(vsem)
        if h + 1 < NHEAD:
            p.vector.wait_ge(tsem, kh_used[(h + 1) % 2])
            p.vector.memset(vt[(h + 1) % 2][:], 1.0).then_inc(vsem, 1)
        p.tensor.wait_ge(khsem, kvk[i])
        p.tensor.wait_ge(vsem, vt_rdy[i])
        p.scalar.wait_ge(vsem, bias_ready)

        for qp in range(NQC // 2):
            exp_val = [0] * 66
            for kt in range(65):
                if kt < 64:
                    p.tensor.wait_ge(ssem, exp_val[kt - 2] if kt >= 2 else 0)
                    if kt == 0:
                        for qi in range(2):
                            p.tensor.wait_ge(vsem, pv_evac[qi])
                    for qi in range(2):
                        g0 = (qp * 2 + qi) * 512
                        mm = p.tensor.matmul(
                            ps_S[qi * 2 + kt % 2][:, :],
                            kh[i][0:64, kt * 128:(kt + 1) * 128],
                            qbuf[0:64, h * TC + g0:h * TC + g0 + 512],
                            start=True, stop=True)
                    qkv = mm.inc(tsem, 1)
                    if qp == 0 and h + 1 < NHEAD:
                        if kt == 0:
                            p.tensor.wait_ge(vhsem, kvv[(h + 1) % 2])
                        vt_step(h + 1, kt)
                    p.scalar.wait_ge(tsem, qkv)
                    for qi in range(2):
                        sv = p.scalar.activation(
                            out=pT[qi * 2 + kt % 2][:, :],
                            in_=ps_S[qi * 2 + kt % 2][:, :], func=AF.Exp,
                            bias=biash[:, kt:kt + 1], scale=0.125)
                    exp_val[kt] = sv.inc(ssem, 1)
                if kt >= 1:
                    k2 = kt - 1
                    p.tensor.wait_ge(ssem, exp_val[k2])
                    for qi in range(2):
                        pv = p.tensor.matmul(ps_pv[qi][:, :],
                                             vt[i][:, k2 * 65:(k2 + 1) * 65],
                                             pT[qi * 2 + k2 % 2][:, :],
                                             start=(k2 == 0), stop=(k2 == 63))
                    pv.inc(tsem, 1)
            for qi in range(2):
                qc = qp * 2 + qi
                g0 = qc * 512
                p.vector.wait_ge(tsem, p.val(tsem))
                p.vector.tensor_copy(rden[0:1, :], ps_pv[qi][64:65, :])
                rv = p.vector.reciprocal(rden[0:1, :], rden[0:1, :]).inc(vsem, 1)
                p.tensor.wait_ge(vsem, rv)
                bb = p.tensor.matmul(ps_bc[:, :], consts["ones"][0:1, 0:64], rden[0:1, :],
                                     start=True, stop=True).inc(tsem, 1)
                p.vector.wait_ge(tsem, bb)
                p.vector.tensor_copy(bcs[:, :], ps_bc[:, :]).then_inc(vsem, 1)
                ev = p.vector.tensor_tensor(
                    attn[rb:rb + 64, ch * TC + g0:ch * TC + g0 + 512],
                    ps_pv[qi][0:64, :], bcs[0:64, :], op=ALU.mult).inc(vsem, 1)
                pv_evac[qi] = ev
                p.tensor.wait_ge(vsem, ev)
        kh_used[i] = p.val(tsem)

    attn_ready = p.val(vsem)
    p.sync.wait_ge(tsem, p.val(tsem))
    _qs.close()

    lnx_t = [p.sb(f"lnx_t{i}", [128, 512], F32) for i in range(2)]
    xs_t = [p.sb(f"xs_t{i}", [128, 512], F32) for i in range(2)]
    lnsem = [p.sem("lns0"), p.sem("lns1")]
    xssem = [p.sem("xss0"), p.sem("xss1")]
    osem = [p.sem("oss0"), p.sem("oss1")]
    ostg = [p.sb(f"ostg{i}", [128, 512], F32) for i in range(2)]
    t1 = p.sb("t1", [128, 512], F32)
    lnx_free = [0, 0]
    xs_free = [0, 0]
    o_free = [0, 0]
    ps_pr = [ps_S[0], ps_S[1]]
    ps_free = [0, 0]
    p.tensor.wait_ge(vsem, attn_ready)
    p.tensor.wait_ge(ssem, p.val(ssem))
    p.tensor.wait_ge(wo_vs, wo_ready)
    p.tensor.wait_ge(bsem, 16)
    si = 0
    for g in range(NQC):
        g0 = g * 512
        for oc in range(DCH):
            i2 = si % 2
            p.sync.wait_ge(vsem, lnx_free[i2])
            lv = p.sync.dma_start(out=lnx_t[i2][:],
                                  in_=lnxs[oc * 128:(oc + 1) * 128, g0:g0 + 512]).inc(lnsem[i2], 16)
            xv = p.sync.dma_start(out=xs_t[i2][:],
                                  in_=xs[oc * 128:(oc + 1) * 128, g0:g0 + 512]).inc(xssem[i2], 16)
            p.tensor.wait_ge(vsem, ps_free[i2])
            for ich in range(DCH):
                mm = p.tensor.matmul(ps_pr[i2][:, :], wo_t[ich][:, oc * 128:(oc + 1) * 128],
                                     attn[:, ich * TC + g0:ich * TC + g0 + 512],
                                     start=(ich == 0), stop=(ich == DCH - 1))
            tv = mm.inc(tsem, 1)
            p.vector.wait_ge(tsem, tv)
            p.vector.wait_ge(lnsem[i2], lv)
            p.vector.wait_ge(xssem[i2], xv)
            p.vector.wait_ge(rsem, 16)
            sfree = p.vector.tensor_scalar(t1[:, :], ps_pr[i2][:, :], bo_s[:, oc:oc + 1], None,
                                           op0=ALU.add).inc(vsem, 1)
            ps_free[i2] = sfree
            p.vector.scalar_tensor_tensor(t1[:, :], t1[:, :], rz_s[:, 0:1],
                                          lnx_t[i2][:], op0=ALU.mult, op1=ALU.add)
            p.vector.wait_ge(osem[i2], o_free[i2])
            ov = p.vector.tensor_tensor(ostg[i2][:], t1[:], xs_t[i2][:], op=ALU.add).inc(vsem, 1)
            lnx_free[i2] = ov
            p.sync.wait_ge(vsem, ov)
            so = p.sync.dma_start(out=x2_out[oc * 128:(oc + 1) * 128, g0:g0 + 512],
                                  in_=ostg[i2][:]).inc(osem[i2], 16)
            o_free[i2] = so
            si += 1
    for sm in osem:
        p.sync.wait_ge(sm, p.val(sm))
    return p.finalize() if own else None


def run_attn_a(st):
    nc = build_attn_a()
    in_maps = []
    for c in range(NCORES):
        sl = slice(c * TC, (c + 1) * TC)
        b = c // (NCORES // NB)
        bsl = slice(b * S, (b + 1) * S)
        in_maps.append({
            "qf": np.ascontiguousarray(st["q_a_fm"][:, sl]),
            "kf": np.ascontiguousarray(st["k_a_fm"][:, bsl]),
            "vf": np.ascontiguousarray(st["v_a_fm"][:, bsl]),
            "lnxs": np.ascontiguousarray(st["lnx_fm"][:, sl]),
            "xs": np.ascontiguousarray(st["x_fm"][:, sl]),
            "woT": _fm(st["L0_attn_a_Wo"]),
            "abo": _packab(st["L0_attn_a_Ao"], st["L0_attn_a_Bo"]),
            "bop": _packbias(st["L0_attn_a_bo"]),
            "rz": st["L0_attn_a_rezero"].reshape(1, 1),
        })
    res = run_bass_kernel_spmd(nc, in_maps, core_ids=list(range(NCORES)))
    st["x_fm"] = np.concatenate([r["x2_out"] for r in res.results], axis=1)
    return st



def build_ffn0(p=None):
    own = p is None
    if own:
        p = Prog("ffn0")
    xs = p.din("xs", [D, TC], F32)
    wk = p.din("wk", [D, D], F32)
    abf = p.din("abf", [16, 2 * D], F32)
    bp = p.din("bp", [128, DCH], F32)
    x_out = p.dout("x_out", [D, TC], F32)

    consts = emit_consts(p)
    lsem = p.sem("lsem")
    bsem = p.sem("bsem")
    gld = p.sem("gld")
    tsem = p.sem("tsem")
    vsem = p.sem("vsem")
    gsem = p.sem("gsem")

    xbuf = p.sb("xbuf", [128, DCH * TC], F32)
    x_ch = [xbuf[:, ch * TC:(ch + 1) * TC] for ch in range(DCH)]
    p.sync.dma_start(out=xbuf[:].rearrange("p (c t) -> p c t", c=DCH),
                     in_=xs.rearrange("(c p) t -> p c t", p=128)).then_inc(lsem, 16)
    bias_s = p.sb("bias_s", [128, DCH], F32)
    p.sync.dma_start(out=bias_s[:], in_=bp[:, :]).then_inc(bsem, 16)

    wbuf = p.sb("wbuf", [128, DCH * D], F32)
    w_ch = [wbuf[:, ch * D:(ch + 1) * D] for ch in range(DCH)]
    p.gpsimd.dma_start(out=wbuf[:].rearrange("p (c t) -> p c t", c=DCH),
                       in_=wk.rearrange("(c p) t -> p c t", p=128)).then_inc(gld, 16)
    ab_s = p.sb("ab_s", [16, 2 * D], BF16)
    p.gpsimd.dma_start(out=ab_s[:], in_=abf[:, :]).then_inc(gld, 32 - 16)
    p.gpsimd.wait_ge(gld, 32)
    colmax = p.sb("colmax", [128, DCH], F32)
    onecol = p.sb("onecol", [128, 1], F32)
    gmaxr = p.sb("gmaxr", [1, 1], F32)
    gmax = p.sb("gmax", [128, 1], F32)
    psl = p.ps("fold_ps", [128, D], F32)
    identf = p.sb("identf", [128, 128], F32)
    p.vector.wait_ge(consts["sem"], consts["val"])
    p.vector.tensor_copy(identf[:], consts["ident"][:])
    p.vector.wait_ge(gld, 32)
    for ch in range(DCH):
        p.vector.tensor_reduce(colmax[:, ch:ch + 1], w_ch[ch], axis=AX.X, op=ALU.max,
                               apply_absolute_value=True)
    p.vector.tensor_reduce(onecol[:, 0:1], colmax[:, 0:DCH], axis=AX.X, op=ALU.max).then_inc(vsem, 1)
    p.tensor.wait_ge(vsem, p.val(vsem))
    tg = p.tensor.transpose(psl[0:1, 0:128], onecol[:, 0:1], identf[:, :]).inc(tsem, 1)
    p.vector.wait_ge(tsem, tg)
    p.vector.tensor_reduce(gmaxr[0:1, 0:1], psl[0:1, 0:128], axis=AX.X, op=ALU.max).then_inc(vsem, 1)
    p.tensor.wait_ge(vsem, p.val(vsem))
    tb = p.tensor.matmul(psl[0:128, 0:1], consts["ones"][0:1, :], gmaxr[0:1, 0:1],
                         start=True, stop=True).inc(tsem, 1)
    scl = p.sb("scl", [128, 1], F32)
    inv = p.sb("inv", [128, 1], F32)
    p.vector.wait_ge(tsem, tb)
    p.vector.tensor_scalar(scl[:, 0:1], psl[0:128, 0:1], 1.0 / 7.0, None, op0=ALU.mult)
    p.vector.reciprocal(inv[:, 0:1], scl[:, 0:1]).then_inc(vsem, 1)
    wt = [p.sb(f"wt{ch}", [128, D], BF16) for ch in range(DCH)]
    wi = p.sb("wi", [128, D], I32)
    wf2 = p.sb("wf2", [128, D], F32)
    wr0 = p.sb("wr0", [128, D], F32)
    wd = p.sb("wd", [128, D], F32)
    for ch in range(DCH):
        p.vector.tensor_scalar(wf2[:], w_ch[ch], inv[:, 0:1], None, op0=ALU.mult)
        p.vector.tensor_copy(wi[:], wf2[:])
        p.vector.tensor_copy(wr0[:], wi[:])
        p.vector.tensor_tensor(wd[:], wf2[:], wr0[:], op=ALU.subtract)
        p.vector.tensor_scalar(wd[:], wd[:], 0.5, None, op0=ALU.is_ge)
        p.vector.tensor_tensor(wr0[:], wr0[:], wd[:], op=ALU.add)
        p.vector.tensor_tensor(wd[:], wf2[:], wr0[:], op=ALU.subtract)
        p.vector.tensor_scalar(wd[:], wd[:], -0.5, None, op0=ALU.is_le)
        p.vector.tensor_tensor(wf2[:], wr0[:], wd[:], op=ALU.subtract)
        p.vector.tensor_scalar(wf2[:], wf2[:], -8.0, 7.0, op0=ALU.max, op1=ALU.min)
        p.vector.tensor_scalar(wf2[:], wf2[:], scl[:, 0:1], None, op0=ALU.mult).then_inc(vsem, 1)
        p.tensor.wait_ge(gld, 32)
        p.tensor.wait_ge(vsem, p.val(vsem))
        for hh in range(2):
            mm = p.tensor.matmul(psl[:, hh * 512:(hh + 1) * 512],
                                 ab_s[0:16, ch * 128:(ch + 1) * 128],
                                 ab_s[0:16, D + hh * 512:D + (hh + 1) * 512],
                                 start=True, stop=True)
        tv = mm.inc(tsem, 1)
        p.vector.wait_ge(tsem, tv)
        vv = p.vector.tensor_tensor(wt[ch][:], wf2[:], psl[:], op=ALU.add).inc(vsem, 1)
        p.tensor.wait_ge(vsem, vv)
    w_ready = p.val(vsem)

    ln = LN(p)
    xn_bf = [p.sb(f"xn{ch}", [128, TC], BF16) for ch in range(DCH)]
    atmp = p.sb("atmp", [128, 512], F32)
    p.tensor.wait_ge(lsem, 16)
    p.scalar.wait_ge(lsem, 16)
    p.vector.wait_ge(lsem, 16)
    NG = TC // 512
    for g in range(NG):
        g0 = g * 512
        bm, br = ln.stats(x_ch, g0, 512, consts)
        for ch in range(DCH):
            p.vector.tensor_tensor(atmp[:, 0:512], x_ch[ch][:, g0:g0 + 512], bm[:, 0:512], op=ALU.add)
            p.vector.tensor_tensor(xn_bf[ch][:, g0:g0 + 512], atmp[:, 0:512], br[:, 0:512],
                                   op=ALU.mult).then_inc(vsem, 1)
    xn_ready = p.val(vsem)

    ps_pr = [p.ps("prA", [128, 512], F32), p.ps("prB", [128, 512], F32)]
    ps_free = [0, 0]
    ostg = [p.sb(f"ostg{i}", [128, 512], F32) for i in range(2)]
    osem = [p.sem("oss0"), p.sem("oss1")]
    o_free = [0, 0]
    p.tensor.wait_ge(vsem, xn_ready)
    p.tensor.wait_ge(bsem, 16)
    si = 0
    for g in range(NG):
        g0 = g * 512
        for oc in range(DCH):
            i2 = si % 2
            p.tensor.wait_ge(vsem, ps_free[i2])
            for ich in range(DCH):
                mm = p.tensor.matmul(ps_pr[i2][:, :], wt[ich][:, oc * 128:(oc + 1) * 128],
                                     xn_bf[ich][:, g0:g0 + 512],
                                     start=(ich == 0), stop=(ich == DCH - 1))
            tv = mm.inc(tsem, 1)
            p.vector.wait_ge(tsem, tv)
            p.vector.wait_ge(osem[i2], o_free[i2])
            p.vector.tensor_scalar(ostg[i2][:], ps_pr[i2][:, :], bias_s[:, oc:oc + 1], None,
                                   op0=ALU.add)
            pf = p.vector.tensor_tensor(ostg[i2][:], ostg[i2][:],
                                        x_ch[oc][:, g0:g0 + 512], op=ALU.add).inc(vsem, 1)
            ps_free[i2] = pf
            p.sync.wait_ge(vsem, pf)
            so = p.sync.dma_start(out=x_out[oc * 128:(oc + 1) * 128, g0:g0 + 512],
                                  in_=ostg[i2][:]).inc(osem[i2], 16)
            o_free[i2] = so
            si += 1
    for sm in osem:
        p.sync.wait_ge(sm, p.val(sm))
    return p.finalize() if own else None


def run_ffn0(st):
    nc = build_ffn0()
    in_maps = []
    for c in range(NCORES):
        sl = slice(c * TC, (c + 1) * TC)
        in_maps.append({
            "xs": np.ascontiguousarray(st["x_fm"][:, sl]),
            "wk": st["L0_ffn_kernel"],
            "abf": np.ascontiguousarray(
                np.concatenate([st["L0_ffn_a"].T, st["L0_ffn_b"]], axis=1)),
            "bp": _packbias(st["L0_ffn_bias"]),
        })
    res = run_bass_kernel_spmd(nc, in_maps, core_ids=list(range(NCORES)))
    st["x_fm"] = np.concatenate([r["x_out"] for r in res.results], axis=1)
    return st



def build_gdn_pre(p=None):
    own = p is None
    if own:
        p = Prog("gdn_pre")
    xs = p.din("xs", [D, TC], F32)
    wq = p.din("wq", [D, D], F32)
    wk = p.din("wk", [D, D], F32)
    wv = p.din("wv", [D, D], F32)
    wba = p.din("wba", [D, 16], F32)
    lnp = p.din("lnp", [128, 2 * DCH], F32)
    y0_out = p.dout("y0_out", [D, TC], F32)
    q_out = p.dout("q_out", [D, TC], BF16)
    k_out = p.dout("k_out", [D, TC], BF16)
    v_out = p.dout("v_out", [D, TC], BF16)
    lnab_out = p.dout("lnab_out", [16, TC], F32)

    consts = emit_consts(p)
    lsem = p.sem("lsem")
    psem = p.sem("psem")
    gld = p.sem("gld")
    tsem = p.sem("tsem")
    vsem = p.sem("vsem")
    ssem = p.sem("ssem")
    stv = [p.sem(f"stv{i}") for i in range(2)]

    xbuf = p.sb("xbuf", [128, DCH * TC], F32)
    x_ch = [xbuf[:, ch * TC:(ch + 1) * TC] for ch in range(DCH)]
    p.sync.dma_start(out=xbuf[:].rearrange("p (c t) -> p c t", c=DCH),
                     in_=xs.rearrange("(c p) t -> p c t", p=128)).then_inc(lsem, 16)
    lnw = p.sb("lnw", [128, 2 * DCH], F32)
    p.sync.dma_start(out=lnw[:], in_=lnp[:, :]).then_inc(psem, 16)
    wq_s = p.sb("wq_s", [128, DCH * D], BF16)
    wk_s = p.sb("wk_s", [128, DCH * D], BF16)
    wv_s = p.sb("wv_s", [128, DCH * D], BF16)
    wba_s = p.sb("wba_s", [128, DCH * 16], BF16)
    for wsrc, wdst in ((wq, wq_s), (wk, wk_s), (wv, wv_s)):
        p.gpsimd.dma_start(out=wdst[:].rearrange("p (c t) -> p c t", c=DCH),
                           in_=wsrc.rearrange("(c p) t -> p c t", p=128)).then_inc(gld, 16)
    p.gpsimd.dma_start(out=wba_s[:].rearrange("p (c t) -> p c t", c=DCH),
                       in_=wba.rearrange("(c p) t -> p c t", p=128)).then_inc(gld, 16)
    wload = p.val(gld)

    ln = LN(p)
    y0_ch = x_ch
    n_bf = [p.sb(f"nbf{ch}", [128, TC], BF16) for ch in range(DCH)]
    atmp = p.sb("atmp", [128, 512], F32)
    y0st = [p.sb(f"y0st{i}", [128, 512], F32) for i in range(2)]
    y0_free = [0, 0]
    p.tensor.wait_ge(lsem, 16)
    p.scalar.wait_ge(lsem, 16)
    p.vector.wait_ge(lsem, 16)
    p.vector.wait_ge(psem, 16)

    NG = TC // 512
    for g in range(NG):
        g0 = g * 512
        bm, br = ln.stats(x_ch, g0, 512, consts)
        for ch in range(DCH):
            i2 = ch % 2
            p.vector.tensor_tensor(atmp[:, 0:512], x_ch[ch][:, g0:g0 + 512], bm[:, 0:512], op=ALU.add)
            p.vector.wait_ge(stv[i2], y0_free[i2])
            vv = p.vector.tensor_tensor(y0st[i2][:], atmp[:, 0:512], br[:, 0:512],
                                        op=ALU.mult).inc(vsem, 1)
            p.vector.tensor_copy(x_ch[ch][:, g0:g0 + 512], y0st[i2][:])
            p.sync.wait_ge(vsem, vv)
            sv = p.sync.dma_start(out=y0_out[ch * 128:(ch + 1) * 128, g0:g0 + 512],
                                  in_=y0st[i2][:]).inc(stv[i2], 16)
            y0_free[i2] = sv
    for g in range(NG):
        g0 = g * 512
        bm, br = ln.stats(y0_ch, g0, 512, consts)
        for ch in range(DCH):
            p.vector.tensor_tensor(atmp[:, 0:512], y0_ch[ch][:, g0:g0 + 512], bm[:, 0:512], op=ALU.add)
            p.vector.tensor_tensor(atmp[:, 0:512], atmp[:, 0:512], br[:, 0:512], op=ALU.mult)
            p.vector.tensor_scalar(n_bf[ch][:, g0:g0 + 512], atmp[:, 0:512],
                                   lnw[:, ch:ch + 1], lnw[:, DCH + ch:DCH + ch + 1],
                                   op0=ALU.mult, op1=ALU.add).then_inc(vsem, 1)
    n_ready = p.val(vsem)

    ps_pr = [p.ps("prA", [128, 512], F32), p.ps("prB", [128, 512], F32)]
    ps_ss = p.ps("ss_ps", [1, 512], F32)
    ps_bc = p.ps("bc_ps", [128, 512], F32)
    sil = p.sb("sil", [128, 512], F32)
    sq2 = p.sb("sq2", [128, 512], F32)
    ssrow = p.sb("ssrow", [1, 512], F32)
    e6 = p.sb("e6", [1, 1], F32)
    p.vector.memset(e6[:], 1e-6).then_inc(vsem, 1)
    stq = [p.sem("stq0"), p.sem("stq1")]
    qstg = [p.sb(f"qstg{i}", [128, 512], BF16) for i in range(2)]
    q_free = [0, 0]
    ps_free = [0, 0]
    sil_free = 0
    p.tensor.wait_ge(vsem, n_ready)
    p.tensor.wait_ge(gld, wload)
    si = 0
    for pi, (wt_s, outd, do_l2) in enumerate(((wq_s, q_out, True), (wk_s, k_out, True),
                                              (wv_s, v_out, False))):
        for g in range(NG):
            g0 = g * 512
            for oc in range(DCH):
                i2 = si % 2
                p.tensor.wait_ge(ssem, ps_free[i2])
                for ich in range(DCH):
                    mm = p.tensor.matmul(
                        ps_pr[i2][:, :], wt_s[:, ich * D + oc * 128:ich * D + (oc + 1) * 128],
                        n_bf[ich][:, g0:g0 + 512], start=(ich == 0), stop=(ich == DCH - 1))
                tv = mm.inc(tsem, 1)
                p.scalar.wait_ge(tsem, tv)
                p.scalar.wait_ge(vsem, sil_free)
                sv = p.scalar.activation(out=sil[:, :], in_=ps_pr[i2][:, :],
                                         func=AF.Silu).inc(ssem, 1)
                ps_free[i2] = sv
                if do_l2:
                    s2 = p.scalar.activation(out=sq2[:, :], in_=sil[:, :],
                                             func=AF.Square).inc(ssem, 1)
                    p.tensor.wait_ge(ssem, s2)
                    t2 = p.tensor.matmul(ps_ss[0:1, :], consts["ones"][:, 0:1], sq2[:, :],
                                         start=True, stop=True).inc(tsem, 1)
                    p.vector.wait_ge(tsem, t2)
                    rv = p.vector.tensor_copy(ssrow[0:1, :], ps_ss[0:1, :]).inc(vsem, 1)
                    p.scalar.wait_ge(vsem, rv)
                    p.scalar.activation(out=ssrow[0:1, :], in_=ssrow[0:1, :], func=AF.Ln,
                                        bias=e6[0:1, 0:1])
                    s3 = p.scalar.activation(out=ssrow[0:1, :], in_=ssrow[0:1, :], func=AF.Exp,
                                             scale=-0.5).inc(ssem, 1)
                    p.tensor.wait_ge(ssem, s3)
                    t3 = p.tensor.matmul(ps_bc[:, :], consts["ones"][0:1, :], ssrow[0:1, :],
                                         start=True, stop=True).inc(tsem, 1)
                    p.vector.wait_ge(tsem, t3)
                    p.vector.wait_ge(stq[i2], q_free[i2])
                    ov = p.vector.tensor_tensor(qstg[i2][:], sil[:, :], ps_bc[:, :],
                                                op=ALU.mult).inc(vsem, 1)
                else:
                    p.vector.wait_ge(stq[i2], q_free[i2])
                    p.vector.wait_ge(ssem, sv)
                    ov = p.vector.tensor_copy(qstg[i2][:], sil[:, :]).inc(vsem, 1)
                sil_free = ov
                p.sync.wait_ge(vsem, ov)
                so = p.sync.dma_start(out=outd[oc * 128:(oc + 1) * 128, g0:g0 + 512],
                                      in_=qstg[i2][:]).inc(stq[i2], 16)
                q_free[i2] = so
                si += 1

    lg = p.sb("lg", [16, 512], F32)
    ab1 = p.sb("ab1", [16, 512], F32)
    ab2 = p.sb("ab2", [16, 512], F32)
    onep = p.sb("onep", [16, 1], F32)
    abst = [p.sb(f"abst{i}", [16, 512], F32) for i in range(2)]
    stab = [p.sem("stab0"), p.sem("stab1")]
    ab_free = [0, 0]
    p.vector.memset(onep[:], 1.0).then_inc(vsem, 1)
    p.tensor.wait_ge(ssem, p.val(ssem))
    for g in range(NG):
        g0 = g * 512
        i2 = g % 2
        p.tensor.wait_ge(vsem, p.val(vsem))
        for ich in range(DCH):
            mm = p.tensor.matmul(ps_pr[0][0:16, :], wba_s[:, ich * 16:(ich + 1) * 16],
                                 n_bf[ich][:, g0:g0 + 512], start=(ich == 0),
                                 stop=(ich == DCH - 1))
        tv = mm.inc(tsem, 1)
        p.vector.wait_ge(tsem, tv)
        p.vector.tensor_copy(lg[:], ps_pr[0][0:16, :]).then_inc(vsem, 1)
        a1v = p.vector.tensor_copy(ab1[:], lg[:]).inc(vsem, 1)
        p.vector.tensor_scalar(ab2[:], lg[:], 0.0, None, op0=ALU.min).then_inc(vsem, 1)
        p.scalar.wait_ge(vsem, a1v)
        p.scalar.activation(out=ab1[:], in_=ab1[:], func=AF.Abs)
        s1 = p.scalar.activation(out=ab1[:], in_=ab1[:], func=AF.Exp, scale=-1.0)
        s2 = p.scalar.activation(out=ab1[:], in_=ab1[:], func=AF.Ln,
                                 bias=onep[0:16, 0:1]).inc(ssem, 1)
        p.vector.wait_ge(ssem, s2)
        p.vector.wait_ge(stab[i2], ab_free[i2])
        ov = p.vector.tensor_tensor(abst[i2][:], ab2[:], ab1[:], op=ALU.subtract).inc(vsem, 1)
        p.sync.wait_ge(vsem, ov)
        so = p.sync.dma_start(out=lnab_out[0:16, g0:g0 + 512], in_=abst[i2][:]).inc(stab[i2], 16)
        ab_free[i2] = so

    for sm in (stv[0], stv[1], stq[0], stq[1], stab[0], stab[1]):
        p.sync.wait_ge(sm, p.val(sm))
    return p.finalize() if own else None


def run_gdn_pre(st):
    nc = build_gdn_pre()
    in_maps = []
    for c in range(NCORES):
        sl = slice(c * TC, (c + 1) * TC)
        lnp = np.ascontiguousarray(np.concatenate(
            [_packbias(st["L0_attn_b_ln1_w"]), _packbias(st["L0_attn_b_ln1_b"])], axis=1))
        in_maps.append({
            "xs": np.ascontiguousarray(st["x_fm"][:, sl]),
            "wq": st["L0_attn_b_Wq"], "wk": st["L0_attn_b_Wk"], "wv": st["L0_attn_b_Wv"],
            "wba": np.ascontiguousarray(np.concatenate(
                [st["L0_attn_b_Wb"], st["L0_attn_b_Wa"]], axis=1)),
            "lnp": lnp,
        })
    res = run_bass_kernel_spmd(nc, in_maps, core_ids=list(range(NCORES)))
    st["gdn_y0"] = np.concatenate([r["y0_out"] for r in res.results], axis=1)
    st["gdn_q"] = np.concatenate([r["q_out"] for r in res.results], axis=1)
    st["gdn_k"] = np.concatenate([r["k_out"] for r in res.results], axis=1)
    st["gdn_v"] = np.concatenate([r["v_out"] for r in res.results], axis=1)
    st["gdn_lnab"] = np.concatenate([r["lnab_out"] for r in res.results], axis=1)
    return st



class Seq:

    def __init__(self, p):
        self.p = p
        self.t = p.sem("sq_t")
        self.v = p.sem("sq_v")
        self.s = p.sem("sq_s")

    def mm(self, *a, **k):
        p = self.p
        p.tensor.wait_ge(self.v, p.val(self.v))
        p.tensor.wait_ge(self.s, p.val(self.s))
        return p.tensor.matmul(*a, **k).inc(self.t, 1)

    def tr(self, *a, **k):
        p = self.p
        p.tensor.wait_ge(self.v, p.val(self.v))
        p.tensor.wait_ge(self.s, p.val(self.s))
        return p.tensor.transpose(*a, **k).inc(self.t, 1)

    def _pre(self, eng):
        p = self.p
        e = getattr(p, eng)
        e.wait_ge(self.t, p.val(self.t))
        if eng != "vector":
            e.wait_ge(self.v, p.val(self.v))
        if eng != "scalar":
            e.wait_ge(self.s, p.val(self.s))
        return e

    def v_(self, method, *a, **k):
        e = self._pre("vector")
        return getattr(_QEng(self.p, "vector"), method)(*a, **k).inc(self.v, 1)

    def s_(self, method, *a, **k):
        e = self._pre("scalar")
        return getattr(_QEng(self.p, "scalar"), method)(*a, **k).inc(self.s, 1)


def build_gdn_run(p=None):
    own = p is None
    if own:
        p = Prog("gdn_run")
    NCHUNK = S // 128
    ins = {}
    for i in range(2):
        ins[f"q{i}"] = p.din(f"q{i}", [128, S], BF16)
        ins[f"k{i}"] = p.din(f"k{i}", [128, S], BF16)
        ins[f"v{i}"] = p.din(f"v{i}", [128, S], BF16)
        ins[f"lnr{i}"] = p.din(f"lnr{i}", [2, S], F32)
    o_out = [p.dout(f"o{i}", [128, S], BF16) for i in range(2)]

    consts = emit_consts(p)
    sq = Seq(p)
    ld = p.sem("ld")
    sto = p.sem("sto")

    maskM = p.sb("maskM", [128, 128], F32)
    maskN = p.sb("maskN", [128, 128], F32)
    uplus = p.sb("uplus", [128, 128], F32)
    identf = p.sb("identf", [128, 128], F32)
    ii = p.sb("ii", [128, 128], F32)
    gsemc = p.sem("gsemc")
    p.gpsimd.iota(ii[:], pattern=[[1, 128]], base=0, channel_multiplier=-1,
                  allow_small_or_imprecise_dtypes=True).then_inc(gsemc, 1)
    p.vector.wait_ge(gsemc, 1)
    sq.v_("tensor_scalar", maskM[:], ii[:], 0.0, None, op0=ALU.is_ge)
    sq.v_("tensor_scalar", maskM[:], maskM[:], -1e30, None, op0=ALU.mult)
    sq.v_("tensor_scalar", maskN[:], ii[:], 0.0, None, op0=ALU.is_lt)
    sq.v_("tensor_scalar", maskN[:], maskN[:], -1e30, None, op0=ALU.mult)
    sq.v_("tensor_scalar", uplus[:], ii[:], 0.0, None, op0=ALU.is_ge)
    sq.v_("tensor_copy", identf[:], consts["ident"][:])

    qb = p.sb("qb", [128, S], BF16)
    kb = p.sb("kb", [128, S], BF16)
    vb = p.sb("vb", [128, S], BF16)
    ob = p.sb("ob", [128, S], BF16)
    lnrow = p.sb("lnrow", [2, S], F32)
    lncol = p.sb("lncol", [128, 2], F32)
    S0 = p.sb("S0", [128, 128], F32)
    S0b = p.sb("S0b", [128, 128], BF16)
    Lrow = p.sb("Lrow", [1, S], F32)
    Lcol = p.sb("Lcol", [128, 1], F32)
    r1row = p.sb("r1row", [1, 128], F32)
    r1col = p.sb("r1col", [128, 1], F32)
    eLrow = p.sb("eLrow", [1, 128], F32)
    ghrow = p.sb("ghrow", [1, 128], F32)
    ghcol = p.sb("ghcol", [128, 1], F32)
    gccol = p.sb("gccol", [128, 1], F32)
    DM = p.sb("DM", [128, 128], F32)
    D2 = p.sb("D2", [128, 128], F32)
    EM = p.sb("EM", [128, 128], F32)
    E2 = p.sb("E2", [128, 128], F32)
    Msb = p.sb("Msb", [128, 128], BF16)
    Nsb = p.sb("Nsb", [128, 128], BF16)
    P0 = p.sb("P0", [128, 128], BF16)
    Uj = [p.sb(f"Uj{j}", [128, 128], BF16) for j in range(6)]
    Pj = [p.sb(f"Pj{j}", [128, 128], BF16) for j in range(6)]
    Bsb = p.sb("Bsb", [128, 128], BF16)
    Tt = p.sb("Tt", [128, 128], BF16)
    Ktok = p.sb("Ktok", [128, 128], BF16)
    Khat = p.sb("Khat", [128, 128], BF16)
    Vtok = p.sb("Vtok", [128, 128], BF16)
    Ktl = p.sb("Ktl", [128, 128], BF16)
    Qtl = p.sb("Qtl", [128, 128], BF16)
    Rsb = p.sb("Rsb", [128, 128], BF16)
    Usb = p.sb("Usb", [128, 128], BF16)
    Osb = p.sb("Osb", [128, 128], BF16)
    psA = p.ps("psA", [128, 512], F32)[:, 0:128]
    psB = p.ps("psB", [128, 512], F32)[:, 0:128]
    psR = p.ps("psR", [128, 512], F32)[:, 0:128]
    psO = p.ps("psO", [128, 512], F32)[:, 0:128]
    psT = p.ps("psT", [128, 1024], BF16)[:, 0:128]
    psL = p.ps("psL", [128, 512], F32)[:, 0:128]

    for chain in range(2):
        p.sync.dma_start(out=qb[:], in_=ins[f"q{chain}"][:, :]).then_inc(ld, 16)
        p.sync.dma_start(out=kb[:], in_=ins[f"k{chain}"][:, :]).then_inc(ld, 16)
        p.sync.dma_start(out=vb[:], in_=ins[f"v{chain}"][:, :]).then_inc(ld, 16)
        p.sync.dma_start(out=lnrow[:], in_=ins[f"lnr{chain}"][:, :]).then_inc(ld, 16)
        ldv = p.val(ld)
        p.tensor.wait_ge(ld, ldv)
        p.vector.wait_ge(ld, ldv)
        p.scalar.wait_ge(ld, ldv)
        sq.v_("memset", S0[:], 0.0)
        sq.v_("memset", S0b[:], 0.0)

        for c in range(NCHUNK):
            sl = slice(c * 128, (c + 1) * 128)
            lac = lncol[:, 0:1]
            lbc = lncol[:, 1:2]
            sq.tr(psL[0:128, 0:2], lnrow[0:2, sl], identf[0:2, 0:2])
            sq.v_("tensor_copy", lncol[:, 1:2], psL[0:128, 0:1])
            sq.v_("tensor_copy", lncol[:, 0:1], psL[0:128, 1:2])
            sq.mm(psL[0:128, 0:1], uplus[:, :], lac, start=True, stop=True)
            sq.v_("tensor_copy", Lcol[:, 0:1], psL[0:128, 0:1])
            sq.mm(psL[0:1, 0:128], lac, uplus[:, :], start=True, stop=True)
            sq.v_("tensor_copy", Lrow[0:1, sl], psL[0:1, 0:128])
            sq.v_("tensor_tensor", r1row[0:1, :], lnrow[0:1, sl], Lrow[0:1, sl], op=ALU.subtract)
            sq.v_("tensor_tensor", r1col[:, 0:1], lbc, Lcol[:, 0:1], op=ALU.subtract)
            sq.s_("activation", out=eLrow[0:1, :], in_=Lrow[0:1, sl], func=AF.Exp)
            sq.s_("activation", out=ghrow[0:1, :], in_=r1row[0:1, :], func=AF.Exp,
                  bias=Lrow[0:1, c * 128 + 127:c * 128 + 128])
            sq.mm(psR[:, :], consts["ones"][0:1, :], eLrow[0:1, :], start=True, stop=True)
            sq.v_("tensor_tensor", Ktl[:, :], kb[:, sl], psR[:, :], op=ALU.mult)
            sq.v_("tensor_tensor", Qtl[:, :], qb[:, sl], psR[:, :], op=ALU.mult)
            sq.mm(psL[0:128, 0:1], consts["ones"][0:1, :],
                  eLrow[0:1, 127:128], start=True, stop=True)
            sq.v_("tensor_copy", gccol[:, 0:1], psL[0:128, 0:1])
            sq.tr(psL[0:128, 0:1], ghrow[0:1, :], identf[0:1, 0:1])
            sq.v_("tensor_copy", ghcol[:, 0:1], psL[0:128, 0:1])
            sq.v_("tensor_scalar", DM[:, :], maskM[:, :], Lcol[:, 0:1], None, op0=ALU.add)
            sq.mm(psR[:, :], consts["ones"][0:1, :], r1row[0:1, :], start=True, stop=True)
            sq.v_("tensor_tensor", DM[:, :], DM[:, :], psR[:, :], op=ALU.add)
            sq.s_("activation", out=EM[:, :], in_=DM[:, :], func=AF.Exp)
            sq.v_("tensor_scalar", D2[:, :], maskN[:, :], r1col[:, 0:1], None, op0=ALU.add)
            sq.mm(psR[:, :], consts["ones"][0:1, :], Lrow[0:1, sl], start=True, stop=True)
            sq.v_("tensor_tensor", D2[:, :], D2[:, :], psR[:, :], op=ALU.add)
            sq.s_("activation", out=E2[:, :], in_=D2[:, :], func=AF.Exp)
            sq.mm(psA[:, :], kb[:, sl], kb[:, sl], start=True, stop=True)
            sq.v_("tensor_tensor", Msb[:, :], EM[:, :], psA[:, :], op=ALU.mult)
            sq.mm(psB[:, :], kb[:, sl], qb[:, sl], start=True, stop=True)
            sq.v_("tensor_tensor", Nsb[:, :], E2[:, :], psB[:, :], op=ALU.mult)
            sq.tr(psT[:, :], Msb[:, :], consts["ident"][:, :])
            sq.v_("tensor_copy", P0[:, :], psT[:, :])
            prevU, prevP = Msb, P0
            for j in range(6):
                sq.mm(psA[:, :], prevP[:, :], prevU[:, :], start=True, stop=True)
                sq.v_("tensor_copy", Uj[j][:, :], psA[:, :])
                sq.tr(psT[:, :], Uj[j][:, :], consts["ident"][:, :])
                sq.v_("tensor_copy", Pj[j][:, :], psT[:, :])
                prevU, prevP = Uj[j], Pj[j]
            sq.v_("tensor_tensor", Bsb[:, :], consts["ident"][:, :], Uj[5][:, :], op=ALU.add)
            for j in (4, 3, 2, 1, 0):
                sq.mm(psA[:, :], Pj[j][:, :], Bsb[:, :], start=True, stop=True)
                sq.v_("tensor_tensor", Bsb[:, :], Bsb[:, :], psA[:, :], op=ALU.add)
            sq.mm(psA[:, :], P0[:, :], Bsb[:, :], start=True, stop=True)
            sq.v_("tensor_tensor", Bsb[:, :], Bsb[:, :], psA[:, :], op=ALU.subtract)
            sq.tr(psT[:, :], Bsb[:, :], consts["ident"][:, :])
            sq.v_("tensor_copy", Tt[:, :], psT[:, :])
            sq.tr(psT[:, :], kb[:, sl], consts["ident"][:, :])
            sq.v_("tensor_copy", Ktok[:, :], psT[:, :])
            sq.v_("tensor_scalar", Khat[:, :], Ktok[:, :], ghcol[:, 0:1], None, op0=ALU.mult)
            sq.tr(psT[:, :], vb[:, sl], consts["ident"][:, :])
            sq.v_("tensor_copy", Vtok[:, :], psT[:, :])
            sq.mm(psO[:, :], Ktl[:, :], S0b[:, :], start=True, stop=True)
            sq.v_("tensor_tensor", Rsb[:, :], Vtok[:, :], psO[:, :], op=ALU.subtract)
            sq.mm(psO[:, :], Tt[:, :], Rsb[:, :], start=True, stop=True)
            sq.v_("tensor_copy", Usb[:, :], psO[:, :])
            sq.mm(psO[:, :], Qtl[:, :], S0b[:, :], start=True, stop=False)
            sq.mm(psO[:, :], Nsb[:, :], Usb[:, :], start=False, stop=True)
            sq.v_("tensor_copy", Osb[:, :], psO[:, :])
            sq.tr(psT[:, :], Osb[:, :], consts["ident"][:, :])
            sq.v_("tensor_copy", ob[:, sl], psT[:, :])
            sq.mm(psA[:, :], Khat[:, :], Usb[:, :], start=True, stop=True)
            sq.v_("scalar_tensor_tensor", S0[:, :], S0[:, :], gccol[:, 0:1], psA[:, :],
                  op0=ALU.mult, op1=ALU.add)
            sq.v_("tensor_copy", S0b[:, :], S0[:, :])

        p.sync.wait_ge(sq.v, p.val(sq.v))
        p.sync.dma_start(out=o_out[chain][:, :], in_=ob[:]).then_inc(sto, 16)
        if chain == 0:
            p.sync.wait_ge(sq.t, p.val(sq.t))
            p.vector.wait_ge(sto, p.val(sto))
    p.sync.wait_ge(sto, p.val(sto))
    return p.finalize() if own else None


def run_gdn(st):
    nc = build_gdn_run()
    in_maps = []
    for c in range(NCORES):
        im = {}
        for i in range(2):
            chain = 2 * c + i
            b, h = chain // 8, chain % 8
            bsl = slice(b * S, (b + 1) * S)
            im[f"q{i}"] = np.ascontiguousarray(st["gdn_q"][h * 128:(h + 1) * 128, bsl])
            im[f"k{i}"] = np.ascontiguousarray(st["gdn_k"][h * 128:(h + 1) * 128, bsl])
            im[f"v{i}"] = np.ascontiguousarray(st["gdn_v"][h * 128:(h + 1) * 128, bsl])
            lnr = np.ascontiguousarray(st["gdn_lnab"][[h, 8 + h], :][:, bsl])
            im[f"lnr{i}"] = lnr
        in_maps.append(im)
    res = run_bass_kernel_spmd(nc, in_maps, core_ids=list(range(NCORES)))
    O = np.zeros((D, T_ALL), BF16NP)
    for c in range(NCORES):
        for i in range(2):
            chain = 2 * c + i
            b, h = chain // 8, chain % 8
            O[h * 128:(h + 1) * 128, b * S:(b + 1) * S] = res.results[c][f"o{i}"]
    st["gdn_o"] = O
    return st



def build_gdn_post(p=None):
    own = p is None
    if own:
        p = Prog("gdn_post")
    xs = p.din("xs", [D, TC], F32)
    y0s = p.din("y0s", [D, TC], F32)
    os_ = p.din("os", [D, TC], BF16)
    wo = p.din("wo", [D, D], F32)
    w1 = p.din("w1", [D, 4 * D], F32)
    w2 = p.din("w2", [4 * D, D], F32)
    lnp = p.din("lnp", [128, 2 * DCH], F32)
    b1p = p.din("b1p", [128, 32], F32)
    b2p = p.din("b2p", [128, DCH], F32)
    x_out = p.dout("x_out", [D, TC], F32)
    xs4_d = p.dtmp("xs4_d", [D, TC], F32)
    t_d = p.dtmp("t_d", [D, TC], F32)

    consts = emit_consts(p)
    gld = p.sem("gld")
    psem = p.sem("psem")
    tsem = p.sem("tsem")
    vsem = p.sem("vsem")
    ssem = p.sem("ssem")

    wo_s = p.sb("wo_s", [128, DCH * D], BF16)
    w1_s = p.sb("w1_s", [128, DCH * 4 * D], BF16)
    p.gpsimd.dma_start(out=wo_s[:].rearrange("p (c t) -> p c t", c=DCH),
                       in_=wo.rearrange("(c p) t -> p c t", p=128)).then_inc(gld, 16)
    p.gpsimd.dma_start(out=w1_s[:].rearrange("p (c t) -> p c t", c=DCH),
                       in_=w1.rearrange("(c p) t -> p c t", p=128)).then_inc(gld, 16)
    wld = p.val(gld)
    lnw = p.sb("lnw", [128, 2 * DCH], F32)
    b1s = p.sb("b1s", [128, 32], F32)
    b2s = p.sb("b2s", [128, DCH], F32)
    p.sync.dma_start(out=lnw[:], in_=lnp[:, :]).then_inc(psem, 16)
    p.sync.dma_start(out=b1s[:], in_=b1p[:, :]).then_inc(psem, 16)
    p.sync.dma_start(out=b2s[:], in_=b2p[:, :]).then_inc(psem, 16)
    pld = p.val(psem)

    NG = TC // 512
    lsem = p.sem("lsem")
    og = p.sb("og", [128, DCH * 512], BF16)
    yg2 = [p.sb(f"yg{i}", [128, 512], F32) for i in range(2)]
    xg32 = [p.sb(f"xg3{i}", [128, 512], F32) for i in range(2)]
    ygsem = [p.sem("ygs0"), p.sem("ygs1")]
    xgsem = [p.sem("xgs0"), p.sem("xgs1")]
    yg_free = [0, 0]
    xg_free = [0, 0]
    ps_pr = [p.ps("prA", [128, 512], F32), p.ps("prB", [128, 512], F32)]
    st4 = [p.sb(f"st4_{i}", [128, 512], F32) for i in range(2)]
    stt = [p.sb(f"stt_{i}", [128, 512], F32) for i in range(2)]
    sem4 = [p.sem("sem4_0"), p.sem("sem4_1")]
    semt = [p.sem("semt_0"), p.sem("semt_1")]
    free4 = [0, 0]
    freet = [0, 0]
    ps_free = [0, 0]
    p.tensor.wait_ge(gld, wld)
    si = 0
    for g in range(NG):
        g0 = g * 512
        p.sync.wait_ge(tsem, p.val(tsem))
        p.sync.dma_start(out=og[:].rearrange("p (c t) -> p c t", c=DCH),
                         in_=os_[:, g0:g0 + 512].rearrange("(c p) t -> p c t", p=128)
                         ).then_inc(lsem, 16)
        lv = p.val(lsem)
        p.tensor.wait_ge(lsem, lv)
        for oc in range(DCH):
            i2 = si % 2
            p.sync.wait_ge(vsem, yg_free[i2])
            yl = p.sync.dma_start(out=yg2[i2][:],
                                  in_=y0s[oc * 128:(oc + 1) * 128, g0:g0 + 512]).inc(ygsem[i2], 16)
            xl = p.sync.dma_start(out=xg32[i2][:],
                                  in_=xs[oc * 128:(oc + 1) * 128, g0:g0 + 512]).inc(xgsem[i2], 16)
            p.tensor.wait_ge(vsem, ps_free[i2])
            for ich in range(DCH):
                mm = p.tensor.matmul(ps_pr[i2][:, :],
                                     wo_s[:, ich * D + oc * 128:ich * D + (oc + 1) * 128],
                                     og[:, ich * 512:(ich + 1) * 512],
                                     start=(ich == 0), stop=(ich == DCH - 1))
            tv = mm.inc(tsem, 1)
            p.vector.wait_ge(tsem, tv)
            p.vector.wait_ge(ygsem[i2], yl)
            p.vector.wait_ge(xgsem[i2], xl)
            p.vector.wait_ge(sem4[i2], free4[i2])
            pf = p.vector.tensor_tensor(st4[i2][:], ps_pr[i2][:, :],
                                        yg2[i2][:], op=ALU.add).inc(vsem, 1)
            ps_free[i2] = pf
            p.vector.wait_ge(semt[i2], freet[i2])
            tv2 = p.vector.tensor_tensor(stt[i2][:], st4[i2][:],
                                         xg32[i2][:], op=ALU.add).inc(vsem, 1)
            yg_free[i2] = tv2
            p.sync.wait_ge(vsem, pf)
            s4 = p.sync.dma_start(out=xs4_d[oc * 128:(oc + 1) * 128, g0:g0 + 512],
                                  in_=st4[i2][:]).inc(sem4[i2], 16)
            free4[i2] = s4
            p.sync.wait_ge(vsem, tv2)
            s5 = p.sync.dma_start(out=t_d[oc * 128:(oc + 1) * 128, g0:g0 + 512],
                                  in_=stt[i2][:]).inc(semt[i2], 16)
            freet[i2] = s5
            si += 1
    for sm in (sem4[0], sem4[1], semt[0], semt[1]):
        p.sync.wait_ge(sm, p.val(sm))

    ln = LN(p)
    xg = p.sb("xg", [128, DCH * 512], F32)
    xg_ch = [xg[:, ch * 512:(ch + 1) * 512] for ch in range(DCH)]
    h2g = [p.sb(f"h2g{ch}", [128, 512], BF16) for ch in range(DCH)]
    ff1g = p.sb("ff1g", [128, 32 * 512], BF16)
    w2s = [p.sb(f"w2s{i}", [128, 32 * 128], BF16) for i in range(2)]
    w2sem = [p.sem("w2s0"), p.sem("w2s1")]
    tg = [p.sb(f"tg{i}", [128, 512], F32) for i in range(2)]
    tgsem = [p.sem("tg0"), p.sem("tg1")]
    ostg = [p.sb(f"ostg{i}", [128, 512], F32) for i in range(2)]
    osem = [p.sem("oss0"), p.sem("oss1")]
    atmp = p.sb("atmp", [128, 512], F32)
    xld = p.sem("xld")
    w2_free = [0, 0]
    tg_free = [0, 0]
    o_free = [0, 0]
    ps_free = [0, 0]
    p.vector.wait_ge(psem, pld)
    for g in range(NG):
        g0 = g * 512
        p.sync.wait_ge(tsem, p.val(tsem))
        p.sync.wait_ge(ssem, p.val(ssem))
        p.sync.wait_ge(vsem, p.val(vsem))
        xl = p.sync.dma_start(out=xg[:].rearrange("p (c t) -> p c t", c=DCH),
                              in_=xs4_d[:, g0:g0 + 512].rearrange("(c p) t -> p c t", p=128)
                              ).inc(xld, 16)
        p.tensor.wait_ge(xld, xl)
        p.scalar.wait_ge(xld, xl)
        p.vector.wait_ge(xld, xl)
        bm, br = ln.stats(xg_ch, 0, 512, consts)
        for ch in range(DCH):
            p.vector.tensor_tensor(atmp[:, 0:512], xg_ch[ch], bm[:, 0:512], op=ALU.add)
            p.vector.tensor_tensor(atmp[:, 0:512], atmp[:, 0:512], br[:, 0:512], op=ALU.mult)
            p.vector.tensor_scalar(h2g[ch][:, 0:512], atmp[:, 0:512],
                                   lnw[:, ch:ch + 1], lnw[:, DCH + ch:DCH + ch + 1],
                                   op0=ALU.mult, op1=ALU.add).then_inc(vsem, 1)
        hv = p.val(vsem)
        p.tensor.wait_ge(vsem, hv)
        for fc in range(32):
            i2 = fc % 2
            p.tensor.wait_ge(ssem, ps_free[i2])
            for ich in range(DCH):
                mm = p.tensor.matmul(ps_pr[i2][:, :],
                                     w1_s[:, ich * 4 * D + fc * 128:ich * 4 * D + (fc + 1) * 128],
                                     h2g[ich][:, 0:512], start=(ich == 0), stop=(ich == DCH - 1))
            tv = mm.inc(tsem, 1)
            p.scalar.wait_ge(tsem, tv)
            sv = p.scalar.activation(out=ff1g[:, fc * 512:(fc + 1) * 512], in_=ps_pr[i2][:, :],
                                     func=AF.Gelu, bias=b1s[:, fc:fc + 1]).inc(ssem, 1)
            ps_free[i2] = sv
        ffv = p.val(ssem)
        p.tensor.wait_ge(ssem, ffv)
        for oc in range(DCH):
            i2 = oc % 2
            p.gpsimd.wait_ge(tsem, w2_free[i2])
            wv = p.gpsimd.dma_start(
                out=w2s[i2][:].rearrange("p (c t) -> p c t", c=32),
                in_=w2[:, oc * 128:(oc + 1) * 128].rearrange("(c p) t -> p c t", p=128)
                ).inc(w2sem[i2], 16)
            p.sync.wait_ge(vsem, tg_free[i2])
            tl = p.sync.dma_start(out=tg[i2][:],
                                  in_=t_d[oc * 128:(oc + 1) * 128, g0:g0 + 512]).inc(tgsem[i2], 16)
            p.tensor.wait_ge(w2sem[i2], wv)
            p.tensor.wait_ge(vsem, ps_free[i2] if False else 0)
            pp = ps_pr[i2]
            p.tensor.wait_ge(vsem, p.val(vsem))
            for kc in range(32):
                mm = p.tensor.matmul(pp[:, :], w2s[i2][:, kc * 128:(kc + 1) * 128],
                                     ff1g[:, kc * 512:(kc + 1) * 512],
                                     start=(kc == 0), stop=(kc == 31))
            tv = mm.inc(tsem, 1)
            w2_free[i2] = tv
            p.vector.wait_ge(tsem, tv)
            p.vector.wait_ge(tgsem[i2], tl)
            p.vector.wait_ge(osem[i2], o_free[i2])
            p.vector.tensor_scalar(ostg[i2][:], pp[:, :], b2s[:, oc:oc + 1], None, op0=ALU.add)
            ov = p.vector.tensor_tensor(ostg[i2][:], ostg[i2][:], tg[i2][:], op=ALU.add).inc(vsem, 1)
            tg_free[i2] = ov
            p.sync.wait_ge(vsem, ov)
            so = p.sync.dma_start(out=x_out[oc * 128:(oc + 1) * 128, g0:g0 + 512],
                                  in_=ostg[i2][:]).inc(osem[i2], 16)
            o_free[i2] = so
    for sm in osem:
        p.sync.wait_ge(sm, p.val(sm))
    return p.finalize() if own else None


def run_gdn_post(st):
    nc = build_gdn_post()
    in_maps = []
    lnp = np.ascontiguousarray(np.concatenate(
        [_packbias(st["L0_attn_b_ln2_w"]), _packbias(st["L0_attn_b_ln2_b"])], axis=1))
    b1p = np.ascontiguousarray(np.stack(
        [st["L0_attn_b_b1"][i * 128:(i + 1) * 128] for i in range(32)], axis=1))
    for c in range(NCORES):
        sl = slice(c * TC, (c + 1) * TC)
        in_maps.append({
            "xs": np.ascontiguousarray(st["x_fm"][:, sl]),
            "y0s": np.ascontiguousarray(st["gdn_y0"][:, sl]),
            "os": np.ascontiguousarray(st["gdn_o"][:, sl]),
            "wo": st["L0_attn_b_Wo"], "w1": st["L0_attn_b_W1"], "w2": st["L0_attn_b_W2"],
            "lnp": lnp, "b1p": b1p, "b2p": _packbias(st["L0_attn_b_b2"]),
        })
    res = run_bass_kernel_spmd(nc, in_maps, core_ids=list(range(NCORES)))
    st["x_fm"] = np.concatenate([r["x_out"] for r in res.results], axis=1)
    return st


def run_gdn_all(st):
    st = run_gdn_pre(st)
    st = run_gdn(st)
    st = run_gdn_post(st)
    return st



def build_lin_pre(p=None):
    own = p is None
    if own:
        p = Prog("lin_pre")
    xs = p.din("xs", [D, TC], F32)
    wqkvT = p.din("wqkvT", [D, 3 * D], F32)
    bqp = p.din("bqp", [128, 24], F32)
    scl = p.din("scl", [1, 1], F32)
    q_out = p.dout("q_out", [D, TC], F32)
    k_out = p.dout("k_out", [D, TC], F32)
    kv_out = p.dout("kv_out", [D, TC], F32)

    consts = emit_consts(p)
    lsem = p.sem("lsem")
    bsem = p.sem("bsem")
    gld = p.sem("gld")
    tsem = p.sem("tsem")
    vsem = p.sem("vsem")
    ssem = p.sem("ssem")

    xbuf = p.sb("xbuf", [128, DCH * TC], F32)
    x_ch = [xbuf[:, ch * TC:(ch + 1) * TC] for ch in range(DCH)]
    p.sync.dma_start(out=xbuf[:].rearrange("p (c t) -> p c t", c=DCH),
                     in_=xs.rearrange("(c p) t -> p c t", p=128)).then_inc(lsem, 16)
    bias_s = p.sb("bias_s", [128, 24], F32)
    p.sync.dma_start(out=bias_s[:], in_=bqp[:, :]).then_inc(bsem, 16)
    scl_s = p.sb("scl_s", [128, 1], F32)
    p.sync.dma_start(out=scl_s[:], in_=scl[0:1, 0:1].broadcast_to([128, 1])).then_inc(bsem, 16)
    wbuf = p.sb("wbuf", [128, DCH * 3 * D], BF16)
    p.gpsimd.dma_start(out=wbuf[:].rearrange("p (c t) -> p c t", c=DCH),
                       in_=wqkvT.rearrange("(c p) t -> p c t", p=128)).then_inc(gld, 16)

    ln = LN(p)
    xn_bf = [p.sb(f"xn{ch}", [128, TC], BF16) for ch in range(DCH)]
    atmp = p.sb("atmp", [128, 512], F32)
    p.tensor.wait_ge(lsem, 16)
    p.scalar.wait_ge(lsem, 16)
    p.vector.wait_ge(lsem, 16)
    NG = TC // 512
    for g in range(NG):
        g0 = g * 512
        bm, br = ln.stats(x_ch, g0, 512, consts)
        for ch in range(DCH):
            p.vector.tensor_tensor(atmp[:, 0:512], x_ch[ch][:, g0:g0 + 512], bm[:, 0:512], op=ALU.add)
            p.vector.tensor_tensor(xn_bf[ch][:, g0:g0 + 512], atmp[:, 0:512], br[:, 0:512],
                                   op=ALU.mult).then_inc(vsem, 1)
    xn_ready = p.val(vsem)

    ps_pr = [p.ps("prA", [128, 512], F32), p.ps("prB", [128, 512], F32)]
    ps_free = [0, 0]
    tq = p.sb("tq", [128, 512], F32)
    te = p.sb("te", [128, 512], F32)
    kkeep = p.sb("kkeep", [128, 512], F32)
    ostg = [p.sb(f"lostg{i}", [128, 512], F32) for i in range(3)]
    osem = [p.sem(f"loss{i}") for i in range(3)]
    o_free = [0, 0, 0]
    p.tensor.wait_ge(vsem, xn_ready)
    p.tensor.wait_ge(gld, 16)
    p.tensor.wait_ge(bsem, 32)
    si = 0

    def proj(oc, pp):
        for ich in range(DCH):
            mm = p.tensor.matmul(pp[:, :], wbuf[:, ich * 3 * D + oc * 128:ich * 3 * D + (oc + 1) * 128],
                                 xn_bf[ich][:, g0:g0 + 512], start=(ich == 0), stop=(ich == DCH - 1))
        return mm.inc(tsem, 1)

    def elu1p(pp, oc, dst):
        p.vector.tensor_scalar(tq[:, :], pp[:, :], bias_s[:, oc:oc + 1], scl_s[:, 0:1],
                               op0=ALU.add, op1=ALU.mult)
        fv = p.vector.tensor_scalar(te[:, :], tq[:, :], 0.0, None, op0=ALU.min).inc(vsem, 1)
        p.scalar.wait_ge(vsem, fv)
        sv = p.scalar.activation(out=te[:, :], in_=te[:, :], func=AF.Exp).inc(ssem, 1)
        p.vector.tensor_scalar(tq[:, :], tq[:, :], 0.0, None, op0=ALU.max)
        p.vector.wait_ge(ssem, sv)
        return p.vector.tensor_tensor(dst, tq[:, :], te[:, :], op=ALU.add).inc(vsem, 1)

    for g in range(NG):
        g0 = g * 512
        for j in range(DCH):
            i2 = si % 2
            p.tensor.wait_ge(vsem, ps_free[i2])
            tv = proj(j, ps_pr[i2])
            p.vector.wait_ge(tsem, tv)
            p.vector.wait_ge(osem[0], o_free[0])
            ev = elu1p(ps_pr[i2], j, ostg[0][:, :])
            ps_free[i2] = ev
            p.sync.wait_ge(vsem, ev)
            o_free[0] = p.sync.dma_start(out=q_out[j * 128:(j + 1) * 128, g0:g0 + 512],
                                         in_=ostg[0][:]).inc(osem[0], 16)
            si += 1
        for j in range(DCH):
            i2 = si % 2
            p.tensor.wait_ge(vsem, ps_free[i2])
            tv = proj(8 + j, ps_pr[i2])
            p.vector.wait_ge(tsem, tv)
            ev = elu1p(ps_pr[i2], 8 + j, kkeep[:, :])
            ps_free[i2] = ev
            p.vector.wait_ge(osem[1], o_free[1])
            kv1 = p.vector.tensor_copy(ostg[1][:], kkeep[:]).inc(vsem, 1)
            p.sync.wait_ge(vsem, kv1)
            o_free[1] = p.sync.dma_start(out=k_out[j * 128:(j + 1) * 128, g0:g0 + 512],
                                         in_=ostg[1][:]).inc(osem[1], 16)
            si += 1
            i2 = si % 2
            p.tensor.wait_ge(vsem, ps_free[i2])
            tv = proj(16 + j, ps_pr[i2])
            p.vector.wait_ge(tsem, tv)
            p.vector.tensor_scalar(tq[:, :], ps_pr[i2][:, :], bias_s[:, 16 + j:17 + j], None,
                                   op0=ALU.add)
            p.vector.wait_ge(osem[2], o_free[2])
            ev2 = p.vector.tensor_tensor(ostg[2][:], kkeep[:], tq[:], op=ALU.mult).inc(vsem, 1)
            ps_free[i2] = ev2
            p.sync.wait_ge(vsem, ev2)
            o_free[2] = p.sync.dma_start(out=kv_out[j * 128:(j + 1) * 128, g0:g0 + 512],
                                         in_=ostg[2][:]).inc(osem[2], 16)
            si += 1
    for sm in osem:
        p.sync.wait_ge(sm, p.val(sm))
    return p.finalize() if own else None


def run_lin_pre(st):
    nc = build_lin_pre()
    in_maps = []
    for c in range(NCORES):
        sl = slice(c * TC, (c + 1) * TC)
        in_maps.append({
            "xs": np.ascontiguousarray(st["x_fm"][:, sl]),
            "wqkvT": _fm(st["L1_attn_a_Wqkv"]),
            "bqp": np.ascontiguousarray(np.stack(
                [st["L1_attn_a_bqkv"][i * 128:(i + 1) * 128] for i in range(24)], axis=1)),
            "scl": st["L1_attn_a_scale"].reshape(1, 1),
        })
    res = run_bass_kernel_spmd(nc, in_maps, core_ids=list(range(NCORES)))
    st["lin_q"] = np.concatenate([r["q_out"] for r in res.results], axis=1)
    st["lin_k"] = np.concatenate([r["k_out"] for r in res.results], axis=1)
    st["lin_kv"] = np.concatenate([r["kv_out"] for r in res.results], axis=1)
    return st



def build_lin_run(p=None):
    own = p is None
    if own:
        p = Prog("lin_run")
    qi = [p.din(f"qi{i}", [128, S], F32) for i in range(2)]
    ki = [p.din(f"ki{i}", [128, S], F32) for i in range(2)]
    kvi = [p.din(f"kvi{i}", [128, S], F32) for i in range(2)]
    y_out = [p.dout(f"y{i}", [128, S], BF16) for i in range(2)]

    consts = emit_consts(p)
    ldk = p.sem("ldk")
    ldkv = p.sem("ldkv")
    ldq = p.sem("ldq")
    sto = p.sem("sto")
    tsem = p.sem("tsem")
    vsem = p.sem("vsem")

    bufA = p.sb("bufA", [128, S], F32)
    bufB = p.sb("bufB", [128, S], F32)
    bufC = p.sb("bufC", [128, S], F32)
    qb = p.sb("qb2", [128, S], F32)
    yb = p.sb("yb", [128, S], BF16)
    prod = p.sb("prod", [128, 512], F32)
    dr = p.sb("dr", [1, 512], F32)
    bc64 = p.sb("bc64", [128, 512], F32)
    ps_d = p.ps("ps_d", [1, 512], F32)
    ps_b = p.ps("ps_b", [128, 512], F32)

    def cumsum(dst_final, src, other):
        cur, nxt = src, other
        for s in [1 << i for i in range(13)]:
            p.vector.tensor_tensor(nxt[:, s:S], cur[:, s:S], cur[:, 0:S - s], op=ALU.add).then_inc(vsem, 1)
            p.vector.tensor_copy(nxt[:, 0:s], cur[:, 0:s]).then_inc(vsem, 1)
            cur, nxt = nxt, cur
        return cur

    for i in range(2):
        p.sync.wait_ge(tsem, p.val(tsem))
        p.sync.wait_ge(vsem, p.val(vsem))
        p.sync.dma_start(out=bufA[:], in_=ki[i][:, :]).then_inc(ldk, 16)
        p.vector.wait_ge(ldk, p.val(ldk))
        kc = cumsum(None, bufA, bufB)
        kfree = bufB if kc is bufA else bufA
        p.sync.wait_ge(vsem, p.val(vsem))
        p.sync.dma_start(out=kfree[:], in_=kvi[i][:, :]).then_inc(ldkv, 16)
        p.vector.wait_ge(ldkv, p.val(ldkv))
        kvc = cumsum(None, kfree, bufC)
        p.sync.dma_start(out=qb[:], in_=qi[i][:, :]).then_inc(ldq, 16)
        p.vector.wait_ge(ldq, p.val(ldq))
        for g in range(S // 512):
            g0 = g * 512
            for hh in range(2):
                rb = hh * 64
                dv = p.vector.tensor_tensor(prod[rb:rb + 64, :], qb[rb:rb + 64, g0:g0 + 512],
                                            kc[rb:rb + 64, g0:g0 + 512], op=ALU.mult).inc(vsem, 1)
                p.tensor.wait_ge(vsem, dv)
                tv = p.tensor.matmul(ps_d[0:1, :], consts["ones"][rb:rb + 64, 0:1],
                                     prod[rb:rb + 64, :], start=True, stop=True).inc(tsem, 1)
                p.vector.wait_ge(tsem, tv)
                p.vector.tensor_scalar(dr[0:1, :], ps_d[0:1, :], 1e-3, None, op0=ALU.max)
                rv = p.vector.reciprocal(dr[0:1, :], dr[0:1, :]).inc(vsem, 1)
                p.tensor.wait_ge(vsem, rv)
                tb = p.tensor.matmul(ps_b[:, :], consts["ones"][0:1, 0:128], dr[0:1, :],
                                     start=True, stop=True).inc(tsem, 1)
                p.vector.wait_ge(tsem, tb)
                p.vector.tensor_tensor(prod[rb:rb + 64, :], qb[rb:rb + 64, g0:g0 + 512],
                                       kvc[rb:rb + 64, g0:g0 + 512], op=ALU.mult)
                yv = p.vector.tensor_tensor(yb[rb:rb + 64, g0:g0 + 512], prod[rb:rb + 64, :],
                                            ps_b[rb:rb + 64, :], op=ALU.mult).inc(vsem, 1)
                p.tensor.wait_ge(vsem, yv)
        yv = p.val(vsem)
        p.sync.wait_ge(vsem, yv)
        p.sync.dma_start(out=y_out[i][:, :], in_=yb[:]).then_inc(sto, 16)
        p.vector.wait_ge(sto, p.val(sto))
    p.sync.wait_ge(sto, p.val(sto))
    return p.finalize() if own else None


def run_lin_run(st):
    nc = build_lin_run()
    in_maps = []
    for c in range(NCORES):
        im = {}
        for i in range(2):
            grp = 4 * c + 2 * i
            b = grp // 16
            h0 = grp % 16
            bsl = slice(b * S, (b + 1) * S)
            rows = slice(h0 * 64, (h0 + 2) * 64)
            im[f"qi{i}"] = np.ascontiguousarray(st["lin_q"][rows, bsl])
            im[f"ki{i}"] = np.ascontiguousarray(st["lin_k"][rows, bsl])
            im[f"kvi{i}"] = np.ascontiguousarray(st["lin_kv"][rows, bsl])
        in_maps.append(im)
    res = run_bass_kernel_spmd(nc, in_maps, core_ids=list(range(NCORES)))
    Y = np.zeros((D, T_ALL), BF16NP)
    for c in range(NCORES):
        for i in range(2):
            grp = 4 * c + 2 * i
            b = grp // 16
            h0 = grp % 16
            Y[h0 * 64:(h0 + 2) * 64, b * S:(b + 1) * S] = res.results[c][f"y{i}"]
    st["lin_y"] = Y
    return st



def build_lin_post(p=None):
    own = p is None
    if own:
        p = Prog("lin_post")
    xs = p.din("xs", [D, TC], F32)
    ys = p.din("ys", [D, TC], BF16)
    wpT = p.din("wpT", [D, D], F32)
    bpp = p.din("bpp", [128, DCH], F32)
    rtT = p.din("rtT", [D, 8], F32)
    x_out = p.dout("x_out", [D, TC], F32)
    htm_out = p.dout("htm_out", [TC, D], BF16)
    lg_out = p.dout("lg_out", [TC, 8], F32)

    consts = emit_consts(p)
    lsem = p.sem("lsem")
    bsem = p.sem("bsem")
    gld = p.sem("gld")
    tsem = p.sem("tsem")
    vsem = p.sem("vsem")
    ssem = p.sem("ssem")

    xbuf = p.sb("xbuf", [128, DCH * TC], F32)
    x_ch = [xbuf[:, ch * TC:(ch + 1) * TC] for ch in range(DCH)]
    ybuf = p.sb("ybuf", [128, DCH * TC], BF16)
    y_ch = [ybuf[:, ch * TC:(ch + 1) * TC] for ch in range(DCH)]
    p.sync.dma_start(out=xbuf[:].rearrange("p (c t) -> p c t", c=DCH),
                     in_=xs.rearrange("(c p) t -> p c t", p=128)).then_inc(lsem, 16)
    p.sync.dma_start(out=ybuf[:].rearrange("p (c t) -> p c t", c=DCH),
                     in_=ys.rearrange("(c p) t -> p c t", p=128)).then_inc(gld, 16)
    bias_s = p.sb("bias_s", [128, DCH], F32)
    p.sync.dma_start(out=bias_s[:], in_=bpp[:, :]).then_inc(bsem, 16)
    rt_s = p.sb("rt_s", [128, DCH * 8], F32)
    p.sync.dma_start(out=rt_s[:].rearrange("p (c t) -> p c t", c=DCH),
                     in_=rtT.rearrange("(c p) t -> p c t", p=128)).then_inc(bsem, 16)
    wpbuf = p.sb("wpbuf", [128, DCH * D], BF16)
    wp_s = [wpbuf[:, i * D:(i + 1) * D] for i in range(DCH)]
    p.gpsimd.dma_start(out=wpbuf[:].rearrange("p (c t) -> p c t", c=DCH),
                       in_=wpT.rearrange("(c p) t -> p c t", p=128)).then_inc(gld, 16)
    wld = p.val(gld)

    ps_pr = [p.ps("prA", [128, 512], F32), p.ps("prB", [128, 512], F32)]
    ps_free = [0, 0]
    ostg = [p.sb(f"ostg{i}", [128, 512], F32) for i in range(2)]
    osem = [p.sem("oss0"), p.sem("oss1")]
    o_free = [0, 0]
    NG = TC // 512
    p.tensor.wait_ge(lsem, 16)
    p.tensor.wait_ge(gld, wld)
    p.tensor.wait_ge(bsem, 32)
    si = 0
    for g in range(NG):
        g0 = g * 512
        for oc in range(DCH):
            i2 = si % 2
            p.tensor.wait_ge(vsem, ps_free[i2])
            for ich in range(DCH):
                mm = p.tensor.matmul(ps_pr[i2][:, :], wp_s[ich][:, oc * 128:(oc + 1) * 128],
                                     y_ch[ich][:, g0:g0 + 512], start=(ich == 0),
                                     stop=(ich == DCH - 1))
            tv = mm.inc(tsem, 1)
            p.vector.wait_ge(tsem, tv)
            p.vector.tensor_scalar(ostg[0][:], ps_pr[i2][:, :], bias_s[:, oc:oc + 1], None,
                                   op0=ALU.add)
            pf = p.vector.tensor_tensor(x_ch[oc][:, g0:g0 + 512], ostg[0][:],
                                        x_ch[oc][:, g0:g0 + 512], op=ALU.add).inc(vsem, 1)
            ps_free[i2] = pf
            p.sync.wait_ge(vsem, pf)
            p.sync.dma_start(out=x_out[oc * 128:(oc + 1) * 128, g0:g0 + 512],
                             in_=x_ch[oc][:, g0:g0 + 512]).then_inc(osem[0], 16)
            si += 1

    ln = LN(p)
    hf = [p.sb(f"hf{ch}", [128, 512], F32) for ch in range(DCH)]
    hb = [p.sb(f"hb{ch}", [128, 512], BF16) for ch in range(DCH)]
    atmp = p.sb("atmp", [128, 512], F32)
    stg_tm = [p.sb(f"stgtm{i}", [128, D], BF16) for i in range(2)]
    stg_lg = [p.sb(f"stglg{i}", [128, 8], F32) for i in range(2)]
    tmsem = [p.sem("tms0"), p.sem("tms1")]
    lgsem = [p.sem("lgs0"), p.sem("lgs1")]
    ps_t = p.ps("ps_t", [128, 512], BF16)
    ps_l = p.ps("ps_l", [128, 512], F32)
    tm_free = [0, 0]
    lg_free = [0, 0]
    for g in range(NG):
        g0 = g * 512
        p.tensor.wait_ge(vsem, p.val(vsem))
        p.scalar.wait_ge(vsem, p.val(vsem))
        bm, br = ln.stats(x_ch, g0, 512, consts)
        for ch in range(DCH):
            p.vector.tensor_tensor(atmp[:, 0:512], x_ch[ch][:, g0:g0 + 512], bm[:, 0:512], op=ALU.add)
            p.vector.tensor_tensor(hf[ch][:, 0:512], atmp[:, 0:512], br[:, 0:512], op=ALU.mult)
            p.vector.tensor_copy(hb[ch][:, 0:512], hf[ch][:, 0:512]).then_inc(vsem, 1)
        hv = p.val(vsem)
        p.tensor.wait_ge(vsem, hv)
        for j in range(4):
            i2 = j % 2
            for ich in range(DCH):
                mm = p.tensor.matmul(ps_l[0:128, 0:8], hf[ich][:, j * 128:(j + 1) * 128],
                                     rt_s[:, ich * 8:(ich + 1) * 8],
                                     start=(ich == 0), stop=(ich == DCH - 1))
            tv = mm.inc(tsem, 1)
            p.vector.wait_ge(tsem, tv)
            p.vector.wait_ge(lgsem[i2], lg_free[i2])
            lgv = p.vector.tensor_copy(stg_lg[i2][:], ps_l[0:128, 0:8]).inc(vsem, 1)
            p.sync.wait_ge(vsem, lgv)
            lg_free[i2] = p.sync.dma_start(out=lg_out[g0 + j * 128:g0 + (j + 1) * 128, :],
                                           in_=stg_lg[i2][:]).inc(lgsem[i2], 16)
            p.vector.wait_ge(tmsem[i2], tm_free[i2])
            for ich in range(DCH):
                tt = p.tensor.transpose(ps_t[:, 0:128], hb[ich][:, j * 128:(j + 1) * 128],
                                        consts["ident"][:, :]).inc(tsem, 1)
                p.vector.wait_ge(tsem, tt)
                ev = p.vector.tensor_copy(stg_tm[i2][:, ich * 128:(ich + 1) * 128],
                                          ps_t[:, 0:128]).inc(vsem, 1)
                p.tensor.wait_ge(vsem, ev)
            p.sync.wait_ge(vsem, p.val(vsem))
            tm_free[i2] = p.sync.dma_start(out=htm_out[g0 + j * 128:g0 + (j + 1) * 128, :],
                                           in_=stg_tm[i2][:]).inc(tmsem[i2], 16)
    p.sync.wait_ge(osem[0], p.val(osem[0]))
    for sm in (tmsem[0], tmsem[1], lgsem[0], lgsem[1]):
        p.sync.wait_ge(sm, p.val(sm))
    return p.finalize() if own else None


def run_lin_post(st):
    nc = build_lin_post()
    in_maps = []
    for c in range(NCORES):
        sl = slice(c * TC, (c + 1) * TC)
        in_maps.append({
            "xs": np.ascontiguousarray(st["x_fm"][:, sl]),
            "ys": np.ascontiguousarray(st["lin_y"][:, sl]),
            "wpT": _fm(st["L1_attn_a_Wproj"]),
            "bpp": _packbias(st["L1_attn_a_bproj"]),
            "rtT": _fm(st["L1_ffn_router_weight"]),
        })
    res = run_bass_kernel_spmd(nc, in_maps, core_ids=list(range(NCORES)))
    st["x_fm"] = np.concatenate([r["x_out"] for r in res.results], axis=1)
    st["moe_h_tm"] = np.concatenate([r["htm_out"] for r in res.results], axis=0)
    st["moe_lg"] = np.concatenate([r["lg_out"] for r in res.results], axis=0)
    return st


def run_lin(st):
    st = run_lin_pre(st)
    st = run_lin_run(st)
    st = run_lin_post(st)
    return st



MOE_CAP = 5120
MOE_EPS = 1.1920929e-07
NEXP = 8
NTILE = T_ALL // 128


def emit_routing(p, lgbuf, consts, sq):
    r = {}
    NT = NTILE
    lg3 = lgbuf[:].rearrange("p (t e) -> p t e", e=8)
    m1 = p.sb("rt_m1", [128, NT], F32)
    m2 = p.sb("rt_m2", [128, NT], F32)
    m1e = p.sb("rt_m1e", [128, NT * 8], F32)
    msk = p.sb("rt_msk", [128, NT * 8], F32)
    lgm = p.sb("rt_lgm", [128, NT * 8], F32)
    w1 = p.sb("rt_w1", [128, NT], F32)
    w2 = p.sb("rt_w2", [128, NT], F32)
    eid1 = p.sb("rt_eid1", [128, NT], F32)
    eid2 = p.sb("rt_eid2", [128, NT], F32)
    tmp1 = p.sb("rt_tmp1", [128, NT], F32)
    sq.v_("tensor_reduce", m1[:], lg3, axis=AX.X, op=ALU.max)
    for j in range(8):
        sq.v_("tensor_copy", m1e[:, j::8], m1[:])
    sq.v_("tensor_tensor", msk[:], lgbuf[:], m1e[:], op=ALU.is_ge)
    sq.v_("scalar_tensor_tensor", lgm[:], msk[:], -1e30, lgbuf[:], op0=ALU.mult, op1=ALU.add)
    sq.v_("tensor_reduce", m2[:], lgm[:].rearrange("p (t e) -> p t e", e=8), axis=AX.X, op=ALU.max)
    sq.v_("tensor_tensor", tmp1[:], m1[:], m2[:], op=ALU.subtract)
    sq.s_("activation", out=w1[:], in_=tmp1[:], func=AF.Sigmoid)
    sq.s_("activation", out=w2[:], in_=tmp1[:], func=AF.Sigmoid, scale=-1.0)
    sq.v_("memset", eid1[:], 0.0)
    for e in range(1, 8):
        sq.v_("scalar_tensor_tensor", eid1[:], msk[:, e::8], float(e), eid1[:],
              op0=ALU.mult, op1=ALU.add)
    for j in range(8):
        sq.v_("tensor_copy", m1e[:, j::8], m2[:])
    sq.v_("tensor_tensor", msk[:], lgm[:], m1e[:], op=ALU.is_ge)
    sq.v_("memset", eid2[:], 0.0)
    for e in range(1, 8):
        sq.v_("scalar_tensor_tensor", eid2[:], msk[:, e::8], float(e), eid2[:],
              op0=ALU.mult, op1=ALU.add)
    r.update(m1=m1, m2=m2, w1=w1, w2=w2, eid1=eid1, eid2=eid2)

    wc = [p.sb(f"rt_wc{e}", [128, NT], F32) for e in range(NEXP)]
    a_ = p.sb("rt_a", [128, NT], F32)
    b_ = p.sb("rt_b", [128, NT], F32)
    w1p = p.sb("rt_w1p", [128, NT], F32)
    w2p = p.sb("rt_w2p", [128, NT], F32)
    sq.v_("tensor_scalar", w1p[:], w1[:], 1.0, None, op0=ALU.add)
    sq.v_("tensor_scalar", w2p[:], w2[:], 1.0, None, op0=ALU.add)
    for e in range(NEXP):
        sq.v_("tensor_scalar", a_[:], eid1[:], float(e), None, op0=ALU.is_equal)
        sq.v_("tensor_scalar", b_[:], eid2[:], float(e), None, op0=ALU.is_equal)
        sq.v_("tensor_tensor", a_[:], a_[:], w1p[:], op=ALU.mult)
        sq.v_("tensor_tensor", b_[:], b_[:], w2p[:], op=ALU.mult)
        sq.v_("tensor_tensor", a_[:], a_[:], b_[:], op=ALU.add)
        sq.v_("tensor_scalar", wc[e][:], a_[:], -1.0, None, op0=ALU.add)
    r["wc"] = wc

    lo = p.sb("rt_lo", [128, NEXP], F32)
    hi = p.sb("rt_hi", [128, NEXP], F32)
    mid = p.sb("rt_mid", [128, NEXP], F32)
    d1 = p.sb("rt_d1", [128, NEXP], F32)
    d2 = p.sb("rt_d2", [128, NEXP], F32)
    nov = p.sb("rt_nov", [128, NEXP], F32)
    cnt_row = p.sb("rt_cntr", [1, NEXP], F32)
    over_row = p.sb("rt_over", [1, NEXP], F32)
    mtile = p.sb("rt_mtile", [128, NT], F32)
    csum = p.sb("rt_csum", [128, 1], F32)
    ps_cnt = p.ps("rt_pscnt", [1, 512], F32)[:, 0:NEXP]
    ps_ov = p.ps("rt_psov", [128, 512], F32)[:, 0:NEXP]
    sq.v_("memset", lo[:], -1.0)
    sq.v_("memset", hi[:], 1.0)
    for it in range(26):
        sq.v_("tensor_tensor", mid[:], lo[:], hi[:], op=ALU.add)
        sq.v_("tensor_scalar", mid[:], mid[:], 0.5, None, op0=ALU.mult)
        for e in range(NEXP):
            sq.v_("tensor_scalar", mtile[:], wc[e][:], mid[:, e:e + 1], None, op0=ALU.is_gt)
            sq.v_("tensor_reduce", csum[:, 0:1], mtile[:], axis=AX.X, op=ALU.add)
            sq.mm(ps_cnt[0:1, e:e + 1], csum[:, 0:1], consts["ones"][:, 0:1],
                  start=True, stop=True)
        sq.v_("tensor_scalar", over_row[0:1, :], ps_cnt[0:1, :], float(MOE_CAP), None,
              op0=ALU.is_gt)
        sq.mm(ps_ov[:, :], consts["ones"][0:1, :], over_row[0:1, :], start=True, stop=True)
        sq.v_("tensor_tensor", d1[:], mid[:], lo[:], op=ALU.subtract)
        sq.v_("tensor_tensor", d1[:], d1[:], ps_ov[:, :], op=ALU.mult)
        sq.v_("tensor_tensor", lo[:], lo[:], d1[:], op=ALU.add)
        sq.v_("tensor_scalar", nov[:], ps_ov[:, :], -1.0, 1.0, op0=ALU.mult, op1=ALU.add)
        sq.v_("tensor_tensor", d2[:], mid[:], hi[:], op=ALU.subtract)
        sq.v_("tensor_tensor", d2[:], d2[:], nov[:], op=ALU.mult)
        sq.v_("tensor_tensor", hi[:], hi[:], d2[:], op=ALU.add)
    r["thr"] = hi

    keep = [p.sb(f"rt_kp{e}", [128, NT], F32) for e in range(NEXP)]
    slot = [p.sb(f"rt_sl{e}", [128, NT], F32) for e in range(NEXP)]
    sa = p.sb("rt_sa", [128, NT], F32)
    sb2 = p.sb("rt_sb", [128, NT], F32)
    ustrict = p.sb("rt_ustrict", [128, 128], F32)
    iot2 = p.sb("rt_iot2", [128, 128], F32)
    rtot = p.sb("rt_rtot", [128, 1], F32)
    ps_off = p.ps("rt_psoff", [128, 512], F32)[:, 0:1]
    giota2 = p.sem("rt_giota2")
    p.gpsimd.iota(iot2[:], pattern=[[1, 128]], base=0, channel_multiplier=-1,
                  allow_small_or_imprecise_dtypes=True).then_inc(giota2, 1)
    p.vector.wait_ge(giota2, 1)
    sq.v_("tensor_scalar", ustrict[:], iot2[:], 0.5, None, op0=ALU.is_gt)
    for e in range(NEXP):
        sq.v_("tensor_scalar", keep[e][:], wc[e][:], hi[:, e:e + 1], None, op0=ALU.is_gt)
        cur, nxt = keep[e], sa
        first = True
        for s in [1 << i for i in range(7)]:
            sq.v_("tensor_tensor", nxt[:, s:NT], cur[:, s:NT], cur[:, 0:NT - s], op=ALU.add)
            sq.v_("tensor_copy", nxt[:, 0:s], cur[:, 0:s])
            if first:
                cur, nxt = nxt, sb2
                first = False
            else:
                cur, nxt = nxt, cur
        inc = cur
        sq.v_("tensor_copy", rtot[:, 0:1], inc[:, NT - 1:NT])
        sq.mm(ps_off[:, 0:1], ustrict[:, :], rtot[:, 0:1], start=True, stop=True)
        sq.v_("tensor_tensor", slot[e][:], inc[:], keep[e][:], op=ALU.subtract)
        sq.v_("tensor_copy", sb2[:, 0:1], ps_off[:, 0:1])
        sq.v_("tensor_scalar", slot[e][:], slot[e][:], sb2[:, 0:1], None, op0=ALU.add)
    r["keep"] = keep
    r["slot"] = slot
    return r


MOEB_PREFETCH = True
MOEB_HALFPOST = True
MOEB_SKIP_POST = False
MOEB_SKIP_FC = False
MOEB_NOFREE = False
MOEB_SKIP_B = False
MOEB_SKIP_GATH = False
MOEB_GW = 512
MOEB_NSLOT = 8
MOEB_DELAY0 = False


def build_moeb(p=None):
    own = p is None
    if own:
        p = Prog("moeb")
    lg = p.din("lg", [T_ALL, 8], F32)
    htm = p.din("htm", [T_ALL, D], BF16)
    preT = p.din("preT", [D, 4 * D], F32)
    gateT = p.din("gateT", [D, 4 * D], F32)
    postT = p.din("postT", [16 * 128, 16 * 128], F32)
    ohp = p.din("ohp", [128, 8], F32)
    eo_out = p.dout("eo_out", [MOE_CAP, D], BF16)
    rid1_out = p.dout("rid1_out", [128, NTILE], F32)
    rid2_out = p.dout("rid2_out", [128, NTILE], F32)
    c1_out = p.dout("c1_out", [128, NTILE], F32)
    c2_out = p.dout("c2_out", [128, NTILE], F32)
    tokl = p.dtmp("tokl", [MOE_CAP + 256, 1], I32)

    consts = emit_consts(p)
    sq = Seq(p)
    lsem = p.sem("lsem")
    gld = p.sem("gld")
    gsc = p.sem("gsc")
    sto = p.sem("sto")

    meta_sem = p.sem("meta_sem")
    stg_r = [p.sb(f"stg_r{i}", [128, NTILE], F32) for i in range(2)]
    stg_c = [p.sb(f"stg_c{i}", [128, NTILE], F32) for i in range(2)]
    _outer_scope = p._scope
    _rt_stack = ExitStack()
    if not MOEB_NOFREE:
        p._scope = (_rt_stack, _outer_scope[1] if _outer_scope else [])

    lgbuf = p.sb("lgbuf", [128, NTILE * 8], F32)
    p.sync.dma_start(out=lgbuf[:].rearrange("p (t e) -> p t e", e=8),
                     in_=lg.rearrange("(t p) e -> p t e", p=128)).then_inc(lsem, 16)
    oh_s = p.sb("oh_s", [128, 8], F32)
    p.sync.dma_start(out=oh_s[:], in_=ohp[:, :]).then_inc(lsem, 16)
    lsv = p.val(lsem)
    for eng in ("tensor", "vector", "scalar", "gpsimd"):
        getattr(p, eng).wait_ge(lsem, lsv)
    rt = emit_routing(p, lgbuf, consts, sq)

    kmine = p.sb("kmine", [128, NTILE], F32)
    smine = p.sb("smine", [128, NTILE], F32)
    sq.v_("memset", kmine[:], 0.0)
    sq.v_("memset", smine[:], 0.0)
    for e in range(NEXP):
        sq.v_("scalar_tensor_tensor", kmine[:], rt["keep"][e][:], oh_s[:, e:e + 1], kmine[:],
              op0=ALU.mult, op1=ALU.add)
        sq.v_("scalar_tensor_tensor", smine[:], rt["slot"][e][:], oh_s[:, e:e + 1], smine[:],
              op0=ALU.mult, op1=ALU.add)
    nk = p.sb("nk", [128, NTILE], F32)
    sq.v_("tensor_scalar", nk[:], kmine[:], -1.0, 1.0, op0=ALU.mult, op1=ALU.add)
    sq.v_("scalar_tensor_tensor", smine[:], nk[:], 1e7, smine[:], op0=ALU.mult, op1=ALU.add)
    sq.v_("tensor_scalar", smine[:], smine[:], float(MOE_CAP + 8), None, op0=ALU.min)
    ti_row = p.sb("ti_row", [128, NTILE], F32)
    g_tir = p.sem("g_tir")
    p.gpsimd.iota(ti_row[:], pattern=[[1, NTILE]], base=0, channel_multiplier=0,
                  allow_small_or_imprecise_dtypes=True).then_inc(g_tir, 1)
    p.vector.wait_ge(g_tir, 1)
    sq.v_("tensor_scalar", nk[:], smine[:], float(MOE_CAP + 8), None, op0=ALU.is_ge)
    sq.v_("tensor_tensor", nk[:], nk[:], ti_row[:], op=ALU.mult)
    sq.v_("tensor_tensor", smine[:], smine[:], nk[:], op=ALU.add)
    smi = p.sb("smi", [128, NTILE], I32)
    sq.v_("tensor_copy", smi[:], smine[:])
    tokid = p.sb("tokid", [128, NTILE], I32)
    g_tok = p.sem("g_tok")
    g_pref = p.sem("g_pref")
    g_pfd = p.sem("g_pfd")
    p.gpsimd.iota(tokid[:], pattern=[[128, NTILE]], base=0, channel_multiplier=1,
                  allow_small_or_imprecise_dtypes=True).then_inc(g_tok, 1)
    pref = p.sb("pref", [128, (MOE_CAP + 256) // 128], I32)
    p.gpsimd.wait_ge(sq.v, p.val(sq.v))
    p.gpsimd.memset(pref[:], 0).then_inc(g_pref, 1)
    p.gpsimd.wait_ge(g_pref, 1)
    p.gpsimd.dma_start(out=tokl[0:MOE_CAP + 256, 0:1].rearrange("(b a) one -> b (a one)", b=128),
                       in_=pref[:, 0:(MOE_CAP + 256) // 128]).then_inc(g_pfd, 16)
    p.gpsimd.wait_ge(g_pfd, 16)
    p.gpsimd.wait_ge(g_tok, 1)
    for ti in range(NTILE):
        p.gpsimd.indirect_dma_start(
            out=tokl[0:128, 0:1], out_offset=bass.IndirectOffsetOnAxis(smi[:, ti:ti + 1], 0),
            in_=tokid[:, ti:ti + 1], in_offset=None).then_inc(gsc, 16)
    scat_done = p.val(gsc)
    p.gpsimd.wait_ge(gsc, scat_done)

    sel1s = p.sb("sel1s", [128, NTILE], F32)
    sel1k = p.sb("sel1k", [128, NTILE], F32)
    sel2s = p.sb("sel2s", [128, NTILE], F32)
    sel2k = p.sb("sel2k", [128, NTILE], F32)
    mt1 = p.sb("mt1", [128, NTILE], F32)
    mt2 = p.sb("mt2", [128, NTILE], F32)
    for dst in (sel1s, sel1k, sel2s, sel2k):
        sq.v_("memset", dst[:], 0.0)
    for e in range(NEXP):
        for (eidt, ss, kk) in ((rt["eid1"], sel1s, sel1k), (rt["eid2"], sel2s, sel2k)):
            sq.v_("tensor_scalar", mt1[:], eidt[:], float(e), None, op0=ALU.is_equal)
            sq.v_("tensor_tensor", mt2[:], mt1[:], rt["slot"][e][:], op=ALU.mult)
            sq.v_("tensor_tensor", ss[:], ss[:], mt2[:], op=ALU.add)
            sq.v_("tensor_tensor", mt2[:], mt1[:], rt["keep"][e][:], op=ALU.mult)
            sq.v_("tensor_tensor", kk[:], kk[:], mt2[:], op=ALU.add)
    for i, (eidt, ss, kk, rout) in enumerate(((rt["eid1"], sel1s, sel1k, rid1_out),
                                              (rt["eid2"], sel2s, sel2k, rid2_out))):
        sq.v_("tensor_scalar", mt1[:], eidt[:], float(MOE_CAP), None, op0=ALU.mult)
        sq.v_("tensor_tensor", mt1[:], mt1[:], ss[:], op=ALU.add)
        sq.v_("tensor_tensor", mt1[:], mt1[:], kk[:], op=ALU.mult)
        sq.v_("tensor_scalar", mt2[:], kk[:], -float(NEXP * MOE_CAP), float(NEXP * MOE_CAP),
              op0=ALU.mult, op1=ALU.add)
        sq.v_("tensor_tensor", stg_r[i][:], mt1[:], mt2[:], op=ALU.add)
        p.sync.wait_ge(sq.v, p.val(sq.v))
        p.sync.dma_start(out=rout[:, :], in_=stg_r[i][:]).then_inc(meta_sem, 16)
    sq.v_("tensor_tensor", mt1[:], rt["w1"][:], sel1k[:], op=ALU.mult)
    sq.v_("tensor_tensor", mt2[:], rt["w2"][:], sel2k[:], op=ALU.mult)
    twt = p.sb("tw", [128, NTILE], F32)
    sq.v_("tensor_tensor", twt[:], mt1[:], mt2[:], op=ALU.add)
    sq.v_("tensor_scalar", twt[:], twt[:], MOE_EPS, None, op0=ALU.max)
    sq.v_("reciprocal", twt[:], twt[:])
    sq.v_("tensor_tensor", stg_c[0][:], mt1[:], twt[:], op=ALU.mult)
    p.sync.wait_ge(sq.v, p.val(sq.v))
    p.sync.dma_start(out=c1_out[:, :], in_=stg_c[0][:]).then_inc(meta_sem, 16)
    sq.v_("tensor_tensor", stg_c[1][:], mt2[:], twt[:], op=ALU.mult)
    p.sync.wait_ge(sq.v, p.val(sq.v))
    p.sync.dma_start(out=c2_out[:, :], in_=stg_c[1][:]).then_inc(meta_sem, 16)

    p.gpsimd.wait_ge(sq.s, p.val(sq.s))
    p.gpsimd.wait_ge(sq.v, p.val(sq.v))
    p.gpsimd.wait_ge(sq.t, p.val(sq.t))
    if not MOEB_NOFREE:
        p._scope = _outer_scope
        _rt_stack.close()

    pre_s = p.sb("pre_s", [128, DCH * 4 * D], BF16)
    gate_s = p.sb("gate_s", [128, DCH * 4 * D], BF16)
    p.gpsimd.dma_start(out=pre_s[:].rearrange("p (c t) -> p c t", c=DCH),
                       in_=preT.rearrange("(c p) t -> p c t", p=128)).then_inc(gld, 16)
    p.gpsimd.dma_start(out=gate_s[:].rearrange("p (c t) -> p c t", c=DCH),
                       in_=gateT.rearrange("(c p) t -> p c t", p=128)).then_inc(gld, 16)
    wld = p.val(gld)
    p.tensor.wait_ge(gld, wld)

    GW = MOEB_GW
    NGRP = MOE_CAP // GW
    NSUB = GW // 128
    NFC = 32
    idxt = [p.sb(f"idxt{i}", [128, NSUB], I32) for i in range(2)]
    xtm = [[p.sb(f"xtm{i}_{j}", [128, D], BF16) for j in range(NSUB)] for i in range(2)]
    xfm = p.sb("xfm", [128, DCH * GW], BF16)
    act = p.sb("act", [128, NFC * GW], BF16) if not MOEB_SKIP_FC else None
    eo_st = [p.sb(f"eo_st{j}", [128, D], BF16) for j in range(NSUB)]
    post_s = [p.sb(f"post_s{i}", [128, 16 * 128], BF16) for i in range(2)] if not MOEB_SKIP_POST else None
    silb = p.sb("silb", [128, GW], F32) if not MOEB_SKIP_FC else None
    obf = [p.sb(f"obf{i}", [128, GW], BF16) for i in range(2)] if not MOEB_SKIP_POST else None
    psem2 = [p.sem("ps2_0"), p.sem("ps2_1")]
    lsem_ix = [p.sem("lsx0"), p.sem("lsx1")]
    gsem_ix = [p.sem("gsx0"), p.sem("gsx1")]
    t_fc = p.sem("t_fc")
    s_fc = p.sem("s_fc")
    v_fc = p.sem("v_fc")
    t_tr = p.sem("t_tr")
    v_tr = p.sem("v_tr")
    t_pc = p.sem("t_pc")
    v_ob = p.sem("v_ob")
    psP = [p.ps(f"psP{i}", [128, 512], F32) for i in range(2)] if not MOEB_SKIP_FC else None
    psG = [p.ps(f"psG{i}", [128, 512], F32) for i in range(2)] if not MOEB_SKIP_FC else None
    psO = [p.ps(f"psO{i}", [128, 512], F32) for i in range(2)] if not MOEB_SKIP_POST else None
    ps_trs = [p.ps(f"ps_tr{i}", [128, 1024], BF16) for i in range(2)]
    NSLOT = 2
    tr_slot_owner = [0] * NSLOT
    gath_done = [0, 0]
    xtm_free = [0, 0]
    post_dma = [0, 0]
    psO_free = [0, 0]
    stores_done = 0

    def prefetch(g):
        if MOEB_SKIP_GATH:
            return
        i = g % 2
        if g == 0 and MOEB_DELAY0:
            p.gpsimd.wait_ge(gld, wld)
        p.sync.wait_ge(gsc, scat_done)
        p.sync.wait_ge(gsem_ix[i], gath_done[i])
        ixl = p.sync.dma_start(out=idxt[i][:, 0:NSUB],
                               in_=tokl[g * GW:(g + 1) * GW, 0:1].rearrange(
                                   "(f pp) one -> pp (f one)", pp=128),
                               allow_slow_non_contiguous=True).inc(lsem_ix[i], 16)
        p.gpsimd.wait_ge(lsem_ix[i], ixl)
        p.gpsimd.wait_ge(t_tr, xtm_free[i])
        for j in range(NSUB):
            p.gpsimd.indirect_dma_start(
                out=xtm[i][j][:, :], out_offset=None,
                in_=htm[:, :], in_offset=bass.IndirectOffsetOnAxis(idxt[i][:, j:j + 1], 0)
            ).then_inc(gsem_ix[i], 16)
        gath_done[i] = p.val(gsem_ix[i])

    if MOEB_PREFETCH:
        prefetch(0)
    for g in range(NGRP):
        i = g % 2
        if not MOEB_PREFETCH:
            prefetch(g)
        p.tensor.wait_ge(gsem_ix[i], gath_done[i])
        p.vector.wait_ge(t_fc, g * NFC)
        for j in range(NSUB if not MOEB_SKIP_B else 0):
            for ch in range(DCH):
                k = j * DCH + ch
                slot = k % NSLOT
                p.tensor.wait_ge(v_tr, tr_slot_owner[slot])
                tv = p.tensor.transpose(ps_trs[slot][:, 0:128],
                                        xtm[i][j][:, ch * 128:(ch + 1) * 128],
                                        consts["ident"][:, :]).inc(t_tr, 1)
                p.vector.wait_ge(t_tr, tv)
                vv = p.vector.tensor_copy(
                    xfm[:, ch * GW + j * 128:ch * GW + (j + 1) * 128],
                    ps_trs[slot][:, 0:128]).inc(v_tr, 1)
                tr_slot_owner[slot] = vv
        xtm_free[i] = p.val(t_tr)
        p.tensor.wait_ge(v_tr, p.val(v_tr))
        if MOEB_PREFETCH and g + 1 < NGRP:
            prefetch(g + 1)
        tbase, sbase, vbase = p.val(t_fc), p.val(s_fc), p.val(v_fc)
        if MOEB_SKIP_FC:
            continue
        p.vector.wait_ge(t_pc, g * 16)
        for fc in range(NFC):
            b = fc % 2
            p.tensor.wait_ge(v_fc, max(0, vbase + fc - 1))
            for ich in range(DCH):
                p.tensor.matmul(psP[b][:, 0:GW],
                                pre_s[:, ich * 4 * D + fc * 128:ich * 4 * D + (fc + 1) * 128],
                                xfm[:, ich * GW:(ich + 1) * GW],
                                start=(ich == 0), stop=(ich == DCH - 1))
            for ich in range(DCH):
                m2_ = p.tensor.matmul(psG[b][:, 0:GW],
                                      gate_s[:, ich * 4 * D + fc * 128:ich * 4 * D + (fc + 1) * 128],
                                      xfm[:, ich * GW:(ich + 1) * GW],
                                      start=(ich == 0), stop=(ich == DCH - 1))
            tv = m2_.inc(t_fc, 1)
            p.scalar.wait_ge(t_fc, tv)
            p.scalar.wait_ge(v_fc, max(0, vbase + fc))
            sv = p.scalar.activation(out=silb[:, :], in_=psG[b][:, 0:GW],
                                     func=AF.Silu).inc(s_fc, 1)
            p.vector.wait_ge(s_fc, sv)
            p.vector.tensor_tensor(act[:, fc * GW:(fc + 1) * GW], psP[b][:, 0:GW],
                                   silb[:, :], op=ALU.mult).then_inc(v_fc, 1)
        pcbase = p.val(t_pc)
        if MOEB_SKIP_POST:
            continue

        def post_dma_issue(s, gate):
            bb = s % 2
            oc2, h2 = divmod(s, 2)
            p.gpsimd.wait_ge(t_pc, max(0, gate))
            post_dma[bb] = p.gpsimd.dma_start(
                out=post_s[bb][:],
                in_=postT[s * 128:(s + 1) * 128, :]).inc(psem2[bb], 16)

        for s in range(2):
            post_dma_issue(s, pcbase - 1 + s)
        for oc in range(DCH):
            ob = oc % 2
            tv = None
            for h in range(2):
                s = oc * 2 + h
                bb = s % 2
                p.tensor.wait_ge(psem2[bb], post_dma[bb])
                if s == 0:
                    p.tensor.wait_ge(v_fc, vbase + NFC)
                if h == 0:
                    p.tensor.wait_ge(v_ob, psO_free[ob])
                for k in range(16):
                    kc = h * 16 + k
                    mm = p.tensor.matmul(psO[ob][:, 0:GW],
                                         post_s[bb][:, k * 128:(k + 1) * 128],
                                         act[:, kc * GW:(kc + 1) * GW],
                                         start=(kc == 0), stop=(kc == 31))
                tv = mm.inc(t_pc, 1)
                if s + 2 < 16:
                    post_dma_issue(s + 2, tv)
            p.vector.wait_ge(t_pc, tv)
            if g and oc == 0:
                p.vector.wait_ge(sto, stores_done)
            cv = p.vector.tensor_copy(obf[ob][:, :], psO[ob][:, 0:GW]).inc(v_ob, 1)
            psO_free[ob] = cv
            for j in range(NSUB):
                slot = j % NSLOT
                p.tensor.wait_ge(v_tr, tr_slot_owner[slot])
                p.tensor.wait_ge(v_ob, cv)
                tv2 = p.tensor.transpose(ps_trs[slot][:, 0:128],
                                         obf[ob][:, j * 128:(j + 1) * 128],
                                         consts["ident"][:, :]).inc(t_tr, 1)
                p.vector.wait_ge(t_tr, tv2)
                vv2 = p.vector.tensor_copy(eo_st[j][:, oc * 128:(oc + 1) * 128],
                                           ps_trs[slot][:, 0:128]).inc(v_tr, 1)
                tr_slot_owner[slot] = vv2
        ev = p.val(v_tr)
        p.sync.wait_ge(v_tr, ev)
        for j in range(NSUB):
            p.sync.dma_start(out=eo_out[g * GW + j * 128:g * GW + (j + 1) * 128, :],
                             in_=eo_st[j][:, :]).then_inc(sto, 16)
        stores_done = p.val(sto)
    p.sync.wait_ge(sto, p.val(sto))
    p.sync.wait_ge(meta_sem, p.val(meta_sem))
    return p.finalize() if own else None


def build_moec(p=None):
    own = p is None
    if own:
        p = Prog("moec")
    xs = p.din("xs", [D, TC], F32)
    eo = p.din("eo", [NEXP * MOE_CAP + 1, D], BF16)
    rid1 = p.din("rid1", [128, 16], F32)
    rid2 = p.din("rid2", [128, 16], F32)
    c1i = p.din("c1i", [128, 16], F32)
    c2i = p.din("c2i", [128, 16], F32)
    x_out = p.dout("x_out", [D, TC], F32)

    consts = emit_consts(p)
    sq = Seq(p)
    lsem = p.sem("lsem")
    gsem2 = p.sem("gsem2")
    sto = p.sem("sto")

    xbuf = p.sb("xbuf", [128, DCH * TC], F32)
    x_ch = [xbuf[:, ch * TC:(ch + 1) * TC] for ch in range(DCH)]
    p.sync.dma_start(out=xbuf[:].rearrange("p (c t) -> p c t", c=DCH),
                     in_=xs.rearrange("(c p) t -> p c t", p=128)).then_inc(lsem, 16)
    meta = p.sb("meta", [128, 4 * 16], F32)
    p.sync.dma_start(out=meta[:, 0:16], in_=rid1[:, :]).then_inc(lsem, 16)
    p.sync.dma_start(out=meta[:, 16:32], in_=rid2[:, :]).then_inc(lsem, 16)
    p.sync.dma_start(out=meta[:, 32:48], in_=c1i[:, :]).then_inc(lsem, 16)
    p.sync.dma_start(out=meta[:, 48:64], in_=c2i[:, :]).then_inc(lsem, 16)
    lv = p.val(lsem)
    for eng in ("tensor", "vector", "scalar", "gpsimd"):
        getattr(p, eng).wait_ge(lsem, lv)
    ridi = p.sb("ridi", [128, 32], I32)
    sq.v_("tensor_copy", ridi[:, 0:16], meta[:, 0:16])
    sq.v_("tensor_copy", ridi[:, 16:32], meta[:, 16:32])
    civ = p.val(sq.v)
    p.gpsimd.wait_ge(sq.v, civ)

    r1 = p.sb("r1", [128, D], BF16)
    r2 = p.sb("r2", [128, D], BF16)
    otm = p.sb("otm", [128, D], BF16)
    ps_t2 = p.ps("ps_t2", [128, 1024], BF16)[:, 0:128]
    for j in range(16):
        g1 = p.gpsimd.indirect_dma_start(
            out=r1[:, :], out_offset=None, in_=eo[:, :],
            in_offset=bass.IndirectOffsetOnAxis(ridi[:, j:j + 1], 0)).inc(gsem2, 16)
        g2 = p.gpsimd.indirect_dma_start(
            out=r2[:, :], out_offset=None, in_=eo[:, :],
            in_offset=bass.IndirectOffsetOnAxis(ridi[:, 16 + j:17 + j], 0)).inc(gsem2, 16)
        p.vector.wait_ge(gsem2, g2)
        sq.v_("tensor_scalar", otm[:, :], r1[:, :], meta[:, 32 + j:33 + j], None, op0=ALU.mult)
        ov = sq.v_("scalar_tensor_tensor", otm[:, :], r2[:, :], meta[:, 48 + j:49 + j],
                   otm[:, :], op0=ALU.mult, op1=ALU.add)
        for ch in range(DCH):
            sq.tr(ps_t2[:, 0:128], otm[:, ch * 128:(ch + 1) * 128], consts["ident"][:, :])
            sq.v_("tensor_tensor", x_ch[ch][:, j * 128:(j + 1) * 128],
                  x_ch[ch][:, j * 128:(j + 1) * 128], ps_t2[:, 0:128], op=ALU.add)
        p.gpsimd.wait_ge(sq.v, p.val(sq.v))
    fv = p.val(sq.v)
    p.sync.wait_ge(sq.v, fv)
    p.sync.dma_start(out=x_out.rearrange("(c p) t -> p c t", p=128),
                     in_=xbuf[:].rearrange("p (c t) -> p c t", c=DCH)).then_inc(sto, 16)
    p.sync.wait_ge(sto, 16)
    return p.finalize() if own else None


def run_moe(st, tag):
    nc = build_moeb()
    in_maps = []
    for c in range(NCORES):
        in_maps.append({
            "lg": st["moe_lg"],
            "htm": st["moe_h_tm"],
            "preT": _fm(st["L1_ffn_ff_pre_act_weight"][c]),
            "gateT": _fm(st["L1_ffn_gate_weight"][c]),
            "postT": _packpost(st["L1_ffn_ff_post_act_weight"][c]),
            "ohp": np.ascontiguousarray(
                np.broadcast_to(np.eye(NEXP, dtype=np.float32)[c], (128, NEXP))),
        })
    res = run_bass_kernel_spmd(nc, in_maps, core_ids=list(range(NCORES)))
    eo = np.concatenate([r["eo_out"] for r in res.results] +
                        [np.zeros((1, D), BF16NP)], axis=0)
    meta = {k: res.results[0][k] for k in ("rid1_out", "rid2_out", "c1_out", "c2_out")}
    nc2 = build_moec()
    in_maps2 = []
    for c in range(NCORES):
        sl = slice(c * TC, (c + 1) * TC)
        tsl = slice(c * 16, (c + 1) * 16)
        in_maps2.append({
            "xs": np.ascontiguousarray(st["x_fm"][:, sl]),
            "eo": eo,
            "rid1": np.ascontiguousarray(meta["rid1_out"][:, tsl]),
            "rid2": np.ascontiguousarray(meta["rid2_out"][:, tsl]),
            "c1i": np.ascontiguousarray(meta["c1_out"][:, tsl]),
            "c2i": np.ascontiguousarray(meta["c2_out"][:, tsl]),
        })
    res2 = run_bass_kernel_spmd(nc2, in_maps2, core_ids=list(range(NCORES)))
    st["x_fm"] = np.concatenate([r["x_out"] for r in res2.results], axis=1)
    return st




def build_txl_pre(p=None):
    own = p is None
    if own:
        p = Prog("txl_pre")
    xs = p.din("xs", [D, TC], F32)
    lnwb = p.din("lnwb", [128, 2 * DCH], F32)
    wq = p.din("wq", [128, 128], F32)
    wk = p.din("wk", [128, 128], F32)
    wv = p.din("wv", [128, 128], F32)
    wpT = p.din("wpT", [128, 128], F32)
    bqs = p.din("bqs", [128, 3], F32)
    ubvb = p.din("ubvb", [128, 16], F32)
    k_out = p.dout("k_out", [D, TC], BF16)
    qu_out = p.dout("qu_out", [D, TC], BF16)
    v_out = p.dout("v_out", [D, TC], BF16)
    zq_out = p.dout("zq_out", [DCH * TC, 128], F32)

    consts = emit_consts(p)
    lsem = p.sem("lsem")
    bsem = p.sem("bsem")
    gld = p.sem("gld")
    tsem = p.sem("tsem")
    vsem = p.sem("vsem")
    ssem = p.sem("ssem")

    xbuf = p.sb("xbuf", [128, DCH * TC], F32)
    x_ch = [xbuf[:, ch * TC:(ch + 1) * TC] for ch in range(DCH)]
    p.sync.dma_start(out=xbuf[:].rearrange("p (c t) -> p c t", c=DCH),
                     in_=xs.rearrange("(c p) t -> p c t", p=128)).then_inc(lsem, 16)
    lnw = p.sb("lnw", [128, 2 * DCH], F32)
    p.sync.dma_start(out=lnw[:], in_=lnwb[:, :]).then_inc(bsem, 16)
    bq_s = p.sb("bq_s", [128, 3], F32)
    p.sync.dma_start(out=bq_s[:], in_=bqs[:, :]).then_inc(bsem, 16)
    ub_s = p.sb("ub_s", [128, 16], F32)
    p.sync.dma_start(out=ub_s[:], in_=ubvb[:, :]).then_inc(bsem, 16)
    wbig = p.sb("wbig", [128, 4 * 128], BF16)
    p.gpsimd.dma_start(out=wbig[:, 0:128], in_=wq[:, :]).then_inc(gld, 16)
    p.gpsimd.dma_start(out=wbig[:, 128:256], in_=wk[:, :]).then_inc(gld, 16)
    p.gpsimd.dma_start(out=wbig[:, 256:384], in_=wv[:, :]).then_inc(gld, 16)
    p.gpsimd.dma_start(out=wbig[:, 384:512], in_=wpT[:, :]).then_inc(gld, 16)
    wld = p.val(gld)

    ln = LN(p)
    xn_bf = [p.sb(f"xn{ch}", [128, TC], BF16) for ch in range(DCH)]
    atmp = p.sb("atmp", [128, 512], F32)
    p.tensor.wait_ge(lsem, 16)
    p.scalar.wait_ge(lsem, 16)
    p.vector.wait_ge(lsem, 16)
    p.vector.wait_ge(bsem, 48)
    NG = TC // 512
    for g in range(NG):
        g0 = g * 512
        bm, br = ln.stats(x_ch, g0, 512, consts)
        for ch in range(DCH):
            p.vector.tensor_tensor(atmp[:, 0:512], x_ch[ch][:, g0:g0 + 512], bm[:, 0:512], op=ALU.add)
            p.vector.tensor_tensor(atmp[:, 0:512], atmp[:, 0:512], br[:, 0:512], op=ALU.mult)
            p.vector.tensor_scalar(xn_bf[ch][:, g0:g0 + 512], atmp[:, 0:512],
                                   lnw[:, ch:ch + 1], lnw[:, DCH + ch:DCH + ch + 1],
                                   op0=ALU.mult, op1=ALU.add).then_inc(vsem, 1)
    xn_ready = p.val(vsem)

    ps_pr = [p.ps("prA", [128, 512], F32), p.ps("prB", [128, 512], F32)]
    ps_z = p.ps("ps_z", [128, 512], F32)
    zpre = p.sb("zpre", [128, 512], BF16)
    stg = [p.sb(f"stg{i}", [128, 512], BF16) for i in range(3)]
    zstg = [p.sb(f"zstg{i}", [128, 128], F32) for i in range(2)]
    osem = [p.sem(f"os{i}") for i in range(3)]
    zsem = [p.sem("zs0"), p.sem("zs1")]
    o_free = [0, 0, 0]
    z_free = [0, 0]
    ps_free = [0, 0]
    zpsfree = [0]
    p.tensor.wait_ge(vsem, xn_ready)
    p.tensor.wait_ge(gld, wld)
    p.tensor.wait_ge(bsem, 48)
    si = 0
    for g in range(NG):
        g0 = g * 512
        for h in range(DCH):
            i2 = si % 2
            p.tensor.wait_ge(vsem, ps_free[i2])
            tq = p.tensor.matmul(ps_pr[i2][:, :], wbig[:, 0:128], xn_bf[h][:, g0:g0 + 512],
                                 start=True, stop=True).inc(tsem, 1)
            p.vector.wait_ge(tsem, tq)
            p.vector.wait_ge(osem[0], o_free[0])
            p.vector.tensor_scalar(stg[0][:], ps_pr[i2][:, :], bq_s[:, 0:1], ub_s[:, h:h + 1],
                                   op0=ALU.add, op1=ALU.add)
            zv = p.vector.tensor_scalar(zpre[:], ps_pr[i2][:, :], bq_s[:, 0:1],
                                        ub_s[:, 8 + h:9 + h], op0=ALU.add,
                                        op1=ALU.add).inc(vsem, 1)
            ps_free[i2] = zv
            p.sync.wait_ge(vsem, zv)
            o_free[0] = p.sync.dma_start(out=qu_out[h * 128:(h + 1) * 128, g0:g0 + 512],
                                         in_=stg[0][:]).inc(osem[0], 16)
            for j in range(4):
                jz = j % 2
                p.tensor.wait_ge(vsem, max(zv, zpsfree[0]))
                tz = p.tensor.matmul(ps_z[:, 0:128], zpre[:, j * 128:(j + 1) * 128],
                                     wbig[:, 384:512], start=True, stop=True).inc(tsem, 1)
                p.vector.wait_ge(tsem, tz)
                p.vector.wait_ge(zsem[jz], z_free[jz])
                ze = p.vector.tensor_copy(zstg[jz][:], ps_z[:, 0:128]).inc(vsem, 1)
                zpsfree[0] = ze
                p.sync.wait_ge(vsem, ze)
                z_free[jz] = p.sync.dma_start(
                    out=zq_out[h * TC + g0 + j * 128:h * TC + g0 + (j + 1) * 128, 0:128],
                    in_=zstg[jz][:]).inc(zsem[jz], 16)
            si += 1
            i2 = si % 2
            p.tensor.wait_ge(vsem, ps_free[i2])
            tk = p.tensor.matmul(ps_pr[i2][:, :], wbig[:, 128:256], xn_bf[h][:, g0:g0 + 512],
                                 start=True, stop=True).inc(tsem, 1)
            p.vector.wait_ge(tsem, tk)
            p.vector.wait_ge(osem[1], o_free[1])
            kv2 = p.vector.tensor_scalar(stg[1][:], ps_pr[i2][:, :], bq_s[:, 1:2], None,
                                         op0=ALU.add).inc(vsem, 1)
            ps_free[i2] = kv2
            p.sync.wait_ge(vsem, kv2)
            o_free[1] = p.sync.dma_start(out=k_out[h * 128:(h + 1) * 128, g0:g0 + 512],
                                         in_=stg[1][:]).inc(osem[1], 16)
            si += 1
            i2 = si % 2
            p.tensor.wait_ge(vsem, ps_free[i2])
            tv2 = p.tensor.matmul(ps_pr[i2][:, :], wbig[:, 256:384], xn_bf[h][:, g0:g0 + 512],
                                  start=True, stop=True).inc(tsem, 1)
            p.vector.wait_ge(tsem, tv2)
            p.vector.wait_ge(osem[2], o_free[2])
            vv2 = p.vector.tensor_scalar(stg[2][:], ps_pr[i2][:, :], bq_s[:, 2:3], None,
                                         op0=ALU.add).inc(vsem, 1)
            ps_free[i2] = vv2
            p.sync.wait_ge(vsem, vv2)
            o_free[2] = p.sync.dma_start(out=v_out[h * 128:(h + 1) * 128, g0:g0 + 512],
                                         in_=stg[2][:]).inc(osem[2], 16)
            si += 1
    for sm in osem + zsem:
        p.sync.wait_ge(sm, p.val(sm))
    return p.finalize() if own else None


def run_txl_pre(st):
    nc = build_txl_pre()
    lnwb = np.ascontiguousarray(np.concatenate(
        [_packbias(st["L1_attn_b_ln_gamma"]), _packbias(st["L1_attn_b_ln_beta"])], axis=1))
    bqs = np.ascontiguousarray(np.stack(
        [st["L1_attn_b_bq"], st["L1_attn_b_bk"], st["L1_attn_b_bv"]], axis=1))
    ubvb = np.ascontiguousarray(np.concatenate(
        [st["L1_attn_b_u_bias"].T, st["L1_attn_b_v_bias"].T], axis=1))
    in_maps = []
    for c in range(NCORES):
        sl = slice(c * TC, (c + 1) * TC)
        in_maps.append({
            "xs": np.ascontiguousarray(st["x_fm"][:, sl]),
            "lnwb": lnwb, "bqs": bqs, "ubvb": ubvb,
            "wq": st["L1_attn_b_Wq"], "wk": st["L1_attn_b_Wk"], "wv": st["L1_attn_b_Wv"],
            "wpT": _fm(st["L1_attn_b_Wp"]),
        })
    res = run_bass_kernel_spmd(nc, in_maps, core_ids=list(range(NCORES)))
    st["txl_k"] = np.concatenate([r["k_out"] for r in res.results], axis=1)
    st["txl_qu"] = np.concatenate([r["qu_out"] for r in res.results], axis=1)
    st["txl_v"] = np.concatenate([r["v_out"] for r in res.results], axis=1)
    zq = np.zeros((T_ALL, D), np.float32)
    for c in range(NCORES):
        zc = res.results[c]["zq_out"].reshape(DCH, TC, 128)
        zq[c * TC:(c + 1) * TC] = np.transpose(zc, (1, 0, 2)).reshape(TC, D)
    st["txl_zq"] = zq
    return st



def build_txl_attn(p=None):
    own = p is None
    if own:
        p = Prog("txl_attn")
    NCH = 2
    JOFF = -2050
    JW = 18560
    kf = [p.din(f"kf{i}", [128, S], BF16) for i in range(NCH)]
    vf = [p.din(f"vf{i}", [128, S], BF16) for i in range(NCH)]
    quf = [p.din(f"quf{i}", [128, S], BF16) for i in range(NCH)]
    zqf = [p.din(f"zqf{i}", [S + 1, 128], F32) for i in range(NCH)]
    hbase = [p.din(f"hbase{i}", [1, 1], F32) for i in range(NCH)]
    a_out = [p.dout(f"a_out{i}", [128, S], BF16) for i in range(NCH)]

    consts = emit_consts(p)
    sqq = Seq(p)
    tsem = p.sem("tsem")
    vsem = p.sem("vsem")
    ssem = p.sem("ssem")
    gld = p.sem("gld")
    bsem = p.sem("bsem")
    khsem = p.sem("khsem")
    vhsem = p.sem("vhsem")
    qsem = p.sem("qsem")
    zsem = p.sem("zsem")
    stsem = p.sem("stsem")
    giota = p.sem("giota")

    qbuf = p.sb("qbuf", [128, S], BF16)

    par = p.sb("par", [128, 1], F32)
    pari = p.sb("pari", [128, 1], I32)
    iotai = p.sb("iotai", [128, 1], I32)
    ihalf = p.sb("ihalf", [128, 1], F32)
    iotf = p.sb("iotf", [128, 1], F32)
    p.gpsimd.iota(iotai[:], pattern=[[0, 1]], base=0, channel_multiplier=1).then_inc(giota, 1)
    p.vector.wait_ge(giota, 1)
    sqq.v_("tensor_scalar", pari[:], iotai[:], 1, None, op0=ALU.bitwise_and)
    sqq.v_("tensor_copy", par[:], pari[:])
    sqq.v_("tensor_copy", iotf[:], iotai[:])
    sqq.v_("tensor_tensor", ihalf[:], iotf[:], par[:], op=ALU.subtract)
    sqq.v_("tensor_scalar", ihalf[:], ihalf[:], 0.5, None, op0=ALU.mult)

    mio = p.sb("mio", [128, 128], F32)
    giota2 = p.sem("giota2")
    p.gpsimd.iota(mio[:], pattern=[[-1, 128]], base=0, channel_multiplier=1,
                  allow_small_or_imprecise_dtypes=True).then_inc(giota2, 1)
    p.vector.wait_ge(giota2, 1)
    flow = [p.sb(f"flow{i}", [128, 128], F32) for i in range(2)]
    fupp = [p.sb(f"fupp{i}", [128, 128], F32) for i in range(2)]
    fdia = p.sb("fdia", [128, 128], F32)
    for i, dd in enumerate((0, 128)):
        sqq.v_("tensor_scalar", flow[i][:], mio[:], float(-dd), None, op0=ALU.is_le)
        sqq.v_("tensor_scalar", fdia[:], mio[:], float(1 - dd), None, op0=ALU.is_equal)
        sqq.v_("tensor_tensor", fupp[i][:], flow[i][:], fdia[:], op=ALU.add)
        sqq.v_("tensor_scalar", fupp[i][:], fupp[i][:], -1.0, 1.0, op0=ALU.mult, op1=ALU.add)
    identf = p.sb("identf", [128, 128], F32)
    sqq.v_("tensor_copy", identf[:], consts["ident"][:])

    kh = p.sb("kh", [128, S], BF16)
    vh = p.sb("vh", [128, S], BF16)
    vt = p.sb("vt", [128, S], BF16)
    pext = p.sb("pext", [128, JW], BF16)
    qh1 = p.sb("qh1", [128, S], BF16)
    qh2 = p.sb("qh2", [128, S], BF16)
    attn = p.sb("attnb", [128, S], BF16)
    hb_c = p.sb("hb_c", [128, 1], F32)
    onecolb = p.sb("onecolb", [128, 1], BF16)
    p.vector.memset(onecolb[:], 1.0).then_inc(vsem, 1)

    omg = p.sb("omg", [128, 1], F32)
    sA = p.sb("sA", [128, 64], F32)
    cA = p.sb("cA", [128, 64], F32)
    sa_t = p.sb("sa_t", [128, 145], F32)
    ca_t = p.sb("ca_t", [128, 145], F32)
    Pt = p.sb("Pt", [128, 145], F32)
    Qt = p.sb("Qt", [128, 145], F32)
    sb_t = p.sb("sb_t", [128, 128], F32)
    cb_t = p.sb("cb_t", [128, 128], F32)
    tg1 = p.sb("tg1", [128, 145], F32)
    tg2 = p.sb("tg2", [128, 145], F32)
    tgi = p.sb("tgi", [128, 145], I32)
    wrow = p.sb("wrow", [1, 145], F32)
    zq_t = p.sb("zq_t", [128, 128], F32)
    zq_s = p.sb("zq_s", [128, 128], F32)
    rot = p.sb("rot", [128, 128], F32)
    rotb = p.sb("rotb", [128, 128], BF16)
    stA = p.sb("stA", [128, 128], F32)
    sS = p.sb("sS", [128, 512], F32)
    dcol = p.sb("dcol", [128, 1], F32)
    drow = p.sb("drow", [1, 128], F32)
    bcsb = p.sb("bcsb", [128, 128], F32)
    pT = [p.sb(f"pT{i}", [128, 512], BF16) for i in range(2)]

    ps_bc2 = p.ps("ps_bc2", [128, 512], F32)
    ps_tr = p.ps("ps_tr", [128, 1024], BF16)[:, 0:128]
    ps_S = [p.ps(f"psS{i}", [128, 512], F32) for i in range(2)]
    ps_p1 = p.ps("ps_p1", [128, 512], F32)
    ps_p2 = p.ps("ps_p2", [128, 512], F32)
    ps_pv = p.ps("ps_pv", [128, 512], F32)
    ps_dn = p.ps("ps_dn", [128, 512], F32)[:, 0:64]

    def fold_round(buf, w):
        sqq.v_("tensor_copy", tgi[:, 0:w], buf[:, 0:w])
        sqq.v_("tensor_copy", tg2[:, 0:w], tgi[:, 0:w])
        sqq.v_("tensor_tensor", buf[:, 0:w], buf[:, 0:w], tg2[:, 0:w], op=ALU.subtract)
        sqq.v_("tensor_scalar", tg2[:, 0:w], buf[:, 0:w], 0.5, None, op0=ALU.is_ge)
        sqq.v_("tensor_tensor", buf[:, 0:w], buf[:, 0:w], tg2[:, 0:w], op=ALU.subtract)
        sqq.v_("tensor_scalar", tg2[:, 0:w], buf[:, 0:w], -0.5, None, op0=ALU.is_le)
        sqq.v_("tensor_tensor", buf[:, 0:w], buf[:, 0:w], tg2[:, 0:w], op=ALU.add)

    NQT = S // 128
    NKT = S // 128
    LN1E4 = math.log(10000.0)

    for h in range(NCH):
        p.vector.wait_ge(tsem, p.val(tsem))
        p.vector.wait_ge(ssem, p.val(ssem))
        p.sync.wait_ge(tsem, p.val(tsem))
        hbl = p.sync.dma_start(out=hb_c[:], in_=hbase[h][0:1, 0:1].broadcast_to([128, 1])
                               ).inc(bsem, 16)
        p.vector.wait_ge(bsem, hbl)
        ql = p.sync.dma_start(out=qbuf[:], in_=quf[h][:, :]).inc(qsem, 16)
        p.tensor.wait_ge(qsem, ql)
        sqq.v_("tensor_scalar", tg1[:, 0:1], ihalf[:, 0:1], hb_c[:, 0:1], None, op0=ALU.add)
        sqq.s_("activation", out=omg[:, 0:1], in_=tg1[:, 0:1], func=AF.Exp, scale=-LN1E4 / 512.0)
        sqq.v_("tensor_scalar", omg[:, 0:1], omg[:, 0:1], 1.0 / TWO_PI, None, op0=ALU.mult)
        p.gpsimd.wait_ge(sqq.s, p.val(sqq.s))
        p.gpsimd.wait_ge(sqq.t, p.val(sqq.t))
        p.gpsimd.iota(wrow[0:1, 0:64], pattern=[[1, 64]], base=0, channel_multiplier=0,
                      allow_small_or_imprecise_dtypes=True).then_inc(giota, 1)
        p.vector.wait_ge(giota, p.val(giota))
        p.scalar.wait_ge(giota, p.val(giota))
        sqq.v_("tensor_scalar", wrow[0:1, 0:64], wrow[0:1, 0:64], hb_c[0:1, 0:1], None,
               op0=ALU.add)
        sqq.s_("activation", out=wrow[0:1, 0:64], in_=wrow[0:1, 0:64], func=AF.Exp,
               scale=-LN1E4 / 512.0)
        sqq.v_("tensor_scalar", wrow[0:1, 0:64], wrow[0:1, 0:64], 1.0 / TWO_PI, None,
               op0=ALU.mult)
        bb = sqq.mm(ps_bc2[:, 0:64], consts["ones"][0:1, :], wrow[0:1, 0:64],
                    start=True, stop=True)
        sqq.v_("tensor_scalar", sA[:, 0:64], ps_bc2[:, 0:64], iotf[:, 0:1], None, op0=ALU.mult)
        fold_round(sA, 64)
        sqq.v_("tensor_scalar", cA[:, 0:64], sA[:, 0:64], 0.25, None, op0=ALU.add)
        fold_round(cA, 64)
        sqq.s_("activation", out=sA[:, 0:64], in_=sA[:, 0:64], func=AF.Sin, scale=TWO_PI)
        sqq.s_("activation", out=cA[:, 0:64], in_=cA[:, 0:64], func=AF.Sin, scale=TWO_PI)
        p.gpsimd.wait_ge(sqq.s, p.val(sqq.s))
        p.gpsimd.wait_ge(sqq.t, p.val(sqq.t))
        p.gpsimd.iota(wrow[0:1, 0:128], pattern=[[1, 128]], base=0, channel_multiplier=0,
                      allow_small_or_imprecise_dtypes=True).then_inc(giota, 1)
        p.vector.wait_ge(giota, p.val(giota))
        p.tensor.wait_ge(giota, p.val(giota))
        bb = sqq.mm(ps_bc2[:, 0:128], consts["ones"][0:1, :], wrow[0:1, 0:128],
                    start=True, stop=True)
        sqq.v_("tensor_scalar", sb_t[:, :], ps_bc2[:, 0:128], omg[:, 0:1], None, op0=ALU.mult)
        fold_round(sb_t, 128)
        sqq.v_("tensor_scalar", cb_t[:, :], sb_t[:, :], 0.25, None, op0=ALU.add)
        fold_round(cb_t, 128)
        sqq.s_("activation", out=sb_t[:, :], in_=sb_t[:, :], func=AF.Sin, scale=TWO_PI)
        sqq.s_("activation", out=cb_t[:, :], in_=cb_t[:, :], func=AF.Sin, scale=TWO_PI)
        p.gpsimd.wait_ge(sqq.s, p.val(sqq.s))
        p.gpsimd.wait_ge(sqq.t, p.val(sqq.t))
        p.gpsimd.iota(wrow[0:1, 0:145], pattern=[[128, 145]], base=JOFF, channel_multiplier=0,
                      allow_small_or_imprecise_dtypes=True).then_inc(giota, 1)
        p.vector.wait_ge(giota, p.val(giota))
        p.tensor.wait_ge(giota, p.val(giota))
        bb = sqq.mm(ps_bc2[:, 0:145], consts["ones"][0:1, :], wrow[0:1, 0:145],
                    start=True, stop=True)
        sqq.v_("tensor_scalar", sa_t[:, :], ps_bc2[:, 0:145], omg[:, 0:1], None, op0=ALU.mult)
        fold_round(sa_t, 145)
        sqq.v_("tensor_scalar", ca_t[:, :], sa_t[:, :], 0.25, None, op0=ALU.add)
        fold_round(ca_t, 145)
        sqq.s_("activation", out=sa_t[:, :], in_=sa_t[:, :], func=AF.Sin, scale=TWO_PI)
        sqq.s_("activation", out=ca_t[:, :], in_=ca_t[:, :], func=AF.Sin, scale=TWO_PI)
        sqq.v_("tensor_tensor", tg1[:, 0:145], sa_t[:, :], ca_t[:, :], op=ALU.subtract)
        sqq.v_("tensor_scalar", tg1[:, 0:145], tg1[:, 0:145], par[:, 0:1], None, op0=ALU.mult)
        sqq.v_("tensor_tensor", Pt[:, :], sa_t[:, :], tg1[:, 0:145], op=ALU.subtract)
        sqq.v_("tensor_tensor", tg1[:, 0:145], ca_t[:, :], sa_t[:, :], op=ALU.add)
        sqq.v_("tensor_scalar", tg1[:, 0:145], tg1[:, 0:145], par[:, 0:1], None, op0=ALU.mult)
        sqq.v_("tensor_tensor", Qt[:, :], ca_t[:, :], tg1[:, 0:145], op=ALU.subtract)
        for a in range(145):
            w = min(128, JW - a * 128)
            sqq.v_("tensor_scalar", sS[:, 0:w], cb_t[:, 0:w], Pt[:, a:a + 1], None, op0=ALU.mult)
            sqq.v_("scalar_tensor_tensor", pext[:, a * 128:a * 128 + w], sb_t[:, 0:w],
                   Qt[:, a:a + 1], sS[:, 0:w], op0=ALU.mult, op1=ALU.add)
        p.sync.wait_ge(tsem, p.val(tsem))
        p.sync.wait_ge(sqq.t, p.val(sqq.t))
        kv_ = p.sync.dma_start(out=kh[:], in_=kf[h][:, :]).inc(khsem, 16)
        vv_ = p.sync.dma_start(out=vh[:], in_=vf[h][:, :]).inc(vhsem, 16)
        p.tensor.wait_ge(vhsem, vv_)
        for kt in range(NKT):
            sqq.tr(ps_tr[:, 0:128], vh[:, kt * 128:(kt + 1) * 128], consts["ident"][:, :])
            sqq.v_("tensor_copy", vt[:, kt * 128:(kt + 1) * 128], ps_tr[:, 0:128])
        p.tensor.wait_ge(khsem, kv_)
        for j in range(NQT):
            p.sync.wait_ge(sqq.v, p.val(sqq.v))
            zl = p.sync.dma_start(out=zq_t[:, :],
                                  in_=zqf[h][j * 128:(j + 1) * 128, 0:128]).inc(zsem, 16)
            zl2 = p.sync.dma_start(out=zq_s[:, :],
                                   in_=zqf[h][j * 128 + 1:(j + 1) * 128 + 1, 0:128]
                                   ).inc(zsem, 16)
            p.vector.wait_ge(zsem, zl2)
            for (zsrc, dstq) in ((zq_t, qh1), (zq_s, qh2)):
                sqq.v_("tensor_tensor", rot[:, 0:128:2], zsrc[:, 0:128:2], cA[:, 0:64], op=ALU.mult)
                sqq.v_("tensor_tensor", tg1[:, 0:64], zsrc[:, 1:128:2], sA[:, 0:64], op=ALU.mult)
                sqq.v_("tensor_tensor", rot[:, 0:128:2], rot[:, 0:128:2], tg1[:, 0:64], op=ALU.add)
                sqq.v_("tensor_tensor", rot[:, 1:128:2], zsrc[:, 1:128:2], cA[:, 0:64], op=ALU.mult)
                sqq.v_("tensor_tensor", tg1[:, 0:64], zsrc[:, 0:128:2], sA[:, 0:64], op=ALU.mult)
                sqq.v_("tensor_tensor", rot[:, 1:128:2], rot[:, 1:128:2], tg1[:, 0:64],
                       op=ALU.subtract)
                sqq.v_("tensor_copy", rotb[:, :], rot[:, :])
                sqq.tr(ps_tr[:, 0:128], rotb[:, :], consts["ident"][:, :])
                sqq.v_("tensor_copy", dstq[:, j * 128:(j + 1) * 128], ps_tr[:, 0:128])
        qrot_done = p.val(sqq.v)
        p.tensor.wait_ge(sqq.v, qrot_done)

        S_done = [0, 0]
        pT_used = [0, 0]
        pv_evac = 0
        for qc in range(NQT // 4):
            q0l = qc * 512
            for kt in range(NKT):
                k0 = kt * 128
                sb_ = kt % 2
                p.tensor.wait_ge(ssem, S_done[sb_])
                straddles = []
                strad_plan = []
                for qi in range(4):
                    q0t = q0l + qi * 128
                    dd = k0 - q0t
                    jj1 = 10241 + k0 - q0t
                    jj2 = 2048 + k0 - q0t
                    qsl = slice(q0t, q0t + 128)
                    osl = slice(qi * 128, (qi + 1) * 128)
                    mmq = qbuf[:, q0t:q0t + 128]
                    if dd <= -128:
                        p.tensor.matmul(ps_S[sb_][:, osl], kh[:, k0:k0 + 128],
                                        mmq, start=True, stop=False)
                        mm = p.tensor.matmul(ps_S[sb_][:, osl], pext[:, jj1:jj1 + 128],
                                             qh1[:, qsl], start=False, stop=True)
                    elif dd >= 256:
                        p.tensor.matmul(ps_S[sb_][:, osl], kh[:, k0:k0 + 128],
                                        mmq, start=True, stop=False)
                        mm = p.tensor.matmul(ps_S[sb_][:, osl], pext[:, jj2:jj2 + 128],
                                             qh2[:, qsl], start=False, stop=True)
                    else:
                        ci = 0 if dd == 0 else 1
                        mm = p.tensor.matmul(ps_S[sb_][:, osl], kh[:, k0:k0 + 128],
                                             mmq, start=True, stop=True)
                        strad_plan.append((qi, ci, jj1, jj2, qsl, osl))
                        straddles.append((qi, ci))
                for (qi, ci, jj1, jj2, qsl, osl) in strad_plan:
                    p.tensor.wait_ge(vsem, p.val(vsem))
                    p.tensor.matmul(ps_p1[:, osl], pext[:, jj1:jj1 + 128],
                                    qh1[:, qsl], start=True, stop=True)
                    mm = p.tensor.matmul(ps_p2[:, osl], pext[:, jj2:jj2 + 128],
                                         qh2[:, qsl], start=True, stop=True)
                tv = mm.inc(tsem, 1)
                p.scalar.wait_ge(tsem, tv)
                p.scalar.wait_ge(tsem, pT_used[sb_])
                if not straddles:
                    sv = p.scalar.activation(out=pT[sb_][:, :], in_=ps_S[sb_][:, :],
                                             func=AF.Exp, scale=1.0 / 32.0).inc(ssem, 1)
                else:
                    p.vector.wait_ge(tsem, tv)
                    done = set()
                    for (qi, ci) in straddles:
                        osl = slice(qi * 128, (qi + 1) * 128)
                        p.vector.tensor_tensor(stA[:, :], ps_p1[:, osl], flow[ci][:, :],
                                               op=ALU.mult)
                        p.vector.tensor_tensor(bcsb[:, :], ps_p2[:, osl], fupp[ci][:, :],
                                               op=ALU.mult)
                        p.vector.tensor_tensor(stA[:, :], stA[:, :], bcsb[:, :], op=ALU.add)
                        p.vector.tensor_tensor(sS[:, osl], stA[:, :], ps_S[sb_][:, osl],
                                               op=ALU.add).then_inc(vsem, 1)
                        done.add(qi)
                    p.scalar.wait_ge(vsem, p.val(vsem))
                    sv = None
                    for qi in range(4):
                        osl = slice(qi * 128, (qi + 1) * 128)
                        src = sS[:, osl] if qi in done else ps_S[sb_][:, osl]
                        sv = p.scalar.activation(out=pT[sb_][:, osl], in_=src,
                                                 func=AF.Exp, scale=1.0 / 32.0).inc(ssem, 1)
                S_done[sb_] = sv
                p.tensor.wait_ge(ssem, sv)
                if kt == 0:
                    p.tensor.wait_ge(vsem, pv_evac)
                p.tensor.matmul(ps_pv[:, :], vt[:, k0:k0 + 128], pT[sb_][:, :],
                                start=(kt == 0), stop=(kt == NKT - 1))
                for qi in range(4):
                    osl = slice(qi * 128, (qi + 1) * 128)
                    mm = p.tensor.matmul(ps_dn[:, qc * 4 + qi:qc * 4 + qi + 1],
                                         pT[sb_][:, osl], onecolb[:, 0:1],
                                         start=(kt == 0 and qi == 0),
                                         stop=(kt == NKT - 1 and qi == 3),
                                         skip_group_check=True)
                pT_used[sb_] = mm.inc(tsem, 1)
            tvf = p.val(tsem)
            p.vector.wait_ge(tsem, tvf)
            for qi in range(4):
                q0t = q0l + qi * 128
                osl = slice(qi * 128, (qi + 1) * 128)
                p.vector.tensor_copy(dcol[:, 0:1], ps_dn[:, qc * 4 + qi:qc * 4 + qi + 1])
                rv = p.vector.reciprocal(dcol[:, 0:1], dcol[:, 0:1]).inc(vsem, 1)
                p.tensor.wait_ge(vsem, rv)
                t1 = p.tensor.transpose(ps_bc2[0:1, 0:128], dcol[:, 0:1],
                                        identf[:, :]).inc(tsem, 1)
                p.vector.wait_ge(tsem, t1)
                cv = p.vector.tensor_copy(drow[0:1, :], ps_bc2[0:1, 0:128]).inc(vsem, 1)
                p.tensor.wait_ge(vsem, cv)
                t2 = p.tensor.matmul(ps_bc2[:, 0:128], consts["ones"][0:1, :], drow[0:1, :],
                                     start=True, stop=True).inc(tsem, 1)
                p.vector.wait_ge(tsem, t2)
                p.vector.tensor_copy(bcsb[:, :], ps_bc2[:, 0:128])
                ev = p.vector.tensor_tensor(attn[:, q0t:q0t + 128],
                                            ps_pv[:, osl], bcsb[:, :], op=ALU.mult).inc(vsem, 1)
            pv_evac = p.val(vsem)
            p.tensor.wait_ge(vsem, pv_evac)
        av = p.val(vsem)
        p.sync.wait_ge(vsem, av)
        p.sync.dma_start(out=a_out[h][:, :], in_=attn[:]).then_inc(stsem, 16)
        p.vector.wait_ge(stsem, p.val(stsem))

    p.sync.wait_ge(stsem, p.val(stsem))
    return p.finalize() if own else None


def run_txl_attn(st):
    nc = build_txl_attn()
    in_maps = []
    for c in range(NCORES):
        im = {}
        for i in range(2):
            pair = 2 * c + i
            b, h = pair // 8, pair % 8
            bsl = slice(b * S, (b + 1) * S)
            rows = slice(h * 128, (h + 1) * 128)
            im[f"kf{i}"] = np.ascontiguousarray(st["txl_k"][rows, bsl])
            im[f"vf{i}"] = np.ascontiguousarray(st["txl_v"][rows, bsl])
            im[f"quf{i}"] = np.ascontiguousarray(st["txl_qu"][rows, bsl])
            zq = st["txl_zq"][b * S:(b + 1) * S, h * 128:(h + 1) * 128]
            zq = np.concatenate([zq, zq[-1:]], axis=0)
            im[f"zqf{i}"] = np.ascontiguousarray(zq)
            im[f"hbase{i}"] = np.full((1, 1), 64.0 * h, np.float32)
        in_maps.append(im)
    res = run_bass_kernel_spmd(nc, in_maps, core_ids=list(range(NCORES)))
    A = np.zeros((D, T_ALL), BF16NP)
    for c in range(NCORES):
        for i in range(2):
            pair = 2 * c + i
            b, h = pair // 8, pair % 8
            A[h * 128:(h + 1) * 128, b * S:(b + 1) * S] = res.results[c][f"a_out{i}"]
    st["txl_attn_fm"] = A
    return st



def build_txl_post(p=None):
    own = p is None
    if own:
        p = Prog("txl_post")
    xs = p.din("xs", [D, TC], F32)
    af = p.din("af", [D, TC], BF16)
    wo = p.din("wo", [D, D], F32)
    bop = p.din("bop", [128, DCH], F32)
    x_out = p.dout("x_out", [D, TC], F32)

    consts = emit_consts(p)
    tsem = p.sem("tsem")
    vsem = p.sem("vsem")
    gld = p.sem("gld")
    bsem = p.sem("bsem")
    asem = p.sem("asem")
    lsem = p.sem("lsem")

    abuf = p.sb("abuf", [128, DCH * TC], BF16)
    p.sync.dma_start(out=abuf[:].rearrange("p (c t) -> p c t", c=DCH),
                     in_=af.rearrange("(c p) t -> p c t", p=128)).then_inc(asem, 16)
    xbuf = p.sb("xbuf", [128, DCH * TC], F32)
    x_ch = [xbuf[:, ch * TC:(ch + 1) * TC] for ch in range(DCH)]
    p.sync.dma_start(out=xbuf[:].rearrange("p (c t) -> p c t", c=DCH),
                     in_=xs.rearrange("(c p) t -> p c t", p=128)).then_inc(lsem, 16)
    wo_s = p.sb("wo_s", [128, DCH * D], BF16)
    p.gpsimd.dma_start(out=wo_s[:].rearrange("p (c t) -> p c t", c=DCH),
                       in_=wo.rearrange("(c p) t -> p c t", p=128)).then_inc(gld, 16)
    bo_s = p.sb("bo_s", [128, DCH], F32)
    p.sync.dma_start(out=bo_s[:], in_=bop[:, :]).then_inc(bsem, 16)

    ps_pr = [p.ps("prA", [128, 512], F32), p.ps("prB", [128, 512], F32)]
    ostg = [p.sb(f"ostg{i}", [128, 512], F32) for i in range(2)]
    osem = [p.sem("oss0"), p.sem("oss1")]
    ps_free = [0, 0]
    o_free = [0, 0]
    p.tensor.wait_ge(asem, 16)
    p.tensor.wait_ge(lsem, 16)
    p.tensor.wait_ge(gld, 16)
    p.tensor.wait_ge(bsem, 16)
    p.vector.wait_ge(lsem, 16)
    si = 0
    for g in range(TC // 512):
        g0 = g * 512
        for oc in range(DCH):
            i2 = si % 2
            p.tensor.wait_ge(vsem, ps_free[i2])
            for ich in range(DCH):
                mm = p.tensor.matmul(ps_pr[i2][:, :],
                                     wo_s[:, ich * D + oc * 128:ich * D + (oc + 1) * 128],
                                     abuf[:, ich * TC + g0:ich * TC + g0 + 512],
                                     start=(ich == 0), stop=(ich == DCH - 1))
            tv = mm.inc(tsem, 1)
            p.vector.wait_ge(tsem, tv)
            p.vector.wait_ge(osem[i2], o_free[i2])
            pf = p.vector.tensor_scalar(ostg[i2][:], ps_pr[i2][:, :], bo_s[:, oc:oc + 1],
                                        None, op0=ALU.add).inc(vsem, 1)
            ps_free[i2] = pf
            ov = p.vector.tensor_tensor(ostg[i2][:], ostg[i2][:],
                                        x_ch[oc][:, g0:g0 + 512], op=ALU.add).inc(vsem, 1)
            p.sync.wait_ge(vsem, ov)
            so = p.sync.dma_start(out=x_out[oc * 128:(oc + 1) * 128, g0:g0 + 512],
                                  in_=ostg[i2][:]).inc(osem[i2], 16)
            o_free[i2] = so
            si += 1
    for sm in osem:
        p.sync.wait_ge(sm, p.val(sm))
    return p.finalize() if own else None


def run_txl_post(st):
    nc = build_txl_post()
    in_maps = []
    for c in range(NCORES):
        sl = slice(c * TC, (c + 1) * TC)
        in_maps.append({
            "xs": np.ascontiguousarray(st["x_fm"][:, sl]),
            "af": np.ascontiguousarray(st["txl_attn_fm"][:, sl]),
            "wo": st["L1_attn_b_Wo"],
            "bop": _packbias(st["L1_attn_b_bo"]),
        })
    res = run_bass_kernel_spmd(nc, in_maps, core_ids=list(range(NCORES)))
    st["x_fm"] = np.concatenate([r["x_out"] for r in res.results], axis=1)
    return st


def run_txl(st):
    st = run_txl_pre(st)
    st = run_txl_attn(st)
    st = run_txl_post(st)
    return st



def build_moe_prep(p=None):
    own = p is None
    if own:
        p = Prog("moe_prep")
    xs = p.din("xs", [D, TC], F32)
    rtT = p.din("rtT", [D, 8], F32)
    htm_out = p.dout("htm_out", [TC, D], BF16)
    lg_out = p.dout("lg_out", [TC, 8], F32)

    consts = emit_consts(p)
    lsem = p.sem("lsem")
    bsem = p.sem("bsem")
    tsem = p.sem("tsem")
    vsem = p.sem("vsem")

    xbuf = p.sb("xbuf", [128, DCH * TC], F32)
    x_ch = [xbuf[:, ch * TC:(ch + 1) * TC] for ch in range(DCH)]
    p.sync.dma_start(out=xbuf[:].rearrange("p (c t) -> p c t", c=DCH),
                     in_=xs.rearrange("(c p) t -> p c t", p=128)).then_inc(lsem, 16)
    rt_s = p.sb("rt_s", [128, DCH * 8], F32)
    p.sync.dma_start(out=rt_s[:].rearrange("p (c t) -> p c t", c=DCH),
                     in_=rtT.rearrange("(c p) t -> p c t", p=128)).then_inc(bsem, 16)

    ln = LN(p)
    hf = [p.sb(f"hf{ch}", [128, 512], F32) for ch in range(DCH)]
    hb = [p.sb(f"hb{ch}", [128, 512], BF16) for ch in range(DCH)]
    atmp = p.sb("atmp", [128, 512], F32)
    stg_tm = [p.sb(f"stgtm{i}", [128, D], BF16) for i in range(2)]
    stg_lg = [p.sb(f"stglg{i}", [128, 8], F32) for i in range(2)]
    tmsem = [p.sem("tms0"), p.sem("tms1")]
    lgsem = [p.sem("lgs0"), p.sem("lgs1")]
    ps_t = p.ps("ps_t", [128, 1024], BF16)[:, 0:512]
    ps_l = p.ps("ps_l", [128, 512], F32)
    tm_free = [0, 0]
    lg_free = [0, 0]
    p.tensor.wait_ge(lsem, 16)
    p.scalar.wait_ge(lsem, 16)
    p.vector.wait_ge(lsem, 16)
    p.tensor.wait_ge(bsem, 16)
    for g in range(TC // 512):
        g0 = g * 512
        p.tensor.wait_ge(vsem, p.val(vsem))
        p.scalar.wait_ge(vsem, p.val(vsem))
        bm, br = ln.stats(x_ch, g0, 512, consts)
        for ch in range(DCH):
            p.vector.tensor_tensor(atmp[:, 0:512], x_ch[ch][:, g0:g0 + 512], bm[:, 0:512], op=ALU.add)
            p.vector.tensor_tensor(hf[ch][:, 0:512], atmp[:, 0:512], br[:, 0:512], op=ALU.mult)
            p.vector.tensor_copy(hb[ch][:, 0:512], hf[ch][:, 0:512]).then_inc(vsem, 1)
        hv = p.val(vsem)
        p.tensor.wait_ge(vsem, hv)
        for j in range(4):
            i2 = j % 2
            for ich in range(DCH):
                mm = p.tensor.matmul(ps_l[0:128, 0:8], hf[ich][:, j * 128:(j + 1) * 128],
                                     rt_s[:, ich * 8:(ich + 1) * 8],
                                     start=(ich == 0), stop=(ich == DCH - 1))
            tv = mm.inc(tsem, 1)
            p.vector.wait_ge(tsem, tv)
            p.vector.wait_ge(lgsem[i2], lg_free[i2])
            lgv = p.vector.tensor_copy(stg_lg[i2][:], ps_l[0:128, 0:8]).inc(vsem, 1)
            p.sync.wait_ge(vsem, lgv)
            lg_free[i2] = p.sync.dma_start(out=lg_out[g0 + j * 128:g0 + (j + 1) * 128, :],
                                           in_=stg_lg[i2][:]).inc(lgsem[i2], 16)
            p.vector.wait_ge(tmsem[i2], tm_free[i2])
            for ich in range(DCH):
                tt = p.tensor.transpose(ps_t[:, 0:128], hb[ich][:, j * 128:(j + 1) * 128],
                                        consts["ident"][:, :]).inc(tsem, 1)
                p.vector.wait_ge(tsem, tt)
                ev = p.vector.tensor_copy(stg_tm[i2][:, ich * 128:(ich + 1) * 128],
                                          ps_t[:, 0:128]).inc(vsem, 1)
                p.tensor.wait_ge(vsem, ev)
            p.sync.wait_ge(vsem, p.val(vsem))
            tm_free[i2] = p.sync.dma_start(out=htm_out[g0 + j * 128:g0 + (j + 1) * 128, :],
                                           in_=stg_tm[i2][:]).inc(tmsem[i2], 16)
    for sm in (tmsem[0], tmsem[1], lgsem[0], lgsem[1]):
        p.sync.wait_ge(sm, p.val(sm))
    return p.finalize() if own else None


def run_moe_prep(st):
    nc = build_moe_prep()
    in_maps = []
    for c in range(NCORES):
        sl = slice(c * TC, (c + 1) * TC)
        in_maps.append({
            "xs": np.ascontiguousarray(st["x_fm"][:, sl]),
            "rtT": _fm(st["L1_ffn_router_weight"]),
        })
    res = run_bass_kernel_spmd(nc, in_maps, core_ids=list(range(NCORES)))
    st["moe_h_tm"] = np.concatenate([r["htm_out"] for r in res.results], axis=0)
    st["moe_lg"] = np.concatenate([r["lg_out"] for r in res.results], axis=0)
    return st


def kernel(**inputs):
    st = prepare_state(inputs)
    st = run_attn_a_pre(st)
    st = run_attn_a(st)
    st = run_ffn0(st)
    st = run_gdn_pre(st)
    st = run_gdn(st)
    st = run_gdn_post(st)
    st = run_ffn0(st)
    st = run_lin_pre(st)
    st = run_lin_run(st)
    st = run_lin_post(st)
    st = run_moe(st, "1")
    st = run_txl_pre(st)
    st = run_txl_attn(st)
    st = run_txl_post(st)
    st = run_moe_prep(st)
    st = run_moe(st, "2")
    x = np.ascontiguousarray(st["x_fm"].T).reshape(NB, S, D)
    return x.astype(np.float32, copy=False)



GRP_ALL = [[0, 1, 2, 3, 4, 5, 6, 7]]
GRP_BATCH = [[0, 1, 2, 3], [4, 5, 6, 7]]
IXW = 176
AG_ADDR = None


def _ag(p, tag, src, rows, cols, dtype, groups, csem):
    g = len(groups[0])
    out = p.dtmp(f"ag_{tag}", [g * rows, cols], dtype,
                 addr_space=AG_ADDR if g == 8 else None)
    p.gpsimd.collective_compute("AllGather", ALU.bypass, groups,
                                [src[0:rows, 0:cols]], [out[:, :]]).then_inc(csem, 1)
    return out


def _agfence(p, csem):
    for eng in ("sync", "gpsimd", "tensor", "vector", "scalar"):
        getattr(p, eng).wait_ge(csem, p.val(csem))


def _gather(p, tag, items, src, nrows, width, dtype):
    sg = p.sem(f"{tag}_g")
    ss = p.sem(f"{tag}_s")
    bufs = [p.sb(f"{tag}_b{i}", [nrows, width], dtype) for i in range(2)]
    bv = [0, 0]
    for n, (dst, idx) in enumerate(items):
        i2 = n % 2
        p.gpsimd.wait_ge(ss, bv[i2])
        gv = p.gpsimd.indirect_dma_start(
            out=bufs[i2][0:nrows, 0:width], out_offset=None, in_=src,
            in_offset=bass.IndirectOffsetOnAxis(idx, 0)).inc(sg, 16)
        p.sync.wait_ge(sg, gv)
        bv[i2] = p.sync.dma_start(out=dst, in_=bufs[i2][0:nrows, 0:width]).inc(ss, 16)
    p.sync.wait_ge(ss, p.val(ss))


def build_fused(upto=None, dumps=()):
    p = Prog("fused")
    W = {}

    def din(name, shape, dtype=F32):
        W[name] = p.nc.dram_tensor(name, list(shape), dtype, kind="ExternalInput").ap()
        return W[name]

    din("xs", [D, TC])
    din("ixp", [128, IXW], I32)
    for nm in ("aa_wqT", "aa_wkT", "aa_wvT", "aa_woT"):
        din(nm, [D, D])
    for nm in ("aa_abq", "aa_abk", "aa_abv", "aa_abo"):
        din(nm, [8, 2 * D])
    din("aa_biasp", [128, 3 * DCH])
    din("aa_bop", [128, DCH])
    din("aa_rz", [1, 1])
    din("f0_wk", [D, D])
    din("f0_abf", [16, 2 * D])
    din("f0_bp", [128, DCH])
    for nm in ("gd_wq", "gd_wk", "gd_wv"):
        din(nm, [D, D])
    din("gd_wba", [D, 16])
    din("gd_lnp", [128, 2 * DCH])
    din("gd_wo", [D, D])
    din("gd_w1", [D, 4 * D])
    din("gd_w2", [4 * D, D])
    din("gd_lnp2", [128, 2 * DCH])
    din("gd_b1p", [128, 32])
    din("gd_b2p", [128, DCH])
    din("li_wqkvT", [D, 3 * D])
    din("li_bqp", [128, 24])
    din("li_scl", [1, 1])
    din("li_wpT", [D, D])
    din("li_bpp", [128, DCH])
    din("li_rtT", [D, 8])
    din("mo_preT", [D, 4 * D])
    din("mo_gateT", [D, 4 * D])
    din("mo_postT", [16 * 128, 16 * 128])
    din("mo_ohp", [128, 8])
    din("tx_lnwb", [128, 2 * DCH])
    for nm in ("tx_wq", "tx_wk", "tx_wv", "tx_wpT"):
        din(nm, [128, 128])
    din("tx_bqs", [128, 3])
    din("tx_ubvb", [128, 16])
    din("tx_wo", [D, D])
    din("tx_bop", [128, DCH])
    din("tx_hbase0", [1, 1])
    din("tx_hbase1", [1, 1])
    x_out = p.nc.dram_tensor("x_out", [D, TC], F32, kind="ExternalOutput").ap()

    consts = emit_consts(p)
    csem = p.sem("csem")
    ld0 = p.sem("ld0")
    ixs = p.sb("ixs_tab", [128, IXW], I32)
    p.sync.dma_start(out=ixs[:], in_=W["ixp"][:, :]).then_inc(ld0, 16)
    p.gpsimd.wait_ge(ld0, 16)

    xA = p.dtmp("xA", [D, TC], F32)
    xB = p.dtmp("xB", [D, TC], F32)

    steps = []

    def step(label, xafter):
        def deco(fn):
            steps.append((label, fn, xafter))
        return deco

    qd = p.dtmp("qd", [D, TC], BF16)
    kd = p.dtmp("kd", [D, TC], BF16)
    vd = p.dtmp("vd", [D, TC], BF16)
    lnxd = p.dtmp("lnxd", [D, TC], F32)
    kfd = p.dtmp("kfd", [D, S], BF16)
    vfd = p.dtmp("vfd", [D, S], BF16)

    @step("p0", None)
    def _():
        with p.phase("p0"):
            p.ext = {"xs": W["xs"], "wqT": W["aa_wqT"], "wkT": W["aa_wkT"],
                     "wvT": W["aa_wvT"], "abq": W["aa_abq"], "abk": W["aa_abk"],
                     "abv": W["aa_abv"], "biasp": W["aa_biasp"], "q_out": qd,
                     "k_out": kd, "v_out": vd, "lnx_out": lnxd}
            build_attn_a_pre(p)
            p.ext = None

    @step("s0", None)
    def _():
        with p.phase("s0"):
            kag = _ag(p, "k", kd, D, TC, BF16, GRP_BATCH, csem)
            vag = _ag(p, "v", vd, D, TC, BF16, GRP_BATCH, csem)
            _agfence(p, csem)
            ps0 = p.sem("ps0")
            p.sync.dma_start(out=kfd.rearrange("d (r t) -> d r t", r=4),
                             in_=kag.rearrange("(r d) t -> d r t", d=D)).then_inc(ps0, 16)
            p.sync.dma_start(out=vfd.rearrange("d (r t) -> d r t", r=4),
                             in_=vag.rearrange("(r d) t -> d r t", d=D)).then_inc(ps0, 16)
            p.sync.wait_ge(ps0, 32)

    @step("p1", xA)
    def _():
        with p.phase("p1"):
            p.ext = {"qf": qd, "kf": kfd, "vf": vfd, "lnxs": lnxd, "xs": W["xs"],
                     "woT": W["aa_woT"], "abo": W["aa_abo"], "bop": W["aa_bop"],
                     "rz": W["aa_rz"], "x2_out": xA}
            build_attn_a(p)
            p.ext = None

    @step("p2", xB)
    def _():
        with p.phase("p2"):
            p.ext = {"xs": xA, "wk": W["f0_wk"], "abf": W["f0_abf"],
                     "bp": W["f0_bp"], "x_out": xB}
            build_ffn0(p)
            p.ext = None

    y0d = p.dtmp("y0d", [D, TC], F32)
    gq = p.dtmp("gq", [D, TC], BF16)
    gk = p.dtmp("gk", [D, TC], BF16)
    gv_ = p.dtmp("gv", [D, TC], BF16)
    glnab = p.dtmp("glnab", [16, TC], F32)
    chq = [p.dtmp(f"chq{i}", [128, S], BF16) for i in range(2)]
    chk = [p.dtmp(f"chk{i}", [128, S], BF16) for i in range(2)]
    chv = [p.dtmp(f"chv{i}", [128, S], BF16) for i in range(2)]
    chln = [p.dtmp(f"chln{i}", [2, S], F32) for i in range(2)]
    god = [p.dtmp(f"god{i}", [128, S], BF16) for i in range(2)]
    osd = p.dtmp("osd", [D, TC], BF16)

    @step("p3", None)
    def _():
        with p.phase("p3"):
            p.ext = {"xs": xB, "wq": W["gd_wq"], "wk": W["gd_wk"], "wv": W["gd_wv"],
                     "wba": W["gd_wba"], "lnp": W["gd_lnp"], "y0_out": y0d,
                     "q_out": gq, "k_out": gk, "v_out": gv_, "lnab_out": glnab}
            build_gdn_pre(p)
            p.ext = None

    @step("s3", None)
    def _():
        with p.phase("s3"):
            qag = _ag(p, "gq", gq, D, TC, BF16, GRP_BATCH, csem)
            kag = _ag(p, "gk", gk, D, TC, BF16, GRP_BATCH, csem)
            vag = _ag(p, "gv", gv_, D, TC, BF16, GRP_BATCH, csem)
            lnag = _ag(p, "gln", glnab, 16, TC, F32, GRP_BATCH, csem)
            _agfence(p, csem)
            for nm, dsts, src in (("q", chq, qag), ("k", chk, kag), ("v", chv, vag)):
                items = [(dsts[i][:, r * TC:(r + 1) * TC], ixs[:, i * 4 + r:i * 4 + r + 1])
                         for i in range(2) for r in range(4)]
                _gather(p, f"g{nm}", items, src[:, :], 128, TC, BF16)
            items = [(chln[i][0:2, r * TC:(r + 1) * TC],
                      ixs[0:2, 160 + i * 4 + r:161 + i * 4 + r])
                     for i in range(2) for r in range(4)]
            _gather(p, "gln", items, lnag[:, :], 2, TC, F32)

    @step("p4", None)
    def _():
        with p.phase("p4"):
            ext = {}
            for i in range(2):
                ext[f"q{i}"] = chq[i]
                ext[f"k{i}"] = chk[i]
                ext[f"v{i}"] = chv[i]
                ext[f"lnr{i}"] = chln[i]
                ext[f"o{i}"] = god[i]
            p.ext = ext
            build_gdn_run(p)
            p.ext = None

    @step("s4", None)
    def _():
        with p.phase("s4"):
            oqb = p.dtmp("oqb", [2 * 512, TC], BF16)
            ps4 = p.sem("ps4")
            for i in range(2):
                p.sync.dma_start(
                    out=oqb[i * 512:(i + 1) * 512, :].rearrange("(qr pp) t -> pp qr t", pp=128),
                    in_=god[i][:, :].rearrange("pp (qr t) -> pp qr t", qr=4)).then_inc(ps4, 16)
            p.sync.wait_ge(ps4, 32)
            p.gpsimd.wait_ge(ps4, 32)
            oag = _ag(p, "o", oqb, 2 * 512, TC, BF16, GRP_BATCH, csem)
            _agfence(p, csem)
            items = [(osd[h * 128:(h + 1) * 128, :], ixs[:, 8 + h:9 + h])
                     for h in range(DCH)]
            _gather(p, "go", items, oag[:, :], 128, TC, BF16)

    @step("p5", xA)
    def _():
        with p.phase("p5"):
            p.ext = {"xs": xB, "y0s": y0d, "os": osd, "wo": W["gd_wo"],
                     "w1": W["gd_w1"], "w2": W["gd_w2"], "lnp": W["gd_lnp2"],
                     "b1p": W["gd_b1p"], "b2p": W["gd_b2p"], "x_out": xA}
            build_gdn_post(p)
            p.ext = None

    @step("p6", xB)
    def _():
        with p.phase("p6"):
            p.ext = {"xs": xA, "wk": W["f0_wk"], "abf": W["f0_abf"],
                     "bp": W["f0_bp"], "x_out": xB}
            build_ffn0(p)
            p.ext = None

    lq = p.dtmp("lq", [D, TC], F32)
    lk = p.dtmp("lk", [D, TC], F32)
    lkv = p.dtmp("lkv", [D, TC], F32)
    lcq = [p.dtmp(f"lcq{i}", [128, S], F32) for i in range(2)]
    lck = [p.dtmp(f"lck{i}", [128, S], F32) for i in range(2)]
    lckv = [p.dtmp(f"lckv{i}", [128, S], F32) for i in range(2)]
    lyd = [p.dtmp(f"lyd{i}", [128, S], BF16) for i in range(2)]
    lys = p.dtmp("lys", [D, TC], BF16)
    htm1 = p.dtmp("htm1", [TC, D], BF16)
    lg1 = p.dtmp("lg1", [TC, 8], F32)

    @step("p7", None)
    def _():
        with p.phase("p7"):
            p.ext = {"xs": xB, "wqkvT": W["li_wqkvT"], "bqp": W["li_bqp"],
                     "scl": W["li_scl"], "q_out": lq, "k_out": lk, "kv_out": lkv}
            build_lin_pre(p)
            p.ext = None

    @step("s7", None)
    def _():
        with p.phase("s7"):
            qag = _ag(p, "lq", lq, D, TC, F32, GRP_BATCH, csem)
            kag = _ag(p, "lk", lk, D, TC, F32, GRP_BATCH, csem)
            kvag = _ag(p, "lkv", lkv, D, TC, F32, GRP_BATCH, csem)
            _agfence(p, csem)
            for nm, dsts, src in (("q", lcq, qag), ("k", lck, kag), ("kv", lckv, kvag)):
                items = [(dsts[i][:, r * TC:(r + 1) * TC], ixs[:, 16 + i * 4 + r:17 + i * 4 + r])
                         for i in range(2) for r in range(4)]
                _gather(p, f"l{nm}", items, src[:, :], 128, TC, F32)

    @step("p8", None)
    def _():
        with p.phase("p8"):
            ext = {}
            for i in range(2):
                ext[f"qi{i}"] = lcq[i]
                ext[f"ki{i}"] = lck[i]
                ext[f"kvi{i}"] = lckv[i]
                ext[f"y{i}"] = lyd[i]
            p.ext = ext
            build_lin_run(p)
            p.ext = None

    @step("s8", None)
    def _():
        with p.phase("s8"):
            yqb = p.dtmp("yqb", [2 * 512, TC], BF16)
            ps8 = p.sem("ps8")
            for i in range(2):
                p.sync.dma_start(
                    out=yqb[i * 512:(i + 1) * 512, :].rearrange("(qr pp) t -> pp qr t", pp=128),
                    in_=lyd[i][:, :].rearrange("pp (qr t) -> pp qr t", qr=4)).then_inc(ps8, 16)
            p.sync.wait_ge(ps8, 32)
            p.gpsimd.wait_ge(ps8, 32)
            yag = _ag(p, "y", yqb, 2 * 512, TC, BF16, GRP_BATCH, csem)
            _agfence(p, csem)
            items = [(lys[j * 128:(j + 1) * 128, :], ixs[:, 24 + j:25 + j])
                     for j in range(DCH)]
            _gather(p, "gy", items, yag[:, :], 128, TC, BF16)

    @step("p9", xA)
    def _():
        with p.phase("p9"):
            p.ext = {"xs": xB, "ys": lys, "wpT": W["li_wpT"], "bpp": W["li_bpp"],
                     "rtT": W["li_rtT"], "x_out": xA, "htm_out": htm1, "lg_out": lg1}
            build_lin_post(p)
            p.ext = None

    def moe_steps(tag, htm_d, lg_d, x_in, x_to):
        htmag = p.dtmp(f"htmag{tag}", [T_ALL, D], BF16, addr_space=AG_ADDR)
        lgag = p.dtmp(f"lgag{tag}", [T_ALL, 8], F32, addr_space=AG_ADDR)
        eod = p.dtmp(f"eod{tag}", [MOE_CAP, D], BF16)
        mraw = [p.dtmp(f"mraw{j}{tag}", [128, NTILE], F32) for j in range(4)]
        eoag = p.dtmp(f"eoag{tag}", [NEXP * MOE_CAP + 128, D], BF16, addr_space=AG_ADDR)
        mslc = [p.dtmp(f"mslc{j}{tag}", [128, 16], F32) for j in range(4)]

        @step(f"sg{tag}", None)
        def _():
            with p.phase(f"sg{tag}"):
                p.gpsimd.collective_compute("AllGather", ALU.bypass, GRP_ALL,
                                            [htm_d[:, :]], [htmag[:, :]]).then_inc(csem, 1)
                p.gpsimd.collective_compute("AllGather", ALU.bypass, GRP_ALL,
                                            [lg_d[:, :]], [lgag[:, :]]).then_inc(csem, 1)
                _agfence(p, csem)

        @step(f"mb{tag}", None)
        def _():
            with p.phase(f"mb{tag}"):
                p.ext = {"lg": lgag, "htm": htmag, "preT": W["mo_preT"],
                         "gateT": W["mo_gateT"], "postT": W["mo_postT"],
                         "ohp": W["mo_ohp"], "eo_out": eod, "rid1_out": mraw[0],
                         "rid2_out": mraw[1], "c1_out": mraw[2], "c2_out": mraw[3]}
                build_moeb(p)
                p.ext = None

        @step(f"sm{tag}", None)
        def _():
            with p.phase(f"sm{tag}"):
                p.gpsimd.collective_compute(
                    "AllGather", ALU.bypass, GRP_ALL, [eod[:, :]],
                    [eoag[0:NEXP * MOE_CAP, :]]).then_inc(csem, 1)
                zs = p.sem("zs")
                stz = p.sem("stz")
                zrow = p.sb("zrow", [1, D], BF16)
                zv = p.vector.memset(zrow[:], 0.0).inc(zs, 1)
                p.sync.wait_ge(zs, zv)
                p.sync.dma_start(out=eoag[NEXP * MOE_CAP:NEXP * MOE_CAP + 1, :],
                                 in_=zrow[:, :]).then_inc(stz, 16)
                lsm = p.sem("lsm")
                tsm = p.sem("tsm")
                vsm = p.sem("vsm")
                gsm = p.sem("gsm")
                ssm = p.sem("ssm")
                identf3 = p.sb("identf3", [128, 128], F32)
                p.vector.wait_ge(consts["sem"], consts["val"])
                idv = p.vector.tensor_copy(identf3[:], consts["ident"][:]).inc(vsm, 1)
                mbuf = p.sb("mbuf", [128, 128], F32)
                mtt = p.sb("mtt", [16, 128], F32)
                mo_ = p.sb("mo", [128, 16], F32)
                pst = p.ps("pst", [128, 512], F32)
                mdt = p.dtmp("mdt", [128, 128], F32)
                for j in range(4):
                    if j:
                        for eng in ("sync", "gpsimd", "tensor", "vector"):
                            getattr(p, eng).wait_ge(ssm, p.val(ssm))
                    lv = p.sync.dma_start(out=mbuf[:], in_=mraw[j][:, :]).inc(lsm, 16)
                    p.tensor.wait_ge(lsm, lv)
                    p.tensor.wait_ge(vsm, idv)
                    tv = p.tensor.transpose(pst[:, 0:128], mbuf[:], identf3[:]).inc(tsm, 1)
                    p.vector.wait_ge(tsm, tv)
                    vv = p.vector.tensor_copy(mbuf[:], pst[:, 0:128]).inc(vsm, 1)
                    p.sync.wait_ge(vsm, vv)
                    sv = p.sync.dma_start(out=mdt[:, :], in_=mbuf[:]).inc(lsm, 16)
                    p.gpsimd.wait_ge(lsm, sv)
                    gv2 = p.gpsimd.indirect_dma_start(
                        out=mtt[:, :], out_offset=None, in_=mdt[:, :],
                        in_offset=bass.IndirectOffsetOnAxis(ixs[0:16, 168:169], 0)
                        ).inc(gsm, 16)
                    p.tensor.wait_ge(gsm, gv2)
                    tv2 = p.tensor.transpose(pst[:, 128:144], mtt[0:16, :],
                                             identf3[0:16, 0:16]).inc(tsm, 1)
                    p.vector.wait_ge(tsm, tv2)
                    vv2 = p.vector.tensor_copy(mo_[:], pst[:, 128:144]).inc(vsm, 1)
                    p.sync.wait_ge(vsm, vv2)
                    p.sync.dma_start(out=mslc[j][:, :], in_=mo_[:]).then_inc(ssm, 16)
                p.sync.wait_ge(ssm, p.val(ssm))
                p.sync.wait_ge(stz, 16)
                _agfence(p, csem)

        @step(f"mc{tag}", x_to)
        def _():
            with p.phase(f"mc{tag}"):
                p.ext = {"xs": x_in, "eo": eoag, "rid1": mslc[0], "rid2": mslc[1],
                         "c1i": mslc[2], "c2i": mslc[3], "x_out": x_to}
                build_moec(p)
                p.ext = None

    moe_steps("1", htm1, lg1, xA, xB)

    tk = p.dtmp("tk", [D, TC], BF16)
    tqu = p.dtmp("tqu", [D, TC], BF16)
    tv_ = p.dtmp("tvv", [D, TC], BF16)
    tzq = p.dtmp("tzq", [DCH * TC, 128], F32)
    tck = [p.dtmp(f"tck{i}", [128, S], BF16) for i in range(2)]
    tcv = [p.dtmp(f"tcv{i}", [128, S], BF16) for i in range(2)]
    tcq = [p.dtmp(f"tcq{i}", [128, S], BF16) for i in range(2)]
    tczq = [p.dtmp(f"tczq{i}", [S + 1, 128], F32) for i in range(2)]
    tad = [p.dtmp(f"tad{i}", [128, S], BF16) for i in range(2)]
    atd = p.dtmp("atd", [D, TC], BF16)
    htm2 = p.dtmp("htm2", [TC, D], BF16)
    lg2 = p.dtmp("lg2", [TC, 8], F32)

    @step("p12", None)
    def _():
        with p.phase("p12"):
            p.ext = {"xs": xB, "lnwb": W["tx_lnwb"], "wq": W["tx_wq"],
                     "wk": W["tx_wk"], "wv": W["tx_wv"], "wpT": W["tx_wpT"],
                     "bqs": W["tx_bqs"], "ubvb": W["tx_ubvb"], "k_out": tk,
                     "qu_out": tqu, "v_out": tv_, "zq_out": tzq}
            build_txl_pre(p)
            p.ext = None

    @step("s12", None)
    def _():
        with p.phase("s12"):
            kag = _ag(p, "tk", tk, D, TC, BF16, GRP_BATCH, csem)
            vag = _ag(p, "tv", tv_, D, TC, BF16, GRP_BATCH, csem)
            qag = _ag(p, "tq", tqu, D, TC, BF16, GRP_BATCH, csem)
            zag = _ag(p, "tz", tzq, DCH * TC, 128, F32, GRP_BATCH, csem)
            _agfence(p, csem)
            for nm, dsts, src in (("k", tck, kag), ("v", tcv, vag), ("q", tcq, qag)):
                items = [(dsts[i][:, r * TC:(r + 1) * TC], ixs[:, i * 4 + r:i * 4 + r + 1])
                         for i in range(2) for r in range(4)]
                _gather(p, f"t{nm}", items, src[:, :], 128, TC, BF16)
            sgz = p.sem("zq_g")
            ssz = p.sem("zq_s")
            zbufs = [p.sb(f"zq_b{i}", [128, 2048], F32) for i in range(2)]
            tz2 = zag.rearrange("(n k) c -> n (k c)", k=16)
            zv2 = [0, 0]
            for i in range(2):
                for r4 in range(4):
                    n4 = i * 4 + r4
                    i2 = n4 % 2
                    p.gpsimd.wait_ge(ssz, zv2[i2])
                    c0 = 32 + n4
                    gvz = p.gpsimd.indirect_dma_start(
                        out=zbufs[i2][:, :], out_offset=None, in_=tz2,
                        in_offset=bass.IndirectOffsetOnAxis(ixs[:, c0:c0 + 1], 0)
                        ).inc(sgz, 16)
                    p.sync.wait_ge(sgz, gvz)
                    sv = p.sync.dma_start(
                        out=tczq[i][r4 * 2048:(r4 + 1) * 2048, :].rearrange(
                            "(pp k) c -> pp (k c)", k=16),
                        in_=zbufs[i2][:, :]).inc(ssz, 16)
                    if r4 == 3:
                        sv = p.sync.dma_start(out=tczq[i][S:S + 1, :],
                                              in_=zbufs[i2][127:128, 1920:2048]
                                              ).inc(ssz, 16)
                    zv2[i2] = sv
            p.sync.wait_ge(ssz, p.val(ssz))

    @step("p13", None)
    def _():
        with p.phase("p13"):
            ext = {}
            for i in range(2):
                ext[f"kf{i}"] = tck[i]
                ext[f"vf{i}"] = tcv[i]
                ext[f"quf{i}"] = tcq[i]
                ext[f"zqf{i}"] = tczq[i]
                ext[f"hbase{i}"] = W[f"tx_hbase{i}"]
                ext[f"a_out{i}"] = tad[i]
            p.ext = ext
            build_txl_attn(p)
            p.ext = None

    @step("s13", None)
    def _():
        with p.phase("s13"):
            aqb = p.dtmp("aqb", [2 * 512, TC], BF16)
            ps13 = p.sem("ps13")
            for i in range(2):
                p.sync.dma_start(
                    out=aqb[i * 512:(i + 1) * 512, :].rearrange("(qr pp) t -> pp qr t", pp=128),
                    in_=tad[i][:, :].rearrange("pp (qr t) -> pp qr t", qr=4)).then_inc(ps13, 16)
            p.sync.wait_ge(ps13, 32)
            p.gpsimd.wait_ge(ps13, 32)
            aag = _ag(p, "a", aqb, 2 * 512, TC, BF16, GRP_BATCH, csem)
            _agfence(p, csem)
            items = [(atd[h * 128:(h + 1) * 128, :], ixs[:, 8 + h:9 + h])
                     for h in range(DCH)]
            _gather(p, "ga", items, aag[:, :], 128, TC, BF16)

    @step("p14", xA)
    def _():
        with p.phase("p14"):
            p.ext = {"xs": xB, "af": atd, "wo": W["tx_wo"], "bop": W["tx_bop"],
                     "x_out": xA}
            build_txl_post(p)
            p.ext = None

    @step("p15", None)
    def _():
        with p.phase("p15"):
            p.ext = {"xs": xA, "rtT": W["li_rtT"], "htm_out": htm2, "lg_out": lg2}
            build_moe_prep(p)
            p.ext = None

    moe_steps("2", htm2, lg2, xA, x_out)

    cur_x = W["xs"]
    for label, fn, xafter in steps:
        fn()
        if xafter is not None:
            cur_x = xafter
        if upto == label:
            with p.phase("dump"):
                dsem = p.sem("dsem")
                p.sync.dma_start(out=x_out[:, :], in_=cur_x[:, :]).then_inc(dsem, 16)
                for nm in dumps:
                    ap, shp, dt = p._dtmps[nm]
                    od = p.nc.dram_tensor(f"dmp_{nm}", shp, dt, kind="ExternalOutput").ap()
                    p.sync.dma_start(out=od[:, :], in_=ap[:, :]).then_inc(dsem, 16)
                p.sync.wait_ge(dsem, p.val(dsem))
            break
    return p.finalize()


def _fused_ixp(c):
    ix = np.zeros((128, IXW), np.int32)
    ar = np.arange(128, dtype=np.int32)
    b_c, qr_c = c // 4, c % 4
    for i in range(2):
        chain = 2 * c + i
        h = chain % 8
        for r in range(4):
            ix[:, i * 4 + r] = r * D + h * 128 + ar
            ix[0, 160 + i * 4 + r] = r * 16 + h
            ix[1, 160 + i * 4 + r] = r * 16 + 8 + h
        for r in range(4):
            ix[:, 32 + i * 4 + r] = (r * DCH * TC + h * TC) // 16 + ar
    for h in range(DCH):
        pair = b_c * 8 + h
        ix[:, 8 + h] = ((pair // 2) % 4) * 1024 + (pair % 2) * 512 + qr_c * 128 + ar
    for i in range(2):
        grp = 4 * c + 2 * i
        h0 = grp % 16
        for r in range(4):
            ix[:, 16 + i * 4 + r] = r * D + (h0 // 2) * 128 + ar
    for j in range(DCH):
        grp = b_c * 16 + 2 * j
        ix[:, 24 + j] = ((grp // 4) % 4) * 1024 + ((grp % 4) // 2) * 512 + qr_c * 128 + ar
    ix[0:16, 168] = c * 16 + ar[0:16]
    return np.ascontiguousarray(ix)


def fused_in_maps(st):
    lnp1 = np.ascontiguousarray(np.concatenate(
        [_packbias(st["L0_attn_b_ln1_w"]), _packbias(st["L0_attn_b_ln1_b"])], axis=1))
    lnp2 = np.ascontiguousarray(np.concatenate(
        [_packbias(st["L0_attn_b_ln2_w"]), _packbias(st["L0_attn_b_ln2_b"])], axis=1))
    b1p = np.ascontiguousarray(np.stack(
        [st["L0_attn_b_b1"][i * 128:(i + 1) * 128] for i in range(32)], axis=1))
    lnwb = np.ascontiguousarray(np.concatenate(
        [_packbias(st["L1_attn_b_ln_gamma"]), _packbias(st["L1_attn_b_ln_beta"])], axis=1))
    bqs = np.ascontiguousarray(np.stack(
        [st["L1_attn_b_bq"], st["L1_attn_b_bk"], st["L1_attn_b_bv"]], axis=1))
    ubvb = np.ascontiguousarray(np.concatenate(
        [st["L1_attn_b_u_bias"].T, st["L1_attn_b_v_bias"].T], axis=1))
    common = {
        "aa_wqT": _fm(st["L0_attn_a_Wq"]), "aa_wkT": _fm(st["L0_attn_a_Wk"]),
        "aa_wvT": _fm(st["L0_attn_a_Wv"]), "aa_woT": _fm(st["L0_attn_a_Wo"]),
        "aa_abq": _packab(st["L0_attn_a_Aq"], st["L0_attn_a_Bq"]),
        "aa_abk": _packab(st["L0_attn_a_Ak"], st["L0_attn_a_Bk"]),
        "aa_abv": _packab(st["L0_attn_a_Av"], st["L0_attn_a_Bv"]),
        "aa_abo": _packab(st["L0_attn_a_Ao"], st["L0_attn_a_Bo"]),
        "aa_biasp": _packbias(st["L0_attn_a_bq"], st["L0_attn_a_bk"], st["L0_attn_a_bv"]),
        "aa_bop": _packbias(st["L0_attn_a_bo"]),
        "aa_rz": st["L0_attn_a_rezero"].reshape(1, 1),
        "f0_wk": st["L0_ffn_kernel"],
        "f0_abf": np.ascontiguousarray(
            np.concatenate([st["L0_ffn_a"].T, st["L0_ffn_b"]], axis=1)),
        "f0_bp": _packbias(st["L0_ffn_bias"]),
        "gd_wq": st["L0_attn_b_Wq"], "gd_wk": st["L0_attn_b_Wk"],
        "gd_wv": st["L0_attn_b_Wv"],
        "gd_wba": np.ascontiguousarray(np.concatenate(
            [st["L0_attn_b_Wb"], st["L0_attn_b_Wa"]], axis=1)),
        "gd_lnp": lnp1, "gd_wo": st["L0_attn_b_Wo"], "gd_w1": st["L0_attn_b_W1"],
        "gd_w2": st["L0_attn_b_W2"], "gd_lnp2": lnp2, "gd_b1p": b1p,
        "gd_b2p": _packbias(st["L0_attn_b_b2"]),
        "li_wqkvT": _fm(st["L1_attn_a_Wqkv"]),
        "li_bqp": np.ascontiguousarray(np.stack(
            [st["L1_attn_a_bqkv"][i * 128:(i + 1) * 128] for i in range(24)], axis=1)),
        "li_scl": st["L1_attn_a_scale"].reshape(1, 1),
        "li_wpT": _fm(st["L1_attn_a_Wproj"]),
        "li_bpp": _packbias(st["L1_attn_a_bproj"]),
        "li_rtT": _fm(st["L1_ffn_router_weight"]),
        "mo_postT": None,
        "tx_lnwb": lnwb, "tx_bqs": bqs, "tx_ubvb": ubvb,
        "tx_wq": st["L1_attn_b_Wq"], "tx_wk": st["L1_attn_b_Wk"],
        "tx_wv": st["L1_attn_b_Wv"], "tx_wpT": _fm(st["L1_attn_b_Wp"]),
        "tx_wo": st["L1_attn_b_Wo"], "tx_bop": _packbias(st["L1_attn_b_bo"]),
    }
    del common["mo_postT"]
    in_maps = []
    for c in range(NCORES):
        sl = slice(c * TC, (c + 1) * TC)
        im = dict(common)
        im["xs"] = np.ascontiguousarray(st["x_fm"][:, sl])
        im["ixp"] = _fused_ixp(c)
        im["mo_preT"] = _fm(st["L1_ffn_ff_pre_act_weight"][c])
        im["mo_gateT"] = _fm(st["L1_ffn_gate_weight"][c])
        im["mo_postT"] = _packpost(st["L1_ffn_ff_post_act_weight"][c])
        im["mo_ohp"] = np.ascontiguousarray(
            np.broadcast_to(np.eye(NEXP, dtype=np.float32)[c], (128, NEXP)))
        for i in range(2):
            im[f"tx_hbase{i}"] = np.full((1, 1), 64.0 * ((2 * c + i) % 8), np.float32)
        in_maps.append(im)
    return in_maps


def kernel_fused(**inputs):
    st = prepare_state(inputs)
    nc = build_fused()
    in_maps = fused_in_maps(st)
    res = run_bass_kernel_spmd(nc, in_maps, core_ids=list(range(NCORES)))
    x_fm = np.concatenate([r["x_out"] for r in res.results], axis=1)
    x = np.ascontiguousarray(x_fm.T).reshape(NB, S, D)
    return x.astype(np.float32, copy=False)


kernel = kernel_fused
